# version 02
# baseline (speedup 1.0000x reference)
import math
from contextlib import ExitStack

import numpy as np
import ml_dtypes

import concourse.bass as bass
import concourse.mybir as mybir
from concourse.bass import IndirectOffsetOnAxis
from concourse.bass_utils import run_bass_kernel_spmd

DT = mybir.dt
F32 = DT.float32
BF16 = DT.bfloat16
U32 = DT.uint32
AF = mybir.ActivationFunctionType
ALU = mybir.AluOpType
AX = mybir.AxisListType
BF = ml_dtypes.bfloat16

NCORES = 8
B_LOC = 2
T = 2048
TOK = B_LOC * T
D = 1024
H = 16
DH = 64
NB = 32
FF = 4096
NE = 16
CAP = 2048
BLEN = 2 * T - 1
BPAD = 4608


class Sem:
    def __init__(self, stack, nc, name):
        self.h = stack.enter_context(nc.semaphore(name))
        self.n = 0

    def inc(self, instr, k=1):
        instr.then_inc(self.h, k)
        self.n += k
        return self.n

    def wait(self, eng, n=None):
        eng.wait_ge(self.h, self.n if n is None else n)


def new_prog():
    nc = bass.Bass("TRN2", target_bir_lowering=False)
    stk = ExitStack()
    sems = {n: Sem(stk, nc, n) for n in ("LD", "GD", "TN", "VC", "SC", "GP")}
    return nc, stk, sems


class _NCProxy:

    def __init__(self, nc, prefix):
        self._nc = nc
        self._p = prefix

    def sbuf_tensor(self, name, *a, **k):
        return self._nc.sbuf_tensor(self._p + name, *a, **k)

    def psum_tensor(self, name, *a, **k):
        return self._nc.psum_tensor(self._p + name, *a, **k)

    def __getattr__(self, a):
        return getattr(self._nc, a)


def _phase_setup(ctx, prefix, decls):
    if ctx is None:
        nc, stk, sems = new_prog()
        reg = {}
    else:
        nc0, sems, reg = ctx
        nc = _NCProxy(nc0, prefix)
        stk = ExitStack()
    io = {}
    for name, shape, dt_, kind in decls:
        if name in reg:
            io[name] = reg[name]
        else:
            k = kind
            if ctx is not None and kind == "ExternalOutput" and name != "xout":
                k = "Internal"
            ap = nc.dram_tensor(name, shape, dt_, kind=k).ap()
            reg[name] = ap
            io[name] = ap
    return nc, stk, sems, io


def emit_fence(nc, sems):
    marks = {k: s.n for k, s in sems.items()}
    with nc.Block() as bf:
        for en in ("tensor", "vector", "scalar", "gpsimd", "sync"):
            def _mk(e, _m=marks):
                for k, s in sems.items():
                    e.wait_ge(s.h, _m[k])
            getattr(bf, en)(_mk)


def _bucket_of(d):
    nb = NB // 2
    n = -d
    ret = (n < 0).astype(np.int32) * nb
    n = np.abs(n)
    max_exact = nb // 2
    is_small = n < max_exact
    n_safe = np.maximum(n, 1).astype(np.float32)
    val = max_exact + (np.log(n_safe / max_exact) / math.log(128 / max_exact)
                       * (nb - max_exact)).astype(np.int32)
    val = np.minimum(val, nb - 1)
    return ret + np.where(is_small, n, val)


def host_consts():
    c = {}
    j = np.arange(BLEN)
    bk = _bucket_of((j - (T - 1)).astype(np.int64))
    oh = np.zeros((NB, BPAD), np.float32)
    oh[bk, j] = 1.0
    c["OH"] = oh.astype(BF)
    c["ONESB"] = np.ones((128, 512), BF)
    zm = np.zeros((4, 128, 512), np.float32)
    ki = np.arange(128)[:, None]
    qi = np.arange(512)[None, :]
    for jj in range(4):
        zm[jj] = (qi >= ki + jj * 128)
    c["ZM"] = zm.astype(BF)
    hd = DH
    inv_freq = (1.0 / 10000.0 ** (np.arange(0, hd, 2, dtype=np.float32) / hd))
    t = np.arange(T, dtype=np.float32)
    freqs = t[:, None] * inv_freq[None, :]
    rope = np.concatenate([np.sin(freqs), np.cos(freqs)], -1).astype(np.float32)
    sinv = rope[:, 0::2]
    cosv = rope[:, 1::2]
    rows = np.tile(np.arange(32), 4)
    c["CC"] = np.ascontiguousarray(cosv[:, rows].T)
    c["SS"] = np.ascontiguousarray(sinv[:, rows].T)
    c["LSTRICT"] = (np.arange(128)[:, None] < np.arange(128)[None, :]).astype(BF)
    c["IOTA16"] = np.tile(np.arange(NE, dtype=np.float32), (128, 1))
    c["LSU16"] = (np.arange(NE)[:, None] < np.arange(NE)[None, :]).astype(np.float32)
    c["LI8"] = (np.arange(8)[:, None] <= np.arange(8)[None, :]).astype(np.float32)
    c["SLOT"] = (np.arange(128)[:, None] + 128.0 * np.arange(16)[None, :]).astype(np.float32)
    return c


S_TT = 0
S_PRAW = 6144
S_PT = 8192
S_RCP = 10240
S_SEV = 10752
S_CTQ = 14848
S_XBF = 0
S_XSQ = 4096
S_MU = 8192
S_RS = 8704
S_TD = 9216
S_OHC = 0
S_BIA = 4608
S_EXB = 9216
SCRB_N = 18944


def emit_ln(nc, sems, onesb, scrB, scrF, src_dram, xn_sb, xn_out_dram, bofs, ps_sum, ps_sq):
    LD, GD, TN, VC, SC = (sems[n] for n in ("LD", "GD", "TN", "VC", "SC"))
    xbf = scrB[:, S_XBF:S_XBF + 4096]
    xsq = scrB[:, S_XSQ:S_XSQ + 4096]
    mu = scrB[:, S_MU:S_MU + 512]
    rs = scrB[:, S_RS:S_RS + 512]
    td = scrB[:, S_TD:S_TD + 512]
    ex2 = scrF[:, 0:512]
    mu2 = scrF[:, 512:1024]
    var = scrF[:, 1024:1536]
    src_r = src_dram.rearrange("(c p) t -> p c t", p=128)

    for blk in range(4):
        t0 = bofs + blk * 512
        tn_w, vc_w, sc_w = TN.n, VC.n, SC.n
        with nc.Block() as b:
            @b.gpsimd
            def _(g):
                g.wait_ge(TN.h, tn_w)
                g.wait_ge(VC.h, vc_w)
                g.wait_ge(SC.h, sc_w)
                GD.inc(g.dma_start(out=xbf.rearrange("p (c t) -> p c t", c=8),
                                   in_=src_r[:, :, t0:t0 + 512]), 16)

            @b.scalar
            def _(s):
                GD.wait(s)
                SC.inc(s.activation(out=xsq, in_=xbf, func=AF.Square))

            @b.tensor
            def _(tn):
                GD.wait(tn)
                for c in range(8):
                    tn.matmul(ps_sum[:], onesb[:, 0:128], xbf[:, c * 512:(c + 1) * 512],
                              start=(c == 0), stop=(c == 7))
                SC.wait(tn)
                for c in range(8):
                    i = tn.matmul(ps_sq[:], onesb[:, 0:128], xsq[:, c * 512:(c + 1) * 512],
                                  start=(c == 0), stop=(c == 7))
                TN.inc(i)

            @b.vector
            def _(v):
                TN.wait(v)
                v.tensor_scalar(out=mu, in0=ps_sum[:], scalar1=1.0 / D, scalar2=None, op0=ALU.mult)
                v.tensor_scalar(out=ex2, in0=ps_sq[:], scalar1=1.0 / D, scalar2=1e-5,
                                op0=ALU.mult, op1=ALU.add)
                v.tensor_tensor(out=mu2, in0=mu, in1=mu, op=ALU.mult)
                VC.inc(v.tensor_tensor(out=var, in0=ex2, in1=mu2, op=ALU.subtract))

            @b.scalar
            def _(s):
                VC.wait(s)
                s.activation(out=var, in_=var, func=AF.Ln)
                SC.inc(s.activation(out=rs, in_=var, func=AF.Exp, scale=-0.5))

            @b.vector
            def _(v):
                SC.wait(v)
                for c in range(8):
                    v.tensor_tensor(out=td, in0=xbf[:, c * 512:(c + 1) * 512], in1=mu,
                                    op=ALU.subtract)
                    i = v.tensor_tensor(out=xn_sb[:, c * T + blk * 512: c * T + blk * 512 + 512],
                                        in0=td, in1=rs, op=ALU.mult)
                VC.inc(i)

            @b.sync
            def _(sy):
                VC.wait(sy)
                if xn_out_dram is not None:
                    xr = xn_out_dram.rearrange("(c p) t -> p c t", p=128)
                    LD.inc(sy.dma_start(
                        out=xr[:, :, t0:t0 + 512],
                        in_=xn_sb[:].rearrange("p (c t) -> p c t", c=8)[:, :, blk * 512:(blk + 1) * 512]), 16)


def emit_attention_batch(nc, sems, sb, layer, b, s_out_dram, wo_bias_col=None):
    LD, GD, TN, VC, SC = (sems[n] for n in ("LD", "GD", "TN", "VC", "SC"))
    is_l1 = (layer == 1)
    scale = (DH ** -0.5) if is_l1 else 1.0
    q_sb, k_sb, v_sb, wo_sb = sb["q_sb"], sb["k_sb"], sb["v_sb"], sb["wo_sb"]
    onesb = sb["onesb"]
    scrB = sb["scrB"]
    psS = [sb[f"psS{i}"] for i in range(4)]
    ps_ctx, ps_den = sb["ps_ctx"], sb["ps_den"]
    ttile = scrB[:, S_TT:S_TT + 6144]
    praw = scrB[:, S_PRAW:S_PRAW + 2048]
    ptile = scrB[:, S_PT:S_PT + 2048]
    rcp = sb["scrF"][:, 0:512]
    sev = scrB[:, S_SEV:S_SEV + 4096]
    ctxq = scrB[:, S_CTQ:S_CTQ + 4096]

    def tile_kind(o):
        if is_l1:
            if o > 384:
                return "skip"
            return "mask" if o >= 0 else "plain"
        if -128 <= o <= 512:
            return "band"
        return "cpos" if o > 512 else "cneg"

    for qb in range(4):
        for hp in range(8):
            hA, hB = 2 * hp, 2 * hp + 1
            if not is_l1:
                vc_w = VC.n
                with nc.Block() as blb:
                    @blb.sync
                    def _(sy):
                        sy.wait_ge(VC.h, vc_w)
                        for hl, h in ((0, hA), (1, hB)):
                            src = bass.AP(sb["expb_d_t"], h * BPAD + 1408,
                                          [[1, 128], [128, 6], [1, 512]])
                            LD.inc(sy.dma_start(
                                out=ttile[:, hl * 3072:(hl + 1) * 3072]
                                .rearrange("p (j t) -> p j t", j=6),
                                in_=src), 16)
            ld_t = LD.n

            alive = [kt for kt in range(16) if tile_kind(kt * 128 - qb * 512) != "skip"]
            n = len(alive)
            tn0, sc0, vc0 = TN.n, SC.n, VC.n
            order = []
            for j in range(n):
                order.append(("qk", j))
                if j >= 2:
                    order.append(("pv", j - 2))
            order += [("pv", n - 2), ("pv", n - 1)] if n >= 2 else [("pv", 0)]
            tn_after_qk, tn_after_pv, sc_after_exp, vc_after_mul = {}, {}, {}, {}
            tn_c, sc_c, vc_c = tn0, sc0, vc0
            for op, j in order:
                tn_c += 1
                (tn_after_qk if op == "qk" else tn_after_pv)[j] = tn_c
            for j in range(n):
                kind = tile_kind(alive[j] * 128 - qb * 512)
                sc_c += 1
                sc_after_exp[j] = sc_c
                if kind in ("band", "mask"):
                    vc_c += 1
                    vc_after_mul[j] = vc_c

            with nc.Block() as bl:
                @bl.tensor
                def _(tn):
                    if hp == 0:
                        tn.wait_ge(SC.h, sc0)
                        tn.wait_ge(VC.h, vc0)
                    for op, j in order:
                        kt = alive[j]
                        o = kt * 128 - qb * 512
                        t = j % 2
                        if op == "qk":
                            if j >= 2:
                                tn.wait_ge(SC.h, sc_after_exp[j - 2])
                                if (j - 2) in vc_after_mul:
                                    tn.wait_ge(VC.h, vc_after_mul[j - 2])
                            for (ps, h) in ((psS[t], hA), (psS[2 + t], hB)):
                                if not is_l1:
                                    c = h // 2
                                    r0 = (h % 2) * 64
                                    i = tn.matmul(
                                        ps[:],
                                        k_sb[r0:r0 + 64, c * T + kt * 128: c * T + kt * 128 + 128],
                                        q_sb[r0:r0 + 64, c * T + qb * 512: c * T + qb * 512 + 512],
                                        start=True, stop=True)
                                else:
                                    gg, s_ = h // 4, h % 4
                                    r0 = 32 * s_
                                    tp = (96, 0) if s_ == 3 else None
                                    for ii, cch in enumerate((2 * gg, 2 * gg + 1)):
                                        i = tn.matmul(
                                            ps[:],
                                            k_sb[r0:r0 + 32, cch * T + kt * 128: cch * T + kt * 128 + 128],
                                            q_sb[r0:r0 + 32, cch * T + qb * 512: cch * T + qb * 512 + 512],
                                            start=(ii == 0), stop=(ii == 1), tile_position=tp)
                            i.then_inc(TN.h, 1)
                        else:
                            kind = tile_kind(o)
                            if kind in ("band", "mask"):
                                tn.wait_ge(VC.h, vc_after_mul[j])
                            else:
                                tn.wait_ge(SC.h, sc_after_exp[j])
                            first = (j == 0)
                            last = (j == n - 1)
                            for hl, h in ((0, hA), (1, hB)):
                                p_ap = ptile[:, (hl * 2 + t) * 512:(hl * 2 + t + 1) * 512]
                                vt = v_sb[:, kt * 1024 + h * 64: kt * 1024 + h * 64 + 64]
                                tn.matmul(ps_ctx[hl * 64:(hl + 1) * 64, :], vt, p_ap,
                                          start=first, stop=last, skip_group_check=True)
                                i = tn.matmul(ps_den[hl * 64:(hl + 1) * 64, :], onesb[:, 0:64], p_ap,
                                              start=first, stop=last, skip_group_check=True)
                            i.then_inc(TN.h, 1)
                    TN.n = tn_c

                @bl.scalar
                def _(s):
                    for j in range(n):
                        kt = alive[j]
                        o = kt * 128 - qb * 512
                        kind = tile_kind(o)
                        t = j % 2
                        if j == 0:
                            s.wait_ge(VC.h, vc0)
                        s.wait_ge(TN.h, tn_after_pv.get(j - 2, tn_after_qk[j]))
                        if (j - 2) in vc_after_mul:
                            s.wait_ge(VC.h, vc_after_mul[j - 2])
                        for (ps, h, hl) in ((psS[t], hA, 0), (psS[2 + t], hB, 1)):
                            if kind in ("band", "mask"):
                                dst = praw[:, (hl * 2 + t) * 512:(hl * 2 + t + 1) * 512]
                            else:
                                dst = ptile[:, (hl * 2 + t) * 512:(hl * 2 + t + 1) * 512]
                            bias = 0.0
                            if kind == "cpos":
                                bias = sb["cb"][:, 2 * h:2 * h + 1]
                            elif kind == "cneg":
                                bias = sb["cb"][:, 2 * h + 1:2 * h + 2]
                            i = s.activation(out=dst, in_=ps[:], func=AF.Exp, scale=scale, bias=bias)
                        i.then_inc(SC.h, 1)
                    SC.n = sc_c

                @bl.vector
                def _(v):
                    first_band = True
                    for j in range(n):
                        kt = alive[j]
                        o = kt * 128 - qb * 512
                        kind = tile_kind(o)
                        if kind not in ("band", "mask"):
                            continue
                        t = j % 2
                        if first_band:
                            if not is_l1:
                                v.wait_ge(LD.h, ld_t)
                            v.wait_ge(TN.h, tn0)
                            first_band = False
                        v.wait_ge(SC.h, sc_after_exp[j])
                        if j >= 2:
                            v.wait_ge(TN.h, tn_after_pv[j - 2])
                        for hl in range(2):
                            if kind == "band":
                                jdx = (o + 128) // 128
                                m = ttile[:, hl * 3072 + jdx * 512: hl * 3072 + (jdx + 1) * 512][:, ::-1]
                            else:
                                m = sb["zm"][:, (o // 128) * 512:(o // 128 + 1) * 512]
                            i = v.tensor_tensor(
                                out=ptile[:, (hl * 2 + t) * 512:(hl * 2 + t + 1) * 512],
                                in0=praw[:, (hl * 2 + t) * 512:(hl * 2 + t + 1) * 512], in1=m,
                                op=ALU.mult)
                        i.then_inc(VC.h, 1)
                    VC.n = vc_c

                @bl.vector
                def _(v):
                    v.wait_ge(TN.h, tn_after_pv[n - 1])
                    v.reciprocal(rcp, ps_den[:])
                    i = v.tensor_tensor(out=ctxq[:, hp * 512:(hp + 1) * 512],
                                        in0=ps_ctx[:], in1=rcp, op=ALU.mult)
                    VC.inc(i)
            vc_g = VC.n
            with nc.Block() as blg:
                @blg.tensor
                def _(tn):
                    tn.wait_ge(VC.h, vc_g)

        vc_w, sc_w, ld_w = VC.n, SC.n, LD.n
        with nc.Block() as bw:
            @bw.tensor
            def _(tn):
                tn.wait_ge(VC.h, vc_w)
                tn.wait_ge(SC.h, sc_w)
                for dc in range(8):
                    if dc >= 2:
                        tn.wait_ge(SC.h, sc_w + dc - 1)
                    for c in range(8):
                        i = tn.matmul(psS[dc % 2][:],
                                      wo_sb[:, c * 1024 + dc * 128: c * 1024 + dc * 128 + 128],
                                      ctxq[:, c * 512:(c + 1) * 512],
                                      start=(c == 0), stop=(c == 7))
                    TN.inc(i)

            @bw.scalar
            def _(s):
                base = TN.n - 8
                s.wait_ge(LD.h, ld_w)
                for dc in range(8):
                    s.wait_ge(TN.h, base + dc + 1)
                    bias = 0.0 if wo_bias_col is None else wo_bias_col[:, dc:dc + 1]
                    i = s.activation(out=sev[:, dc * 512:(dc + 1) * 512], in_=psS[dc % 2][:],
                                     func=AF.Copy if wo_bias_col is None else AF.Identity,
                                     bias=bias)
                    SC.inc(i)

            @bw.sync
            def _(sy):
                SC.wait(sy)
                sr = s_out_dram.rearrange("(c p) t -> p c t", p=128)
                LD.inc(sy.dma_start(
                    out=sr[:, :, b * T + qb * 512: b * T + qb * 512 + 512],
                    in_=sev.rearrange("p (c t) -> p c t", c=8)), 16)


A_DECLS = [("xT", [D, TOK], F32, "ExternalInput"), ("wq", [D, D], F32, "ExternalInput"),
           ("wk", [D, D], F32, "ExternalInput"), ("wv", [D, D], F32, "ExternalInput"),
           ("wo", [D, D], F32, "ExternalInput"), ("relemb", [NB, H], F32, "ExternalInput"),
           ("OH", [NB, BPAD], BF16, "ExternalInput"), ("ONESB", [128, 512], BF16, "ExternalInput"),
           ("s0", [D, TOK], BF16, "ExternalOutput"), ("xn0", [D, TOK], BF16, "ExternalOutput")]


def build_A(debug=False, ctx=None):
    nc, stk, sems, io = _phase_setup(ctx, "A_", A_DECLS)
    LD, GD, TN, VC, SC = (sems[n] for n in ("LD", "GD", "TN", "VC", "SC"))
    xT, wq, wk, wv, wo, relemb, OH, ONESB, s0, xn0 = (
        io[n] for n in ("xT", "wq", "wk", "wv", "wo", "relemb", "OH", "ONESB", "s0", "xn0"))

    if debug:
        dbg_expb = nc.dram_tensor("dbg_expb", [H, BPAD], BF16, kind="ExternalOutput").ap()
        dbg_cb = nc.dram_tensor("dbg_cb", [128, 32], BF16, kind="ExternalOutput").ap()
        dbg_k = nc.dram_tensor("dbg_k", [128, 8 * T], BF16, kind="ExternalOutput").ap()
        dbg_q = nc.dram_tensor("dbg_q", [128, 8 * T], BF16, kind="ExternalOutput").ap()
        dbg_v = nc.dram_tensor("dbg_v", [128, 16 * 1024], BF16, kind="ExternalOutput").ap()
        dbg_ctx = nc.dram_tensor("dbg_ctx", [128, 8 * 512], BF16, kind="ExternalOutput").ap()
    expb_d = nc.dram_tensor("expb_d", [H, BPAD], BF16, kind="Internal").ap()
    cvec_d = nc.dram_tensor("cvec_d", [H, 2], BF16, kind="Internal").ap()

    with stk:
        e = stk.enter_context
        onesb = e(nc.sbuf_tensor("onesb", [128, 512], BF16))
        xn_sb = e(nc.sbuf_tensor("xn_sb", [128, 8 * T], BF16))
        k_sb = e(nc.sbuf_tensor("k_sb", [128, 8 * T], BF16))
        q_sb = e(nc.sbuf_tensor("q_sb", [128, 8 * T], BF16))
        v_sb = e(nc.sbuf_tensor("v_sb", [128, 16 * 1024], BF16))
        wbuf = e(nc.sbuf_tensor("wbuf", [128, 8 * 1024], BF16))
        cb = e(nc.sbuf_tensor("cb", [128, 32], BF16))
        scrB = e(nc.sbuf_tensor("scrB", [128, SCRB_N], BF16))
        scrF = e(nc.sbuf_tensor("scrF", [128, 1536], F32))
        sb = {f"psS{i}": e(nc.psum_tensor(f"psS{i}", [128, 512], F32)) for i in range(4)}
        sb["ps_ctx"] = e(nc.psum_tensor("ps_ctx", [128, 512], F32))
        sb["ps_den"] = e(nc.psum_tensor("ps_den", [128, 512], F32))
        sb.update(onesb=onesb, q_sb=q_sb, k_sb=k_sb, v_sb=v_sb, wo_sb=wbuf,
                  scrB=scrB, scrF=scrF, cb=cb, expb_d_t=expb_d.tensor, zm=None)

        oh_sb = scrB[0:NB, S_OHC:S_OHC + BPAD]
        bias_sb = scrB[0:H, S_BIA:S_BIA + BPAD]
        expb_sb = scrB[0:H, S_EXB:S_EXB + BPAD]
        relb_sb = scrB[0:NB, 13824:13824 + H]
        with ExitStack() as st0:
            rel_sb = st0.enter_context(nc.sbuf_tensor("rel_sb", [NB, H], F32))
            NJ = BPAD // 512
            tn0, vc0 = TN.n, VC.n
            with nc.Block() as b0:
                @b0.sync
                def _(sy):
                    LD.inc(sy.dma_start(out=onesb[:], in_=ONESB), 16)
                    LD.inc(sy.dma_start(out=rel_sb[:], in_=relemb), 16)
                    LD.inc(sy.dma_start(out=oh_sb, in_=OH), 16)

                @b0.vector
                def _(v):
                    LD.wait(v)
                    VC.inc(v.tensor_copy(relb_sb, rel_sb[:]))

                @b0.tensor
                def _(tn):
                    tn.wait_ge(VC.h, vc0 + 1)
                    for jj in range(NJ):
                        if jj >= 2:
                            tn.wait_ge(VC.h, vc0 + jj)
                        i = tn.matmul(sb[f"psS{jj % 2}"][0:H, :], relb_sb,
                                      oh_sb[:, jj * 512:(jj + 1) * 512], start=True, stop=True)
                        TN.inc(i)

                @b0.vector
                def _(v):
                    for jj in range(NJ):
                        v.wait_ge(TN.h, tn0 + jj + 1)
                        i = v.tensor_copy(bias_sb[:, jj * 512:(jj + 1) * 512],
                                          sb[f"psS{jj % 2}"][0:H, :])
                        VC.inc(i)

                @b0.scalar
                def _(s):
                    VC.wait(s)
                    SC.inc(s.activation(out=expb_sb, in_=bias_sb, func=AF.Exp))

                @b0.sync
                def _(sy):
                    SC.wait(sy)
                    LD.inc(sy.dma_start(out=expb_d, in_=expb_sb), 16)
                    LD.inc(sy.dma_start(out=cvec_d[:, 0:1], in_=bias_sb[:, 4094:4095],
                                        allow_slow_non_contiguous=True), 16)
                    LD.inc(sy.dma_start(out=cvec_d[:, 1:2], in_=bias_sb[:, 0:1],
                                        allow_slow_non_contiguous=True), 16)
                    sy.wait_ge(LD.h, LD.n)
                    for h in range(H):
                        src = bass.AP(cvec_d.tensor, h * 2, [[0, 128], [1, 2]])
                        LD.inc(sy.dma_start(out=cb[:, 2 * h:2 * h + 2], in_=src), 16)

        for b in range(B_LOC):
            emit_ln(nc, sems, onesb, scrB, scrF, xT, xn_sb, xn0, b * T,
                    sb["psS0"], sb["psS1"])

            def load_w(src):
                tn_w = TN.n
                with nc.Block() as blw:
                    @blw.gpsimd
                    def _(g):
                        g.wait_ge(TN.h, tn_w)
                        GD.inc(g.dma_start(out=wbuf[:].rearrange("p (c f) -> p c f", c=8),
                                           in_=src.rearrange("(c p) f -> p c f", p=128)), 16)
                return GD.n

            def proj_kq(gd_mark, dst):
                tn0_, sc0_, vc0_ = TN.n, SC.n, VC.n
                with nc.Block() as bp:
                    @bp.tensor
                    def _(tn):
                        tn.wait_ge(GD.h, gd_mark)
                        tn.wait_ge(VC.h, vc0_)
                        tn.wait_ge(SC.h, sc0_)
                        for jj in range(32):
                            fc, nb_ = jj // 4, jj % 4
                            if jj >= 2:
                                tn.wait_ge(SC.h, sc0_ + jj - 1)
                            for dc in range(8):
                                i = tn.matmul(sb[f"psS{jj % 2}"][:],
                                              wbuf[:, dc * 1024 + fc * 128: dc * 1024 + fc * 128 + 128],
                                              xn_sb[:, dc * T + nb_ * 512: dc * T + nb_ * 512 + 512],
                                              start=(dc == 0), stop=(dc == 7))
                            TN.inc(i)

                    @bp.scalar
                    def _(s):
                        for jj in range(32):
                            fc, nb_ = jj // 4, jj % 4
                            s.wait_ge(TN.h, tn0_ + jj + 1)
                            i = s.activation(out=dst[:, fc * T + nb_ * 512: fc * T + nb_ * 512 + 512],
                                             in_=sb[f"psS{jj % 2}"][:], func=AF.Copy)
                            SC.inc(i)

            def proj_v(gd_mark):
                tn0_, sc0_, vc0_ = TN.n, SC.n, VC.n
                with nc.Block() as bp:
                    @bp.tensor
                    def _(tn):
                        tn.wait_ge(GD.h, gd_mark)
                        tn.wait_ge(VC.h, vc0_)
                        tn.wait_ge(SC.h, sc0_)
                        for jj in range(32):
                            tt, fb = jj // 2, jj % 2
                            if jj >= 2:
                                tn.wait_ge(VC.h, vc0_ + jj - 1)
                            for dc in range(8):
                                i = tn.matmul(sb[f"psS{jj % 2}"][:],
                                              xn_sb[:, dc * T + tt * 128: dc * T + tt * 128 + 128],
                                              wbuf[:, dc * 1024 + fb * 512: dc * 1024 + fb * 512 + 512],
                                              start=(dc == 0), stop=(dc == 7))
                            TN.inc(i)

                    @bp.vector
                    def _(v):
                        for jj in range(32):
                            tt, fb = jj // 2, jj % 2
                            v.wait_ge(TN.h, tn0_ + jj + 1)
                            i = v.tensor_copy(v_sb[:, tt * 1024 + fb * 512: tt * 1024 + (fb + 1) * 512],
                                              sb[f"psS{jj % 2}"][:])
                            VC.inc(i)

            proj_kq(load_w(wk), k_sb)
            proj_v(load_w(wv))
            proj_kq(load_w(wq), q_sb)
            gd_mo = load_w(wo)
            with nc.Block() as bwait:
                @bwait.tensor
                def _(tn):
                    tn.wait_ge(GD.h, gd_mo)
            emit_attention_batch(nc, sems, sb, 0, b, s0, wo_bias_col=None)

        if debug:
            with nc.Block() as bd:
                @bd.sync
                def _(sy):
                    sy.wait_ge(TN.h, TN.n)
                    sy.wait_ge(VC.h, VC.n)
                    sy.wait_ge(SC.h, SC.n)
                    LD.inc(sy.dma_start(out=scrB[0:H, 0:BPAD], in_=expb_d), 16)
                    sy.wait_ge(LD.h, LD.n)
                    LD.inc(sy.dma_start(out=dbg_expb, in_=scrB[0:H, 0:BPAD]), 16)
                    LD.inc(sy.dma_start(out=dbg_cb, in_=cb[:]), 16)
                    LD.inc(sy.dma_start(out=dbg_k, in_=k_sb[:]), 16)
                    LD.inc(sy.dma_start(out=dbg_q, in_=q_sb[:]), 16)
                    LD.inc(sy.dma_start(out=dbg_v, in_=v_sb[:]), 16)
                    LD.inc(sy.dma_start(out=dbg_ctx, in_=scrB[:, S_CTQ:S_CTQ + 4096]), 16)
        with nc.Block() as bf:
            @bf.sync
            def _(sy):
                sy.wait_ge(LD.h, LD.n)
                sy.wait_ge(GD.h, GD.n)
    return nc




B_DECLS = [("xT", [D, TOK], F32, "ExternalInput"), ("xn0", [D, TOK], BF16, "ExternalInput"),
           ("s0", [D, TOK], BF16, "ExternalInput"), ("wfcT", [D, FF], F32, "ExternalInput"),
           ("wprojT", [FF, D], F32, "ExternalInput"), ("lnw", [128, 32], F32, "ExternalInput"),
           ("lnb", [128, 32], F32, "ExternalInput"), ("ONESB", [128, 512], BF16, "ExternalInput"),
           ("ONESF", [128, 128], F32, "ExternalInput"), ("x1", [D, TOK], F32, "ExternalOutput")]


def build_B(debug=False, ctx=None):
    nc, stk, sems, io = _phase_setup(ctx, "B_", B_DECLS)
    LD, GD, TN, VC, SC, GP = (sems[n] for n in ("LD", "GD", "TN", "VC", "SC", "GP"))
    xT, xn0, s0, wfcT, wprojT, lnw, lnb, ONESB, ONESF, x1 = (
        io[n] for n in ("xT", "xn0", "s0", "wfcT", "wprojT", "lnw", "lnb", "ONESB", "ONESF", "x1"))
    if debug:
        wfcq_d = nc.dram_tensor("wfcq_d", [D, FF], BF16, kind="ExternalOutput").ap()
        wprojq_d = nc.dram_tensor("wprojq_d", [FF, D], BF16, kind="ExternalOutput").ap()
        dbg_h = nc.dram_tensor("dbg_h", [128, 32 * 256], BF16, kind="ExternalOutput").ap()
        dbg_mu = nc.dram_tensor("dbg_mu", [128, 256], BF16, kind="ExternalOutput").ap()
        dbg_rs = nc.dram_tensor("dbg_rs", [128, 256], BF16, kind="ExternalOutput").ap()
        dbg_cv = nc.dram_tensor("dbg_cv", [1, 1024], BF16, kind="ExternalOutput").ap()
        dbg_thr = nc.dram_tensor("dbg_thr", [128, 4], F32, kind="ExternalOutput").ap()
        dbg_parts = nc.dram_tensor("dbg_parts", [128, 66], F32, kind="ExternalOutput").ap()
    else:
        wfcq_d = nc.dram_tensor("wfcq_d", [D, FF], BF16, kind="Internal").ap()
        wprojq_d = nc.dram_tensor("wprojq_d", [FF, D], BF16, kind="Internal").ap()

    TB = 256
    NTB = TOK // TB

    with stk:
        e = stk.enter_context
        onesb = e(nc.sbuf_tensor("onesb", [128, 512], BF16))
        onesf = e(nc.sbuf_tensor("onesf", [128, 128], F32))
        lnw_sb = e(nc.sbuf_tensor("lnw_sb", [128, 32], F32))
        lnb_sb = e(nc.sbuf_tensor("lnb_sb", [128, 32], F32))
        lnb_bf = e(nc.sbuf_tensor("lnb_bf", [128, 32], BF16))
        wfcq = e(nc.sbuf_tensor("wfcq", [128, 8 * FF], BF16))
        wpjq = e(nc.sbuf_tensor("wpjq", [128, 32 * 1024], BF16))
        cvec = e(nc.sbuf_tensor("cvec", [128, 1024], BF16))
        h_sb = e(nc.sbuf_tensor("h_sb", [128, 32 * TB], BF16))
        hsq = e(nc.sbuf_tensor("hsq", [128, TB], BF16))
        xn_blk = e(nc.sbuf_tensor("xn_blk", [128, 8 * TB], BF16))
        s_blk = e(nc.sbuf_tensor("s_blk", [128, 8 * TB], BF16))
        x_blk = e(nc.sbuf_tensor("x_blk", [128, 8 * TB], F32))
        out_sb = e(nc.sbuf_tensor("out_sb", [128, 8 * TB], F32))
        g_sb = e(nc.sbuf_tensor("g_sb", [128, TB], BF16))
        t2_sb = e(nc.sbuf_tensor("t2_sb", [128, TB], BF16))
        mu_sb = e(nc.sbuf_tensor("mu_sb", [128, TB], BF16))
        rs_sb = e(nc.sbuf_tensor("rs_sb", [128, TB], BF16))
        td_sb = e(nc.sbuf_tensor("td_sb", [128, TB], BF16))
        scf = e(nc.sbuf_tensor("scf", [128, 3 * TB], F32))
        parts = e(nc.sbuf_tensor("parts", [128, 66], F32))
        thr2 = e(nc.sbuf_tensor("thr2", [128, 4], F32))
        ps = [e(nc.psum_tensor(f"psB{i}", [128, 512], F32)) for i in range(4)]
        ps_g = e(nc.psum_tensor("ps_g", [128, 4], F32))
        wscr = [x_blk[:, 0:2048], out_sb[:, 0:2048]]
        a_scr = h_sb[:, 0:2048]
        b_scr = h_sb[:, 2048:4096]
        q_scr = h_sb[:, 4096:6144]

        with nc.Block() as b0:
            @b0.sync
            def _(sy):
                LD.inc(sy.dma_start(out=onesb[:], in_=ONESB), 16)
                LD.inc(sy.dma_start(out=onesf[:], in_=ONESF), 16)
                LD.inc(sy.dma_start(out=lnw_sb[:], in_=lnw), 16)
                LD.inc(sy.dma_start(out=lnb_sb[:], in_=lnb), 16)

            @b0.vector
            def _(v):
                LD.wait(v)
                VC.inc(v.tensor_copy(lnb_bf[:], lnb_sb[:]))

        for mi, (wsrc, wdst) in enumerate(((wfcT, wfcq_d), (wprojT, wprojq_d))):
            npc = (D * FF) // (128 * 2048)
            ld0, vc0 = LD.n, VC.n
            with nc.Block() as bq:
                @bq.sync
                def _(sy):
                    for pc in range(npc):
                        if pc >= 2:
                            sy.wait_ge(VC.h, vc0 + pc - 1)
                        LD.inc(sy.dma_start(out=wscr[pc % 2],
                                            in_=bass.AP(wsrc.tensor, pc * 128 * 2048,
                                                        [[2048, 128], [1, 2048]])), 16)

                @bq.vector
                def _(v):
                    for pc in range(npc):
                        v.wait_ge(LD.h, ld0 + (pc + 1) * 16)
                        i = v.tensor_reduce(out=parts[:, pc:pc + 1], in_=wscr[pc % 2], axis=AX.X,
                                            op=ALU.add, apply_absolute_value=True)
                        VC.inc(i)

                tnq = TN.n
                with ExitStack() as _s:
                    pass
                @bq.tensor
                def _(tn):
                    VC.wait(tn)
                    i = tn.matmul(ps_g[0:npc, 2:3], parts[:, 0:npc], onesf[:, 0:1],
                                  start=True, stop=True)
                    TN.inc(i)

                @bq.scalar
                def _(s):
                    s.wait_ge(TN.h, tnq + 1)
                    SC.inc(s.copy(parts[0:npc, npc + 2:npc + 3], ps_g[0:npc, 2:3]))

                @bq.tensor
                def _(tn):
                    SC.wait(tn)
                    i = tn.matmul(ps_g[0:1, mi:mi + 1], parts[0:npc, npc + 2:npc + 3], onesf[0:npc, 0:1],
                                  start=True, stop=True)
                    TN.inc(i)

                @bq.vector
                def _(v):
                    TN.wait(v)
                    i = v.tensor_scalar(out=parts[:, npc + 1:npc + 2][0:1, :], in0=ps_g[0:1, mi:mi + 1],
                                        scalar1=0.5 / (D * FF), scalar2=0.5e-6, op0=ALU.mult, op1=ALU.add)
                    VC.inc(i)

                @bq.tensor
                def _(tn):
                    TN.wait(tn)
                    VC.wait(tn)
                    i = tn.matmul(ps[2][0:128, 0:1], onesf[0:1, :], parts[0:1, npc + 1:npc + 2],
                                  start=True, stop=True)
                    TN.inc(i)

                @bq.vector
                def _(v):
                    TN.wait(v)
                    v.tensor_copy(thr2[:, 2 * mi:2 * mi + 1], ps[2][:, 0:1])
                    i = v.tensor_scalar(out=thr2[:, 2 * mi + 1:2 * mi + 2], in0=ps[2][:, 0:1],
                                        scalar1=-1.0, scalar2=None, op0=ALU.mult)
                    VC.inc(i)

            ld1, vc1, gp1 = LD.n, VC.n, GP.n
            with nc.Block() as bq2:
                @bq2.sync
                def _(sy):
                    for pc in range(npc):
                        if pc >= 2:
                            sy.wait_ge(VC.h, vc1 + 2 * (pc - 1))
                        LD.inc(sy.dma_start(out=wscr[pc % 2],
                                            in_=bass.AP(wsrc.tensor, pc * 128 * 2048,
                                                        [[2048, 128], [1, 2048]])), 16)
                        sy.wait_ge(VC.h, vc1 + 2 * pc + 2)
                        dst_r = bass.AP(wdst.tensor, pc * 128 * 2048,
                                        [[2048, 128], [1, 2048]])
                        LD.inc(sy.dma_start(out=dst_r, in_=q_scr), 16)

                @bq2.gpsimd
                def _(g):
                    for pc in range(npc):
                        g.wait_ge(LD.h, ld1 + pc * 32 + 16)
                        if pc == 0:
                            g.wait_ge(VC.h, vc1)
                        if pc >= 1:
                            g.wait_ge(VC.h, vc1 + 2 * pc)
                        i = g.tensor_scalar(out=a_scr, in0=wscr[pc % 2], scalar1=thr2[:, 2 * mi:2 * mi + 1],
                                            scalar2=None, op0=ALU.is_gt)
                        GP.inc(i)

                @bq2.vector
                def _(v):
                    for pc in range(npc):
                        v.wait_ge(LD.h, ld1 + pc * 32 + 16)
                        if pc == 0:
                            v.wait_ge(VC.h, vc1)
                        v.tensor_scalar(out=b_scr, in0=wscr[pc % 2], scalar1=thr2[:, 2 * mi + 1:2 * mi + 2],
                                        scalar2=None, op0=ALU.is_lt)
                        v.wait_ge(GP.h, gp1 + pc + 1)
                        if pc >= 1:
                            v.wait_ge(LD.h, ld1 + pc * 32)
                        i = v.tensor_tensor(out=q_scr, in0=a_scr, in1=b_scr, op=ALU.subtract)
                        VC.inc(i, 2)

        with nc.Block() as bl:
            @bl.sync
            def _(sy):
                sy.wait_ge(LD.h, LD.n)
                for dc in range(8):
                    LD.inc(sy.dma_start(out=wfcq[:, dc * FF:(dc + 1) * FF],
                                        in_=bass.AP(wfcq_d.tensor, dc * 128 * FF, [[FF, 128], [1, FF]])), 16)
                for fc in range(32):
                    LD.inc(sy.dma_start(out=wpjq[:, fc * 1024:(fc + 1) * 1024],
                                        in_=bass.AP(wprojq_d.tensor, fc * 128 * D, [[D, 128], [1, D]])), 16)

        tnc = TN.n
        with nc.Block() as bc:
            @bc.tensor
            def _(tn):
                LD.wait(tn)
                for db in range(2):
                    for fc in range(32):
                        i = tn.matmul(ps[db][0:1, :], lnb_bf[:, fc:fc + 1],
                                      wpjq[:, fc * 1024 + db * 512: fc * 1024 + db * 512 + 512],
                                      start=(fc == 0), stop=(fc == 31))
                    TN.inc(i)

            @bc.vector
            def _(v):
                for db in range(2):
                    v.wait_ge(TN.h, tnc + db + 1)
                    i = v.tensor_copy(cvec[0:1, db * 512:(db + 1) * 512], ps[db][0:1, :])
                    VC.inc(i)

        xr_r = xT.rearrange("(c p) t -> p c t", p=128)
        xn_r = xn0.rearrange("(c p) t -> p c t", p=128)
        s_r = s0.rearrange("(c p) t -> p c t", p=128)
        x1_r = x1.rearrange("(c p) t -> p c t", p=128)
        for tb in range(NTB):
            T0 = tb * TB
            tn_w, vc_w, sc_w, ld_w = TN.n, VC.n, SC.n, LD.n
            with nc.Block() as bld:
                @bld.sync
                def _(sy):
                    sy.wait_ge(TN.h, tn_w)
                    sy.wait_ge(VC.h, vc_w)
                    sy.wait_ge(SC.h, sc_w)
                    LD.inc(sy.dma_start(out=xn_blk[:].rearrange("p (c t) -> p c t", c=8),
                                        in_=xn_r[:, :, T0:T0 + TB]), 16)
                    LD.inc(sy.dma_start(out=s_blk[:].rearrange("p (c t) -> p c t", c=8),
                                        in_=s_r[:, :, T0:T0 + TB]), 16)
                    LD.inc(sy.dma_start(out=x_blk[:].rearrange("p (c t) -> p c t", c=8),
                                        in_=xr_r[:, :, T0:T0 + TB]), 16)

            tn0_, sc0_ = TN.n, SC.n
            tn_after_h, tn_after_st = {}, {}
            tcnt = tn0_
            torder = []
            for fc in range(32):
                torder.append(("h", fc))
                if fc >= 1:
                    torder.append(("st", fc - 1))
            torder.append(("st", 31))
            for op, fc in torder:
                tcnt += 1
                (tn_after_h if op == "h" else tn_after_st)[fc] = tcnt

            with nc.Block() as bh:
                @bh.tensor
                def _(tn):
                    tn.wait_ge(LD.h, ld_w + 48)
                    for op, fc in torder:
                        if op == "h":
                            if fc >= 2:
                                tn.wait_ge(SC.h, sc0_ + 2 * (fc - 2) + 1)
                            for dc in range(8):
                                i = tn.matmul(ps[fc % 2][:, 0:TB],
                                              wfcq[:, dc * FF + fc * 128: dc * FF + fc * 128 + 128],
                                              xn_blk[:, dc * TB:(dc + 1) * TB],
                                              start=(dc == 0), stop=(dc == 7))
                            i.then_inc(TN.h, 1)
                        else:
                            tn.wait_ge(SC.h, sc0_ + 2 * fc + 2)
                            tn.matmul(ps[2][:, 0:TB], onesb[:, 0:128], h_sb[:, fc * TB:(fc + 1) * TB],
                                      start=(fc == 0), stop=(fc == 31), skip_group_check=True)
                            i = tn.matmul(ps[3][:, 0:TB], onesb[:, 0:128], hsq[:],
                                          start=(fc == 0), stop=(fc == 31), skip_group_check=True)
                            i.then_inc(TN.h, 1)
                    TN.n = tcnt

                @bh.scalar
                def _(s):
                    for fc in range(32):
                        s.wait_ge(TN.h, tn_after_h[fc])
                        if fc >= 1:
                            s.wait_ge(TN.h, tn_after_st[fc - 1])
                        i = s.activation(out=h_sb[:, fc * TB:(fc + 1) * TB], in_=ps[fc % 2][:, 0:TB],
                                         func=AF.Gelu)
                        i.then_inc(SC.h, 1)
                        SC.n += 1
                        i = s.activation(out=hsq[:], in_=h_sb[:, fc * TB:(fc + 1) * TB], func=AF.Square)
                        i.then_inc(SC.h, 1)
                        SC.n += 1

            vc0_ = VC.n
            with nc.Block() as bs:
                @bs.vector
                def _(v):
                    v.wait_ge(TN.h, tn_after_st[31])
                    v.tensor_scalar(out=mu_sb[:], in0=ps[2][:, 0:TB], scalar1=1.0 / FF, scalar2=None,
                                    op0=ALU.mult)
                    v.tensor_scalar(out=scf[:, 0:TB], in0=ps[3][:, 0:TB], scalar1=1.0 / FF,
                                    scalar2=1e-5, op0=ALU.mult, op1=ALU.add)
                    v.tensor_tensor(out=scf[:, TB:2 * TB], in0=mu_sb[:], in1=mu_sb[:], op=ALU.mult)
                    VC.inc(v.tensor_tensor(out=scf[:, 2 * TB:3 * TB], in0=scf[:, 0:TB],
                                           in1=scf[:, TB:2 * TB], op=ALU.subtract))

                @bs.scalar
                def _(s):
                    VC.wait(s)
                    s.activation(out=scf[:, 2 * TB:3 * TB], in_=scf[:, 2 * TB:3 * TB], func=AF.Ln)
                    SC.inc(s.activation(out=rs_sb[:], in_=scf[:, 2 * TB:3 * TB], func=AF.Exp, scale=-0.5))

                @bs.vector
                def _(v):
                    SC.wait(v)
                    for fc in range(32):
                        v.tensor_tensor(out=td_sb[:], in0=h_sb[:, fc * TB:(fc + 1) * TB], in1=mu_sb[:],
                                        op=ALU.subtract)
                        i = v.scalar_tensor_tensor(out=h_sb[:, fc * TB:(fc + 1) * TB], in0=td_sb[:],
                                                   scalar=lnw_sb[:, fc:fc + 1], in1=rs_sb[:],
                                                   op0=ALU.mult, op1=ALU.mult)
                    VC.inc(i)

            tnp, scp, vcp = TN.n, SC.n, VC.n
            with nc.Block() as bp:
                @bp.tensor
                def _(tn):
                    tn.wait_ge(VC.h, vcp)
                    for dc in range(8):
                        if dc >= 2:
                            tn.wait_ge(SC.h, scp + dc - 1)
                        for fc in range(32):
                            tn.matmul(ps[dc % 2][:, 0:TB],
                                      wpjq[:, fc * 1024 + dc * 128: fc * 1024 + dc * 128 + 128],
                                      h_sb[:, fc * TB:(fc + 1) * TB],
                                      start=(fc == 0), stop=False)
                        i = tn.matmul(ps[dc % 2][:, 0:TB], cvec[0:1, dc * 128:(dc + 1) * 128],
                                      onesb[0:1, 0:TB], start=False, stop=True)
                        TN.inc(i)

                @bp.scalar
                def _(s):
                    for dc in range(8):
                        s.wait_ge(TN.h, tnp + dc + 1)
                        i = s.activation(out=g_sb[:], in_=ps[dc % 2][:, 0:TB], func=AF.Silu)
                        SC.inc(i)
                        s.wait_ge(VC.h, vcp + 2 * dc + 1)

                @bp.vector
                def _(v):
                    for dc in range(8):
                        v.wait_ge(SC.h, scp + dc + 1)
                        v.tensor_tensor(out=t2_sb[:], in0=g_sb[:], in1=s_blk[:, dc * TB:(dc + 1) * TB],
                                        op=ALU.mult)
                        VC.n += 1
                        i = v.tensor_tensor(out=out_sb[:, dc * TB:(dc + 1) * TB], in0=t2_sb[:],
                                            in1=x_blk[:, dc * TB:(dc + 1) * TB], op=ALU.add)
                        i.then_inc(VC.h, 2)
                        VC.n += 1

                @bp.sync
                def _(sy):
                    VC.wait(sy)
                    LD.inc(sy.dma_start(out=x1_r[:, :, T0:T0 + TB],
                                        in_=out_sb[:].rearrange("p (c t) -> p c t", c=8)), 16)

        if debug:
            with nc.Block() as bd:
                @bd.sync
                def _(sy):
                    sy.wait_ge(TN.h, TN.n)
                    sy.wait_ge(VC.h, VC.n)
                    sy.wait_ge(SC.h, SC.n)
                    LD.inc(sy.dma_start(out=dbg_h, in_=h_sb[:]), 16)
                    LD.inc(sy.dma_start(out=dbg_mu, in_=mu_sb[:]), 16)
                    LD.inc(sy.dma_start(out=dbg_rs, in_=rs_sb[:]), 16)
                    LD.inc(sy.dma_start(out=dbg_cv, in_=cvec[0:1, :]), 16)
                    LD.inc(sy.dma_start(out=dbg_thr, in_=thr2[:]), 16)
                    LD.inc(sy.dma_start(out=dbg_parts, in_=parts[:]), 16)
        with nc.Block() as bf:
            @bf.sync
            def _(sy):
                sy.wait_ge(LD.h, LD.n)
                sy.wait_ge(GD.h, GD.n)

    return nc


C_DECLS = [("x1", [D, TOK], F32, "ExternalInput"), ("qkvw", [D, 3 * D], F32, "ExternalInput"),
           ("qkvb", [128, 24], F32, "ExternalInput"), ("outw", [D, D], F32, "ExternalInput"),
           ("outb", [128, 8], F32, "ExternalInput"), ("wg", [D, NE], F32, "ExternalInput"),
           ("CCb", [128, T], BF16, "ExternalInput"), ("SSb", [128, T], BF16, "ExternalInput"),
           ("ZM4", [128, 4 * 512], BF16, "ExternalInput"), ("ONESB", [128, 512], BF16, "ExternalInput"),
           ("ONESF", [128, 128], F32, "ExternalInput"), ("LSTF", [128, 128], F32, "ExternalInput"),
           ("IOTA16", [128, NE], F32, "ExternalInput"), ("LSU16F", [NE, NE], F32, "ExternalInput"),
           ("EYEB", [128, 128], BF16, "ExternalInput"), ("BV", [1, D], F32, "ExternalInput"),
           ("s1", [D, TOK], BF16, "ExternalOutput"), ("sorted_o", [TOK, D], BF16, "ExternalOutput"),
           ("counts_o", [1, NE], F32, "ExternalOutput"), ("idxo", [128, 32], F32, "ExternalOutput"),
           ("loco", [128, 32], F32, "ExternalOutput"), ("gso", [128, 32], F32, "ExternalOutput")]


def build_C(core_id=0, debug=False, ctx=None):
    nc, stk, sems, io = _phase_setup(ctx, "C_", C_DECLS)
    LD, GD, TN, VC, SC, GP = (sems[n] for n in ("LD", "GD", "TN", "VC", "SC", "GP"))
    (x1, qkvw, qkvb, outw, outb, wg, CCb, SSb, ZM4, ONESB, ONESF, LSTF, IOTA16,
     LSU16F, EYEB, BV, s1, sorted_o, counts_o, idxo, loco, gso) = (
        io[n] for n in ("x1", "qkvw", "qkvb", "outw", "outb", "wg", "CCb", "SSb", "ZM4",
                        "ONESB", "ONESF", "LSTF", "IOTA16", "LSU16F", "EYEB", "BV",
                        "s1", "sorted_o", "counts_o", "idxo", "loco", "gso"))
    xnnat_d = nc.dram_tensor("xnnat_d", [TOK, D], BF16, kind="Internal").ap()
    cnt_d = nc.dram_tensor("cnt_d", [1, 64], F32, kind="Internal").ap()

    with stk:
        e = stk.enter_context
        onesb = e(nc.sbuf_tensor("onesb", [128, 512], BF16))
        onesf = e(nc.sbuf_tensor("onesf", [128, 128], F32))
        lstf = e(nc.sbuf_tensor("lstf", [128, 128], F32))
        iota16 = e(nc.sbuf_tensor("iota16", [128, NE], F32))
        lsu16 = e(nc.sbuf_tensor("lsu16", [NE, NE], F32))
        eyeb = e(nc.sbuf_tensor("eyeb", [128, 128], BF16))
        zm_sb = e(nc.sbuf_tensor("zm_sb", [128, 4 * 512], BF16))
        qkvb_sb = e(nc.sbuf_tensor("qkvb_sb", [128, 24], F32))
        outb_sb = e(nc.sbuf_tensor("outb_sb", [128, 8], F32))
        wg_sb = e(nc.sbuf_tensor("wg_sb", [128, 8 * NE], BF16))
        bvrow = e(nc.sbuf_tensor("bvrow", [1, D], BF16))
        xn_sb = e(nc.sbuf_tensor("xn_sb", [128, 8 * T], BF16))
        k_sb = e(nc.sbuf_tensor("k_sb", [128, 8 * T], BF16))
        q_sb = e(nc.sbuf_tensor("q_sb", [128, 8 * T], BF16))
        v_sb = e(nc.sbuf_tensor("v_sb", [128, 16 * 1024], BF16))
        wbuf = e(nc.sbuf_tensor("wbuf", [128, 8 * 1024], BF16))
        scrB = e(nc.sbuf_tensor("scrB", [128, SCRB_N], BF16))
        scrF = e(nc.sbuf_tensor("scrF", [128, 1536], F32))
        eq_sb = e(nc.sbuf_tensor("eq_sb", [128, 32 * NE], F32))
        idx_sb = e(nc.sbuf_tensor("idx_sb", [128, 32], F32))
        loc_sb = e(nc.sbuf_tensor("loc_sb", [128, 32], F32))
        gs_sb = e(nc.sbuf_tensor("gs_sb", [128, 32], F32))
        l_sb = e(nc.sbuf_tensor("l_sb", [128, NE], F32))
        et_sb = e(nc.sbuf_tensor("et_sb", [128, NE], F32))
        m_sb = e(nc.sbuf_tensor("m_sb", [128, 4], F32))
        carry = e(nc.sbuf_tensor("carry", [1, NE], F32))
        lst_bc = e(nc.sbuf_tensor("lst_bc", [128, NE], F32))
        nat_sb = e(nc.sbuf_tensor("nat_sb", [128, D], BF16))
        slot_sb = e(nc.sbuf_tensor("slot_sb", [128, 2], F32))
        slot_u = e(nc.sbuf_tensor("slot_u", [128, 1], U32))

        sb = {f"psS{i}": e(nc.psum_tensor(f"psS{i}", [128, 512], F32)) for i in range(4)}
        sb["ps_ctx"] = e(nc.psum_tensor("ps_ctx", [128, 512], F32))
        sb["ps_den"] = e(nc.psum_tensor("ps_den", [128, 512], F32))
        ps_tr = e(nc.psum_tensor("ps_tr", [128, 1024], BF16))
        sb.update(onesb=onesb, q_sb=q_sb, k_sb=k_sb, v_sb=v_sb, wo_sb=wbuf,
                  scrB=scrB, scrF=scrF, cb=None, expb_d_t=None, zm=zm_sb)
        rope_e = scrB[:, S_TT:S_TT + 2048]
        rope_o = scrB[:, S_TT + 2048:S_TT + 4096]
        rt1 = scrB[:, S_TT + 4096:S_TT + 6144]
        rt2 = scrB[:, S_PRAW:S_PRAW + 2048]
        cc_sb = scrB[:, S_SEV:S_SEV + 2048]
        ss_sb = scrB[:, S_CTQ:S_CTQ + 2048]

        with nc.Block() as b0:
            @b0.sync
            def _(sy):
                for d, s_ in ((onesb[:], ONESB), (onesf[:], ONESF), (lstf[:], LSTF),
                              (iota16[:], IOTA16), (lsu16[:], LSU16F), (eyeb[:], EYEB),
                              (zm_sb[:], ZM4),
                              (qkvb_sb[:], qkvb), (outb_sb[:], outb)):
                    LD.inc(sy.dma_start(out=d, in_=s_), 16)

            @b0.gpsimd
            def _(g):
                GD.inc(g.dma_start(out=wg_sb[:].rearrange("p (c f) -> p c f", c=8),
                                   in_=wg.rearrange("(c p) f -> p c f", p=128)), 16)
                GD.inc(g.dma_start(out=bvrow[:], in_=BV), 16)

            @b0.vector
            def _(v):
                LD.wait(v)
                VC.inc(v.memset(carry[:], 0.0))

        for b in range(B_LOC):
            emit_ln(nc, sems, onesb, scrB, scrF, x1, xn_sb, None, b * T,
                    sb["psS0"], sb["psS1"])

            for tt in range(16):
                g_ = b * 16 + tt
                tn0r, vc0r, sc0r = TN.n, VC.n, SC.n
                with nc.Block() as br:
                    @br.tensor
                    def _(tn):
                        if g_ == 0:
                            GD.wait(tn)
                        tn.wait_ge(VC.h, vc0r)
                        for dc in range(8):
                            i = tn.matmul(sb["psS0"][:, 0:NE],
                                          xn_sb[:, dc * T + tt * 128: dc * T + tt * 128 + 128],
                                          wg_sb[:, dc * NE:(dc + 1) * NE],
                                          start=(dc == 0), stop=(dc == 7))
                        TN.inc(i)
                        for dc in range(8):
                            i = tn.matmul(ps_tr[:, dc * 128:(dc + 1) * 128],
                                          xn_sb[:, dc * T + tt * 128: dc * T + tt * 128 + 128],
                                          eyeb[:], start=True, stop=True, is_transpose=True)
                            TN.inc(i)

                    @br.vector
                    def _(v):
                        def step(instr):
                            VC.inc(instr)
                            v.wait_ge(VC.h, VC.n)
                        v.wait_ge(TN.h, tn0r + 1)
                        step(v.tensor_copy(l_sb[:], sb["psS0"][:, 0:NE]))
                        step(v.tensor_reduce(out=m_sb[:, 0:1], in_=l_sb[:], axis=AX.X, op=ALU.max))
                        i = v.tensor_scalar(out=m_sb[:, 1:2], in0=m_sb[:, 0:1], scalar1=-1.0,
                                            scalar2=None, op0=ALU.mult)
                        VC.inc(i)

                    @br.scalar
                    def _(s):
                        VC.wait(s)
                        SC.inc(s.activation(out=et_sb[:], in_=l_sb[:], func=AF.Exp,
                                            bias=m_sb[:, 1:2], accum_out=m_sb[:, 2:3]))

                    @br.vector
                    def _(v):
                        def step(instr):
                            VC.inc(instr)
                            v.wait_ge(VC.h, VC.n)
                        SC.wait(v)
                        step(v.reciprocal(gs_sb[:, g_:g_ + 1], m_sb[:, 2:3]))
                        step(v.tensor_scalar(out=eq_sb[:, g_ * NE:(g_ + 1) * NE], in0=l_sb[:],
                                             scalar1=m_sb[:, 0:1], scalar2=None, op0=ALU.is_ge))
                        step(v.tensor_tensor(out=et_sb[:], in0=eq_sb[:, g_ * NE:(g_ + 1) * NE],
                                             in1=iota16[:], op=ALU.mult))
                        i = v.tensor_reduce(out=idx_sb[:, g_:g_ + 1], in_=et_sb[:], axis=AX.X,
                                            op=ALU.add)
                        VC.inc(i)

                    @br.tensor
                    def _(tn):
                        VC.wait(tn)
                        i = tn.matmul(sb["psS1"][0:1, 0:NE], onesf[:, 0:1],
                                      eq_sb[:, g_ * NE:(g_ + 1) * NE], start=True, stop=True)
                        TN.inc(i)
                        tn.matmul(sb["psS2"][:, 0:NE], lstf[:],
                                  eq_sb[:, g_ * NE:(g_ + 1) * NE], start=True, stop=False)
                        i = tn.matmul(sb["psS2"][:, 0:NE], onesf[0:1, :], carry[:],
                                      start=False, stop=True)
                        TN.inc(i)

                    @br.vector
                    def _(v):
                        def step(instr):
                            VC.inc(instr)
                            v.wait_ge(VC.h, VC.n)
                        TN.wait(v)
                        step(v.tensor_tensor(out=et_sb[:], in0=eq_sb[:, g_ * NE:(g_ + 1) * NE],
                                             in1=sb["psS2"][:, 0:NE], op=ALU.mult))
                        step(v.tensor_reduce(out=loc_sb[:, g_:g_ + 1], in_=et_sb[:], axis=AX.X,
                                             op=ALU.add))
                        i = v.tensor_tensor(out=carry[:], in0=carry[:], in1=sb["psS1"][0:1, 0:NE],
                                            op=ALU.add)
                        VC.inc(i)

                    @br.vector
                    def _(v):
                        v.wait_ge(TN.h, TN.n)
                        for dc in range(8):
                            i = v.tensor_copy(nat_sb[:, dc * 128:(dc + 1) * 128],
                                              ps_tr[:, dc * 128:(dc + 1) * 128])
                        VC.inc(i)

                    @br.sync
                    def _(sy):
                        VC.wait(sy)
                        LD.inc(sy.dma_start(out=bass.AP(xnnat_d.tensor, g_ * 128 * D,
                                                        [[D, 128], [1, D]]),
                                            in_=nat_sb[:]), 16)

            def load_third(i3):
                tn_w = TN.n
                with nc.Block() as blw:
                    @blw.gpsimd
                    def _(g):
                        g.wait_ge(TN.h, tn_w)
                        GD.inc(g.dma_start(out=wbuf[:].rearrange("p (c f) -> p c f", c=8),
                                           in_=qkvw.rearrange("(c p) f -> p c f", p=128)[:, :, i3 * D:(i3 + 1) * D]), 16)
                return GD.n

            def proj_qk(gd_mark, dst, i3):
                for gg in range(4):
                    tn0_, sc0_, vc0_ = TN.n, SC.n, VC.n
                    with nc.Block() as bp:
                        @bp.tensor
                        def _(tn):
                            tn.wait_ge(GD.h, gd_mark)
                            tn.wait_ge(VC.h, vc0_)
                            for jj in range(8):
                                fc, nb_ = 2 * gg + jj // 4, jj % 4
                                if jj >= 2:
                                    tn.wait_ge(SC.h, sc0_ + jj - 1)
                                for dc in range(8):
                                    i = tn.matmul(sb[f"psS{jj % 2}"][:],
                                                  wbuf[:, dc * 1024 + fc * 128: dc * 1024 + fc * 128 + 128],
                                                  xn_sb[:, dc * T + nb_ * 512: dc * T + nb_ * 512 + 512],
                                                  start=(dc == 0), stop=(dc == 7))
                                TN.inc(i)

                        @bp.scalar
                        def _(s):
                            for jj in range(8):
                                fc, nb_ = 2 * gg + jj // 4, jj % 4
                                s.wait_ge(TN.h, tn0_ + jj + 1)
                                dstt = rope_e if (jj // 4) == 0 else rope_o
                                i = s.activation(out=dstt[:, nb_ * 512:(nb_ + 1) * 512],
                                                 in_=sb[f"psS{jj % 2}"][:], func=AF.Identity,
                                                 bias=qkvb_sb[:, i3 * 8 + fc: i3 * 8 + fc + 1])
                                SC.inc(i)

                        @bp.vector
                        def _(v):
                            SC.wait(v)
                            v.wait_ge(LD.h, ld_rope)
                            cE, cO = 2 * gg, 2 * gg + 1
                            v.tensor_tensor(out=rt1, in0=rope_e, in1=cc_sb, op=ALU.mult)
                            v.tensor_tensor(out=rt2, in0=rope_o, in1=ss_sb, op=ALU.mult)
                            v.tensor_tensor(out=dst[:, cE * T:(cE + 1) * T], in0=rt1, in1=rt2,
                                            op=ALU.subtract)
                            v.tensor_tensor(out=rt1, in0=rope_e, in1=ss_sb, op=ALU.mult)
                            v.tensor_tensor(out=rt2, in0=rope_o, in1=cc_sb, op=ALU.mult)
                            i = v.tensor_tensor(out=dst[:, cO * T:(cO + 1) * T], in0=rt1, in1=rt2,
                                                op=ALU.add)
                            VC.inc(i)

            def proj_v(gd_mark):
                tn0_, vc0_ = TN.n, VC.n
                with nc.Block() as bp:
                    @bp.tensor
                    def _(tn):
                        tn.wait_ge(GD.h, gd_mark)
                        tn.wait_ge(VC.h, vc0_)
                        tn.wait_ge(GD.h, GD.n)
                        for jj in range(32):
                            tt, fb = jj // 2, jj % 2
                            if jj >= 2:
                                tn.wait_ge(VC.h, vc0_ + jj - 1)
                            for dc in range(8):
                                tn.matmul(sb[f"psS{jj % 2}"][:],
                                          xn_sb[:, dc * T + tt * 128: dc * T + tt * 128 + 128],
                                          wbuf[:, dc * 1024 + fb * 512: dc * 1024 + fb * 512 + 512],
                                          start=(dc == 0), stop=False)
                            i = tn.matmul(sb[f"psS{jj % 2}"][:], onesb[0:1, 0:128],
                                          bvrow[0:1, fb * 512:(fb + 1) * 512],
                                          start=False, stop=True)
                            TN.inc(i)

                    @bp.vector
                    def _(v):
                        for jj in range(32):
                            tt, fb = jj // 2, jj % 2
                            v.wait_ge(TN.h, tn0_ + jj + 1)
                            i = v.tensor_copy(v_sb[:, tt * 1024 + fb * 512: tt * 1024 + (fb + 1) * 512],
                                              sb[f"psS{jj % 2}"][:])
                            VC.inc(i)

            tnw0, scw0, ldw0 = TN.n, SC.n, LD.n
            with nc.Block() as brt:
                @brt.sync
                def _(sy):
                    sy.wait_ge(TN.h, tnw0)
                    sy.wait_ge(SC.h, scw0)
                    sy.wait_ge(LD.h, ldw0)
                    LD.inc(sy.dma_start(out=cc_sb, in_=CCb), 16)
                    LD.inc(sy.dma_start(out=ss_sb, in_=SSb), 16)
            ld_rope = LD.n

            proj_qk(load_third(0), q_sb, 0)
            proj_qk(load_third(1), k_sb, 1)
            proj_v(load_third(2))
            gd_mo = None
            tnw_ = TN.n
            with nc.Block() as blw2:
                @blw2.gpsimd
                def _(g):
                    g.wait_ge(TN.h, tnw_)
                    GD.inc(g.dma_start(out=wbuf[:].rearrange("p (c f) -> p c f", c=8),
                                       in_=outw.rearrange("(c p) f -> p c f", p=128)), 16)
                gd_mo = GD.n
            with nc.Block() as bwait:
                @bwait.tensor
                def _(tn):
                    tn.wait_ge(GD.h, gd_mo)
            emit_attention_batch(nc, sems, sb, 1, b, s1, wo_bias_col=outb_sb)

        with nc.Block() as bc1:
            @bc1.sync
            def _(sy):
                VC.wait(sy)
                LD.inc(sy.dma_start(out=counts_o, in_=carry[:]), 16)
                LD.inc(sy.dma_start(out=cnt_d[:, 0:NE], in_=carry[:]), 16)
                LD.inc(sy.dma_start(out=idxo, in_=idx_sb[:]), 16)
                LD.inc(sy.dma_start(out=loco, in_=loc_sb[:]), 16)
                LD.inc(sy.dma_start(out=gso, in_=gs_sb[:]), 16)
                sy.wait_ge(LD.h, LD.n)
                LD.inc(sy.dma_start(out=l_sb[0:NE, 0:1], in_=bass.AP(cnt_d.tensor, 0, [[1, NE], [1, 1]])), 16)

        tnl = TN.n
        with nc.Block() as bc2:
            @bc2.tensor
            def _(tn):
                LD.wait(tn)
                i = tn.matmul(sb["psS3"][0:1, 0:NE], l_sb[0:NE, 0:1], lsu16[:],
                              start=True, stop=True)
                TN.inc(i)

            @bc2.vector
            def _(v):
                v.wait_ge(TN.h, tnl + 1)
                VC.inc(v.tensor_copy(et_sb[0:1, 0:NE], sb["psS3"][0:1, 0:NE]))

            @bc2.sync
            def _(sy):
                VC.wait(sy)
                LD.inc(sy.dma_start(out=cnt_d[:, NE:2 * NE], in_=et_sb[0:1, 0:NE]), 16)
                sy.wait_ge(LD.h, LD.n)
                LD.inc(sy.dma_start(out=lst_bc[:], in_=bass.AP(cnt_d.tensor, NE, [[0, 128], [1, NE]])), 16)

        for g_ in range(32):
            ld0s, vc0s, gp0s = LD.n, VC.n, GP.n
            with nc.Block() as bs:
                @bs.sync
                def _(sy):
                    sy.wait_ge(GP.h, gp0s)
                    LD.inc(sy.dma_start(out=nat_sb[:], in_=bass.AP(xnnat_d.tensor, g_ * 128 * D,
                                                                   [[D, 128], [1, D]])), 16)

                @bs.vector
                def _(v):
                    def step(instr):
                        VC.inc(instr)
                        v.wait_ge(VC.h, VC.n)
                    v.wait_ge(LD.h, ld0s + 16)
                    step(v.tensor_tensor(out=et_sb[:], in0=eq_sb[:, g_ * NE:(g_ + 1) * NE],
                                         in1=lst_bc[:], op=ALU.mult))
                    step(v.tensor_reduce(out=slot_sb[:, 0:1], in_=et_sb[:], axis=AX.X, op=ALU.add))
                    step(v.tensor_tensor(out=slot_sb[:, 1:2], in0=slot_sb[:, 0:1],
                                         in1=loc_sb[:, g_:g_ + 1], op=ALU.add))
                    i = v.tensor_copy(slot_u[:], slot_sb[:, 1:2])
                    VC.inc(i)

                @bs.gpsimd
                def _(g):
                    v_ = VC.n
                    g.wait_ge(VC.h, v_)
                    i = g.indirect_dma_start(
                        out=sorted_o, out_offset=IndirectOffsetOnAxis(slot_u[:, 0:1], axis=0),
                        in_=nat_sb[:], in_offset=None)
                    GP.inc(i, 16)

        with nc.Block() as bf:
            @bf.sync
            def _(sy):
                sy.wait_ge(LD.h, LD.n)
                sy.wait_ge(GD.h, GD.n)
                sy.wait_ge(GP.h, GP.n)
    return nc


D_DECLS = [("sorted_all", [8 * TOK, D], BF16, "ExternalInput"),
           ("counts_all", [8, NE], F32, "ExternalInput"),
           ("ESEL", [NE, 2], F32, "ExternalInput"), ("SELM", [NE, 2], F32, "ExternalInput"),
           ("w1c", [2, D, FF], F32, "ExternalInput"), ("w2c", [2, FF, D], F32, "ExternalInput"),
           ("LI8", [8, 8], F32, "ExternalInput"), ("SLOTC", [128, NE], F32, "ExternalInput"),
           ("C4096", [128, 8], F32, "ExternalInput"), ("EYEB", [128, 128], BF16, "ExternalInput"),
           ("eo", [2 * CAP, D], BF16, "ExternalOutput")]


def build_D(ctx=None):
    nc, stk, sems, io = _phase_setup(ctx, "D_", D_DECLS)
    LD, GD, TN, VC, SC, GP = (sems[n] for n in ("LD", "GD", "TN", "VC", "SC", "GP"))
    (sorted_all, counts_all, ESEL, SELM, w1c, w2c, LI8, SLOTC, C4096, EYEB, eo) = (
        io[n] for n in ("sorted_all", "counts_all", "ESEL", "SELM", "w1c", "w2c",
                        "LI8", "SLOTC", "C4096", "EYEB", "eo"))
    w2bf_d = nc.dram_tensor("w2bf_d", [2, FF, D], BF16, kind="Internal").ap()
    tbl_d = nc.dram_tensor("tbl_d", [1, 64], F32, kind="Internal").ap()
    TB = 256

    with stk:
        e = stk.enter_context
        li8 = e(nc.sbuf_tensor("li8", [8, 8], F32))
        selm = e(nc.sbuf_tensor("selm", [NE, 2], F32))
        esel = e(nc.sbuf_tensor("esel", [NE, 2], F32))
        csel = e(nc.sbuf_tensor("csel", [8, 2], F32))
        slotc = e(nc.sbuf_tensor("slotc", [128, NE], F32))
        c4096 = e(nc.sbuf_tensor("c4096", [128, 8], F32))
        eyeb = e(nc.sbuf_tensor("eyeb", [128, 128], BF16))

        cntsT = e(nc.sbuf_tensor("cntsT", [NE, 8], F32))
        small = e(nc.sbuf_tensor("small", [8, 4], F32))
        cumt = e(nc.sbuf_tensor("cumt", [128, 8], F32))
        cumpt = e(nc.sbuf_tensor("cumpt", [128, 8], F32))
        lstt = e(nc.sbuf_tensor("lstt", [128, 8], F32))
        addt = e(nc.sbuf_tensor("addt", [128, 8], F32))
        mt1 = e(nc.sbuf_tensor("mt1", [128, NE], F32))
        mt2 = e(nc.sbuf_tensor("mt2", [128, NE], F32))
        acc = e(nc.sbuf_tensor("acc", [128, NE], F32))
        gidx_u = e(nc.sbuf_tensor("gidx_u", [128, NE], U32))
        w1_sb = e(nc.sbuf_tensor("w1_sb", [128, 8 * FF], BF16))
        w2b = [e(nc.sbuf_tensor(f"w2b{i}", [128, D], BF16)) for i in range(2)]
        gath = e(nc.sbuf_tensor("gath", [128, 2 * D], BF16))
        dispT = e(nc.sbuf_tensor("dispT", [128, 8 * TB], BF16))
        h_sb = e(nc.sbuf_tensor("h_sb", [128, 32 * TB], BF16))
        out_sb = e(nc.sbuf_tensor("out_sb", [128, 2 * D], BF16))
        psO = [e(nc.psum_tensor(f"psO{i}", [128, 512], F32)) for i in range(6)]
        ps_trb = e(nc.psum_tensor("ps_trb", [128, 1024], BF16))

        with nc.Block() as b0:
            @b0.sync
            def _(sy):
                for d, s_ in ((li8[:], LI8), (selm[:], SELM), (slotc[:], SLOTC),
                              (c4096[:], C4096), (eyeb[:], EYEB), (esel[:], ESEL)):
                    LD.inc(sy.dma_start(out=d, in_=s_), 16)
                LD.inc(sy.dma_start(out=cntsT[:], in_=bass.AP(counts_all.tensor, 0,
                                                              [[1, NE], [NE, 8]]),
                                    allow_slow_non_contiguous=True), 16)

            tn_d0 = TN.n
            @b0.tensor
            def _(tn):
                LD.wait(tn)
                i = tn.matmul(psO[2][0:8, 0:2], cntsT[:], esel[:], start=True, stop=True)
                TN.inc(i)

            @b0.vector
            def _(v):
                v.wait_ge(TN.h, tn_d0 + 1)
                VC.inc(v.tensor_copy(csel[:], psO[2][0:8, 0:2]))

            @b0.tensor
            def _(tn):
                VC.wait(tn)
                i = tn.matmul(psO[0][0:8, 0:2], li8[:], csel[:], start=True, stop=True)
                TN.inc(i)
                i = tn.matmul(psO[1][0:8, 0:2], cntsT[:], selm[:], start=True, stop=True)
                TN.inc(i)

            @b0.vector
            def _(v):
                TN.wait(v)
                v.tensor_copy(small[:, 0:2], psO[0][0:8, 0:2])
                i = v.tensor_copy(small[:, 2:4], psO[1][0:8, 0:2])
                VC.inc(i)

            @b0.sync
            def _(sy):
                VC.wait(sy)
                LD.inc(sy.dma_start(out=bass.AP(tbl_d.tensor, 0, [[2, 8], [1, 2]]),
                                    in_=small[:, 0:2]), 16)
                LD.inc(sy.dma_start(out=bass.AP(tbl_d.tensor, 16, [[2, 8], [1, 2]]),
                                    in_=small[:, 2:4]), 16)
                sy.wait_ge(LD.h, LD.n)

        gd0, ld0 = GD.n, LD.n
        with nc.Block() as bw2:
            @bw2.gpsimd
            def _(g):
                for jj in range(64):
                    el, fc = jj // 32, jj % 32
                    if jj >= 2:
                        g.wait_ge(LD.h, ld0 + (jj - 1) * 16)
                    GD.inc(g.dma_start(out=w2b[jj % 2][:],
                                       in_=bass.AP(w2c.tensor, el * FF * D + fc * 128 * D,
                                                   [[D, 128], [1, D]])), 16)

            @bw2.sync
            def _(sy):
                for jj in range(64):
                    el, fc = jj // 32, jj % 32
                    sy.wait_ge(GD.h, gd0 + (jj + 1) * 16)
                    LD.inc(sy.dma_start(out=bass.AP(w2bf_d.tensor, el * FF * D + fc * 128 * D,
                                                    [[D, 128], [1, D]]),
                                        in_=w2b[jj % 2][:]), 16)

        for el in range(2):
            tnw = TN.n
            with nc.Block() as bw1:
                @bw1.gpsimd
                def _(g):
                    g.wait_ge(TN.h, tnw)
                    g.wait_ge(LD.h, LD.n)
                    for dc in range(8):
                        GD.inc(g.dma_start(out=w1_sb[:, dc * FF:(dc + 1) * FF],
                                           in_=bass.AP(w1c.tensor, el * D * FF + dc * 128 * FF,
                                                       [[FF, 128], [1, FF]])), 16)
            gd_w1 = GD.n

            ld1, vc1, gp1 = LD.n, VC.n, GP.n
            with nc.Block() as bgi:
                @bgi.sync
                def _(sy):
                    sy.wait_ge(VC.h, vc1)
                    sy.wait_ge(GP.h, gp1)
                    LD.inc(sy.dma_start(out=cumt[:], in_=bass.AP(tbl_d.tensor, el, [[0, 128], [2, 8]]), allow_slow_non_contiguous=True), 16)
                    LD.inc(sy.dma_start(out=cumpt[:, 1:8], in_=bass.AP(tbl_d.tensor, el, [[0, 128], [2, 7]]), allow_slow_non_contiguous=True), 16)
                    LD.inc(sy.dma_start(out=lstt[:], in_=bass.AP(tbl_d.tensor, 16 + el, [[0, 128], [2, 8]]), allow_slow_non_contiguous=True), 16)

                @bgi.vector
                def _(v):
                    def step(instr):
                        VC.inc(instr)
                        v.wait_ge(VC.h, VC.n)
                    v.wait_ge(LD.h, ld1 + 48)
                    step(v.memset(cumpt[:, 0:1], 0.0))
                    step(v.tensor_tensor(out=addt[:], in0=lstt[:], in1=cumpt[:], op=ALU.subtract))
                    step(v.tensor_tensor(out=addt[:], in0=addt[:], in1=c4096[:], op=ALU.add))
                    step(v.memset(acc[:], 0.0))
                    for cc_ in range(8):
                        step(v.tensor_scalar(out=mt1[:], in0=slotc[:], scalar1=cumpt[:, cc_:cc_ + 1],
                                             scalar2=None, op0=ALU.is_ge))
                        step(v.tensor_scalar(out=mt2[:], in0=slotc[:], scalar1=cumt[:, cc_:cc_ + 1],
                                             scalar2=None, op0=ALU.is_lt))
                        step(v.tensor_tensor(out=mt1[:], in0=mt1[:], in1=mt2[:], op=ALU.mult))
                        step(v.tensor_scalar(out=mt2[:], in0=slotc[:], scalar1=addt[:, cc_:cc_ + 1],
                                             scalar2=None, op0=ALU.add))
                        step(v.tensor_tensor(out=mt1[:], in0=mt1[:], in1=mt2[:], op=ALU.mult))
                        step(v.tensor_tensor(out=acc[:], in0=acc[:], in1=mt1[:], op=ALU.add))
                    step(v.tensor_copy(gidx_u[:], acc[:]))
            vc_gi = VC.n

            for tb in range(8):
                gp2, tn2, vc2, sc2, ld2 = GP.n, TN.n, VC.n, SC.n, LD.n
                with nc.Block() as bg:
                    @bg.gpsimd
                    def _(g):
                        g.wait_ge(VC.h, vc_gi)
                        g.wait_ge(TN.h, tn2)
                        for i4 in range(2):
                            f_ = tb * 2 + i4
                            GP.inc(g.indirect_dma_start(
                                out=gath[:, i4 * D:(i4 + 1) * D], out_offset=None,
                                in_=sorted_all,
                                in_offset=IndirectOffsetOnAxis(gidx_u[:, f_:f_ + 1], axis=0)), 16)

                with nc.Block() as bt:
                    @bt.tensor
                    def _(tn):
                        tn.wait_ge(GP.h, gp2 + 32)
                        tn.wait_ge(VC.h, vc2)
                        for i4 in range(2):
                            if i4 == 1:
                                tn.wait_ge(VC.h, vc2 + 1)
                            for dc in range(8):
                                i = tn.matmul(ps_trb[:, dc * 128:(dc + 1) * 128],
                                              gath[:, i4 * D + dc * 128: i4 * D + dc * 128 + 128],
                                              eyeb[:], start=True, stop=True, is_transpose=True)
                            TN.inc(i)

                    @bt.vector
                    def _(v):
                        for i4 in range(2):
                            v.wait_ge(TN.h, tn2 + i4 + 1)
                            for dc in range(8):
                                i = v.tensor_copy(dispT[:, dc * TB + i4 * 128: dc * TB + (i4 + 1) * 128],
                                                  ps_trb[:, dc * 128:(dc + 1) * 128])
                            VC.inc(i)

                tn3, sc3 = TN.n, SC.n
                with nc.Block() as bh:
                    @bh.tensor
                    def _(tn):
                        VC.wait(tn)
                        tn.wait_ge(GD.h, gd_w1)
                        for fc in range(32):
                            if fc >= 2:
                                tn.wait_ge(SC.h, sc3 + fc - 1)
                            for dc in range(8):
                                i = tn.matmul(psO[4 + (fc % 2)][:, 0:TB],
                                              w1_sb[:, dc * FF + fc * 128: dc * FF + fc * 128 + 128],
                                              dispT[:, dc * TB:(dc + 1) * TB],
                                              start=(dc == 0), stop=(dc == 7))
                            TN.inc(i)

                    @bh.scalar
                    def _(s):
                        for fc in range(32):
                            s.wait_ge(TN.h, tn3 + fc + 1)
                            i = s.activation(out=h_sb[:, fc * TB:(fc + 1) * TB],
                                             in_=psO[4 + (fc % 2)][:, 0:TB], func=AF.Relu)
                            SC.inc(i)

                tn4, sc4, vc4, ld4 = TN.n, SC.n, VC.n, LD.n
                with nc.Block() as bo:
                    @bo.sync
                    def _(sy):
                        sy.wait_ge(VC.h, vc4)
                        for fc in range(32):
                            if fc >= 2:
                                sy.wait_ge(TN.h, tn4 + fc - 1)
                            LD.inc(sy.dma_start(out=w2b[fc % 2][:],
                                                in_=bass.AP(w2bf_d.tensor, el * FF * D + fc * 128 * D,
                                                            [[D, 128], [1, D]])), 16)

                    @bo.tensor
                    def _(tn):
                        for fc in range(32):
                            tn.wait_ge(LD.h, ld4 + (fc + 1) * 16)
                            tn.wait_ge(SC.h, sc4 - 32 + fc + 1)
                            for tt in range(2):
                                for db in range(2):
                                    i = tn.matmul(psO[tt * 2 + db][:],
                                                  h_sb[:, fc * TB + tt * 128: fc * TB + (tt + 1) * 128]
                                                  if tt == 0 else
                                                  h_sb[:, fc * TB + 128: fc * TB + 256],
                                                  w2b[fc % 2][:, db * 512:(db + 1) * 512],
                                                  start=(fc == 0), stop=(fc == 31))
                            TN.inc(i)

                    @bo.vector
                    def _(v):
                        v.wait_ge(TN.h, TN.n)
                        for tt in range(2):
                            for db in range(2):
                                i = v.tensor_copy(out_sb[:, tt * D + db * 512: tt * D + (db + 1) * 512],
                                                  psO[tt * 2 + db][:])
                        VC.inc(i)

                    @bo.sync
                    def _(sy):
                        VC.wait(sy)
                        LD.inc(sy.dma_start(
                            out=bass.AP(eo.tensor, el * CAP * D + tb * TB * D,
                                        [[D, 128], [128 * D, 2], [1, D]]),
                            in_=out_sb[:].rearrange("p (t d) -> p t d", t=2)), 16)

        with nc.Block() as bf:
            @bf.sync
            def _(sy):
                sy.wait_ge(LD.h, LD.n)
                sy.wait_ge(GD.h, GD.n)
                sy.wait_ge(GP.h, GP.n)
    return nc


E_DECLS = [("x1", [D, TOK], F32, "ExternalInput"), ("s1", [D, TOK], BF16, "ExternalInput"),
           ("eo_all", [NE * CAP, D], BF16, "ExternalInput"),
           ("idxo", [128, 32], F32, "ExternalInput"), ("loco", [128, 32], F32, "ExternalInput"),
           ("gso", [128, 32], F32, "ExternalInput"), ("EYEB", [128, 128], BF16, "ExternalInput"),
           ("xout", [D, TOK], F32, "ExternalOutput")]


def build_E(ctx=None):
    nc, stk, sems, io = _phase_setup(ctx, "E_", E_DECLS)
    LD, GD, TN, VC, SC, GP = (sems[n] for n in ("LD", "GD", "TN", "VC", "SC", "GP"))
    (x1, s1, eo_all, idxo, loco, gso, EYEB, xout) = (
        io[n] for n in ("x1", "s1", "eo_all", "idxo", "loco", "gso", "EYEB", "xout"))

    with stk:
        e = stk.enter_context
        eyeb = e(nc.sbuf_tensor("eyeb", [128, 128], BF16))
        idx_sb = e(nc.sbuf_tensor("idx_sb", [128, 32], F32))
        loc_sb = e(nc.sbuf_tensor("loc_sb", [128, 32], F32))
        gs_sb = e(nc.sbuf_tensor("gs_sb", [128, 32], F32))
        tmp = e(nc.sbuf_tensor("tmp", [128, 8], F32))
        gi_u = e(nc.sbuf_tensor("gi_u", [128, 1], U32))
        moe = e(nc.sbuf_tensor("moe", [128, D], BF16))
        gnat = e(nc.sbuf_tensor("gnat", [128, D], BF16))
        gT = e(nc.sbuf_tensor("gT", [128, D], BF16))
        s1t = e(nc.sbuf_tensor("s1t", [128, D], BF16))
        x1t = e(nc.sbuf_tensor("x1t", [128, D], F32))
        t2 = e(nc.sbuf_tensor("t2", [128, D], BF16))
        outt = e(nc.sbuf_tensor("outt", [128, D], F32))
        ps_trb = e(nc.psum_tensor("ps_trb", [128, 1024], BF16))

        x1_r = x1.rearrange("(c p) t -> p c t", p=128)
        s1_r = s1.rearrange("(c p) t -> p c t", p=128)
        xo_r = xout.rearrange("(c p) t -> p c t", p=128)

        with nc.Block() as b0:
            @b0.sync
            def _(sy):
                for d, s_ in ((eyeb[:], EYEB), (idx_sb[:], idxo), (loc_sb[:], loco),
                              (gs_sb[:], gso)):
                    LD.inc(sy.dma_start(out=d, in_=s_), 16)

        for g_ in range(32):
            ld0, vc0, gp0, tn0, sc0 = LD.n, VC.n, GP.n, TN.n, SC.n
            with nc.Block() as bt:
                @bt.vector
                def _(v):
                    def step(instr):
                        VC.inc(instr)
                        v.wait_ge(VC.h, VC.n)
                    if g_ == 0:
                        v.wait_ge(LD.h, ld0)
                    else:
                        v.wait_ge(LD.h, ld0)
                        v.wait_ge(GP.h, gp0)
                    step(v.tensor_scalar(out=tmp[:, 0:1], in0=loc_sb[:, g_:g_ + 1],
                                         scalar1=float(CAP - 1), scalar2=None, op0=ALU.min))
                    step(v.scalar_tensor_tensor(out=tmp[:, 1:2], in0=idx_sb[:, g_:g_ + 1],
                                                scalar=float(CAP), in1=tmp[:, 0:1],
                                                op0=ALU.mult, op1=ALU.add))
                    step(v.tensor_copy(gi_u[:], tmp[:, 1:2]))
                    step(v.tensor_scalar(out=tmp[:, 2:3], in0=loc_sb[:, g_:g_ + 1],
                                         scalar1=float(CAP), scalar2=None, op0=ALU.is_lt))
                    step(v.tensor_tensor(out=tmp[:, 3:4], in0=gs_sb[:, g_:g_ + 1],
                                         in1=tmp[:, 2:3], op=ALU.mult))

                @bt.gpsimd
                def _(g):
                    VC.wait(g)
                    GP.inc(g.indirect_dma_start(
                        out=moe[:], out_offset=None, in_=eo_all,
                        in_offset=IndirectOffsetOnAxis(gi_u[:, 0:1], axis=0)), 16)

                @bt.vector
                def _(v):
                    GP.wait(v)
                    if g_ > 0:
                        v.wait_ge(SC.h, sc0)
                    i = v.tensor_scalar(out=gnat[:], in0=moe[:], scalar1=tmp[:, 3:4],
                                        scalar2=None, op0=ALU.mult)
                    VC.inc(i)

                @bt.scalar
                def _(s):
                    VC.wait(s)
                    SC.inc(s.activation(out=gnat[:], in_=gnat[:], func=AF.Silu))

                @bt.tensor
                def _(tn):
                    SC.wait(tn)
                    if g_ > 0:
                        tn.wait_ge(VC.h, vc0)
                    for dc in range(8):
                        i = tn.matmul(ps_trb[:, dc * 128:(dc + 1) * 128],
                                      gnat[:, dc * 128:(dc + 1) * 128], eyeb[:],
                                      start=True, stop=True, is_transpose=True)
                    TN.inc(i)

                @bt.sync
                def _(sy):
                    sy.wait_ge(VC.h, vc0)
                    LD.inc(sy.dma_start(out=s1t[:].rearrange("p (c t) -> p c t", c=8),
                                        in_=s1_r[:, :, g_ * 128:(g_ + 1) * 128]), 16)
                    LD.inc(sy.dma_start(out=x1t[:].rearrange("p (c t) -> p c t", c=8),
                                        in_=x1_r[:, :, g_ * 128:(g_ + 1) * 128]), 16)

                @bt.vector
                def _(v):
                    v.wait_ge(TN.h, TN.n)
                    for dc in range(8):
                        i = v.tensor_copy(gT[:, dc * 128:(dc + 1) * 128],
                                          ps_trb[:, dc * 128:(dc + 1) * 128])
                    VC.inc(i)
                    v.wait_ge(LD.h, ld0 + (32 if g_ == 0 else 32))
                    v.wait_ge(LD.h, LD.n)
                    v.tensor_tensor(out=t2[:], in0=s1t[:], in1=gT[:], op=ALU.mult)
                    i = v.tensor_tensor(out=outt[:], in0=t2[:], in1=x1t[:], op=ALU.add)
                    VC.inc(i)

                @bt.sync
                def _(sy):
                    VC.wait(sy)
                    LD.inc(sy.dma_start(out=xo_r[:, :, g_ * 128:(g_ + 1) * 128],
                                        in_=outt[:].rearrange("p (c t) -> p c t", c=8)), 16)

        with nc.Block() as bf:
            @bf.sync
            def _(sy):
                sy.wait_ge(LD.h, LD.n)
                sy.wait_ge(GP.h, GP.n)
    return nc


_PROGS = {}


def _prog(name, builder):
    if name not in _PROGS:
        _PROGS[name] = builder()
    return _PROGS[name]


def _qkv_perm():
    perm = np.zeros(3072, np.int64)
    for h in range(H):
        gg, s = h // 4, h % 4
        for d_ in range(DH):
            fold = h * DH + d_
            if d_ % 2 == 0:
                new = (2 * gg) * 128 + s * 32 + d_ // 2
            else:
                new = (2 * gg + 1) * 128 + s * 32 + (d_ - 1) // 2
            perm[new] = fold
            perm[1024 + new] = 1024 + fold
    perm[2048:] = np.arange(2048, 3072)
    return perm


def kernel_unfused(**inputs):
    f = np.float32
    cst = host_consts()
    x = np.ascontiguousarray(np.asarray(inputs["L0_attn_x"], f))
    onesf = np.ones((128, 128), f)
    lstf = (np.arange(128)[:, None] < np.arange(128)[None, :]).astype(f)
    zm4 = np.ascontiguousarray(np.concatenate(list(cst["ZM"]), axis=1))
    eyeb = np.eye(128).astype(BF)
    perm = _qkv_perm()

    xT = [np.ascontiguousarray(x[2 * c:2 * c + 2].reshape(TOK, D).T) for c in range(NCORES)]

    wq = np.ascontiguousarray(np.asarray(inputs["L0_attn_Wq"], f))
    wk = np.ascontiguousarray(np.asarray(inputs["L0_attn_Wk"], f))
    wv = np.ascontiguousarray(np.asarray(inputs["L0_attn_Wv"], f))
    wo = np.ascontiguousarray(np.asarray(inputs["L0_attn_Wo"], f))
    rel = np.ascontiguousarray(np.asarray(inputs["L0_attn_rel_emb"], f))
    resA = run_bass_kernel_spmd(
        _prog("A", build_A),
        [dict(xT=xT[c], wq=wq, wk=wk, wv=wv, wo=wo, relemb=rel,
              OH=cst["OH"], ONESB=cst["ONESB"]) for c in range(NCORES)],
        core_ids=list(range(NCORES)))

    wfcT = np.ascontiguousarray(np.asarray(inputs["L0_ffn_W_fc"], f).T)
    wprojT = np.ascontiguousarray(np.asarray(inputs["L0_ffn_W_proj"], f).T)
    lnw = np.ascontiguousarray(np.asarray(inputs["L0_ffn_ln_w"], f).reshape(32, 128).T)
    lnb = np.ascontiguousarray(np.asarray(inputs["L0_ffn_ln_b"], f).reshape(32, 128).T)
    resB = run_bass_kernel_spmd(
        _prog("B", build_B),
        [dict(xT=xT[c], xn0=np.asarray(resA.results[c]["xn0"]),
              s0=np.asarray(resA.results[c]["s0"]),
              wfcT=wfcT, wprojT=wprojT, lnw=lnw, lnb=lnb,
              ONESB=cst["ONESB"], ONESF=onesf) for c in range(NCORES)],
        core_ids=list(range(NCORES)))
    x1 = [np.asarray(resB.results[c]["x1"]) for c in range(NCORES)]

    qkvw_p = np.ascontiguousarray(np.asarray(inputs["L1_attn_qkv_w"], f)[:, perm])
    qkvb_p = np.ascontiguousarray(np.asarray(inputs["L1_attn_qkv_b"], f)[perm].reshape(24, 128).T)
    bv = np.ascontiguousarray(np.asarray(inputs["L1_attn_qkv_b"], f)[2048:3072][None, :])
    outw = np.ascontiguousarray(np.asarray(inputs["L1_attn_out_w"], f))
    outb = np.ascontiguousarray(np.asarray(inputs["L1_attn_out_b"], f).reshape(8, 128).T)
    wg = np.ascontiguousarray(np.asarray(inputs["L1_ffn_Wg"], f))
    resC = run_bass_kernel_spmd(
        _prog("C", build_C),
        [dict(x1=x1[c], qkvw=qkvw_p, qkvb=qkvb_p, outw=outw, outb=outb, wg=wg,
              CCb=cst["CC"].astype(BF), SSb=cst["SS"].astype(BF), ZM4=zm4.astype(BF), BV=bv,
              ONESB=cst["ONESB"], ONESF=onesf, LSTF=lstf,
              IOTA16=cst["IOTA16"], LSU16F=cst["LSU16"], EYEB=eyeb) for c in range(NCORES)],
        core_ids=list(range(NCORES)))

    sorted_all = np.ascontiguousarray(np.concatenate(
        [np.asarray(resC.results[c]["sorted_o"]) for c in range(NCORES)], 0))
    counts_all = np.ascontiguousarray(np.concatenate(
        [np.asarray(resC.results[c]["counts_o"]) for c in range(NCORES)], 0))
    w1 = np.asarray(inputs["L1_ffn_w1"], f)
    w2 = np.asarray(inputs["L1_ffn_w2"], f)
    dmaps = []
    for c in range(NCORES):
        selm = (np.arange(NE)[:, None] < (2 * c + np.arange(2))[None, :]).astype(f)
        esel = (np.arange(NE)[:, None] == (2 * c + np.arange(2))[None, :]).astype(f)
        dmaps.append(dict(
            sorted_all=sorted_all, counts_all=counts_all,
            ESEL=np.ascontiguousarray(esel),
            SELM=np.ascontiguousarray(selm),
            w1c=np.ascontiguousarray(w1[2 * c:2 * c + 2]),
            w2c=np.ascontiguousarray(w2[2 * c:2 * c + 2]),
            LI8=cst["LI8"], SLOTC=cst["SLOT"],
            C4096=(4096.0 * np.arange(8, dtype=f))[None, :].repeat(128, 0),
            EYEB=eyeb))
    resD = run_bass_kernel_spmd(_prog("D", build_D), dmaps, core_ids=list(range(NCORES)))

    eo_all = np.ascontiguousarray(np.concatenate(
        [np.asarray(resD.results[c]["eo"]) for c in range(NCORES)], 0))
    resE = run_bass_kernel_spmd(
        _prog("E", build_E),
        [dict(x1=x1[c], s1=np.asarray(resC.results[c]["s1"]), eo_all=eo_all,
              idxo=np.asarray(resC.results[c]["idxo"]),
              loco=np.asarray(resC.results[c]["loco"]),
              gso=np.asarray(resC.results[c]["gso"]), EYEB=eyeb) for c in range(NCORES)],
        core_ids=list(range(NCORES)))

    out = np.concatenate(
        [np.asarray(resE.results[c]["xout"], f).T.reshape(B_LOC, T, D) for c in range(NCORES)], 0)
    return out



def emit_allgather(nc, sems, cc, in_ap, out_ap):
    with nc.Block() as b:
        @b.gpsimd
        def _(g):
            i = g.collective_compute("AllGather", ALU.bypass,
                                     replica_groups=[list(range(NCORES))],
                                     ins=[in_ap.opt()], outs=[out_ap.opt()])
            cc.inc(i, 1)
    m = cc.n
    with nc.Block() as b2:
        for en in ("tensor", "vector", "scalar", "gpsimd", "sync"):
            def _mk(e, _m=m, _cc=cc):
                e.wait_ge(_cc.h, _m)
            getattr(b2, en)(_mk)


def build_fused():
    nc, stk, sems = new_prog()
    reg = {}
    ctx = (nc, sems, reg)
    with stk:
        cc = [Sem(stk, nc, f"CC{i}") for i in range(3)]
        build_A(ctx=ctx)
        emit_fence(nc, sems)
        build_B(ctx=ctx)
        emit_fence(nc, sems)
        build_C(ctx=ctx)
        emit_fence(nc, sems)
        reg["sorted_all"] = nc.dram_tensor("sorted_all", [8 * TOK, D], BF16, kind="Internal").ap()
        reg["counts_all"] = nc.dram_tensor("counts_all", [8, NE], F32, kind="Internal").ap()
        emit_allgather(nc, sems, cc[0], reg["sorted_o"], reg["sorted_all"])
        emit_allgather(nc, sems, cc[1], reg["counts_o"], reg["counts_all"])
        build_D(ctx=ctx)
        emit_fence(nc, sems)
        reg["eo_all"] = nc.dram_tensor("eo_all", [NE * CAP, D], BF16, kind="Internal").ap()
        emit_allgather(nc, sems, cc[2], reg["eo"], reg["eo_all"])
        build_E(ctx=ctx)
        emit_fence(nc, sems)
    return nc


def _host_inmaps(inputs):
    f = np.float32
    cst = host_consts()
    x = np.ascontiguousarray(np.asarray(inputs["L0_attn_x"], f))
    onesf = np.ones((128, 128), f)
    lstf = (np.arange(128)[:, None] < np.arange(128)[None, :]).astype(f)
    zm4 = np.ascontiguousarray(np.concatenate(list(cst["ZM"]), axis=1))
    eyeb = np.eye(128).astype(BF)
    perm = _qkv_perm()
    w1 = np.asarray(inputs["L1_ffn_w1"], f)
    w2 = np.asarray(inputs["L1_ffn_w2"], f)
    shared = dict(
        wq=np.ascontiguousarray(np.asarray(inputs["L0_attn_Wq"], f)),
        wk=np.ascontiguousarray(np.asarray(inputs["L0_attn_Wk"], f)),
        wv=np.ascontiguousarray(np.asarray(inputs["L0_attn_Wv"], f)),
        wo=np.ascontiguousarray(np.asarray(inputs["L0_attn_Wo"], f)),
        relemb=np.ascontiguousarray(np.asarray(inputs["L0_attn_rel_emb"], f)),
        OH=cst["OH"], ONESB=cst["ONESB"], ONESF=onesf,
        wfcT=np.ascontiguousarray(np.asarray(inputs["L0_ffn_W_fc"], f).T),
        wprojT=np.ascontiguousarray(np.asarray(inputs["L0_ffn_W_proj"], f).T),
        lnw=np.ascontiguousarray(np.asarray(inputs["L0_ffn_ln_w"], f).reshape(32, 128).T),
        lnb=np.ascontiguousarray(np.asarray(inputs["L0_ffn_ln_b"], f).reshape(32, 128).T),
        qkvw=np.ascontiguousarray(np.asarray(inputs["L1_attn_qkv_w"], f)[:, perm]),
        qkvb=np.ascontiguousarray(np.asarray(inputs["L1_attn_qkv_b"], f)[perm].reshape(24, 128).T),
        BV=np.ascontiguousarray(np.asarray(inputs["L1_attn_qkv_b"], f)[2048:3072][None, :]),
        outw=np.ascontiguousarray(np.asarray(inputs["L1_attn_out_w"], f)),
        outb=np.ascontiguousarray(np.asarray(inputs["L1_attn_out_b"], f).reshape(8, 128).T),
        wg=np.ascontiguousarray(np.asarray(inputs["L1_ffn_Wg"], f)),
        CCb=cst["CC"].astype(BF), SSb=cst["SS"].astype(BF), ZM4=zm4.astype(BF),
        LSTF=lstf, IOTA16=cst["IOTA16"], LSU16F=cst["LSU16"], EYEB=eyeb,
        LI8=cst["LI8"], SLOTC=cst["SLOT"],
        C4096=(4096.0 * np.arange(8, dtype=f))[None, :].repeat(128, 0),
    )
    maps = []
    for c in range(NCORES):
        m = dict(shared)
        m["xT"] = np.ascontiguousarray(x[2 * c:2 * c + 2].reshape(TOK, D).T)
        m["SELM"] = np.ascontiguousarray(
            (np.arange(NE)[:, None] < (2 * c + np.arange(2))[None, :]).astype(f))
        m["ESEL"] = np.ascontiguousarray(
            (np.arange(NE)[:, None] == (2 * c + np.arange(2))[None, :]).astype(f))
        m["w1c"] = np.ascontiguousarray(w1[2 * c:2 * c + 2])
        m["w2c"] = np.ascontiguousarray(w2[2 * c:2 * c + 2])
        maps.append(m)
    return maps


def kernel(**inputs):
    f = np.float32
    maps = _host_inmaps(inputs)
    res = run_bass_kernel_spmd(_prog("F", build_fused), maps, core_ids=list(range(NCORES)))
    out = np.concatenate(
        [np.asarray(res.results[c]["xout"], f).T.reshape(B_LOC, T, D) for c in range(NCORES)], 0)
    return out


# version 04
# speedup vs baseline: 1.0695x; 1.0695x over previous
import math
from contextlib import ExitStack

import numpy as np
import ml_dtypes

import concourse.bass as bass
import concourse.mybir as mybir
from concourse.bass import IndirectOffsetOnAxis
from concourse.bass_utils import run_bass_kernel_spmd

DT = mybir.dt
F32 = DT.float32
BF16 = DT.bfloat16
U32 = DT.uint32
AF = mybir.ActivationFunctionType
ALU = mybir.AluOpType
AX = mybir.AxisListType
BF = ml_dtypes.bfloat16

NCORES = 8
B_LOC = 2
T = 2048
TOK = B_LOC * T
D = 1024
H = 16
DH = 64
NB = 32
FF = 4096
NE = 16
CAP = 2048
BLEN = 2 * T - 1
BPAD = 4608


class Sem:
    def __init__(self, stack, nc, name):
        self.h = stack.enter_context(nc.semaphore(name))
        self.n = 0

    def inc(self, instr, k=1):
        instr.then_inc(self.h, k)
        self.n += k
        return self.n

    def wait(self, eng, n=None):
        eng.wait_ge(self.h, self.n if n is None else n)


def new_prog():
    nc = bass.Bass("TRN2", target_bir_lowering=False)
    stk = ExitStack()
    sems = {n: Sem(stk, nc, n) for n in ("LD", "GD", "TN", "VC", "SC", "GP")}
    return nc, stk, sems


class _NCProxy:

    def __init__(self, nc, prefix):
        self._nc = nc
        self._p = prefix

    def sbuf_tensor(self, name, *a, **k):
        return self._nc.sbuf_tensor(self._p + name, *a, **k)

    def psum_tensor(self, name, *a, **k):
        return self._nc.psum_tensor(self._p + name, *a, **k)

    def __getattr__(self, a):
        return getattr(self._nc, a)


def _phase_setup(ctx, prefix, decls):
    if ctx is None:
        nc, stk, sems = new_prog()
        reg = {}
    else:
        nc0, sems, reg = ctx
        nc = _NCProxy(nc0, prefix)
        stk = ExitStack()
    io = {}
    for name, shape, dt_, kind in decls:
        if name in reg:
            io[name] = reg[name]
        else:
            k = kind
            if ctx is not None and kind == "ExternalOutput" and name != "xout":
                k = "Internal"
            ap = nc.dram_tensor(name, shape, dt_, kind=k).ap()
            reg[name] = ap
            io[name] = ap
    return nc, stk, sems, io


def emit_fence(nc, sems):
    marks = {k: s.n for k, s in sems.items()}
    with nc.Block() as bf:
        for en in ("tensor", "vector", "scalar", "gpsimd", "sync"):
            def _mk(e, _m=marks):
                for k, s in sems.items():
                    e.wait_ge(s.h, _m[k])
            getattr(bf, en)(_mk)


def _bucket_of(d):
    nb = NB // 2
    n = -d
    ret = (n < 0).astype(np.int32) * nb
    n = np.abs(n)
    max_exact = nb // 2
    is_small = n < max_exact
    n_safe = np.maximum(n, 1).astype(np.float32)
    val = max_exact + (np.log(n_safe / max_exact) / math.log(128 / max_exact)
                       * (nb - max_exact)).astype(np.int32)
    val = np.minimum(val, nb - 1)
    return ret + np.where(is_small, n, val)


def host_consts():
    c = {}
    j = np.arange(BLEN)
    bk = _bucket_of((j - (T - 1)).astype(np.int64))
    oh = np.zeros((NB, BPAD), np.float32)
    oh[bk, j] = 1.0
    c["OH"] = oh.astype(BF)
    c["ONESB"] = np.ones((128, 512), BF)
    zm = np.zeros((4, 128, 512), np.float32)
    ki = np.arange(128)[:, None]
    qi = np.arange(512)[None, :]
    for jj in range(4):
        zm[jj] = (qi >= ki + jj * 128)
    c["ZM"] = zm.astype(BF)
    hd = DH
    inv_freq = (1.0 / 10000.0 ** (np.arange(0, hd, 2, dtype=np.float32) / hd))
    t = np.arange(T, dtype=np.float32)
    freqs = t[:, None] * inv_freq[None, :]
    rope = np.concatenate([np.sin(freqs), np.cos(freqs)], -1).astype(np.float32)
    sinv = rope[:, 0::2]
    cosv = rope[:, 1::2]
    rows = np.tile(np.arange(32), 4)
    c["CC"] = np.ascontiguousarray(cosv[:, rows].T)
    c["SS"] = np.ascontiguousarray(sinv[:, rows].T)
    c["LSTRICT"] = (np.arange(128)[:, None] < np.arange(128)[None, :]).astype(BF)
    c["IOTA16"] = np.tile(np.arange(NE, dtype=np.float32), (128, 1))
    c["LSU16"] = (np.arange(NE)[:, None] < np.arange(NE)[None, :]).astype(np.float32)
    c["LI8"] = (np.arange(8)[:, None] <= np.arange(8)[None, :]).astype(np.float32)
    c["SLOT"] = (np.arange(128)[:, None] + 128.0 * np.arange(16)[None, :]).astype(np.float32)
    return c


S_TT = 0
S_PRAW = 6144
S_PT = 8192
S_RCP = 10240
S_SEV = 10752
S_CTQ = 14848
S_XBF = 0
S_XSQ = 4096
S_MU = 8192
S_RS = 8704
S_TD = 9216
S_OHC = 0
S_BIA = 4608
S_EXB = 9216
SCRB_N = 18944


def emit_ln(nc, sems, onesb, scrB, scrF, src_dram, xn_sb, xn_out_dram, bofs, ps_sum, ps_sq):
    LD, GD, TN, VC, SC = (sems[n] for n in ("LD", "GD", "TN", "VC", "SC"))
    xbf = scrB[:, S_XBF:S_XBF + 4096]
    xsq = scrB[:, S_XSQ:S_XSQ + 4096]
    mu = scrB[:, S_MU:S_MU + 512]
    rs = scrB[:, S_RS:S_RS + 512]
    td = scrB[:, S_TD:S_TD + 512]
    ex2 = scrF[:, 0:512]
    mu2 = scrF[:, 512:1024]
    var = scrF[:, 1024:1536]
    src_r = src_dram.rearrange("(c p) t -> p c t", p=128)

    for blk in range(4):
        t0 = bofs + blk * 512
        tn_w, vc_w, sc_w = TN.n, VC.n, SC.n
        with nc.Block() as b:
            @b.gpsimd
            def _(g):
                g.wait_ge(TN.h, tn_w)
                g.wait_ge(VC.h, vc_w)
                g.wait_ge(SC.h, sc_w)
                GD.inc(g.dma_start(out=xbf.rearrange("p (c t) -> p c t", c=8),
                                   in_=src_r[:, :, t0:t0 + 512]), 16)

            @b.scalar
            def _(s):
                GD.wait(s)
                SC.inc(s.activation(out=xsq, in_=xbf, func=AF.Square))

            @b.tensor
            def _(tn):
                GD.wait(tn)
                for c in range(8):
                    tn.matmul(ps_sum[:], onesb[:, 0:128], xbf[:, c * 512:(c + 1) * 512],
                              start=(c == 0), stop=(c == 7))
                SC.wait(tn)
                for c in range(8):
                    i = tn.matmul(ps_sq[:], onesb[:, 0:128], xsq[:, c * 512:(c + 1) * 512],
                                  start=(c == 0), stop=(c == 7))
                TN.inc(i)

            @b.vector
            def _(v):
                TN.wait(v)
                v.tensor_scalar(out=mu, in0=ps_sum[:], scalar1=1.0 / D, scalar2=None, op0=ALU.mult)
                v.tensor_scalar(out=ex2, in0=ps_sq[:], scalar1=1.0 / D, scalar2=1e-5,
                                op0=ALU.mult, op1=ALU.add)
                v.tensor_tensor(out=mu2, in0=mu, in1=mu, op=ALU.mult)
                VC.inc(v.tensor_tensor(out=var, in0=ex2, in1=mu2, op=ALU.subtract))

            @b.scalar
            def _(s):
                VC.wait(s)
                s.activation(out=var, in_=var, func=AF.Ln)
                SC.inc(s.activation(out=rs, in_=var, func=AF.Exp, scale=-0.5))

            @b.vector
            def _(v):
                SC.wait(v)
                for c in range(8):
                    v.tensor_tensor(out=td, in0=xbf[:, c * 512:(c + 1) * 512], in1=mu,
                                    op=ALU.subtract)
                    i = v.tensor_tensor(out=xn_sb[:, c * T + blk * 512: c * T + blk * 512 + 512],
                                        in0=td, in1=rs, op=ALU.mult)
                VC.inc(i)

            @b.sync
            def _(sy):
                VC.wait(sy)
                if xn_out_dram is not None:
                    xr = xn_out_dram.rearrange("(c p) t -> p c t", p=128)
                    LD.inc(sy.dma_start(
                        out=xr[:, :, t0:t0 + 512],
                        in_=xn_sb[:].rearrange("p (c t) -> p c t", c=8)[:, :, blk * 512:(blk + 1) * 512]), 16)


def emit_attention_batch(nc, sems, sb, layer, b, s_out_dram, wo_bias_col=None):
    LD, GD, TN, VC, SC = (sems[n] for n in ("LD", "GD", "TN", "VC", "SC"))
    is_l1 = (layer == 1)
    scale = (DH ** -0.5) if is_l1 else 1.0
    q_sb, k_sb, v_sb, wo_sb = sb["q_sb"], sb["k_sb"], sb["v_sb"], sb["wo_sb"]
    onesb = sb["onesb"]
    scrB = sb["scrB"]
    psS = [sb[f"psS{i}"] for i in range(4)]
    ps_ctx, ps_den = sb["ps_ctx"], sb["ps_den"]
    ttile = scrB[:, S_TT:S_TT + 6144]
    praw = scrB[:, S_PRAW:S_PRAW + 2048]
    ptile = scrB[:, S_PT:S_PT + 2048]
    rcp = sb["scrF"][:, 0:512]
    sev = scrB[:, S_SEV:S_SEV + 4096]
    ctxq = scrB[:, S_CTQ:S_CTQ + 4096]

    def tile_kind(o):
        if is_l1:
            if o > 384:
                return "skip"
            return "mask" if o >= 0 else "plain"
        if -128 <= o <= 512:
            return "band"
        return "cpos" if o > 512 else "cneg"

    for qb in range(4):
        for hp in range(8):
            hA, hB = 2 * hp, 2 * hp + 1
            if not is_l1:
                vc_w = VC.n
                with nc.Block() as blb:
                    @blb.sync
                    def _(sy):
                        sy.wait_ge(VC.h, vc_w)
                        for hl, h in ((0, hA), (1, hB)):
                            src = bass.AP(sb["expb_d_t"], h * BPAD + 1408,
                                          [[1, 128], [128, 6], [1, 512]])
                            LD.inc(sy.dma_start(
                                out=ttile[:, hl * 3072:(hl + 1) * 3072]
                                .rearrange("p (j t) -> p j t", j=6),
                                in_=src), 16)
            ld_t = LD.n

            alive = [kt for kt in range(16) if tile_kind(kt * 128 - qb * 512) != "skip"]
            n = len(alive)
            tn0, sc0, vc0 = TN.n, SC.n, VC.n
            order = []
            for j in range(n):
                order.append(("qk", j))
                if j >= 2:
                    order.append(("pv", j - 2))
            order += [("pv", n - 2), ("pv", n - 1)] if n >= 2 else [("pv", 0)]
            tn_after_qk, tn_after_pv, sc_after_exp, vc_after_mul = {}, {}, {}, {}
            tn_c, sc_c, vc_c = tn0, sc0, vc0
            for op, j in order:
                tn_c += 1
                (tn_after_qk if op == "qk" else tn_after_pv)[j] = tn_c
            for j in range(n):
                kind = tile_kind(alive[j] * 128 - qb * 512)
                sc_c += 1
                sc_after_exp[j] = sc_c
                if kind in ("band", "mask"):
                    vc_c += 1
                    vc_after_mul[j] = vc_c

            with nc.Block() as bl:
                @bl.tensor
                def _(tn):
                    if hp == 0:
                        tn.wait_ge(SC.h, sc0)
                        tn.wait_ge(VC.h, vc0)
                    for op, j in order:
                        kt = alive[j]
                        o = kt * 128 - qb * 512
                        t = j % 2
                        if op == "qk":
                            if j >= 2:
                                tn.wait_ge(SC.h, sc_after_exp[j - 2])
                                if (j - 2) in vc_after_mul:
                                    tn.wait_ge(VC.h, vc_after_mul[j - 2])
                            for (ps, h) in ((psS[t], hA), (psS[2 + t], hB)):
                                if not is_l1:
                                    c = h // 2
                                    r0 = (h % 2) * 64
                                    i = tn.matmul(
                                        ps[:],
                                        k_sb[r0:r0 + 64, c * T + kt * 128: c * T + kt * 128 + 128],
                                        q_sb[r0:r0 + 64, c * T + qb * 512: c * T + qb * 512 + 512],
                                        start=True, stop=True)
                                else:
                                    gg, s_ = h // 4, h % 4
                                    r0 = 32 * s_
                                    tp = (96, 0) if s_ == 3 else None
                                    for ii, cch in enumerate((2 * gg, 2 * gg + 1)):
                                        i = tn.matmul(
                                            ps[:],
                                            k_sb[r0:r0 + 32, cch * T + kt * 128: cch * T + kt * 128 + 128],
                                            q_sb[r0:r0 + 32, cch * T + qb * 512: cch * T + qb * 512 + 512],
                                            start=(ii == 0), stop=(ii == 1), tile_position=tp)
                            i.then_inc(TN.h, 1)
                        else:
                            kind = tile_kind(o)
                            if kind in ("band", "mask"):
                                tn.wait_ge(VC.h, vc_after_mul[j])
                            else:
                                tn.wait_ge(SC.h, sc_after_exp[j])
                            first = (j == 0)
                            last = (j == n - 1)
                            for hl, h in ((0, hA), (1, hB)):
                                p_ap = ptile[:, (hl * 2 + t) * 512:(hl * 2 + t + 1) * 512]
                                vt = v_sb[:, kt * 1024 + h * 64: kt * 1024 + h * 64 + 64]
                                tn.matmul(ps_ctx[hl * 64:(hl + 1) * 64, :], vt, p_ap,
                                          start=first, stop=last, skip_group_check=True)
                                i = tn.matmul(ps_den[hl * 64:(hl + 1) * 64, :], onesb[:, 0:64], p_ap,
                                              start=first, stop=last, skip_group_check=True)
                            i.then_inc(TN.h, 1)
                    TN.n = tn_c

                @bl.scalar
                def _(s):
                    for j in range(n):
                        kt = alive[j]
                        o = kt * 128 - qb * 512
                        kind = tile_kind(o)
                        t = j % 2
                        if j == 0:
                            s.wait_ge(VC.h, vc0)
                        s.wait_ge(TN.h, tn_after_pv.get(j - 2, tn_after_qk[j]))
                        if (j - 2) in vc_after_mul:
                            s.wait_ge(VC.h, vc_after_mul[j - 2])
                        for (ps, h, hl) in ((psS[t], hA, 0), (psS[2 + t], hB, 1)):
                            if kind in ("band", "mask"):
                                dst = praw[:, (hl * 2 + t) * 512:(hl * 2 + t + 1) * 512]
                            else:
                                dst = ptile[:, (hl * 2 + t) * 512:(hl * 2 + t + 1) * 512]
                            bias = 0.0
                            if kind == "cpos":
                                bias = sb["cb"][:, 2 * h:2 * h + 1]
                            elif kind == "cneg":
                                bias = sb["cb"][:, 2 * h + 1:2 * h + 2]
                            i = s.activation(out=dst, in_=ps[:], func=AF.Exp, scale=scale, bias=bias)
                        i.then_inc(SC.h, 1)
                    SC.n = sc_c

                @bl.vector
                def _(v):
                    first_band = True
                    for j in range(n):
                        kt = alive[j]
                        o = kt * 128 - qb * 512
                        kind = tile_kind(o)
                        if kind not in ("band", "mask"):
                            continue
                        t = j % 2
                        if first_band:
                            if not is_l1:
                                v.wait_ge(LD.h, ld_t)
                            v.wait_ge(TN.h, tn0)
                            first_band = False
                        v.wait_ge(SC.h, sc_after_exp[j])
                        if j >= 2:
                            v.wait_ge(TN.h, tn_after_pv[j - 2])
                        for hl in range(2):
                            if kind == "band":
                                jdx = (o + 128) // 128
                                m = ttile[:, hl * 3072 + jdx * 512: hl * 3072 + (jdx + 1) * 512][:, ::-1]
                            else:
                                m = sb["zm"][:, (o // 128) * 512:(o // 128 + 1) * 512]
                            i = v.tensor_tensor(
                                out=ptile[:, (hl * 2 + t) * 512:(hl * 2 + t + 1) * 512],
                                in0=praw[:, (hl * 2 + t) * 512:(hl * 2 + t + 1) * 512], in1=m,
                                op=ALU.mult)
                        i.then_inc(VC.h, 1)
                    VC.n = vc_c

                @bl.vector
                def _(v):
                    v.wait_ge(TN.h, tn_after_pv[n - 1])
                    v.reciprocal(rcp, ps_den[:])
                    i = v.tensor_tensor(out=ctxq[:, hp * 512:(hp + 1) * 512],
                                        in0=ps_ctx[:], in1=rcp, op=ALU.mult)
                    VC.inc(i)
            vc_g = VC.n
            with nc.Block() as blg:
                @blg.tensor
                def _(tn):
                    tn.wait_ge(VC.h, vc_g)

        vc_w, sc_w, ld_w = VC.n, SC.n, LD.n
        with nc.Block() as bw:
            @bw.tensor
            def _(tn):
                tn.wait_ge(VC.h, vc_w)
                tn.wait_ge(SC.h, sc_w)
                for dc in range(8):
                    if dc >= 2:
                        tn.wait_ge(SC.h, sc_w + dc - 1)
                    for c in range(8):
                        i = tn.matmul(psS[dc % 2][:],
                                      wo_sb[:, c * 1024 + dc * 128: c * 1024 + dc * 128 + 128],
                                      ctxq[:, c * 512:(c + 1) * 512],
                                      start=(c == 0), stop=(c == 7))
                    TN.inc(i)

            @bw.scalar
            def _(s):
                base = TN.n - 8
                s.wait_ge(LD.h, ld_w)
                for dc in range(8):
                    s.wait_ge(TN.h, base + dc + 1)
                    bias = 0.0 if wo_bias_col is None else wo_bias_col[:, dc:dc + 1]
                    i = s.activation(out=sev[:, dc * 512:(dc + 1) * 512], in_=psS[dc % 2][:],
                                     func=AF.Copy if wo_bias_col is None else AF.Identity,
                                     bias=bias)
                    SC.inc(i)

            @bw.sync
            def _(sy):
                SC.wait(sy)
                sr = s_out_dram.rearrange("(c p) t -> p c t", p=128)
                LD.inc(sy.dma_start(
                    out=sr[:, :, b * T + qb * 512: b * T + qb * 512 + 512],
                    in_=sev.rearrange("p (c t) -> p c t", c=8)), 16)


A_DECLS = [("xT", [D, TOK], F32, "ExternalInput"), ("wq", [D, D], F32, "ExternalInput"),
           ("wk", [D, D], F32, "ExternalInput"), ("wv", [D, D], F32, "ExternalInput"),
           ("wo", [D, D], F32, "ExternalInput"), ("relemb", [NB, H], F32, "ExternalInput"),
           ("OH", [NB, BPAD], BF16, "ExternalInput"), ("ONESB", [128, 512], BF16, "ExternalInput"),
           ("s0", [D, TOK], BF16, "ExternalOutput"), ("xn0", [D, TOK], BF16, "ExternalOutput")]


def build_A(debug=False, ctx=None):
    nc, stk, sems, io = _phase_setup(ctx, "A_", A_DECLS)
    LD, GD, TN, VC, SC = (sems[n] for n in ("LD", "GD", "TN", "VC", "SC"))
    xT, wq, wk, wv, wo, relemb, OH, ONESB, s0, xn0 = (
        io[n] for n in ("xT", "wq", "wk", "wv", "wo", "relemb", "OH", "ONESB", "s0", "xn0"))

    if debug:
        dbg_expb = nc.dram_tensor("dbg_expb", [H, BPAD], BF16, kind="ExternalOutput").ap()
        dbg_cb = nc.dram_tensor("dbg_cb", [128, 32], BF16, kind="ExternalOutput").ap()
        dbg_k = nc.dram_tensor("dbg_k", [128, 8 * T], BF16, kind="ExternalOutput").ap()
        dbg_q = nc.dram_tensor("dbg_q", [128, 8 * T], BF16, kind="ExternalOutput").ap()
        dbg_v = nc.dram_tensor("dbg_v", [128, 16 * 1024], BF16, kind="ExternalOutput").ap()
        dbg_ctx = nc.dram_tensor("dbg_ctx", [128, 8 * 512], BF16, kind="ExternalOutput").ap()
    expb_d = nc.dram_tensor("expb_d", [H, BPAD], BF16, kind="Internal").ap()
    cvec_d = nc.dram_tensor("cvec_d", [H, 2], BF16, kind="Internal").ap()

    with stk:
        e = stk.enter_context
        onesb = e(nc.sbuf_tensor("onesb", [128, 512], BF16))
        xn_sb = e(nc.sbuf_tensor("xn_sb", [128, 8 * T], BF16))
        k_sb = e(nc.sbuf_tensor("k_sb", [128, 8 * T], BF16))
        q_sb = e(nc.sbuf_tensor("q_sb", [128, 8 * T], BF16))
        v_sb = e(nc.sbuf_tensor("v_sb", [128, 16 * 1024], BF16))
        wbuf = e(nc.sbuf_tensor("wbuf", [128, 8 * 1024], BF16))
        cb = e(nc.sbuf_tensor("cb", [128, 32], BF16))
        scrB = e(nc.sbuf_tensor("scrB", [128, SCRB_N], BF16))
        scrF = e(nc.sbuf_tensor("scrF", [128, 1536], F32))
        sb = {f"psS{i}": e(nc.psum_tensor(f"psS{i}", [128, 512], F32)) for i in range(4)}
        sb["ps_ctx"] = e(nc.psum_tensor("ps_ctx", [128, 512], F32))
        sb["ps_den"] = e(nc.psum_tensor("ps_den", [128, 512], F32))
        sb.update(onesb=onesb, q_sb=q_sb, k_sb=k_sb, v_sb=v_sb, wo_sb=wbuf,
                  scrB=scrB, scrF=scrF, cb=cb, expb_d_t=expb_d.tensor, zm=None)

        oh_sb = scrB[0:NB, S_OHC:S_OHC + BPAD]
        bias_sb = scrB[0:H, S_BIA:S_BIA + BPAD]
        expb_sb = scrB[0:H, S_EXB:S_EXB + BPAD]
        relb_sb = scrB[0:NB, 13824:13824 + H]
        with ExitStack() as st0:
            rel_sb = st0.enter_context(nc.sbuf_tensor("rel_sb", [NB, H], F32))
            NJ = BPAD // 512
            tn0, vc0 = TN.n, VC.n
            with nc.Block() as b0:
                @b0.sync
                def _(sy):
                    LD.inc(sy.dma_start(out=onesb[:], in_=ONESB), 16)
                    LD.inc(sy.dma_start(out=rel_sb[:], in_=relemb), 16)
                    LD.inc(sy.dma_start(out=oh_sb, in_=OH), 16)

                @b0.vector
                def _(v):
                    LD.wait(v)
                    VC.inc(v.tensor_copy(relb_sb, rel_sb[:]))

                @b0.tensor
                def _(tn):
                    tn.wait_ge(VC.h, vc0 + 1)
                    for jj in range(NJ):
                        if jj >= 2:
                            tn.wait_ge(VC.h, vc0 + jj)
                        i = tn.matmul(sb[f"psS{jj % 2}"][0:H, :], relb_sb,
                                      oh_sb[:, jj * 512:(jj + 1) * 512], start=True, stop=True)
                        TN.inc(i)

                @b0.vector
                def _(v):
                    for jj in range(NJ):
                        v.wait_ge(TN.h, tn0 + jj + 1)
                        i = v.tensor_copy(bias_sb[:, jj * 512:(jj + 1) * 512],
                                          sb[f"psS{jj % 2}"][0:H, :])
                        VC.inc(i)

                @b0.scalar
                def _(s):
                    VC.wait(s)
                    SC.inc(s.activation(out=expb_sb, in_=bias_sb, func=AF.Exp))

                @b0.sync
                def _(sy):
                    SC.wait(sy)
                    LD.inc(sy.dma_start(out=expb_d, in_=expb_sb), 16)
                    LD.inc(sy.dma_start(out=cvec_d[:, 0:1], in_=bias_sb[:, 4094:4095],
                                        allow_slow_non_contiguous=True), 16)
                    LD.inc(sy.dma_start(out=cvec_d[:, 1:2], in_=bias_sb[:, 0:1],
                                        allow_slow_non_contiguous=True), 16)
                    sy.wait_ge(LD.h, LD.n)
                    for h in range(H):
                        src = bass.AP(cvec_d.tensor, h * 2, [[0, 128], [1, 2]])
                        LD.inc(sy.dma_start(out=cb[:, 2 * h:2 * h + 2], in_=src), 16)

        for b in range(B_LOC):
            emit_ln(nc, sems, onesb, scrB, scrF, xT, xn_sb, xn0, b * T,
                    sb["psS0"], sb["psS1"])

            def load_w(src):
                tn_w = TN.n
                with nc.Block() as blw:
                    @blw.gpsimd
                    def _(g):
                        g.wait_ge(TN.h, tn_w)
                        GD.inc(g.dma_start(out=wbuf[:].rearrange("p (c f) -> p c f", c=8),
                                           in_=src.rearrange("(c p) f -> p c f", p=128)), 16)
                return GD.n

            def proj_kq(gd_mark, dst):
                tn0_, sc0_, vc0_ = TN.n, SC.n, VC.n
                with nc.Block() as bp:
                    @bp.tensor
                    def _(tn):
                        tn.wait_ge(GD.h, gd_mark)
                        tn.wait_ge(VC.h, vc0_)
                        tn.wait_ge(SC.h, sc0_)
                        for jj in range(32):
                            fc, nb_ = jj // 4, jj % 4
                            if jj >= 2:
                                tn.wait_ge(SC.h, sc0_ + jj - 1)
                            for dc in range(8):
                                i = tn.matmul(sb[f"psS{jj % 2}"][:],
                                              wbuf[:, dc * 1024 + fc * 128: dc * 1024 + fc * 128 + 128],
                                              xn_sb[:, dc * T + nb_ * 512: dc * T + nb_ * 512 + 512],
                                              start=(dc == 0), stop=(dc == 7))
                            TN.inc(i)

                    @bp.scalar
                    def _(s):
                        for jj in range(32):
                            fc, nb_ = jj // 4, jj % 4
                            s.wait_ge(TN.h, tn0_ + jj + 1)
                            i = s.activation(out=dst[:, fc * T + nb_ * 512: fc * T + nb_ * 512 + 512],
                                             in_=sb[f"psS{jj % 2}"][:], func=AF.Copy)
                            SC.inc(i)

            def proj_v(gd_mark):
                tn0_, sc0_, vc0_ = TN.n, SC.n, VC.n
                with nc.Block() as bp:
                    @bp.tensor
                    def _(tn):
                        tn.wait_ge(GD.h, gd_mark)
                        tn.wait_ge(VC.h, vc0_)
                        tn.wait_ge(SC.h, sc0_)
                        for jj in range(32):
                            tt, fb = jj // 2, jj % 2
                            if jj >= 2:
                                tn.wait_ge(VC.h, vc0_ + jj - 1)
                            for dc in range(8):
                                i = tn.matmul(sb[f"psS{jj % 2}"][:],
                                              xn_sb[:, dc * T + tt * 128: dc * T + tt * 128 + 128],
                                              wbuf[:, dc * 1024 + fb * 512: dc * 1024 + fb * 512 + 512],
                                              start=(dc == 0), stop=(dc == 7))
                            TN.inc(i)

                    @bp.vector
                    def _(v):
                        for jj in range(32):
                            tt, fb = jj // 2, jj % 2
                            v.wait_ge(TN.h, tn0_ + jj + 1)
                            i = v.tensor_copy(v_sb[:, tt * 1024 + fb * 512: tt * 1024 + (fb + 1) * 512],
                                              sb[f"psS{jj % 2}"][:])
                            VC.inc(i)

            proj_kq(load_w(wk), k_sb)
            proj_v(load_w(wv))
            proj_kq(load_w(wq), q_sb)
            gd_mo = load_w(wo)
            with nc.Block() as bwait:
                @bwait.tensor
                def _(tn):
                    tn.wait_ge(GD.h, gd_mo)
            emit_attention_batch(nc, sems, sb, 0, b, s0, wo_bias_col=None)

        if debug:
            with nc.Block() as bd:
                @bd.sync
                def _(sy):
                    sy.wait_ge(TN.h, TN.n)
                    sy.wait_ge(VC.h, VC.n)
                    sy.wait_ge(SC.h, SC.n)
                    LD.inc(sy.dma_start(out=scrB[0:H, 0:BPAD], in_=expb_d), 16)
                    sy.wait_ge(LD.h, LD.n)
                    LD.inc(sy.dma_start(out=dbg_expb, in_=scrB[0:H, 0:BPAD]), 16)
                    LD.inc(sy.dma_start(out=dbg_cb, in_=cb[:]), 16)
                    LD.inc(sy.dma_start(out=dbg_k, in_=k_sb[:]), 16)
                    LD.inc(sy.dma_start(out=dbg_q, in_=q_sb[:]), 16)
                    LD.inc(sy.dma_start(out=dbg_v, in_=v_sb[:]), 16)
                    LD.inc(sy.dma_start(out=dbg_ctx, in_=scrB[:, S_CTQ:S_CTQ + 4096]), 16)
        with nc.Block() as bf:
            @bf.sync
            def _(sy):
                sy.wait_ge(LD.h, LD.n)
                sy.wait_ge(GD.h, GD.n)
    return nc




B_DECLS = [("xT", [D, TOK], F32, "ExternalInput"), ("xn0", [D, TOK], BF16, "ExternalInput"),
           ("s0", [D, TOK], BF16, "ExternalInput"), ("wfcT", [D, FF], F32, "ExternalInput"),
           ("wprojT", [FF, D], F32, "ExternalInput"), ("lnw", [128, 32], F32, "ExternalInput"),
           ("lnb", [128, 32], F32, "ExternalInput"), ("ONESB", [128, 512], BF16, "ExternalInput"),
           ("ONESF", [128, 128], F32, "ExternalInput"), ("x1", [D, TOK], F32, "ExternalOutput")]


def build_B(debug=False, ctx=None):
    nc, stk, sems, io = _phase_setup(ctx, "B_", B_DECLS)
    LD, GD, TN, VC, SC, GP = (sems[n] for n in ("LD", "GD", "TN", "VC", "SC", "GP"))
    xT, xn0, s0, wfcT, wprojT, lnw, lnb, ONESB, ONESF, x1 = (
        io[n] for n in ("xT", "xn0", "s0", "wfcT", "wprojT", "lnw", "lnb", "ONESB", "ONESF", "x1"))
    if debug:
        wfcq_d = nc.dram_tensor("wfcq_d", [D, FF], BF16, kind="ExternalOutput").ap()
        wprojq_d = nc.dram_tensor("wprojq_d", [FF, D], BF16, kind="ExternalOutput").ap()
        dbg_h = nc.dram_tensor("dbg_h", [128, 32 * 256], BF16, kind="ExternalOutput").ap()
        dbg_mu = nc.dram_tensor("dbg_mu", [128, 256], BF16, kind="ExternalOutput").ap()
        dbg_rs = nc.dram_tensor("dbg_rs", [128, 256], BF16, kind="ExternalOutput").ap()
        dbg_cv = nc.dram_tensor("dbg_cv", [1, 1024], BF16, kind="ExternalOutput").ap()
        dbg_thr = nc.dram_tensor("dbg_thr", [128, 4], F32, kind="ExternalOutput").ap()
        dbg_parts = nc.dram_tensor("dbg_parts", [128, 66], F32, kind="ExternalOutput").ap()
    else:
        wfcq_d = nc.dram_tensor("wfcq_d", [D, FF], BF16, kind="Internal").ap()
        wprojq_d = nc.dram_tensor("wprojq_d", [FF, D], BF16, kind="Internal").ap()

    TB = 256
    NTB = TOK // TB

    with stk:
        e = stk.enter_context
        onesb = e(nc.sbuf_tensor("onesb", [128, 512], BF16))
        onesf = e(nc.sbuf_tensor("onesf", [128, 128], F32))
        lnw_sb = e(nc.sbuf_tensor("lnw_sb", [128, 32], F32))
        lnb_sb = e(nc.sbuf_tensor("lnb_sb", [128, 32], F32))
        lnb_bf = e(nc.sbuf_tensor("lnb_bf", [128, 32], BF16))
        wfcq = e(nc.sbuf_tensor("wfcq", [128, 8 * FF], BF16))
        wpjq = e(nc.sbuf_tensor("wpjq", [128, 32 * 1024], BF16))
        cvec = e(nc.sbuf_tensor("cvec", [128, 1024], BF16))
        h_sb = e(nc.sbuf_tensor("h_sb", [128, 32 * TB], BF16))
        hsq = e(nc.sbuf_tensor("hsq", [128, TB], BF16))
        xn_blk = e(nc.sbuf_tensor("xn_blk", [128, 8 * TB], BF16))
        s_blk = e(nc.sbuf_tensor("s_blk", [128, 8 * TB], BF16))
        x_blk = e(nc.sbuf_tensor("x_blk", [128, 8 * TB], F32))
        out_sb = e(nc.sbuf_tensor("out_sb", [128, 8 * TB], F32))
        g_sb = e(nc.sbuf_tensor("g_sb", [128, TB], BF16))
        t2_sb = e(nc.sbuf_tensor("t2_sb", [128, TB], BF16))
        mu_sb = e(nc.sbuf_tensor("mu_sb", [128, TB], BF16))
        rs_sb = e(nc.sbuf_tensor("rs_sb", [128, TB], BF16))
        td_sb = e(nc.sbuf_tensor("td_sb", [128, TB], BF16))
        scf = e(nc.sbuf_tensor("scf", [128, 3 * TB], F32))
        parts = e(nc.sbuf_tensor("parts", [128, 66], F32))
        thr2 = e(nc.sbuf_tensor("thr2", [128, 4], F32))
        ps = [e(nc.psum_tensor(f"psB{i}", [128, 512], F32)) for i in range(4)]
        ps_g = e(nc.psum_tensor("ps_g", [128, 4], F32))
        wscr = [x_blk[:, 0:2048], out_sb[:, 0:2048]]
        a_scr = h_sb[:, 0:2048]
        b_scr = h_sb[:, 2048:4096]
        q_scr = h_sb[:, 4096:6144]

        with nc.Block() as b0:
            @b0.sync
            def _(sy):
                LD.inc(sy.dma_start(out=onesb[:], in_=ONESB), 16)
                LD.inc(sy.dma_start(out=onesf[:], in_=ONESF), 16)
                LD.inc(sy.dma_start(out=lnw_sb[:], in_=lnw), 16)
                LD.inc(sy.dma_start(out=lnb_sb[:], in_=lnb), 16)

            @b0.vector
            def _(v):
                LD.wait(v)
                VC.inc(v.tensor_copy(lnb_bf[:], lnb_sb[:]))

        for mi, (wsrc, wdst) in enumerate(((wfcT, wfcq_d), (wprojT, wprojq_d))):
            npc = (D * FF) // (128 * 2048)
            ld0, vc0 = LD.n, VC.n
            with nc.Block() as bq:
                @bq.sync
                def _(sy):
                    for pc in range(npc):
                        if pc >= 2:
                            sy.wait_ge(VC.h, vc0 + pc - 1)
                        LD.inc(sy.dma_start(out=wscr[pc % 2],
                                            in_=bass.AP(wsrc.tensor, pc * 128 * 2048,
                                                        [[2048, 128], [1, 2048]])), 16)

                @bq.vector
                def _(v):
                    for pc in range(npc):
                        v.wait_ge(LD.h, ld0 + (pc + 1) * 16)
                        i = v.tensor_reduce(out=parts[:, pc:pc + 1], in_=wscr[pc % 2], axis=AX.X,
                                            op=ALU.add, apply_absolute_value=True)
                        VC.inc(i)

                tnq = TN.n
                with ExitStack() as _s:
                    pass
                @bq.tensor
                def _(tn):
                    VC.wait(tn)
                    i = tn.matmul(ps_g[0:npc, 2:3], parts[:, 0:npc], onesf[:, 0:1],
                                  start=True, stop=True)
                    TN.inc(i)

                @bq.scalar
                def _(s):
                    s.wait_ge(TN.h, tnq + 1)
                    SC.inc(s.copy(parts[0:npc, npc + 2:npc + 3], ps_g[0:npc, 2:3]))

                @bq.tensor
                def _(tn):
                    SC.wait(tn)
                    i = tn.matmul(ps_g[0:1, mi:mi + 1], parts[0:npc, npc + 2:npc + 3], onesf[0:npc, 0:1],
                                  start=True, stop=True)
                    TN.inc(i)

                @bq.vector
                def _(v):
                    TN.wait(v)
                    i = v.tensor_scalar(out=parts[:, npc + 1:npc + 2][0:1, :], in0=ps_g[0:1, mi:mi + 1],
                                        scalar1=0.5 / (D * FF), scalar2=0.5e-6, op0=ALU.mult, op1=ALU.add)
                    VC.inc(i)

                @bq.tensor
                def _(tn):
                    TN.wait(tn)
                    VC.wait(tn)
                    i = tn.matmul(ps[2][0:128, 0:1], onesf[0:1, :], parts[0:1, npc + 1:npc + 2],
                                  start=True, stop=True)
                    TN.inc(i)

                @bq.vector
                def _(v):
                    TN.wait(v)
                    v.tensor_copy(thr2[:, 2 * mi:2 * mi + 1], ps[2][:, 0:1])
                    i = v.tensor_scalar(out=thr2[:, 2 * mi + 1:2 * mi + 2], in0=ps[2][:, 0:1],
                                        scalar1=-1.0, scalar2=None, op0=ALU.mult)
                    VC.inc(i)

            ld1, vc1, gp1 = LD.n, VC.n, GP.n
            with nc.Block() as bq2:
                @bq2.sync
                def _(sy):
                    for pc in range(npc):
                        if pc >= 2:
                            sy.wait_ge(VC.h, vc1 + 2 * (pc - 1))
                        LD.inc(sy.dma_start(out=wscr[pc % 2],
                                            in_=bass.AP(wsrc.tensor, pc * 128 * 2048,
                                                        [[2048, 128], [1, 2048]])), 16)
                        sy.wait_ge(VC.h, vc1 + 2 * pc + 2)
                        dst_r = bass.AP(wdst.tensor, pc * 128 * 2048,
                                        [[2048, 128], [1, 2048]])
                        LD.inc(sy.dma_start(out=dst_r, in_=q_scr), 16)

                @bq2.vector
                def _(v):
                    for pc in range(npc):
                        v.wait_ge(LD.h, ld1 + pc * 32 + 16)
                        if pc == 0:
                            v.wait_ge(VC.h, vc1)
                        v.tensor_scalar(out=b_scr, in0=wscr[pc % 2], scalar1=thr2[:, 2 * mi + 1:2 * mi + 2],
                                        scalar2=None, op0=ALU.is_lt)
                        if pc >= 1:
                            v.wait_ge(LD.h, ld1 + pc * 32)
                        i = v.scalar_tensor_tensor(out=q_scr, in0=wscr[pc % 2],
                                                   scalar=thr2[:, 2 * mi:2 * mi + 1], in1=b_scr,
                                                   op0=ALU.is_gt, op1=ALU.subtract)
                        VC.inc(i, 2)

        with nc.Block() as bl:
            @bl.sync
            def _(sy):
                sy.wait_ge(LD.h, LD.n)
                for dc in range(8):
                    LD.inc(sy.dma_start(out=wfcq[:, dc * FF:(dc + 1) * FF],
                                        in_=bass.AP(wfcq_d.tensor, dc * 128 * FF, [[FF, 128], [1, FF]])), 16)
                for fc in range(32):
                    LD.inc(sy.dma_start(out=wpjq[:, fc * 1024:(fc + 1) * 1024],
                                        in_=bass.AP(wprojq_d.tensor, fc * 128 * D, [[D, 128], [1, D]])), 16)

        tnc = TN.n
        with nc.Block() as bc:
            @bc.tensor
            def _(tn):
                LD.wait(tn)
                for db in range(2):
                    for fc in range(32):
                        i = tn.matmul(ps[db][0:1, :], lnb_bf[:, fc:fc + 1],
                                      wpjq[:, fc * 1024 + db * 512: fc * 1024 + db * 512 + 512],
                                      start=(fc == 0), stop=(fc == 31))
                    TN.inc(i)

            @bc.vector
            def _(v):
                for db in range(2):
                    v.wait_ge(TN.h, tnc + db + 1)
                    i = v.tensor_copy(cvec[0:1, db * 512:(db + 1) * 512], ps[db][0:1, :])
                    VC.inc(i)

        xr_r = xT.rearrange("(c p) t -> p c t", p=128)
        xn_r = xn0.rearrange("(c p) t -> p c t", p=128)
        s_r = s0.rearrange("(c p) t -> p c t", p=128)
        x1_r = x1.rearrange("(c p) t -> p c t", p=128)
        for tb in range(NTB):
            T0 = tb * TB
            tn_w, vc_w, sc_w, ld_w = TN.n, VC.n, SC.n, LD.n
            with nc.Block() as bld:
                @bld.sync
                def _(sy):
                    sy.wait_ge(TN.h, tn_w)
                    sy.wait_ge(VC.h, vc_w)
                    sy.wait_ge(SC.h, sc_w)
                    LD.inc(sy.dma_start(out=xn_blk[:].rearrange("p (c t) -> p c t", c=8),
                                        in_=xn_r[:, :, T0:T0 + TB]), 16)
                    LD.inc(sy.dma_start(out=s_blk[:].rearrange("p (c t) -> p c t", c=8),
                                        in_=s_r[:, :, T0:T0 + TB]), 16)
                    LD.inc(sy.dma_start(out=x_blk[:].rearrange("p (c t) -> p c t", c=8),
                                        in_=xr_r[:, :, T0:T0 + TB]), 16)

            tn0_, sc0_ = TN.n, SC.n
            tn_after_h, tn_after_st = {}, {}
            tcnt = tn0_
            torder = []
            for fc in range(32):
                torder.append(("h", fc))
                if fc >= 1:
                    torder.append(("st", fc - 1))
            torder.append(("st", 31))
            for op, fc in torder:
                tcnt += 1
                (tn_after_h if op == "h" else tn_after_st)[fc] = tcnt

            with nc.Block() as bh:
                @bh.tensor
                def _(tn):
                    tn.wait_ge(LD.h, ld_w + 48)
                    for op, fc in torder:
                        if op == "h":
                            if fc >= 2:
                                tn.wait_ge(SC.h, sc0_ + 2 * (fc - 2) + 1)
                            for dc in range(8):
                                i = tn.matmul(ps[fc % 2][:, 0:TB],
                                              wfcq[:, dc * FF + fc * 128: dc * FF + fc * 128 + 128],
                                              xn_blk[:, dc * TB:(dc + 1) * TB],
                                              start=(dc == 0), stop=(dc == 7))
                            i.then_inc(TN.h, 1)
                        else:
                            tn.wait_ge(SC.h, sc0_ + 2 * fc + 2)
                            tn.matmul(ps[2][:, 0:TB], onesb[:, 0:128], h_sb[:, fc * TB:(fc + 1) * TB],
                                      start=(fc == 0), stop=(fc == 31), skip_group_check=True)
                            i = tn.matmul(ps[3][:, 0:TB], onesb[:, 0:128], hsq[:],
                                          start=(fc == 0), stop=(fc == 31), skip_group_check=True)
                            i.then_inc(TN.h, 1)
                    TN.n = tcnt

                @bh.scalar
                def _(s):
                    for fc in range(32):
                        s.wait_ge(TN.h, tn_after_h[fc])
                        if fc >= 1:
                            s.wait_ge(TN.h, tn_after_st[fc - 1])
                        i = s.activation(out=h_sb[:, fc * TB:(fc + 1) * TB], in_=ps[fc % 2][:, 0:TB],
                                         func=AF.Gelu)
                        i.then_inc(SC.h, 1)
                        SC.n += 1
                        i = s.activation(out=hsq[:], in_=h_sb[:, fc * TB:(fc + 1) * TB], func=AF.Square)
                        i.then_inc(SC.h, 1)
                        SC.n += 1

            vc0_ = VC.n
            with nc.Block() as bs:
                @bs.vector
                def _(v):
                    v.wait_ge(TN.h, tn_after_st[31])
                    v.tensor_scalar(out=mu_sb[:], in0=ps[2][:, 0:TB], scalar1=1.0 / FF, scalar2=None,
                                    op0=ALU.mult)
                    v.tensor_scalar(out=scf[:, 0:TB], in0=ps[3][:, 0:TB], scalar1=1.0 / FF,
                                    scalar2=1e-5, op0=ALU.mult, op1=ALU.add)
                    v.tensor_tensor(out=scf[:, TB:2 * TB], in0=mu_sb[:], in1=mu_sb[:], op=ALU.mult)
                    VC.inc(v.tensor_tensor(out=scf[:, 2 * TB:3 * TB], in0=scf[:, 0:TB],
                                           in1=scf[:, TB:2 * TB], op=ALU.subtract))

                @bs.scalar
                def _(s):
                    VC.wait(s)
                    s.activation(out=scf[:, 2 * TB:3 * TB], in_=scf[:, 2 * TB:3 * TB], func=AF.Ln)
                    SC.inc(s.activation(out=rs_sb[:], in_=scf[:, 2 * TB:3 * TB], func=AF.Exp, scale=-0.5))

                @bs.vector
                def _(v):
                    SC.wait(v)
                    for fc in range(32):
                        v.tensor_tensor(out=td_sb[:], in0=h_sb[:, fc * TB:(fc + 1) * TB], in1=mu_sb[:],
                                        op=ALU.subtract)
                        i = v.scalar_tensor_tensor(out=h_sb[:, fc * TB:(fc + 1) * TB], in0=td_sb[:],
                                                   scalar=lnw_sb[:, fc:fc + 1], in1=rs_sb[:],
                                                   op0=ALU.mult, op1=ALU.mult)
                    VC.inc(i)

            tnp, scp, vcp = TN.n, SC.n, VC.n
            with nc.Block() as bp:
                @bp.tensor
                def _(tn):
                    tn.wait_ge(VC.h, vcp)
                    for dc in range(8):
                        if dc >= 2:
                            tn.wait_ge(SC.h, scp + dc - 1)
                        for fc in range(32):
                            tn.matmul(ps[dc % 2][:, 0:TB],
                                      wpjq[:, fc * 1024 + dc * 128: fc * 1024 + dc * 128 + 128],
                                      h_sb[:, fc * TB:(fc + 1) * TB],
                                      start=(fc == 0), stop=False)
                        i = tn.matmul(ps[dc % 2][:, 0:TB], cvec[0:1, dc * 128:(dc + 1) * 128],
                                      onesb[0:1, 0:TB], start=False, stop=True)
                        TN.inc(i)

                @bp.scalar
                def _(s):
                    for dc in range(8):
                        s.wait_ge(TN.h, tnp + dc + 1)
                        i = s.activation(out=g_sb[:], in_=ps[dc % 2][:, 0:TB], func=AF.Silu)
                        SC.inc(i)
                        s.wait_ge(VC.h, vcp + 2 * dc + 1)

                @bp.vector
                def _(v):
                    for dc in range(8):
                        v.wait_ge(SC.h, scp + dc + 1)
                        v.tensor_tensor(out=t2_sb[:], in0=g_sb[:], in1=s_blk[:, dc * TB:(dc + 1) * TB],
                                        op=ALU.mult)
                        VC.n += 1
                        i = v.tensor_tensor(out=out_sb[:, dc * TB:(dc + 1) * TB], in0=t2_sb[:],
                                            in1=x_blk[:, dc * TB:(dc + 1) * TB], op=ALU.add)
                        i.then_inc(VC.h, 2)
                        VC.n += 1

                @bp.sync
                def _(sy):
                    VC.wait(sy)
                    LD.inc(sy.dma_start(out=x1_r[:, :, T0:T0 + TB],
                                        in_=out_sb[:].rearrange("p (c t) -> p c t", c=8)), 16)

        if debug:
            with nc.Block() as bd:
                @bd.sync
                def _(sy):
                    sy.wait_ge(TN.h, TN.n)
                    sy.wait_ge(VC.h, VC.n)
                    sy.wait_ge(SC.h, SC.n)
                    LD.inc(sy.dma_start(out=dbg_h, in_=h_sb[:]), 16)
                    LD.inc(sy.dma_start(out=dbg_mu, in_=mu_sb[:]), 16)
                    LD.inc(sy.dma_start(out=dbg_rs, in_=rs_sb[:]), 16)
                    LD.inc(sy.dma_start(out=dbg_cv, in_=cvec[0:1, :]), 16)
                    LD.inc(sy.dma_start(out=dbg_thr, in_=thr2[:]), 16)
                    LD.inc(sy.dma_start(out=dbg_parts, in_=parts[:]), 16)
        with nc.Block() as bf:
            @bf.sync
            def _(sy):
                sy.wait_ge(LD.h, LD.n)
                sy.wait_ge(GD.h, GD.n)

    return nc


C_DECLS = [("x1", [D, TOK], F32, "ExternalInput"), ("qkvw", [D, 3 * D], F32, "ExternalInput"),
           ("qkvb", [128, 24], F32, "ExternalInput"), ("outw", [D, D], F32, "ExternalInput"),
           ("outb", [128, 8], F32, "ExternalInput"), ("wg", [D, NE], F32, "ExternalInput"),
           ("CCb", [128, T], BF16, "ExternalInput"), ("SSb", [128, T], BF16, "ExternalInput"),
           ("ZM4", [128, 4 * 512], BF16, "ExternalInput"), ("ONESB", [128, 512], BF16, "ExternalInput"),
           ("ONESF", [128, 128], F32, "ExternalInput"), ("LSTF", [128, 128], F32, "ExternalInput"),
           ("IOTA16", [128, NE], F32, "ExternalInput"), ("LSU16F", [NE, NE], F32, "ExternalInput"),
           ("EYEB", [128, 128], BF16, "ExternalInput"), ("BV", [1, D], F32, "ExternalInput"),
           ("s1", [D, TOK], BF16, "ExternalOutput"), ("sorted_o", [TOK, D], BF16, "ExternalOutput"),
           ("counts_o", [1, NE], F32, "ExternalOutput"), ("idxo", [128, 32], F32, "ExternalOutput"),
           ("loco", [128, 32], F32, "ExternalOutput"), ("gso", [128, 32], F32, "ExternalOutput"),
           ("xn1T_d", [D, TOK], BF16, "Internal"), ("xnnat_d", [TOK, D], BF16, "Internal"),
           ("cnt_d", [1, 64], F32, "Internal")]


def build_C(core_id=0, debug=False, ctx=None, part="all"):
    nc, stk, sems, io = _phase_setup(ctx, "C_" + ("" if part == "all" else part + "_"), C_DECLS)
    LD, GD, TN, VC, SC, GP = (sems[n] for n in ("LD", "GD", "TN", "VC", "SC", "GP"))
    (x1, qkvw, qkvb, outw, outb, wg, CCb, SSb, ZM4, ONESB, ONESF, LSTF, IOTA16,
     LSU16F, EYEB, BV, s1, sorted_o, counts_o, idxo, loco, gso, xn1T_d, xnnat_d,
     cnt_d) = (
        io[n] for n in ("x1", "qkvw", "qkvb", "outw", "outb", "wg", "CCb", "SSb", "ZM4",
                        "ONESB", "ONESF", "LSTF", "IOTA16", "LSU16F", "EYEB", "BV",
                        "s1", "sorted_o", "counts_o", "idxo", "loco", "gso", "xn1T_d",
                        "xnnat_d", "cnt_d"))

    with stk:
        e = stk.enter_context
        onesb = e(nc.sbuf_tensor("onesb", [128, 512], BF16))
        onesf = e(nc.sbuf_tensor("onesf", [128, 128], F32))
        lstf = e(nc.sbuf_tensor("lstf", [128, 128], F32))
        iota16 = e(nc.sbuf_tensor("iota16", [128, NE], F32))
        lsu16 = e(nc.sbuf_tensor("lsu16", [NE, NE], F32))
        eyeb = e(nc.sbuf_tensor("eyeb", [128, 128], BF16))
        zm_sb = e(nc.sbuf_tensor("zm_sb", [128, 4 * 512], BF16))
        qkvb_sb = e(nc.sbuf_tensor("qkvb_sb", [128, 24], F32))
        outb_sb = e(nc.sbuf_tensor("outb_sb", [128, 8], F32))
        wg_sb = e(nc.sbuf_tensor("wg_sb", [128, 8 * NE], BF16))
        bvrow = e(nc.sbuf_tensor("bvrow", [1, D], BF16))
        xn_sb = e(nc.sbuf_tensor("xn_sb", [128, 8 * T], BF16))
        k_sb = e(nc.sbuf_tensor("k_sb", [128, 8 * T], BF16))
        q_sb = e(nc.sbuf_tensor("q_sb", [128, 8 * T], BF16))
        v_sb = e(nc.sbuf_tensor("v_sb", [128, 16 * 1024], BF16))
        wbuf = e(nc.sbuf_tensor("wbuf", [128, 8 * 1024], BF16))
        scrB = e(nc.sbuf_tensor("scrB", [128, SCRB_N], BF16))
        scrF = e(nc.sbuf_tensor("scrF", [128, 1536], F32))
        eq_sb = e(nc.sbuf_tensor("eq_sb", [128, 32 * NE], F32))
        idx_sb = e(nc.sbuf_tensor("idx_sb", [128, 32], F32))
        loc_sb = e(nc.sbuf_tensor("loc_sb", [128, 32], F32))
        gs_sb = e(nc.sbuf_tensor("gs_sb", [128, 32], F32))
        l_sb = e(nc.sbuf_tensor("l_sb", [128, NE], F32))
        et_sb = e(nc.sbuf_tensor("et_sb", [128, NE], F32))
        m_sb = e(nc.sbuf_tensor("m_sb", [128, 4], F32))
        carry = e(nc.sbuf_tensor("carry", [1, NE], F32))
        lst_bc = e(nc.sbuf_tensor("lst_bc", [128, NE], F32))
        nat_sb = e(nc.sbuf_tensor("nat_sb", [128, D], BF16))
        slot_sb = e(nc.sbuf_tensor("slot_sb", [128, 2], F32))
        slot_u = e(nc.sbuf_tensor("slot_u", [128, 1], U32))

        sb = {f"psS{i}": e(nc.psum_tensor(f"psS{i}", [128, 512], F32)) for i in range(4)}
        sb["ps_ctx"] = e(nc.psum_tensor("ps_ctx", [128, 512], F32))
        sb["ps_den"] = e(nc.psum_tensor("ps_den", [128, 512], F32))
        ps_tr = e(nc.psum_tensor("ps_tr", [128, 1024], BF16))
        sb.update(onesb=onesb, q_sb=q_sb, k_sb=k_sb, v_sb=v_sb, wo_sb=wbuf,
                  scrB=scrB, scrF=scrF, cb=None, expb_d_t=None, zm=zm_sb)
        rope_e = scrB[:, S_TT:S_TT + 2048]
        rope_o = scrB[:, S_TT + 2048:S_TT + 4096]
        rt1 = scrB[:, S_TT + 4096:S_TT + 6144]
        rt2 = scrB[:, S_PRAW:S_PRAW + 2048]
        cc_sb = scrB[:, S_SEV:S_SEV + 2048]
        ss_sb = scrB[:, S_CTQ:S_CTQ + 2048]

        with nc.Block() as b0:
            @b0.sync
            def _(sy):
                for d, s_ in ((onesb[:], ONESB), (onesf[:], ONESF), (lstf[:], LSTF),
                              (iota16[:], IOTA16), (lsu16[:], LSU16F), (eyeb[:], EYEB),
                              (zm_sb[:], ZM4),
                              (qkvb_sb[:], qkvb), (outb_sb[:], outb)):
                    LD.inc(sy.dma_start(out=d, in_=s_), 16)

            @b0.gpsimd
            def _(g):
                GD.inc(g.dma_start(out=wg_sb[:].rearrange("p (c f) -> p c f", c=8),
                                   in_=wg.rearrange("(c p) f -> p c f", p=128)), 16)
                GD.inc(g.dma_start(out=bvrow[:], in_=BV), 16)

            @b0.vector
            def _(v):
                LD.wait(v)
                VC.inc(v.memset(carry[:], 0.0))

        for b in (range(B_LOC) if part in ("all", "pre") else []):
            emit_ln(nc, sems, onesb, scrB, scrF, x1, xn_sb, xn1T_d, b * T,
                    sb["psS0"], sb["psS1"])

            for tt in range(16):
                g_ = b * 16 + tt
                tn0r, vc0r, sc0r = TN.n, VC.n, SC.n
                with nc.Block() as br:
                    @br.tensor
                    def _(tn):
                        if g_ == 0:
                            GD.wait(tn)
                        tn.wait_ge(VC.h, vc0r)
                        for dc in range(8):
                            i = tn.matmul(sb["psS0"][:, 0:NE],
                                          xn_sb[:, dc * T + tt * 128: dc * T + tt * 128 + 128],
                                          wg_sb[:, dc * NE:(dc + 1) * NE],
                                          start=(dc == 0), stop=(dc == 7))
                        TN.inc(i)
                        for dc in range(8):
                            i = tn.matmul(ps_tr[:, dc * 128:(dc + 1) * 128],
                                          xn_sb[:, dc * T + tt * 128: dc * T + tt * 128 + 128],
                                          eyeb[:], start=True, stop=True, is_transpose=True)
                            TN.inc(i)

                    @br.vector
                    def _(v):
                        def step(instr):
                            VC.inc(instr)
                            v.wait_ge(VC.h, VC.n)
                        v.wait_ge(TN.h, tn0r + 1)
                        step(v.tensor_copy(l_sb[:], sb["psS0"][:, 0:NE]))
                        step(v.tensor_reduce(out=m_sb[:, 0:1], in_=l_sb[:], axis=AX.X, op=ALU.max))
                        i = v.tensor_scalar(out=m_sb[:, 1:2], in0=m_sb[:, 0:1], scalar1=-1.0,
                                            scalar2=None, op0=ALU.mult)
                        VC.inc(i)

                    @br.scalar
                    def _(s):
                        VC.wait(s)
                        SC.inc(s.activation(out=et_sb[:], in_=l_sb[:], func=AF.Exp,
                                            bias=m_sb[:, 1:2], accum_out=m_sb[:, 2:3]))

                    @br.vector
                    def _(v):
                        def step(instr):
                            VC.inc(instr)
                            v.wait_ge(VC.h, VC.n)
                        SC.wait(v)
                        step(v.reciprocal(gs_sb[:, g_:g_ + 1], m_sb[:, 2:3]))
                        step(v.tensor_scalar(out=eq_sb[:, g_ * NE:(g_ + 1) * NE], in0=l_sb[:],
                                             scalar1=m_sb[:, 0:1], scalar2=None, op0=ALU.is_ge))
                        step(v.tensor_tensor(out=et_sb[:], in0=eq_sb[:, g_ * NE:(g_ + 1) * NE],
                                             in1=iota16[:], op=ALU.mult))
                        i = v.tensor_reduce(out=idx_sb[:, g_:g_ + 1], in_=et_sb[:], axis=AX.X,
                                            op=ALU.add)
                        VC.inc(i)

                    @br.tensor
                    def _(tn):
                        VC.wait(tn)
                        i = tn.matmul(sb["psS1"][0:1, 0:NE], onesf[:, 0:1],
                                      eq_sb[:, g_ * NE:(g_ + 1) * NE], start=True, stop=True)
                        TN.inc(i)
                        tn.matmul(sb["psS2"][:, 0:NE], lstf[:],
                                  eq_sb[:, g_ * NE:(g_ + 1) * NE], start=True, stop=False)
                        i = tn.matmul(sb["psS2"][:, 0:NE], onesf[0:1, :], carry[:],
                                      start=False, stop=True)
                        TN.inc(i)

                    @br.vector
                    def _(v):
                        def step(instr):
                            VC.inc(instr)
                            v.wait_ge(VC.h, VC.n)
                        TN.wait(v)
                        step(v.tensor_tensor(out=et_sb[:], in0=eq_sb[:, g_ * NE:(g_ + 1) * NE],
                                             in1=sb["psS2"][:, 0:NE], op=ALU.mult))
                        step(v.tensor_reduce(out=loc_sb[:, g_:g_ + 1], in_=et_sb[:], axis=AX.X,
                                             op=ALU.add))
                        i = v.tensor_tensor(out=carry[:], in0=carry[:], in1=sb["psS1"][0:1, 0:NE],
                                            op=ALU.add)
                        VC.inc(i)

                    @br.vector
                    def _(v):
                        v.wait_ge(TN.h, TN.n)
                        for dc in range(8):
                            i = v.tensor_copy(nat_sb[:, dc * 128:(dc + 1) * 128],
                                              ps_tr[:, dc * 128:(dc + 1) * 128])
                        VC.inc(i)

                    @br.sync
                    def _(sy):
                        VC.wait(sy)
                        LD.inc(sy.dma_start(out=bass.AP(xnnat_d.tensor, g_ * 128 * D,
                                                        [[D, 128], [1, D]]),
                                            in_=nat_sb[:]), 16)

        if part in ("all", "pre"):
            with nc.Block() as bc1:
                @bc1.sync
                def _(sy):
                    VC.wait(sy)
                    LD.inc(sy.dma_start(out=counts_o, in_=carry[:]), 16)
                    LD.inc(sy.dma_start(out=cnt_d[:, 0:NE], in_=carry[:]), 16)
                    LD.inc(sy.dma_start(out=idxo, in_=idx_sb[:]), 16)
                    LD.inc(sy.dma_start(out=loco, in_=loc_sb[:]), 16)
                    LD.inc(sy.dma_start(out=gso, in_=gs_sb[:]), 16)
                    sy.wait_ge(LD.h, LD.n)
                    LD.inc(sy.dma_start(out=l_sb[0:NE, 0:1], in_=bass.AP(cnt_d.tensor, 0, [[1, NE], [1, 1]])), 16)

            tnl = TN.n
            with nc.Block() as bc2:
                @bc2.tensor
                def _(tn):
                    LD.wait(tn)
                    i = tn.matmul(sb["psS3"][0:1, 0:NE], l_sb[0:NE, 0:1], lsu16[:],
                                  start=True, stop=True)
                    TN.inc(i)

                @bc2.vector
                def _(v):
                    v.wait_ge(TN.h, tnl + 1)
                    VC.inc(v.tensor_copy(et_sb[0:1, 0:NE], sb["psS3"][0:1, 0:NE]))

                @bc2.sync
                def _(sy):
                    VC.wait(sy)
                    LD.inc(sy.dma_start(out=cnt_d[:, NE:2 * NE], in_=et_sb[0:1, 0:NE]), 16)
                    sy.wait_ge(LD.h, LD.n)
                    LD.inc(sy.dma_start(out=lst_bc[:], in_=bass.AP(cnt_d.tensor, NE, [[0, 128], [1, NE]])), 16)

            for g_ in range(32):
                ld0s, vc0s, gp0s = LD.n, VC.n, GP.n
                with nc.Block() as bs:
                    @bs.sync
                    def _(sy):
                        sy.wait_ge(GP.h, gp0s)
                        LD.inc(sy.dma_start(out=nat_sb[:], in_=bass.AP(xnnat_d.tensor, g_ * 128 * D,
                                                                       [[D, 128], [1, D]])), 16)

                    @bs.vector
                    def _(v):
                        def step(instr):
                            VC.inc(instr)
                            v.wait_ge(VC.h, VC.n)
                        v.wait_ge(LD.h, ld0s + 16)
                        step(v.tensor_tensor(out=et_sb[:], in0=eq_sb[:, g_ * NE:(g_ + 1) * NE],
                                             in1=lst_bc[:], op=ALU.mult))
                        step(v.tensor_reduce(out=slot_sb[:, 0:1], in_=et_sb[:], axis=AX.X, op=ALU.add))
                        step(v.tensor_tensor(out=slot_sb[:, 1:2], in0=slot_sb[:, 0:1],
                                             in1=loc_sb[:, g_:g_ + 1], op=ALU.add))
                        i = v.tensor_copy(slot_u[:], slot_sb[:, 1:2])
                        VC.inc(i)

                    @bs.gpsimd
                    def _(g):
                        v_ = VC.n
                        g.wait_ge(VC.h, v_)
                        i = g.indirect_dma_start(
                            out=sorted_o, out_offset=IndirectOffsetOnAxis(slot_u[:, 0:1], axis=0),
                            in_=nat_sb[:], in_offset=None)
                        GP.inc(i, 16)


        for b in (range(B_LOC) if part in ("all", "post") else []):
            tnr, vcr, scr_ = TN.n, VC.n, SC.n
            with nc.Block() as brl:
                @brl.sync
                def _(sy):
                    sy.wait_ge(TN.h, tnr)
                    sy.wait_ge(VC.h, vcr)
                    sy.wait_ge(SC.h, scr_)
                    LD.inc(sy.dma_start(
                        out=xn_sb[:].rearrange("p (c t) -> p c t", c=8),
                        in_=xn1T_d.rearrange("(c p) t -> p c t", p=128)[:, :, b * T:(b + 1) * T]), 16)
            ld_xn = LD.n
            with nc.Block() as brw:
                @brw.tensor
                def _(tn):
                    tn.wait_ge(LD.h, ld_xn)

            def load_third(i3):
                tn_w = TN.n
                with nc.Block() as blw:
                    @blw.gpsimd
                    def _(g):
                        g.wait_ge(TN.h, tn_w)
                        GD.inc(g.dma_start(out=wbuf[:].rearrange("p (c f) -> p c f", c=8),
                                           in_=qkvw.rearrange("(c p) f -> p c f", p=128)[:, :, i3 * D:(i3 + 1) * D]), 16)
                return GD.n

            def proj_qk(gd_mark, dst, i3):
                for gg in range(4):
                    tn0_, sc0_, vc0_ = TN.n, SC.n, VC.n
                    with nc.Block() as bp:
                        @bp.tensor
                        def _(tn):
                            tn.wait_ge(GD.h, gd_mark)
                            tn.wait_ge(VC.h, vc0_)
                            for jj in range(8):
                                fc, nb_ = 2 * gg + jj // 4, jj % 4
                                if jj >= 2:
                                    tn.wait_ge(SC.h, sc0_ + jj - 1)
                                for dc in range(8):
                                    i = tn.matmul(sb[f"psS{jj % 2}"][:],
                                                  wbuf[:, dc * 1024 + fc * 128: dc * 1024 + fc * 128 + 128],
                                                  xn_sb[:, dc * T + nb_ * 512: dc * T + nb_ * 512 + 512],
                                                  start=(dc == 0), stop=(dc == 7))
                                TN.inc(i)

                        @bp.scalar
                        def _(s):
                            for jj in range(8):
                                fc, nb_ = 2 * gg + jj // 4, jj % 4
                                s.wait_ge(TN.h, tn0_ + jj + 1)
                                dstt = rope_e if (jj // 4) == 0 else rope_o
                                i = s.activation(out=dstt[:, nb_ * 512:(nb_ + 1) * 512],
                                                 in_=sb[f"psS{jj % 2}"][:], func=AF.Identity,
                                                 bias=qkvb_sb[:, i3 * 8 + fc: i3 * 8 + fc + 1])
                                SC.inc(i)

                        @bp.vector
                        def _(v):
                            SC.wait(v)
                            v.wait_ge(LD.h, ld_rope)
                            cE, cO = 2 * gg, 2 * gg + 1
                            v.tensor_tensor(out=rt1, in0=rope_e, in1=cc_sb, op=ALU.mult)
                            v.tensor_tensor(out=rt2, in0=rope_o, in1=ss_sb, op=ALU.mult)
                            v.tensor_tensor(out=dst[:, cE * T:(cE + 1) * T], in0=rt1, in1=rt2,
                                            op=ALU.subtract)
                            v.tensor_tensor(out=rt1, in0=rope_e, in1=ss_sb, op=ALU.mult)
                            v.tensor_tensor(out=rt2, in0=rope_o, in1=cc_sb, op=ALU.mult)
                            i = v.tensor_tensor(out=dst[:, cO * T:(cO + 1) * T], in0=rt1, in1=rt2,
                                                op=ALU.add)
                            VC.inc(i)

            def proj_v(gd_mark):
                tn0_, vc0_ = TN.n, VC.n
                with nc.Block() as bp:
                    @bp.tensor
                    def _(tn):
                        tn.wait_ge(GD.h, gd_mark)
                        tn.wait_ge(VC.h, vc0_)
                        tn.wait_ge(GD.h, GD.n)
                        for jj in range(32):
                            tt, fb = jj // 2, jj % 2
                            if jj >= 2:
                                tn.wait_ge(VC.h, vc0_ + jj - 1)
                            for dc in range(8):
                                tn.matmul(sb[f"psS{jj % 2}"][:],
                                          xn_sb[:, dc * T + tt * 128: dc * T + tt * 128 + 128],
                                          wbuf[:, dc * 1024 + fb * 512: dc * 1024 + fb * 512 + 512],
                                          start=(dc == 0), stop=False)
                            i = tn.matmul(sb[f"psS{jj % 2}"][:], onesb[0:1, 0:128],
                                          bvrow[0:1, fb * 512:(fb + 1) * 512],
                                          start=False, stop=True)
                            TN.inc(i)

                    @bp.vector
                    def _(v):
                        for jj in range(32):
                            tt, fb = jj // 2, jj % 2
                            v.wait_ge(TN.h, tn0_ + jj + 1)
                            i = v.tensor_copy(v_sb[:, tt * 1024 + fb * 512: tt * 1024 + (fb + 1) * 512],
                                              sb[f"psS{jj % 2}"][:])
                            VC.inc(i)

            tnw0, scw0, ldw0 = TN.n, SC.n, LD.n
            with nc.Block() as brt:
                @brt.sync
                def _(sy):
                    sy.wait_ge(TN.h, tnw0)
                    sy.wait_ge(SC.h, scw0)
                    sy.wait_ge(LD.h, ldw0)
                    LD.inc(sy.dma_start(out=cc_sb, in_=CCb), 16)
                    LD.inc(sy.dma_start(out=ss_sb, in_=SSb), 16)
            ld_rope = LD.n

            proj_qk(load_third(0), q_sb, 0)
            proj_qk(load_third(1), k_sb, 1)
            proj_v(load_third(2))
            gd_mo = None
            tnw_ = TN.n
            with nc.Block() as blw2:
                @blw2.gpsimd
                def _(g):
                    g.wait_ge(TN.h, tnw_)
                    GD.inc(g.dma_start(out=wbuf[:].rearrange("p (c f) -> p c f", c=8),
                                       in_=outw.rearrange("(c p) f -> p c f", p=128)), 16)
                gd_mo = GD.n
            with nc.Block() as bwait:
                @bwait.tensor
                def _(tn):
                    tn.wait_ge(GD.h, gd_mo)
            emit_attention_batch(nc, sems, sb, 1, b, s1, wo_bias_col=outb_sb)

        with nc.Block() as bf:
            @bf.sync
            def _(sy):
                sy.wait_ge(LD.h, LD.n)
                sy.wait_ge(GD.h, GD.n)
                sy.wait_ge(GP.h, GP.n)
    return nc


W2C_DECLS = [("w2c", [2, FF, D], F32, "ExternalInput"),
             ("w2bf_d", [2, FF, D], BF16, "Internal")]


def build_w2cast(ctx=None):
    nc, stk, sems, io = _phase_setup(ctx, "W_", W2C_DECLS)
    LD, GD = sems["LD"], sems["GD"]
    w2c, w2bf_d = io["w2c"], io["w2bf_d"]
    with stk:
        e = stk.enter_context
        w2b = [e(nc.sbuf_tensor(f"w2b{i}", [128, D], BF16)) for i in range(2)]
        gd0 = GD.n
        with nc.Block() as bw2:
            @bw2.gpsimd
            def _(g):
                for jj in range(64):
                    el, fc = jj // 32, jj % 32
                    if jj >= 2:
                        g.wait_ge(GD.h, gd0 + (2 * (jj - 2) + 2) * 16)
                    GD.inc(g.dma_start(out=w2b[jj % 2][:],
                                       in_=bass.AP(w2c.tensor, el * FF * D + fc * 128 * D,
                                                   [[D, 128], [1, D]])), 16)
                    g.wait_ge(GD.h, gd0 + (2 * jj + 1) * 16)
                    GD.inc(g.dma_start(out=bass.AP(w2bf_d.tensor, el * FF * D + fc * 128 * D,
                                                   [[D, 128], [1, D]]),
                                       in_=w2b[jj % 2][:]), 16)
    return nc


D_DECLS = [("sorted_all", [8 * TOK, D], BF16, "ExternalInput"),
           ("counts_all", [8, NE], F32, "ExternalInput"),
           ("ESEL", [NE, 2], F32, "ExternalInput"), ("SELM", [NE, 2], F32, "ExternalInput"),
           ("w1c", [2, D, FF], F32, "ExternalInput"), ("w2c", [2, FF, D], F32, "ExternalInput"),
           ("LI8", [8, 8], F32, "ExternalInput"), ("SLOTC", [128, NE], F32, "ExternalInput"),
           ("C4096", [128, 8], F32, "ExternalInput"), ("EYEB", [128, 128], BF16, "ExternalInput"),
           ("eo", [2 * CAP, D], BF16, "ExternalOutput"),
           ("w2bf_d", [2, FF, D], BF16, "Internal")]


def build_D(ctx=None, skip_w2cast=False, eo_cb=None):
    nc, stk, sems, io = _phase_setup(ctx, "D_", D_DECLS)
    LD, GD, TN, VC, SC, GP = (sems[n] for n in ("LD", "GD", "TN", "VC", "SC", "GP"))
    (sorted_all, counts_all, ESEL, SELM, w1c, w2c, LI8, SLOTC, C4096, EYEB, eo,
     w2bf_d) = (
        io[n] for n in ("sorted_all", "counts_all", "ESEL", "SELM", "w1c", "w2c",
                        "LI8", "SLOTC", "C4096", "EYEB", "eo", "w2bf_d"))
    tbl_d = nc.dram_tensor("tbl_d", [1, 64], F32, kind="Internal").ap()
    TB = 256

    with stk:
        e = stk.enter_context
        li8 = e(nc.sbuf_tensor("li8", [8, 8], F32))
        selm = e(nc.sbuf_tensor("selm", [NE, 2], F32))
        esel = e(nc.sbuf_tensor("esel", [NE, 2], F32))
        csel = e(nc.sbuf_tensor("csel", [8, 2], F32))
        slotc = e(nc.sbuf_tensor("slotc", [128, NE], F32))
        c4096 = e(nc.sbuf_tensor("c4096", [128, 8], F32))
        eyeb = e(nc.sbuf_tensor("eyeb", [128, 128], BF16))

        cntsT = e(nc.sbuf_tensor("cntsT", [NE, 8], F32))
        small = e(nc.sbuf_tensor("small", [8, 4], F32))
        cumt = e(nc.sbuf_tensor("cumt", [128, 8], F32))
        cumpt = e(nc.sbuf_tensor("cumpt", [128, 8], F32))
        lstt = e(nc.sbuf_tensor("lstt", [128, 8], F32))
        addt = e(nc.sbuf_tensor("addt", [128, 8], F32))
        mt1 = e(nc.sbuf_tensor("mt1", [128, NE], F32))
        mt2 = e(nc.sbuf_tensor("mt2", [128, NE], F32))
        acc = e(nc.sbuf_tensor("acc", [128, NE], F32))
        gidx_u = e(nc.sbuf_tensor("gidx_u", [128, NE], U32))
        w1_sb = e(nc.sbuf_tensor("w1_sb", [128, 8 * FF], BF16))
        w2b = [e(nc.sbuf_tensor(f"w2b{i}", [128, D], BF16)) for i in range(2)]
        gath = e(nc.sbuf_tensor("gath", [128, 2 * D], BF16))
        dispT = e(nc.sbuf_tensor("dispT", [128, 8 * TB], BF16))
        h_sb = e(nc.sbuf_tensor("h_sb", [128, 32 * TB], BF16))
        out_sb = e(nc.sbuf_tensor("out_sb", [128, 2 * D], BF16))
        psO = [e(nc.psum_tensor(f"psO{i}", [128, 512], F32)) for i in range(6)]
        ps_trb = e(nc.psum_tensor("ps_trb", [128, 1024], BF16))

        with nc.Block() as b0:
            @b0.sync
            def _(sy):
                for d, s_ in ((li8[:], LI8), (selm[:], SELM), (slotc[:], SLOTC),
                              (c4096[:], C4096), (eyeb[:], EYEB), (esel[:], ESEL)):
                    LD.inc(sy.dma_start(out=d, in_=s_), 16)
                LD.inc(sy.dma_start(out=cntsT[:], in_=bass.AP(counts_all.tensor, 0,
                                                              [[1, NE], [NE, 8]]),
                                    allow_slow_non_contiguous=True), 16)

            tn_d0 = TN.n
            @b0.tensor
            def _(tn):
                LD.wait(tn)
                i = tn.matmul(psO[2][0:8, 0:2], cntsT[:], esel[:], start=True, stop=True)
                TN.inc(i)

            @b0.vector
            def _(v):
                v.wait_ge(TN.h, tn_d0 + 1)
                VC.inc(v.tensor_copy(csel[:], psO[2][0:8, 0:2]))

            @b0.tensor
            def _(tn):
                VC.wait(tn)
                i = tn.matmul(psO[0][0:8, 0:2], li8[:], csel[:], start=True, stop=True)
                TN.inc(i)
                i = tn.matmul(psO[1][0:8, 0:2], cntsT[:], selm[:], start=True, stop=True)
                TN.inc(i)

            @b0.vector
            def _(v):
                TN.wait(v)
                v.tensor_copy(small[:, 0:2], psO[0][0:8, 0:2])
                i = v.tensor_copy(small[:, 2:4], psO[1][0:8, 0:2])
                VC.inc(i)

            @b0.sync
            def _(sy):
                VC.wait(sy)
                LD.inc(sy.dma_start(out=bass.AP(tbl_d.tensor, 0, [[2, 8], [1, 2]]),
                                    in_=small[:, 0:2]), 16)
                LD.inc(sy.dma_start(out=bass.AP(tbl_d.tensor, 16, [[2, 8], [1, 2]]),
                                    in_=small[:, 2:4]), 16)
                sy.wait_ge(LD.h, LD.n)

        if not skip_w2cast:
            gd0 = GD.n
            with nc.Block() as bw2:
                @bw2.gpsimd
                def _(g):
                    for jj in range(64):
                        el, fc = jj // 32, jj % 32
                        if jj >= 2:
                            g.wait_ge(GD.h, gd0 + (2 * (jj - 2) + 2) * 16)
                        GD.inc(g.dma_start(out=w2b[jj % 2][:],
                                           in_=bass.AP(w2c.tensor, el * FF * D + fc * 128 * D,
                                                       [[D, 128], [1, D]])), 16)
                        g.wait_ge(GD.h, gd0 + (2 * jj + 1) * 16)
                        GD.inc(g.dma_start(out=bass.AP(w2bf_d.tensor, el * FF * D + fc * 128 * D,
                                                       [[D, 128], [1, D]]),
                                           in_=w2b[jj % 2][:]), 16)

        for el in range(2):
            tnw = TN.n
            with nc.Block() as bw1:
                @bw1.gpsimd
                def _(g):
                    g.wait_ge(TN.h, tnw)
                    g.wait_ge(LD.h, LD.n)
                    for dc in range(8):
                        GD.inc(g.dma_start(out=w1_sb[:, dc * FF:(dc + 1) * FF],
                                           in_=bass.AP(w1c.tensor, el * D * FF + dc * 128 * FF,
                                                       [[FF, 128], [1, FF]])), 16)
            gd_w1 = GD.n

            ld1, vc1, gp1 = LD.n, VC.n, GP.n
            with nc.Block() as bgi:
                @bgi.sync
                def _(sy):
                    sy.wait_ge(VC.h, vc1)
                    sy.wait_ge(GP.h, gp1)
                    LD.inc(sy.dma_start(out=cumt[:], in_=bass.AP(tbl_d.tensor, el, [[0, 128], [2, 8]]), allow_slow_non_contiguous=True), 16)
                    LD.inc(sy.dma_start(out=cumpt[:, 1:8], in_=bass.AP(tbl_d.tensor, el, [[0, 128], [2, 7]]), allow_slow_non_contiguous=True), 16)
                    LD.inc(sy.dma_start(out=lstt[:], in_=bass.AP(tbl_d.tensor, 16 + el, [[0, 128], [2, 8]]), allow_slow_non_contiguous=True), 16)

                @bgi.vector
                def _(v):
                    def step(instr):
                        VC.inc(instr)
                        v.wait_ge(VC.h, VC.n)
                    v.wait_ge(LD.h, ld1 + 48)
                    step(v.memset(cumpt[:, 0:1], 0.0))
                    step(v.tensor_tensor(out=addt[:], in0=lstt[:], in1=cumpt[:], op=ALU.subtract))
                    step(v.tensor_tensor(out=addt[:], in0=addt[:], in1=c4096[:], op=ALU.add))
                    step(v.memset(acc[:], 0.0))
                    for cc_ in range(8):
                        step(v.tensor_scalar(out=mt1[:], in0=slotc[:], scalar1=cumpt[:, cc_:cc_ + 1],
                                             scalar2=None, op0=ALU.is_ge))
                        step(v.tensor_scalar(out=mt2[:], in0=slotc[:], scalar1=cumt[:, cc_:cc_ + 1],
                                             scalar2=None, op0=ALU.is_lt))
                        step(v.tensor_tensor(out=mt1[:], in0=mt1[:], in1=mt2[:], op=ALU.mult))
                        step(v.tensor_scalar(out=mt2[:], in0=slotc[:], scalar1=addt[:, cc_:cc_ + 1],
                                             scalar2=None, op0=ALU.add))
                        step(v.tensor_tensor(out=mt1[:], in0=mt1[:], in1=mt2[:], op=ALU.mult))
                        step(v.tensor_tensor(out=acc[:], in0=acc[:], in1=mt1[:], op=ALU.add))
                    step(v.tensor_copy(gidx_u[:], acc[:]))
            vc_gi = VC.n

            for tb in range(8):
                gp2, tn2, vc2, sc2, ld2 = GP.n, TN.n, VC.n, SC.n, LD.n
                with nc.Block() as bg:
                    @bg.gpsimd
                    def _(g):
                        g.wait_ge(VC.h, vc_gi)
                        g.wait_ge(TN.h, tn2)
                        for i4 in range(2):
                            f_ = tb * 2 + i4
                            GP.inc(g.indirect_dma_start(
                                out=gath[:, i4 * D:(i4 + 1) * D], out_offset=None,
                                in_=sorted_all,
                                in_offset=IndirectOffsetOnAxis(gidx_u[:, f_:f_ + 1], axis=0)), 16)

                with nc.Block() as bt:
                    @bt.tensor
                    def _(tn):
                        tn.wait_ge(GP.h, gp2 + 32)
                        tn.wait_ge(VC.h, vc2)
                        for i4 in range(2):
                            if i4 == 1:
                                tn.wait_ge(VC.h, vc2 + 1)
                            for dc in range(8):
                                i = tn.matmul(ps_trb[:, dc * 128:(dc + 1) * 128],
                                              gath[:, i4 * D + dc * 128: i4 * D + dc * 128 + 128],
                                              eyeb[:], start=True, stop=True, is_transpose=True)
                            TN.inc(i)

                    @bt.vector
                    def _(v):
                        for i4 in range(2):
                            v.wait_ge(TN.h, tn2 + i4 + 1)
                            for dc in range(8):
                                i = v.tensor_copy(dispT[:, dc * TB + i4 * 128: dc * TB + (i4 + 1) * 128],
                                                  ps_trb[:, dc * 128:(dc + 1) * 128])
                            VC.inc(i)

                tn3, sc3 = TN.n, SC.n
                with nc.Block() as bh:
                    @bh.tensor
                    def _(tn):
                        VC.wait(tn)
                        tn.wait_ge(GD.h, gd_w1)
                        for fc in range(32):
                            if fc >= 2:
                                tn.wait_ge(SC.h, sc3 + fc - 1)
                            for dc in range(8):
                                i = tn.matmul(psO[4 + (fc % 2)][:, 0:TB],
                                              w1_sb[:, dc * FF + fc * 128: dc * FF + fc * 128 + 128],
                                              dispT[:, dc * TB:(dc + 1) * TB],
                                              start=(dc == 0), stop=(dc == 7))
                            TN.inc(i)

                    @bh.scalar
                    def _(s):
                        for fc in range(32):
                            s.wait_ge(TN.h, tn3 + fc + 1)
                            i = s.activation(out=h_sb[:, fc * TB:(fc + 1) * TB],
                                             in_=psO[4 + (fc % 2)][:, 0:TB], func=AF.Relu)
                            SC.inc(i)

                tn4, sc4, vc4, ld4 = TN.n, SC.n, VC.n, LD.n
                with nc.Block() as bo:
                    @bo.sync
                    def _(sy):
                        sy.wait_ge(VC.h, vc4)
                        for fc in range(32):
                            if fc >= 2:
                                sy.wait_ge(TN.h, tn4 + fc - 1)
                            LD.inc(sy.dma_start(out=w2b[fc % 2][:],
                                                in_=bass.AP(w2bf_d.tensor, el * FF * D + fc * 128 * D,
                                                            [[D, 128], [1, D]])), 16)

                    @bo.tensor
                    def _(tn):
                        for fc in range(32):
                            tn.wait_ge(LD.h, ld4 + (fc + 1) * 16)
                            tn.wait_ge(SC.h, sc4 - 32 + fc + 1)
                            for tt in range(2):
                                for db in range(2):
                                    i = tn.matmul(psO[tt * 2 + db][:],
                                                  h_sb[:, fc * TB + tt * 128: fc * TB + (tt + 1) * 128]
                                                  if tt == 0 else
                                                  h_sb[:, fc * TB + 128: fc * TB + 256],
                                                  w2b[fc % 2][:, db * 512:(db + 1) * 512],
                                                  start=(fc == 0), stop=(fc == 31))
                            TN.inc(i)

                    @bo.vector
                    def _(v):
                        v.wait_ge(TN.h, TN.n)
                        for tt in range(2):
                            for db in range(2):
                                i = v.tensor_copy(out_sb[:, tt * D + db * 512: tt * D + (db + 1) * 512],
                                                  psO[tt * 2 + db][:])
                        VC.inc(i)

                    @bo.sync
                    def _(sy):
                        VC.wait(sy)
                        LD.inc(sy.dma_start(
                            out=bass.AP(eo.tensor, el * CAP * D + tb * TB * D,
                                        [[D, 128], [128 * D, 2], [1, D]]),
                            in_=out_sb[:].rearrange("p (t d) -> p t d", t=2)), 16)

        with nc.Block() as bf:
            @bf.sync
            def _(sy):
                sy.wait_ge(LD.h, LD.n)
                sy.wait_ge(GD.h, GD.n)
                sy.wait_ge(GP.h, GP.n)
    return nc


E_DECLS = [("x1", [D, TOK], F32, "ExternalInput"), ("s1", [D, TOK], BF16, "ExternalInput"),
           ("eo_all", [NE * CAP, D], BF16, "ExternalInput"),
           ("idxo", [128, 32], F32, "ExternalInput"), ("loco", [128, 32], F32, "ExternalInput"),
           ("gso", [128, 32], F32, "ExternalInput"), ("EYEB", [128, 128], BF16, "ExternalInput"),
           ("xout", [D, TOK], F32, "ExternalOutput")]


def build_E(ctx=None):
    nc, stk, sems, io = _phase_setup(ctx, "E_", E_DECLS)
    LD, GD, TN, VC, SC, GP = (sems[n] for n in ("LD", "GD", "TN", "VC", "SC", "GP"))
    (x1, s1, eo_all, idxo, loco, gso, EYEB, xout) = (
        io[n] for n in ("x1", "s1", "eo_all", "idxo", "loco", "gso", "EYEB", "xout"))

    with stk:
        e = stk.enter_context
        eyeb = e(nc.sbuf_tensor("eyeb", [128, 128], BF16))
        idx_sb = e(nc.sbuf_tensor("idx_sb", [128, 32], F32))
        loc_sb = e(nc.sbuf_tensor("loc_sb", [128, 32], F32))
        gs_sb = e(nc.sbuf_tensor("gs_sb", [128, 32], F32))
        tmp = e(nc.sbuf_tensor("tmp", [128, 8], F32))
        gi_u = e(nc.sbuf_tensor("gi_u", [128, 1], U32))
        moe = e(nc.sbuf_tensor("moe", [128, D], BF16))
        gnat = e(nc.sbuf_tensor("gnat", [128, D], BF16))
        gT = e(nc.sbuf_tensor("gT", [128, D], BF16))
        s1t = e(nc.sbuf_tensor("s1t", [128, D], BF16))
        x1t = e(nc.sbuf_tensor("x1t", [128, D], F32))
        t2 = e(nc.sbuf_tensor("t2", [128, D], BF16))
        outt = e(nc.sbuf_tensor("outt", [128, D], F32))
        ps_trb = e(nc.psum_tensor("ps_trb", [128, 1024], BF16))

        x1_r = x1.rearrange("(c p) t -> p c t", p=128)
        s1_r = s1.rearrange("(c p) t -> p c t", p=128)
        xo_r = xout.rearrange("(c p) t -> p c t", p=128)

        with nc.Block() as b0:
            @b0.sync
            def _(sy):
                for d, s_ in ((eyeb[:], EYEB), (idx_sb[:], idxo), (loc_sb[:], loco),
                              (gs_sb[:], gso)):
                    LD.inc(sy.dma_start(out=d, in_=s_), 16)

        for g_ in range(32):
            ld0, vc0, gp0, tn0, sc0 = LD.n, VC.n, GP.n, TN.n, SC.n
            with nc.Block() as bt:
                @bt.vector
                def _(v):
                    def step(instr):
                        VC.inc(instr)
                        v.wait_ge(VC.h, VC.n)
                    if g_ == 0:
                        v.wait_ge(LD.h, ld0)
                    else:
                        v.wait_ge(LD.h, ld0)
                        v.wait_ge(GP.h, gp0)
                    step(v.tensor_scalar(out=tmp[:, 0:1], in0=loc_sb[:, g_:g_ + 1],
                                         scalar1=float(CAP - 1), scalar2=None, op0=ALU.min))
                    step(v.scalar_tensor_tensor(out=tmp[:, 1:2], in0=idx_sb[:, g_:g_ + 1],
                                                scalar=float(CAP), in1=tmp[:, 0:1],
                                                op0=ALU.mult, op1=ALU.add))
                    step(v.tensor_copy(gi_u[:], tmp[:, 1:2]))
                    step(v.tensor_scalar(out=tmp[:, 2:3], in0=loc_sb[:, g_:g_ + 1],
                                         scalar1=float(CAP), scalar2=None, op0=ALU.is_lt))
                    step(v.tensor_tensor(out=tmp[:, 3:4], in0=gs_sb[:, g_:g_ + 1],
                                         in1=tmp[:, 2:3], op=ALU.mult))

                @bt.gpsimd
                def _(g):
                    VC.wait(g)
                    GP.inc(g.indirect_dma_start(
                        out=moe[:], out_offset=None, in_=eo_all,
                        in_offset=IndirectOffsetOnAxis(gi_u[:, 0:1], axis=0)), 16)

                @bt.vector
                def _(v):
                    GP.wait(v)
                    if g_ > 0:
                        v.wait_ge(SC.h, sc0)
                    i = v.tensor_scalar(out=gnat[:], in0=moe[:], scalar1=tmp[:, 3:4],
                                        scalar2=None, op0=ALU.mult)
                    VC.inc(i)

                @bt.scalar
                def _(s):
                    VC.wait(s)
                    SC.inc(s.activation(out=gnat[:], in_=gnat[:], func=AF.Silu))

                @bt.tensor
                def _(tn):
                    SC.wait(tn)
                    if g_ > 0:
                        tn.wait_ge(VC.h, vc0)
                    for dc in range(8):
                        i = tn.matmul(ps_trb[:, dc * 128:(dc + 1) * 128],
                                      gnat[:, dc * 128:(dc + 1) * 128], eyeb[:],
                                      start=True, stop=True, is_transpose=True)
                    TN.inc(i)

                @bt.sync
                def _(sy):
                    sy.wait_ge(VC.h, vc0)
                    LD.inc(sy.dma_start(out=s1t[:].rearrange("p (c t) -> p c t", c=8),
                                        in_=s1_r[:, :, g_ * 128:(g_ + 1) * 128]), 16)
                    LD.inc(sy.dma_start(out=x1t[:].rearrange("p (c t) -> p c t", c=8),
                                        in_=x1_r[:, :, g_ * 128:(g_ + 1) * 128]), 16)

                @bt.vector
                def _(v):
                    v.wait_ge(TN.h, TN.n)
                    for dc in range(8):
                        i = v.tensor_copy(gT[:, dc * 128:(dc + 1) * 128],
                                          ps_trb[:, dc * 128:(dc + 1) * 128])
                    VC.inc(i)
                    v.wait_ge(LD.h, ld0 + (32 if g_ == 0 else 32))
                    v.wait_ge(LD.h, LD.n)
                    v.tensor_tensor(out=t2[:], in0=s1t[:], in1=gT[:], op=ALU.mult)
                    i = v.tensor_tensor(out=outt[:], in0=t2[:], in1=x1t[:], op=ALU.add)
                    VC.inc(i)

                @bt.sync
                def _(sy):
                    VC.wait(sy)
                    LD.inc(sy.dma_start(out=xo_r[:, :, g_ * 128:(g_ + 1) * 128],
                                        in_=outt[:].rearrange("p (c t) -> p c t", c=8)), 16)

        with nc.Block() as bf:
            @bf.sync
            def _(sy):
                sy.wait_ge(LD.h, LD.n)
                sy.wait_ge(GP.h, GP.n)
    return nc


_PROGS = {}


def _prog(name, builder):
    if name not in _PROGS:
        _PROGS[name] = builder()
    return _PROGS[name]


def _qkv_perm():
    perm = np.zeros(3072, np.int64)
    for h in range(H):
        gg, s = h // 4, h % 4
        for d_ in range(DH):
            fold = h * DH + d_
            if d_ % 2 == 0:
                new = (2 * gg) * 128 + s * 32 + d_ // 2
            else:
                new = (2 * gg + 1) * 128 + s * 32 + (d_ - 1) // 2
            perm[new] = fold
            perm[1024 + new] = 1024 + fold
    perm[2048:] = np.arange(2048, 3072)
    return perm


def kernel_unfused(**inputs):
    f = np.float32
    cst = host_consts()
    x = np.ascontiguousarray(np.asarray(inputs["L0_attn_x"], f))
    onesf = np.ones((128, 128), f)
    lstf = (np.arange(128)[:, None] < np.arange(128)[None, :]).astype(f)
    zm4 = np.ascontiguousarray(np.concatenate(list(cst["ZM"]), axis=1))
    eyeb = np.eye(128).astype(BF)
    perm = _qkv_perm()

    xT = [np.ascontiguousarray(x[2 * c:2 * c + 2].reshape(TOK, D).T) for c in range(NCORES)]

    wq = np.ascontiguousarray(np.asarray(inputs["L0_attn_Wq"], f))
    wk = np.ascontiguousarray(np.asarray(inputs["L0_attn_Wk"], f))
    wv = np.ascontiguousarray(np.asarray(inputs["L0_attn_Wv"], f))
    wo = np.ascontiguousarray(np.asarray(inputs["L0_attn_Wo"], f))
    rel = np.ascontiguousarray(np.asarray(inputs["L0_attn_rel_emb"], f))
    resA = run_bass_kernel_spmd(
        _prog("A", build_A),
        [dict(xT=xT[c], wq=wq, wk=wk, wv=wv, wo=wo, relemb=rel,
              OH=cst["OH"], ONESB=cst["ONESB"]) for c in range(NCORES)],
        core_ids=list(range(NCORES)))

    wfcT = np.ascontiguousarray(np.asarray(inputs["L0_ffn_W_fc"], f).T)
    wprojT = np.ascontiguousarray(np.asarray(inputs["L0_ffn_W_proj"], f).T)
    lnw = np.ascontiguousarray(np.asarray(inputs["L0_ffn_ln_w"], f).reshape(32, 128).T)
    lnb = np.ascontiguousarray(np.asarray(inputs["L0_ffn_ln_b"], f).reshape(32, 128).T)
    resB = run_bass_kernel_spmd(
        _prog("B", build_B),
        [dict(xT=xT[c], xn0=np.asarray(resA.results[c]["xn0"]),
              s0=np.asarray(resA.results[c]["s0"]),
              wfcT=wfcT, wprojT=wprojT, lnw=lnw, lnb=lnb,
              ONESB=cst["ONESB"], ONESF=onesf) for c in range(NCORES)],
        core_ids=list(range(NCORES)))
    x1 = [np.asarray(resB.results[c]["x1"]) for c in range(NCORES)]

    qkvw_p = np.ascontiguousarray(np.asarray(inputs["L1_attn_qkv_w"], f)[:, perm])
    qkvb_p = np.ascontiguousarray(np.asarray(inputs["L1_attn_qkv_b"], f)[perm].reshape(24, 128).T)
    bv = np.ascontiguousarray(np.asarray(inputs["L1_attn_qkv_b"], f)[2048:3072][None, :])
    outw = np.ascontiguousarray(np.asarray(inputs["L1_attn_out_w"], f))
    outb = np.ascontiguousarray(np.asarray(inputs["L1_attn_out_b"], f).reshape(8, 128).T)
    wg = np.ascontiguousarray(np.asarray(inputs["L1_ffn_Wg"], f))
    resC = run_bass_kernel_spmd(
        _prog("C", build_C),
        [dict(x1=x1[c], qkvw=qkvw_p, qkvb=qkvb_p, outw=outw, outb=outb, wg=wg,
              CCb=cst["CC"].astype(BF), SSb=cst["SS"].astype(BF), ZM4=zm4.astype(BF), BV=bv,
              ONESB=cst["ONESB"], ONESF=onesf, LSTF=lstf,
              IOTA16=cst["IOTA16"], LSU16F=cst["LSU16"], EYEB=eyeb) for c in range(NCORES)],
        core_ids=list(range(NCORES)))

    sorted_all = np.ascontiguousarray(np.concatenate(
        [np.asarray(resC.results[c]["sorted_o"]) for c in range(NCORES)], 0))
    counts_all = np.ascontiguousarray(np.concatenate(
        [np.asarray(resC.results[c]["counts_o"]) for c in range(NCORES)], 0))
    w1 = np.asarray(inputs["L1_ffn_w1"], f)
    w2 = np.asarray(inputs["L1_ffn_w2"], f)
    dmaps = []
    for c in range(NCORES):
        selm = (np.arange(NE)[:, None] < (2 * c + np.arange(2))[None, :]).astype(f)
        esel = (np.arange(NE)[:, None] == (2 * c + np.arange(2))[None, :]).astype(f)
        dmaps.append(dict(
            sorted_all=sorted_all, counts_all=counts_all,
            ESEL=np.ascontiguousarray(esel),
            SELM=np.ascontiguousarray(selm),
            w1c=np.ascontiguousarray(w1[2 * c:2 * c + 2]),
            w2c=np.ascontiguousarray(w2[2 * c:2 * c + 2]),
            LI8=cst["LI8"], SLOTC=cst["SLOT"],
            C4096=(4096.0 * np.arange(8, dtype=f))[None, :].repeat(128, 0),
            EYEB=eyeb))
    resD = run_bass_kernel_spmd(_prog("D", build_D), dmaps, core_ids=list(range(NCORES)))

    eo_all = np.ascontiguousarray(np.concatenate(
        [np.asarray(resD.results[c]["eo"]) for c in range(NCORES)], 0))
    resE = run_bass_kernel_spmd(
        _prog("E", build_E),
        [dict(x1=x1[c], s1=np.asarray(resC.results[c]["s1"]), eo_all=eo_all,
              idxo=np.asarray(resC.results[c]["idxo"]),
              loco=np.asarray(resC.results[c]["loco"]),
              gso=np.asarray(resC.results[c]["gso"]), EYEB=eyeb) for c in range(NCORES)],
        core_ids=list(range(NCORES)))

    out = np.concatenate(
        [np.asarray(resE.results[c]["xout"], f).T.reshape(B_LOC, T, D) for c in range(NCORES)], 0)
    return out



def emit_allgather_issue(nc, sems, cc, in_ap, out_ap, waits=()):
    with nc.Block() as b:
        @b.gpsimd
        def _(g):
            for sem, val in waits:
                g.wait_ge(sem.h, val)
            i = g.collective_compute("AllGather", ALU.bypass,
                                     replica_groups=[list(range(NCORES))],
                                     ins=[in_ap.opt()], outs=[out_ap.opt()])
            cc.inc(i, 1)


def emit_cc_wait(nc, ccs):
    with nc.Block() as b2:
        for en in ("tensor", "vector", "scalar", "gpsimd", "sync"):
            def _mk(e, _ccs=tuple((c, c.n) for c in ccs)):
                for c, m in _ccs:
                    e.wait_ge(c.h, m)
            getattr(b2, en)(_mk)


def emit_allgather(nc, sems, cc, in_ap, out_ap):
    emit_allgather_issue(nc, sems, cc, in_ap, out_ap)
    emit_cc_wait(nc, [cc])


def build_fused():
    nc, stk, sems = new_prog()
    reg = {}
    ctx = (nc, sems, reg)
    with stk:
        cc = [Sem(stk, nc, f"CC{i}") for i in range(4)]
        build_A(ctx=ctx)
        emit_fence(nc, sems)
        build_w2cast(ctx=ctx)
        build_B(ctx=ctx)
        emit_fence(nc, sems)
        build_C(ctx=ctx, part="pre")
        emit_fence(nc, sems)
        reg["sorted_all"] = nc.dram_tensor("sorted_all", [8 * TOK, D], BF16, kind="Internal").ap()
        reg["counts_all"] = nc.dram_tensor("counts_all", [8, NE], F32, kind="Internal").ap()
        emit_allgather_issue(nc, sems, cc[0], reg["sorted_o"], reg["sorted_all"])
        emit_allgather_issue(nc, sems, cc[1], reg["counts_o"], reg["counts_all"])
        build_C(ctx=ctx, part="post")
        emit_fence(nc, sems)
        emit_cc_wait(nc, [cc[0], cc[1]])
        reg["eo_all"] = nc.dram_tensor("eo_all", [NE * CAP, D], BF16, kind="Internal").ap()

        build_D(ctx=ctx, skip_w2cast=True)
        emit_fence(nc, sems)
        emit_allgather(nc, sems, cc[2], reg["eo"], reg["eo_all"])
        build_E(ctx=ctx)
        emit_fence(nc, sems)
    return nc


def _host_inmaps(inputs):
    f = np.float32
    cst = host_consts()
    x = np.ascontiguousarray(np.asarray(inputs["L0_attn_x"], f))
    onesf = np.ones((128, 128), f)
    lstf = (np.arange(128)[:, None] < np.arange(128)[None, :]).astype(f)
    zm4 = np.ascontiguousarray(np.concatenate(list(cst["ZM"]), axis=1))
    eyeb = np.eye(128).astype(BF)
    perm = _qkv_perm()
    w1 = np.asarray(inputs["L1_ffn_w1"], f)
    w2 = np.asarray(inputs["L1_ffn_w2"], f)
    shared = dict(
        wq=np.ascontiguousarray(np.asarray(inputs["L0_attn_Wq"], f)),
        wk=np.ascontiguousarray(np.asarray(inputs["L0_attn_Wk"], f)),
        wv=np.ascontiguousarray(np.asarray(inputs["L0_attn_Wv"], f)),
        wo=np.ascontiguousarray(np.asarray(inputs["L0_attn_Wo"], f)),
        relemb=np.ascontiguousarray(np.asarray(inputs["L0_attn_rel_emb"], f)),
        OH=cst["OH"], ONESB=cst["ONESB"], ONESF=onesf,
        wfcT=np.ascontiguousarray(np.asarray(inputs["L0_ffn_W_fc"], f).T),
        wprojT=np.ascontiguousarray(np.asarray(inputs["L0_ffn_W_proj"], f).T),
        lnw=np.ascontiguousarray(np.asarray(inputs["L0_ffn_ln_w"], f).reshape(32, 128).T),
        lnb=np.ascontiguousarray(np.asarray(inputs["L0_ffn_ln_b"], f).reshape(32, 128).T),
        qkvw=np.ascontiguousarray(np.asarray(inputs["L1_attn_qkv_w"], f)[:, perm]),
        qkvb=np.ascontiguousarray(np.asarray(inputs["L1_attn_qkv_b"], f)[perm].reshape(24, 128).T),
        BV=np.ascontiguousarray(np.asarray(inputs["L1_attn_qkv_b"], f)[2048:3072][None, :]),
        outw=np.ascontiguousarray(np.asarray(inputs["L1_attn_out_w"], f)),
        outb=np.ascontiguousarray(np.asarray(inputs["L1_attn_out_b"], f).reshape(8, 128).T),
        wg=np.ascontiguousarray(np.asarray(inputs["L1_ffn_Wg"], f)),
        CCb=cst["CC"].astype(BF), SSb=cst["SS"].astype(BF), ZM4=zm4.astype(BF),
        LSTF=lstf, IOTA16=cst["IOTA16"], LSU16F=cst["LSU16"], EYEB=eyeb,
        LI8=cst["LI8"], SLOTC=cst["SLOT"],
        C4096=(4096.0 * np.arange(8, dtype=f))[None, :].repeat(128, 0),
    )
    maps = []
    for c in range(NCORES):
        m = dict(shared)
        m["xT"] = np.ascontiguousarray(x[2 * c:2 * c + 2].reshape(TOK, D).T)
        m["SELM"] = np.ascontiguousarray(
            (np.arange(NE)[:, None] < (2 * c + np.arange(2))[None, :]).astype(f))
        m["ESEL"] = np.ascontiguousarray(
            (np.arange(NE)[:, None] == (2 * c + np.arange(2))[None, :]).astype(f))
        m["w1c"] = np.ascontiguousarray(w1[2 * c:2 * c + 2])
        m["w2c"] = np.ascontiguousarray(w2[2 * c:2 * c + 2])
        maps.append(m)
    return maps


def kernel(**inputs):
    f = np.float32
    maps = _host_inmaps(inputs)
    res = run_bass_kernel_spmd(_prog("F", build_fused), maps, core_ids=list(range(NCORES)))
    out = np.concatenate(
        [np.asarray(res.results[c]["xout"], f).T.reshape(B_LOC, T, D) for c in range(NCORES)], 0)
    return out


# version 05
# speedup vs baseline: 1.0841x; 1.0137x over previous
import math
from contextlib import ExitStack

import numpy as np
import ml_dtypes

import concourse.bass as bass
import concourse.mybir as mybir
from concourse.bass import IndirectOffsetOnAxis
from concourse.bass_utils import run_bass_kernel_spmd

DT = mybir.dt
F32 = DT.float32
BF16 = DT.bfloat16
U32 = DT.uint32
AF = mybir.ActivationFunctionType
ALU = mybir.AluOpType
AX = mybir.AxisListType
BF = ml_dtypes.bfloat16

NCORES = 8
B_LOC = 2
T = 2048
TOK = B_LOC * T
D = 1024
H = 16
DH = 64
NB = 32
FF = 4096
NE = 16
CAP = 2048
BLEN = 2 * T - 1
BPAD = 4608


class Sem:
    def __init__(self, stack, nc, name):
        self.h = stack.enter_context(nc.semaphore(name))
        self.n = 0

    def inc(self, instr, k=1):
        instr.then_inc(self.h, k)
        self.n += k
        return self.n

    def wait(self, eng, n=None):
        eng.wait_ge(self.h, self.n if n is None else n)


def new_prog():
    nc = bass.Bass("TRN2", target_bir_lowering=False)
    stk = ExitStack()
    sems = {n: Sem(stk, nc, n) for n in ("LD", "GD", "TN", "VC", "SC", "GP")}
    return nc, stk, sems


class _NCProxy:

    def __init__(self, nc, prefix):
        self._nc = nc
        self._p = prefix

    def sbuf_tensor(self, name, *a, **k):
        return self._nc.sbuf_tensor(self._p + name, *a, **k)

    def psum_tensor(self, name, *a, **k):
        return self._nc.psum_tensor(self._p + name, *a, **k)

    def __getattr__(self, a):
        return getattr(self._nc, a)


def _phase_setup(ctx, prefix, decls):
    if ctx is None:
        nc, stk, sems = new_prog()
        reg = {}
    else:
        nc0, sems, reg = ctx
        nc = _NCProxy(nc0, prefix)
        stk = ExitStack()
    io = {}
    for name, shape, dt_, kind in decls:
        if name in reg:
            io[name] = reg[name]
        else:
            k = kind
            if ctx is not None and kind == "ExternalOutput" and name != "xout":
                k = "Internal"
            ap = nc.dram_tensor(name, shape, dt_, kind=k).ap()
            reg[name] = ap
            io[name] = ap
    return nc, stk, sems, io


def emit_fence(nc, sems):
    marks = {k: s.n for k, s in sems.items()}
    with nc.Block() as bf:
        for en in ("tensor", "vector", "scalar", "gpsimd", "sync"):
            def _mk(e, _m=marks):
                for k, s in sems.items():
                    e.wait_ge(s.h, _m[k])
            getattr(bf, en)(_mk)


def _bucket_of(d):
    nb = NB // 2
    n = -d
    ret = (n < 0).astype(np.int32) * nb
    n = np.abs(n)
    max_exact = nb // 2
    is_small = n < max_exact
    n_safe = np.maximum(n, 1).astype(np.float32)
    val = max_exact + (np.log(n_safe / max_exact) / math.log(128 / max_exact)
                       * (nb - max_exact)).astype(np.int32)
    val = np.minimum(val, nb - 1)
    return ret + np.where(is_small, n, val)


def host_consts():
    c = {}
    j = np.arange(BLEN)
    bk = _bucket_of((j - (T - 1)).astype(np.int64))
    oh = np.zeros((NB, BPAD), np.float32)
    oh[bk, j] = 1.0
    c["OH"] = oh.astype(BF)
    c["ONESB"] = np.ones((128, 512), BF)
    zm = np.zeros((4, 128, 512), np.float32)
    ki = np.arange(128)[:, None]
    qi = np.arange(512)[None, :]
    for jj in range(4):
        zm[jj] = (qi >= ki + jj * 128)
    c["ZM"] = zm.astype(BF)
    hd = DH
    inv_freq = (1.0 / 10000.0 ** (np.arange(0, hd, 2, dtype=np.float32) / hd))
    t = np.arange(T, dtype=np.float32)
    freqs = t[:, None] * inv_freq[None, :]
    rope = np.concatenate([np.sin(freqs), np.cos(freqs)], -1).astype(np.float32)
    sinv = rope[:, 0::2]
    cosv = rope[:, 1::2]
    rows = np.tile(np.arange(32), 4)
    c["CC"] = np.ascontiguousarray(cosv[:, rows].T)
    c["SS"] = np.ascontiguousarray(sinv[:, rows].T)
    c["LSTRICT"] = (np.arange(128)[:, None] < np.arange(128)[None, :]).astype(BF)
    c["IOTA16"] = np.tile(np.arange(NE, dtype=np.float32), (128, 1))
    c["LSU16"] = (np.arange(NE)[:, None] < np.arange(NE)[None, :]).astype(np.float32)
    c["LI8"] = (np.arange(8)[:, None] <= np.arange(8)[None, :]).astype(np.float32)
    c["SLOT"] = (np.arange(128)[:, None] + 128.0 * np.arange(16)[None, :]).astype(np.float32)
    return c


S_TT = 0
S_PRAW = 6144
S_PT = 8192
S_RCP = 10240
S_SEV = 10752
S_CTQ = 14848
S_XBF = 0
S_XSQ = 4096
S_MU = 8192
S_RS = 8704
S_TD = 9216
S_OHC = 0
S_BIA = 4608
S_EXB = 9216
SCRB_N = 18944


def emit_ln(nc, sems, onesb, scrB, scrF, src_dram, xn_sb, xn_out_dram, bofs, ps_sum, ps_sq):
    LD, GD, TN, VC, SC = (sems[n] for n in ("LD", "GD", "TN", "VC", "SC"))
    xbf = scrB[:, S_XBF:S_XBF + 4096]
    xsq = scrB[:, S_XSQ:S_XSQ + 4096]
    mu = scrB[:, S_MU:S_MU + 512]
    rs = scrB[:, S_RS:S_RS + 512]
    td = scrB[:, S_TD:S_TD + 512]
    ex2 = scrF[:, 0:512]
    mu2 = scrF[:, 512:1024]
    var = scrF[:, 1024:1536]
    src_r = src_dram.rearrange("(c p) t -> p c t", p=128)

    for blk in range(4):
        t0 = bofs + blk * 512
        tn_w, vc_w, sc_w = TN.n, VC.n, SC.n
        with nc.Block() as b:
            @b.gpsimd
            def _(g):
                g.wait_ge(TN.h, tn_w)
                g.wait_ge(VC.h, vc_w)
                g.wait_ge(SC.h, sc_w)
                GD.inc(g.dma_start(out=xbf.rearrange("p (c t) -> p c t", c=8),
                                   in_=src_r[:, :, t0:t0 + 512]), 16)

            @b.scalar
            def _(s):
                GD.wait(s)
                SC.inc(s.activation(out=xsq, in_=xbf, func=AF.Square))

            @b.tensor
            def _(tn):
                GD.wait(tn)
                for c in range(8):
                    tn.matmul(ps_sum[:], onesb[:, 0:128], xbf[:, c * 512:(c + 1) * 512],
                              start=(c == 0), stop=(c == 7))
                SC.wait(tn)
                for c in range(8):
                    i = tn.matmul(ps_sq[:], onesb[:, 0:128], xsq[:, c * 512:(c + 1) * 512],
                                  start=(c == 0), stop=(c == 7))
                TN.inc(i)

            @b.vector
            def _(v):
                TN.wait(v)
                v.tensor_scalar(out=mu, in0=ps_sum[:], scalar1=1.0 / D, scalar2=None, op0=ALU.mult)
                v.tensor_scalar(out=ex2, in0=ps_sq[:], scalar1=1.0 / D, scalar2=1e-5,
                                op0=ALU.mult, op1=ALU.add)
                v.tensor_tensor(out=mu2, in0=mu, in1=mu, op=ALU.mult)
                VC.inc(v.tensor_tensor(out=var, in0=ex2, in1=mu2, op=ALU.subtract))

            @b.scalar
            def _(s):
                VC.wait(s)
                s.activation(out=var, in_=var, func=AF.Ln)
                SC.inc(s.activation(out=rs, in_=var, func=AF.Exp, scale=-0.5))

            @b.vector
            def _(v):
                SC.wait(v)
                for c in range(8):
                    v.tensor_tensor(out=td, in0=xbf[:, c * 512:(c + 1) * 512], in1=mu,
                                    op=ALU.subtract)
                    i = v.tensor_tensor(out=xn_sb[:, c * T + blk * 512: c * T + blk * 512 + 512],
                                        in0=td, in1=rs, op=ALU.mult)
                VC.inc(i)

            @b.sync
            def _(sy):
                VC.wait(sy)
                if xn_out_dram is not None:
                    xr = xn_out_dram.rearrange("(c p) t -> p c t", p=128)
                    LD.inc(sy.dma_start(
                        out=xr[:, :, t0:t0 + 512],
                        in_=xn_sb[:].rearrange("p (c t) -> p c t", c=8)[:, :, blk * 512:(blk + 1) * 512]), 16)


def emit_attention_batch(nc, sems, sb, layer, b, s_out_dram, wo_bias_col=None):
    LD, GD, TN, VC, SC = (sems[n] for n in ("LD", "GD", "TN", "VC", "SC"))
    is_l1 = (layer == 1)
    scale = (DH ** -0.5) if is_l1 else 1.0
    q_sb, k_sb, v_sb, wo_sb = sb["q_sb"], sb["k_sb"], sb["v_sb"], sb["wo_sb"]
    onesb = sb["onesb"]
    scrB = sb["scrB"]
    psS = [sb[f"psS{i}"] for i in range(4)]
    ps_ctxs = sb["ps_ctx"] if isinstance(sb["ps_ctx"], list) else [sb["ps_ctx"]]
    ps_dens = sb["ps_den"] if isinstance(sb["ps_den"], list) else [sb["ps_den"]]
    nrot = len(ps_ctxs)
    evac_mark = {i: 0 for i in range(nrot)}
    ttile = scrB[:, S_TT:S_TT + 6144]
    praw = scrB[:, S_PRAW:S_PRAW + 2048]
    ptile = scrB[:, S_PT:S_PT + 2048]
    rcp = sb["scrF"][:, 0:512]
    sev = scrB[:, S_SEV:S_SEV + 4096]
    ctxq = scrB[:, S_CTQ:S_CTQ + 4096]

    def tile_kind(o):
        if is_l1:
            if o > 384:
                return "skip"
            return "mask" if o >= 0 else "plain"
        if -128 <= o <= 512:
            return "band"
        return "cpos" if o > 512 else "cneg"

    for qb in range(4):
        for hp in range(8):
            hA, hB = 2 * hp, 2 * hp + 1
            par = (qb * 8 + hp) % nrot
            ps_ctx, ps_den = ps_ctxs[par], ps_dens[par]
            if not is_l1:
                vc_w = VC.n
                with nc.Block() as blb:
                    @blb.sync
                    def _(sy):
                        sy.wait_ge(VC.h, vc_w)
                        for hl, h in ((0, hA), (1, hB)):
                            src = bass.AP(sb["expb_d_t"], h * BPAD + 1408,
                                          [[1, 128], [128, 6], [1, 512]])
                            LD.inc(sy.dma_start(
                                out=ttile[:, hl * 3072:(hl + 1) * 3072]
                                .rearrange("p (j t) -> p j t", j=6),
                                in_=src), 16)
            ld_t = LD.n

            alive = [kt for kt in range(16) if tile_kind(kt * 128 - qb * 512) != "skip"]
            n = len(alive)
            tn0, sc0, vc0 = TN.n, SC.n, VC.n
            order = []
            for j in range(n):
                order.append(("qk", j))
                if j >= 2:
                    order.append(("pv", j - 2))
            order += [("pv", n - 2), ("pv", n - 1)] if n >= 2 else [("pv", 0)]
            tn_after_qk, tn_after_pv, sc_after_exp, vc_after_mul = {}, {}, {}, {}
            tn_c, sc_c, vc_c = tn0, sc0, vc0
            for op, j in order:
                tn_c += 1
                (tn_after_qk if op == "qk" else tn_after_pv)[j] = tn_c
            for j in range(n):
                kind = tile_kind(alive[j] * 128 - qb * 512)
                sc_c += 1
                sc_after_exp[j] = sc_c
                if kind in ("band", "mask"):
                    vc_c += 1
                    vc_after_mul[j] = vc_c

            with nc.Block() as bl:
                @bl.tensor
                def _(tn):
                    if hp == 0:
                        tn.wait_ge(SC.h, sc0)
                        tn.wait_ge(VC.h, vc0)
                    for op, j in order:
                        kt = alive[j]
                        o = kt * 128 - qb * 512
                        t = j % 2
                        if op == "qk":
                            if j >= 2:
                                tn.wait_ge(SC.h, sc_after_exp[j - 2])
                                if (j - 2) in vc_after_mul:
                                    tn.wait_ge(VC.h, vc_after_mul[j - 2])
                            for (ps, h) in ((psS[t], hA), (psS[2 + t], hB)):
                                if not is_l1:
                                    c = h // 2
                                    r0 = (h % 2) * 64
                                    i = tn.matmul(
                                        ps[:],
                                        k_sb[r0:r0 + 64, c * T + kt * 128: c * T + kt * 128 + 128],
                                        q_sb[r0:r0 + 64, c * T + qb * 512: c * T + qb * 512 + 512],
                                        start=True, stop=True)
                                else:
                                    gg, s_ = h // 4, h % 4
                                    r0 = 32 * s_
                                    tp = (96, 0) if s_ == 3 else None
                                    for ii, cch in enumerate((2 * gg, 2 * gg + 1)):
                                        i = tn.matmul(
                                            ps[:],
                                            k_sb[r0:r0 + 32, cch * T + kt * 128: cch * T + kt * 128 + 128],
                                            q_sb[r0:r0 + 32, cch * T + qb * 512: cch * T + qb * 512 + 512],
                                            start=(ii == 0), stop=(ii == 1), tile_position=tp)
                            i.then_inc(TN.h, 1)
                        else:
                            kind = tile_kind(o)
                            if kind in ("band", "mask"):
                                tn.wait_ge(VC.h, vc_after_mul[j])
                            else:
                                tn.wait_ge(SC.h, sc_after_exp[j])
                            if j == 0:
                                tn.wait_ge(VC.h, evac_mark[par])
                            first = (j == 0)
                            last = (j == n - 1)
                            for hl, h in ((0, hA), (1, hB)):
                                p_ap = ptile[:, (hl * 2 + t) * 512:(hl * 2 + t + 1) * 512]
                                vt = v_sb[:, kt * 1024 + h * 64: kt * 1024 + h * 64 + 64]
                                tn.matmul(ps_ctx[hl * 64:(hl + 1) * 64, :], vt, p_ap,
                                          start=first, stop=last, skip_group_check=True)
                                i = tn.matmul(ps_den[hl * 64:(hl + 1) * 64, :], onesb[:, 0:64], p_ap,
                                              start=first, stop=last, skip_group_check=True)
                            i.then_inc(TN.h, 1)
                    TN.n = tn_c

                @bl.scalar
                def _(s):
                    for j in range(n):
                        kt = alive[j]
                        o = kt * 128 - qb * 512
                        kind = tile_kind(o)
                        t = j % 2
                        if j == 0:
                            s.wait_ge(VC.h, vc0)
                        s.wait_ge(TN.h, tn_after_pv.get(j - 2, tn_after_qk[j]))
                        if (j - 2) in vc_after_mul:
                            s.wait_ge(VC.h, vc_after_mul[j - 2])
                        for (ps, h, hl) in ((psS[t], hA, 0), (psS[2 + t], hB, 1)):
                            if kind in ("band", "mask"):
                                dst = praw[:, (hl * 2 + t) * 512:(hl * 2 + t + 1) * 512]
                            else:
                                dst = ptile[:, (hl * 2 + t) * 512:(hl * 2 + t + 1) * 512]
                            bias = 0.0
                            if kind == "cpos":
                                bias = sb["cb"][:, 2 * h:2 * h + 1]
                            elif kind == "cneg":
                                bias = sb["cb"][:, 2 * h + 1:2 * h + 2]
                            i = s.activation(out=dst, in_=ps[:], func=AF.Exp, scale=scale, bias=bias)
                        i.then_inc(SC.h, 1)
                    SC.n = sc_c

                @bl.vector
                def _(v):
                    first_band = True
                    for j in range(n):
                        kt = alive[j]
                        o = kt * 128 - qb * 512
                        kind = tile_kind(o)
                        if kind not in ("band", "mask"):
                            continue
                        t = j % 2
                        if first_band:
                            if not is_l1:
                                v.wait_ge(LD.h, ld_t)
                            v.wait_ge(TN.h, tn0)
                            first_band = False
                        v.wait_ge(SC.h, sc_after_exp[j])
                        if j >= 2:
                            v.wait_ge(TN.h, tn_after_pv[j - 2])
                        for hl in range(2):
                            if kind == "band":
                                jdx = (o + 128) // 128
                                m = ttile[:, hl * 3072 + jdx * 512: hl * 3072 + (jdx + 1) * 512][:, ::-1]
                            else:
                                m = sb["zm"][:, (o // 128) * 512:(o // 128 + 1) * 512]
                            i = v.tensor_tensor(
                                out=ptile[:, (hl * 2 + t) * 512:(hl * 2 + t + 1) * 512],
                                in0=praw[:, (hl * 2 + t) * 512:(hl * 2 + t + 1) * 512], in1=m,
                                op=ALU.mult)
                        i.then_inc(VC.h, 1)
                    VC.n = vc_c

                @bl.vector
                def _(v):
                    v.wait_ge(TN.h, tn_after_pv[n - 1])
                    v.reciprocal(rcp, ps_den[:])
                    i = v.tensor_tensor(out=ctxq[:, hp * 512:(hp + 1) * 512],
                                        in0=ps_ctx[:], in1=rcp, op=ALU.mult)
                    VC.inc(i)
            evac_mark[par] = VC.n

        vc_w, sc_w, ld_w = VC.n, SC.n, LD.n
        with nc.Block() as bw:
            @bw.tensor
            def _(tn):
                tn.wait_ge(VC.h, vc_w)
                tn.wait_ge(SC.h, sc_w)
                for dc in range(8):
                    if dc >= 2:
                        tn.wait_ge(SC.h, sc_w + dc - 1)
                    for c in range(8):
                        i = tn.matmul(psS[dc % 2][:],
                                      wo_sb[:, c * 1024 + dc * 128: c * 1024 + dc * 128 + 128],
                                      ctxq[:, c * 512:(c + 1) * 512],
                                      start=(c == 0), stop=(c == 7))
                    TN.inc(i)

            @bw.scalar
            def _(s):
                base = TN.n - 8
                s.wait_ge(LD.h, ld_w)
                for dc in range(8):
                    s.wait_ge(TN.h, base + dc + 1)
                    bias = 0.0 if wo_bias_col is None else wo_bias_col[:, dc:dc + 1]
                    i = s.activation(out=sev[:, dc * 512:(dc + 1) * 512], in_=psS[dc % 2][:],
                                     func=AF.Copy if wo_bias_col is None else AF.Identity,
                                     bias=bias)
                    SC.inc(i)

            @bw.sync
            def _(sy):
                SC.wait(sy)
                sr = s_out_dram.rearrange("(c p) t -> p c t", p=128)
                LD.inc(sy.dma_start(
                    out=sr[:, :, b * T + qb * 512: b * T + qb * 512 + 512],
                    in_=sev.rearrange("p (c t) -> p c t", c=8)), 16)


A_DECLS = [("xT", [D, TOK], F32, "ExternalInput"), ("wq", [D, D], F32, "ExternalInput"),
           ("wk", [D, D], F32, "ExternalInput"), ("wv", [D, D], F32, "ExternalInput"),
           ("wo", [D, D], F32, "ExternalInput"), ("relemb", [NB, H], F32, "ExternalInput"),
           ("OH", [NB, BPAD], BF16, "ExternalInput"), ("ONESB", [128, 512], BF16, "ExternalInput"),
           ("s0", [D, TOK], BF16, "ExternalOutput"), ("xn0", [D, TOK], BF16, "ExternalOutput")]


def build_A(debug=False, ctx=None):
    nc, stk, sems, io = _phase_setup(ctx, "A_", A_DECLS)
    LD, GD, TN, VC, SC = (sems[n] for n in ("LD", "GD", "TN", "VC", "SC"))
    xT, wq, wk, wv, wo, relemb, OH, ONESB, s0, xn0 = (
        io[n] for n in ("xT", "wq", "wk", "wv", "wo", "relemb", "OH", "ONESB", "s0", "xn0"))

    if debug:
        dbg_expb = nc.dram_tensor("dbg_expb", [H, BPAD], BF16, kind="ExternalOutput").ap()
        dbg_cb = nc.dram_tensor("dbg_cb", [128, 32], BF16, kind="ExternalOutput").ap()
        dbg_k = nc.dram_tensor("dbg_k", [128, 8 * T], BF16, kind="ExternalOutput").ap()
        dbg_q = nc.dram_tensor("dbg_q", [128, 8 * T], BF16, kind="ExternalOutput").ap()
        dbg_v = nc.dram_tensor("dbg_v", [128, 16 * 1024], BF16, kind="ExternalOutput").ap()
        dbg_ctx = nc.dram_tensor("dbg_ctx", [128, 8 * 512], BF16, kind="ExternalOutput").ap()
    expb_d = nc.dram_tensor("expb_d", [H, BPAD], BF16, kind="Internal").ap()
    cvec_d = nc.dram_tensor("cvec_d", [H, 2], BF16, kind="Internal").ap()

    with stk:
        e = stk.enter_context
        onesb = e(nc.sbuf_tensor("onesb", [128, 512], BF16))
        xn_sb = e(nc.sbuf_tensor("xn_sb", [128, 8 * T], BF16))
        k_sb = e(nc.sbuf_tensor("k_sb", [128, 8 * T], BF16))
        q_sb = e(nc.sbuf_tensor("q_sb", [128, 8 * T], BF16))
        v_sb = e(nc.sbuf_tensor("v_sb", [128, 16 * 1024], BF16))
        wbuf = e(nc.sbuf_tensor("wbuf", [128, 8 * 1024], BF16))
        cb = e(nc.sbuf_tensor("cb", [128, 32], BF16))
        scrB = e(nc.sbuf_tensor("scrB", [128, SCRB_N], BF16))
        scrF = e(nc.sbuf_tensor("scrF", [128, 1536], F32))
        sb = {f"psS{i}": e(nc.psum_tensor(f"psS{i}", [128, 512], F32)) for i in range(4)}
        sb["ps_ctx"] = [e(nc.psum_tensor(f"ps_ctx{i}", [128, 512], F32)) for i in range(2)]
        sb["ps_den"] = [e(nc.psum_tensor(f"ps_den{i}", [128, 512], F32)) for i in range(2)]
        sb.update(onesb=onesb, q_sb=q_sb, k_sb=k_sb, v_sb=v_sb, wo_sb=wbuf,
                  scrB=scrB, scrF=scrF, cb=cb, expb_d_t=expb_d.tensor, zm=None)

        oh_sb = scrB[0:NB, S_OHC:S_OHC + BPAD]
        bias_sb = scrB[0:H, S_BIA:S_BIA + BPAD]
        expb_sb = scrB[0:H, S_EXB:S_EXB + BPAD]
        relb_sb = scrB[0:NB, 13824:13824 + H]
        with ExitStack() as st0:
            rel_sb = st0.enter_context(nc.sbuf_tensor("rel_sb", [NB, H], F32))
            NJ = BPAD // 512
            tn0, vc0 = TN.n, VC.n
            with nc.Block() as b0:
                @b0.sync
                def _(sy):
                    LD.inc(sy.dma_start(out=onesb[:], in_=ONESB), 16)
                    LD.inc(sy.dma_start(out=rel_sb[:], in_=relemb), 16)
                    LD.inc(sy.dma_start(out=oh_sb, in_=OH), 16)

                @b0.vector
                def _(v):
                    LD.wait(v)
                    VC.inc(v.tensor_copy(relb_sb, rel_sb[:]))

                @b0.tensor
                def _(tn):
                    tn.wait_ge(VC.h, vc0 + 1)
                    for jj in range(NJ):
                        if jj >= 2:
                            tn.wait_ge(VC.h, vc0 + jj)
                        i = tn.matmul(sb[f"psS{jj % 2}"][0:H, :], relb_sb,
                                      oh_sb[:, jj * 512:(jj + 1) * 512], start=True, stop=True)
                        TN.inc(i)

                @b0.vector
                def _(v):
                    for jj in range(NJ):
                        v.wait_ge(TN.h, tn0 + jj + 1)
                        i = v.tensor_copy(bias_sb[:, jj * 512:(jj + 1) * 512],
                                          sb[f"psS{jj % 2}"][0:H, :])
                        VC.inc(i)

                @b0.scalar
                def _(s):
                    VC.wait(s)
                    SC.inc(s.activation(out=expb_sb, in_=bias_sb, func=AF.Exp))

                @b0.sync
                def _(sy):
                    SC.wait(sy)
                    LD.inc(sy.dma_start(out=expb_d, in_=expb_sb), 16)
                    LD.inc(sy.dma_start(out=cvec_d[:, 0:1], in_=bias_sb[:, 4094:4095],
                                        allow_slow_non_contiguous=True), 16)
                    LD.inc(sy.dma_start(out=cvec_d[:, 1:2], in_=bias_sb[:, 0:1],
                                        allow_slow_non_contiguous=True), 16)
                    sy.wait_ge(LD.h, LD.n)
                    for h in range(H):
                        src = bass.AP(cvec_d.tensor, h * 2, [[0, 128], [1, 2]])
                        LD.inc(sy.dma_start(out=cb[:, 2 * h:2 * h + 2], in_=src), 16)

        for b in range(B_LOC):
            emit_ln(nc, sems, onesb, scrB, scrF, xT, xn_sb, xn0, b * T,
                    sb["psS0"], sb["psS1"])

            def load_w(src):
                tn_w = TN.n
                with nc.Block() as blw:
                    @blw.gpsimd
                    def _(g):
                        g.wait_ge(TN.h, tn_w)
                        GD.inc(g.dma_start(out=wbuf[:].rearrange("p (c f) -> p c f", c=8),
                                           in_=src.rearrange("(c p) f -> p c f", p=128)), 16)
                return GD.n

            def proj_kq(gd_mark, dst):
                tn0_, sc0_, vc0_ = TN.n, SC.n, VC.n
                with nc.Block() as bp:
                    @bp.tensor
                    def _(tn):
                        tn.wait_ge(GD.h, gd_mark)
                        tn.wait_ge(VC.h, vc0_)
                        tn.wait_ge(SC.h, sc0_)
                        for jj in range(32):
                            fc, nb_ = jj // 4, jj % 4
                            if jj >= 2:
                                tn.wait_ge(SC.h, sc0_ + jj - 1)
                            for dc in range(8):
                                i = tn.matmul(sb[f"psS{jj % 2}"][:],
                                              wbuf[:, dc * 1024 + fc * 128: dc * 1024 + fc * 128 + 128],
                                              xn_sb[:, dc * T + nb_ * 512: dc * T + nb_ * 512 + 512],
                                              start=(dc == 0), stop=(dc == 7))
                            TN.inc(i)

                    @bp.scalar
                    def _(s):
                        for jj in range(32):
                            fc, nb_ = jj // 4, jj % 4
                            s.wait_ge(TN.h, tn0_ + jj + 1)
                            i = s.activation(out=dst[:, fc * T + nb_ * 512: fc * T + nb_ * 512 + 512],
                                             in_=sb[f"psS{jj % 2}"][:], func=AF.Copy)
                            SC.inc(i)

            def proj_v(gd_mark):
                tn0_, sc0_, vc0_ = TN.n, SC.n, VC.n
                with nc.Block() as bp:
                    @bp.tensor
                    def _(tn):
                        tn.wait_ge(GD.h, gd_mark)
                        tn.wait_ge(VC.h, vc0_)
                        tn.wait_ge(SC.h, sc0_)
                        for jj in range(32):
                            tt, fb = jj // 2, jj % 2
                            if jj >= 2:
                                tn.wait_ge(VC.h, vc0_ + jj - 1)
                            for dc in range(8):
                                i = tn.matmul(sb[f"psS{jj % 2}"][:],
                                              xn_sb[:, dc * T + tt * 128: dc * T + tt * 128 + 128],
                                              wbuf[:, dc * 1024 + fb * 512: dc * 1024 + fb * 512 + 512],
                                              start=(dc == 0), stop=(dc == 7))
                            TN.inc(i)

                    @bp.vector
                    def _(v):
                        for jj in range(32):
                            tt, fb = jj // 2, jj % 2
                            v.wait_ge(TN.h, tn0_ + jj + 1)
                            i = v.tensor_copy(v_sb[:, tt * 1024 + fb * 512: tt * 1024 + (fb + 1) * 512],
                                              sb[f"psS{jj % 2}"][:])
                            VC.inc(i)

            proj_kq(load_w(wk), k_sb)
            proj_v(load_w(wv))
            proj_kq(load_w(wq), q_sb)
            gd_mo = load_w(wo)
            with nc.Block() as bwait:
                @bwait.tensor
                def _(tn):
                    tn.wait_ge(GD.h, gd_mo)
            emit_attention_batch(nc, sems, sb, 0, b, s0, wo_bias_col=None)

        if debug:
            with nc.Block() as bd:
                @bd.sync
                def _(sy):
                    sy.wait_ge(TN.h, TN.n)
                    sy.wait_ge(VC.h, VC.n)
                    sy.wait_ge(SC.h, SC.n)
                    LD.inc(sy.dma_start(out=scrB[0:H, 0:BPAD], in_=expb_d), 16)
                    sy.wait_ge(LD.h, LD.n)
                    LD.inc(sy.dma_start(out=dbg_expb, in_=scrB[0:H, 0:BPAD]), 16)
                    LD.inc(sy.dma_start(out=dbg_cb, in_=cb[:]), 16)
                    LD.inc(sy.dma_start(out=dbg_k, in_=k_sb[:]), 16)
                    LD.inc(sy.dma_start(out=dbg_q, in_=q_sb[:]), 16)
                    LD.inc(sy.dma_start(out=dbg_v, in_=v_sb[:]), 16)
                    LD.inc(sy.dma_start(out=dbg_ctx, in_=scrB[:, S_CTQ:S_CTQ + 4096]), 16)
        with nc.Block() as bf:
            @bf.sync
            def _(sy):
                sy.wait_ge(LD.h, LD.n)
                sy.wait_ge(GD.h, GD.n)
    return nc




B_DECLS = [("xT", [D, TOK], F32, "ExternalInput"), ("xn0", [D, TOK], BF16, "ExternalInput"),
           ("s0", [D, TOK], BF16, "ExternalInput"), ("wfcT", [D, FF], F32, "ExternalInput"),
           ("wprojT", [FF, D], F32, "ExternalInput"), ("lnw", [128, 32], F32, "ExternalInput"),
           ("lnb", [128, 32], F32, "ExternalInput"), ("ONESB", [128, 512], BF16, "ExternalInput"),
           ("ONESF", [128, 128], F32, "ExternalInput"), ("x1", [D, TOK], F32, "ExternalOutput")]


def build_B(debug=False, ctx=None):
    nc, stk, sems, io = _phase_setup(ctx, "B_", B_DECLS)
    LD, GD, TN, VC, SC, GP = (sems[n] for n in ("LD", "GD", "TN", "VC", "SC", "GP"))
    xT, xn0, s0, wfcT, wprojT, lnw, lnb, ONESB, ONESF, x1 = (
        io[n] for n in ("xT", "xn0", "s0", "wfcT", "wprojT", "lnw", "lnb", "ONESB", "ONESF", "x1"))
    if debug:
        wfcq_d = nc.dram_tensor("wfcq_d", [D, FF], BF16, kind="ExternalOutput").ap()
        wprojq_d = nc.dram_tensor("wprojq_d", [FF, D], BF16, kind="ExternalOutput").ap()
        dbg_h = nc.dram_tensor("dbg_h", [128, 32 * 256], BF16, kind="ExternalOutput").ap()
        dbg_mu = nc.dram_tensor("dbg_mu", [128, 256], BF16, kind="ExternalOutput").ap()
        dbg_rs = nc.dram_tensor("dbg_rs", [128, 256], BF16, kind="ExternalOutput").ap()
        dbg_cv = nc.dram_tensor("dbg_cv", [1, 1024], BF16, kind="ExternalOutput").ap()
        dbg_thr = nc.dram_tensor("dbg_thr", [128, 4], F32, kind="ExternalOutput").ap()
        dbg_parts = nc.dram_tensor("dbg_parts", [128, 66], F32, kind="ExternalOutput").ap()
    else:
        wfcq_d = nc.dram_tensor("wfcq_d", [D, FF], BF16, kind="Internal").ap()
        wprojq_d = nc.dram_tensor("wprojq_d", [FF, D], BF16, kind="Internal").ap()

    TB = 256
    NTB = TOK // TB

    with stk:
        e = stk.enter_context
        onesb = e(nc.sbuf_tensor("onesb", [128, 512], BF16))
        onesf = e(nc.sbuf_tensor("onesf", [128, 128], F32))
        lnw_sb = e(nc.sbuf_tensor("lnw_sb", [128, 32], F32))
        lnb_sb = e(nc.sbuf_tensor("lnb_sb", [128, 32], F32))
        lnb_bf = e(nc.sbuf_tensor("lnb_bf", [128, 32], BF16))
        wfcq = e(nc.sbuf_tensor("wfcq", [128, 8 * FF], BF16))
        wpjq = e(nc.sbuf_tensor("wpjq", [128, 32 * 1024], BF16))
        cvec = e(nc.sbuf_tensor("cvec", [128, 1024], BF16))
        h_sb = e(nc.sbuf_tensor("h_sb", [128, 32 * TB], BF16))
        hsq = e(nc.sbuf_tensor("hsq", [128, TB], BF16))
        xn_blk = [e(nc.sbuf_tensor(f"xn_blk{i}", [128, 8 * TB], BF16)) for i in range(2)]
        s_blk = [e(nc.sbuf_tensor(f"s_blk{i}", [128, 8 * TB], BF16)) for i in range(2)]
        x_blk = [e(nc.sbuf_tensor(f"x_blk{i}", [128, 8 * TB], F32)) for i in range(2)]
        out_sb = e(nc.sbuf_tensor("out_sb", [128, 8 * TB], F32))
        g_sb = e(nc.sbuf_tensor("g_sb", [128, TB], BF16))
        t2_sb = e(nc.sbuf_tensor("t2_sb", [128, TB], BF16))
        mu_sb = e(nc.sbuf_tensor("mu_sb", [128, TB], BF16))
        rs_sb = e(nc.sbuf_tensor("rs_sb", [128, TB], BF16))
        td_sb = e(nc.sbuf_tensor("td_sb", [128, TB], BF16))
        scf = e(nc.sbuf_tensor("scf", [128, 3 * TB], F32))
        parts = e(nc.sbuf_tensor("parts", [128, 66], F32))
        thr2 = e(nc.sbuf_tensor("thr2", [128, 4], F32))
        ps = [e(nc.psum_tensor(f"psB{i}", [128, 512], F32)) for i in range(4)]
        ps_g = e(nc.psum_tensor("ps_g", [128, 4], F32))
        wscr = [x_blk[0][:, 0:2048], x_blk[1][:, 0:2048]]
        a_scr = h_sb[:, 0:2048]
        b_scr = h_sb[:, 2048:4096]
        q_scr = h_sb[:, 4096:6144]

        with nc.Block() as b0:
            @b0.sync
            def _(sy):
                LD.inc(sy.dma_start(out=onesb[:], in_=ONESB), 16)
                LD.inc(sy.dma_start(out=onesf[:], in_=ONESF), 16)
                LD.inc(sy.dma_start(out=lnw_sb[:], in_=lnw), 16)
                LD.inc(sy.dma_start(out=lnb_sb[:], in_=lnb), 16)

            @b0.vector
            def _(v):
                LD.wait(v)
                VC.inc(v.tensor_copy(lnb_bf[:], lnb_sb[:]))

        for mi, (wsrc, wdst) in enumerate(((wfcT, wfcq_d), (wprojT, wprojq_d))):
            npc = (D * FF) // (128 * 2048)
            ld0, vc0 = LD.n, VC.n
            with nc.Block() as bq:
                @bq.sync
                def _(sy):
                    for pc in range(npc):
                        if pc >= 2:
                            sy.wait_ge(VC.h, vc0 + pc - 1)
                        LD.inc(sy.dma_start(out=wscr[pc % 2],
                                            in_=bass.AP(wsrc.tensor, pc * 128 * 2048,
                                                        [[2048, 128], [1, 2048]])), 16)

                @bq.vector
                def _(v):
                    for pc in range(npc):
                        v.wait_ge(LD.h, ld0 + (pc + 1) * 16)
                        i = v.tensor_reduce(out=parts[:, pc:pc + 1], in_=wscr[pc % 2], axis=AX.X,
                                            op=ALU.add, apply_absolute_value=True)
                        VC.inc(i)

                tnq = TN.n
                with ExitStack() as _s:
                    pass
                @bq.tensor
                def _(tn):
                    VC.wait(tn)
                    i = tn.matmul(ps_g[0:npc, 2:3], parts[:, 0:npc], onesf[:, 0:1],
                                  start=True, stop=True)
                    TN.inc(i)

                @bq.scalar
                def _(s):
                    s.wait_ge(TN.h, tnq + 1)
                    SC.inc(s.copy(parts[0:npc, npc + 2:npc + 3], ps_g[0:npc, 2:3]))

                @bq.tensor
                def _(tn):
                    SC.wait(tn)
                    i = tn.matmul(ps_g[0:1, mi:mi + 1], parts[0:npc, npc + 2:npc + 3], onesf[0:npc, 0:1],
                                  start=True, stop=True)
                    TN.inc(i)

                @bq.vector
                def _(v):
                    TN.wait(v)
                    i = v.tensor_scalar(out=parts[:, npc + 1:npc + 2][0:1, :], in0=ps_g[0:1, mi:mi + 1],
                                        scalar1=0.5 / (D * FF), scalar2=0.5e-6, op0=ALU.mult, op1=ALU.add)
                    VC.inc(i)

                @bq.tensor
                def _(tn):
                    TN.wait(tn)
                    VC.wait(tn)
                    i = tn.matmul(ps[2][0:128, 0:1], onesf[0:1, :], parts[0:1, npc + 1:npc + 2],
                                  start=True, stop=True)
                    TN.inc(i)

                @bq.vector
                def _(v):
                    TN.wait(v)
                    v.tensor_copy(thr2[:, 2 * mi:2 * mi + 1], ps[2][:, 0:1])
                    i = v.tensor_scalar(out=thr2[:, 2 * mi + 1:2 * mi + 2], in0=ps[2][:, 0:1],
                                        scalar1=-1.0, scalar2=None, op0=ALU.mult)
                    VC.inc(i)

            ld1, vc1, gp1 = LD.n, VC.n, GP.n
            with nc.Block() as bq2:
                @bq2.sync
                def _(sy):
                    for pc in range(npc):
                        if pc >= 2:
                            sy.wait_ge(VC.h, vc1 + 2 * (pc - 1))
                        LD.inc(sy.dma_start(out=wscr[pc % 2],
                                            in_=bass.AP(wsrc.tensor, pc * 128 * 2048,
                                                        [[2048, 128], [1, 2048]])), 16)
                        sy.wait_ge(VC.h, vc1 + 2 * pc + 2)
                        dst_r = bass.AP(wdst.tensor, pc * 128 * 2048,
                                        [[2048, 128], [1, 2048]])
                        LD.inc(sy.dma_start(out=dst_r, in_=q_scr), 16)

                @bq2.vector
                def _(v):
                    for pc in range(npc):
                        v.wait_ge(LD.h, ld1 + pc * 32 + 16)
                        if pc == 0:
                            v.wait_ge(VC.h, vc1)
                        v.tensor_scalar(out=b_scr, in0=wscr[pc % 2], scalar1=thr2[:, 2 * mi + 1:2 * mi + 2],
                                        scalar2=None, op0=ALU.is_lt)
                        if pc >= 1:
                            v.wait_ge(LD.h, ld1 + pc * 32)
                        i = v.scalar_tensor_tensor(out=q_scr, in0=wscr[pc % 2],
                                                   scalar=thr2[:, 2 * mi:2 * mi + 1], in1=b_scr,
                                                   op0=ALU.is_gt, op1=ALU.subtract)
                        VC.inc(i, 2)

        with nc.Block() as bl:
            @bl.sync
            def _(sy):
                sy.wait_ge(LD.h, LD.n)
                for dc in range(8):
                    LD.inc(sy.dma_start(out=wfcq[:, dc * FF:(dc + 1) * FF],
                                        in_=bass.AP(wfcq_d.tensor, dc * 128 * FF, [[FF, 128], [1, FF]])), 16)
                for fc in range(32):
                    LD.inc(sy.dma_start(out=wpjq[:, fc * 1024:(fc + 1) * 1024],
                                        in_=bass.AP(wprojq_d.tensor, fc * 128 * D, [[D, 128], [1, D]])), 16)

        tnc = TN.n
        with nc.Block() as bc:
            @bc.tensor
            def _(tn):
                LD.wait(tn)
                for db in range(2):
                    for fc in range(32):
                        i = tn.matmul(ps[db][0:1, :], lnb_bf[:, fc:fc + 1],
                                      wpjq[:, fc * 1024 + db * 512: fc * 1024 + db * 512 + 512],
                                      start=(fc == 0), stop=(fc == 31))
                    TN.inc(i)

            @bc.vector
            def _(v):
                for db in range(2):
                    v.wait_ge(TN.h, tnc + db + 1)
                    i = v.tensor_copy(cvec[0:1, db * 512:(db + 1) * 512], ps[db][0:1, :])
                    VC.inc(i)

        xr_r = xT.rearrange("(c p) t -> p c t", p=128)
        xn_r = xn0.rearrange("(c p) t -> p c t", p=128)
        s_r = s0.rearrange("(c p) t -> p c t", p=128)
        x1_r = x1.rearrange("(c p) t -> p c t", p=128)
        blk_done_marks = {}
        vc_q = VC.n
        for tb in range(NTB):
            T0 = tb * TB
            pr = tb % 2
            with nc.Block() as bld:
                @bld.sync
                def _(sy):
                    if tb < 2:
                        sy.wait_ge(VC.h, vc_q)
                    if tb >= 2:
                        mk = blk_done_marks[tb - 2]
                        sy.wait_ge(TN.h, mk["tn"])
                        sy.wait_ge(VC.h, mk["vc"])
                        sy.wait_ge(SC.h, mk["sc"])
                    LD.inc(sy.dma_start(out=xn_blk[pr][:].rearrange("p (c t) -> p c t", c=8),
                                        in_=xn_r[:, :, T0:T0 + TB]), 16)
                    LD.inc(sy.dma_start(out=s_blk[pr][:].rearrange("p (c t) -> p c t", c=8),
                                        in_=s_r[:, :, T0:T0 + TB]), 16)
                    LD.inc(sy.dma_start(out=x_blk[pr][:].rearrange("p (c t) -> p c t", c=8),
                                        in_=xr_r[:, :, T0:T0 + TB]), 16)
            ld_w = LD.n - 48

            tn0_, sc0_ = TN.n, SC.n
            tn_after_h, tn_after_st = {}, {}
            tcnt = tn0_
            torder = []
            for fc in range(32):
                torder.append(("h", fc))
                if fc >= 1:
                    torder.append(("st", fc - 1))
            torder.append(("st", 31))
            for op, fc in torder:
                tcnt += 1
                (tn_after_h if op == "h" else tn_after_st)[fc] = tcnt

            with nc.Block() as bh:
                @bh.tensor
                def _(tn):
                    tn.wait_ge(LD.h, ld_w + 48)
                    if tb >= 1:
                        tn.wait_ge(SC.h, blk_done_marks[tb - 1]["sc"])
                    for op, fc in torder:
                        if op == "h":
                            if fc >= 2:
                                tn.wait_ge(SC.h, sc0_ + 2 * (fc - 2) + 1)
                            for dc in range(8):
                                i = tn.matmul(ps[fc % 2][:, 0:TB],
                                              wfcq[:, dc * FF + fc * 128: dc * FF + fc * 128 + 128],
                                              xn_blk[pr][:, dc * TB:(dc + 1) * TB],
                                              start=(dc == 0), stop=(dc == 7))
                            i.then_inc(TN.h, 1)
                        else:
                            tn.wait_ge(SC.h, sc0_ + 2 * fc + 2)
                            tn.matmul(ps[2][:, 0:TB], onesb[:, 0:128], h_sb[:, fc * TB:(fc + 1) * TB],
                                      start=(fc == 0), stop=(fc == 31), skip_group_check=True)
                            i = tn.matmul(ps[3][:, 0:TB], onesb[:, 0:128], hsq[:],
                                          start=(fc == 0), stop=(fc == 31), skip_group_check=True)
                            i.then_inc(TN.h, 1)
                    TN.n = tcnt

                @bh.scalar
                def _(s):
                    for fc in range(32):
                        s.wait_ge(TN.h, tn_after_h[fc])
                        if fc >= 1:
                            s.wait_ge(TN.h, tn_after_st[fc - 1])
                        i = s.activation(out=h_sb[:, fc * TB:(fc + 1) * TB], in_=ps[fc % 2][:, 0:TB],
                                         func=AF.Gelu)
                        i.then_inc(SC.h, 1)
                        SC.n += 1
                        i = s.activation(out=hsq[:], in_=h_sb[:, fc * TB:(fc + 1) * TB], func=AF.Square)
                        i.then_inc(SC.h, 1)
                        SC.n += 1

            vc0_ = VC.n
            with nc.Block() as bs:
                @bs.vector
                def _(v):
                    v.wait_ge(TN.h, tn_after_st[31])
                    v.tensor_scalar(out=mu_sb[:], in0=ps[2][:, 0:TB], scalar1=1.0 / FF, scalar2=None,
                                    op0=ALU.mult)
                    v.tensor_scalar(out=scf[:, 0:TB], in0=ps[3][:, 0:TB], scalar1=1.0 / FF,
                                    scalar2=1e-5, op0=ALU.mult, op1=ALU.add)
                    v.tensor_tensor(out=scf[:, TB:2 * TB], in0=mu_sb[:], in1=mu_sb[:], op=ALU.mult)
                    VC.inc(v.tensor_tensor(out=scf[:, 2 * TB:3 * TB], in0=scf[:, 0:TB],
                                           in1=scf[:, TB:2 * TB], op=ALU.subtract))

                @bs.scalar
                def _(s):
                    VC.wait(s)
                    s.activation(out=scf[:, 2 * TB:3 * TB], in_=scf[:, 2 * TB:3 * TB], func=AF.Ln)
                    SC.inc(s.activation(out=rs_sb[:], in_=scf[:, 2 * TB:3 * TB], func=AF.Exp, scale=-0.5))

                @bs.vector
                def _(v):
                    SC.wait(v)
                    for fc in range(32):
                        v.tensor_tensor(out=td_sb[:], in0=h_sb[:, fc * TB:(fc + 1) * TB], in1=mu_sb[:],
                                        op=ALU.subtract)
                        i = v.scalar_tensor_tensor(out=h_sb[:, fc * TB:(fc + 1) * TB], in0=td_sb[:],
                                                   scalar=lnw_sb[:, fc:fc + 1], in1=rs_sb[:],
                                                   op0=ALU.mult, op1=ALU.mult)
                    VC.inc(i)

            tnp, scp, vcp = TN.n, SC.n, VC.n
            with nc.Block() as bp:
                @bp.tensor
                def _(tn):
                    tn.wait_ge(VC.h, vcp)
                    for dc in range(8):
                        if dc >= 2:
                            tn.wait_ge(SC.h, scp + dc - 1)
                        for fc in range(32):
                            tn.matmul(ps[dc % 2][:, 0:TB],
                                      wpjq[:, fc * 1024 + dc * 128: fc * 1024 + dc * 128 + 128],
                                      h_sb[:, fc * TB:(fc + 1) * TB],
                                      start=(fc == 0), stop=False)
                        i = tn.matmul(ps[dc % 2][:, 0:TB], cvec[0:1, dc * 128:(dc + 1) * 128],
                                      onesb[0:1, 0:TB], start=False, stop=True)
                        TN.inc(i)

                @bp.scalar
                def _(s):
                    for dc in range(8):
                        s.wait_ge(TN.h, tnp + dc + 1)
                        i = s.activation(out=g_sb[:], in_=ps[dc % 2][:, 0:TB], func=AF.Silu)
                        SC.inc(i)
                        s.wait_ge(VC.h, vcp + 2 * dc + 1)

                @bp.vector
                def _(v):
                    for dc in range(8):
                        v.wait_ge(SC.h, scp + dc + 1)
                        v.tensor_tensor(out=t2_sb[:], in0=g_sb[:], in1=s_blk[pr][:, dc * TB:(dc + 1) * TB],
                                        op=ALU.mult)
                        VC.n += 1
                        i = v.tensor_tensor(out=out_sb[:, dc * TB:(dc + 1) * TB], in0=t2_sb[:],
                                            in1=x_blk[pr][:, dc * TB:(dc + 1) * TB], op=ALU.add)
                        i.then_inc(VC.h, 2)
                        VC.n += 1

                @bp.sync
                def _(sy):
                    VC.wait(sy)
                    LD.inc(sy.dma_start(out=x1_r[:, :, T0:T0 + TB],
                                        in_=out_sb[:].rearrange("p (c t) -> p c t", c=8)), 16)
            blk_done_marks[tb] = dict(tn=TN.n, vc=VC.n, sc=SC.n)

        if debug:
            with nc.Block() as bd:
                @bd.sync
                def _(sy):
                    sy.wait_ge(TN.h, TN.n)
                    sy.wait_ge(VC.h, VC.n)
                    sy.wait_ge(SC.h, SC.n)
                    LD.inc(sy.dma_start(out=dbg_h, in_=h_sb[:]), 16)
                    LD.inc(sy.dma_start(out=dbg_mu, in_=mu_sb[:]), 16)
                    LD.inc(sy.dma_start(out=dbg_rs, in_=rs_sb[:]), 16)
                    LD.inc(sy.dma_start(out=dbg_cv, in_=cvec[0:1, :]), 16)
                    LD.inc(sy.dma_start(out=dbg_thr, in_=thr2[:]), 16)
                    LD.inc(sy.dma_start(out=dbg_parts, in_=parts[:]), 16)
        with nc.Block() as bf:
            @bf.sync
            def _(sy):
                sy.wait_ge(LD.h, LD.n)
                sy.wait_ge(GD.h, GD.n)

    return nc


C_DECLS = [("x1", [D, TOK], F32, "ExternalInput"), ("qkvw", [D, 3 * D], F32, "ExternalInput"),
           ("qkvb", [128, 24], F32, "ExternalInput"), ("outw", [D, D], F32, "ExternalInput"),
           ("outb", [128, 8], F32, "ExternalInput"), ("wg", [D, NE], F32, "ExternalInput"),
           ("CCb", [128, T], BF16, "ExternalInput"), ("SSb", [128, T], BF16, "ExternalInput"),
           ("ZM4", [128, 4 * 512], BF16, "ExternalInput"), ("ONESB", [128, 512], BF16, "ExternalInput"),
           ("ONESF", [128, 128], F32, "ExternalInput"), ("LSTF", [128, 128], F32, "ExternalInput"),
           ("IOTA16", [128, NE], F32, "ExternalInput"), ("LSU16F", [NE, NE], F32, "ExternalInput"),
           ("EYEB", [128, 128], BF16, "ExternalInput"), ("BV", [1, D], F32, "ExternalInput"),
           ("s1", [D, TOK], BF16, "ExternalOutput"), ("sorted_o", [TOK, D], BF16, "ExternalOutput"),
           ("counts_o", [1, NE], F32, "ExternalOutput"), ("idxo", [128, 32], F32, "ExternalOutput"),
           ("loco", [128, 32], F32, "ExternalOutput"), ("gso", [128, 32], F32, "ExternalOutput"),
           ("xn1T_d", [D, TOK], BF16, "Internal"), ("xnnat_d", [TOK, D], BF16, "Internal"),
           ("cnt_d", [1, 64], F32, "Internal")]


def build_C(core_id=0, debug=False, ctx=None, part="all"):
    nc, stk, sems, io = _phase_setup(ctx, "C_" + ("" if part == "all" else part + "_"), C_DECLS)
    LD, GD, TN, VC, SC, GP = (sems[n] for n in ("LD", "GD", "TN", "VC", "SC", "GP"))
    (x1, qkvw, qkvb, outw, outb, wg, CCb, SSb, ZM4, ONESB, ONESF, LSTF, IOTA16,
     LSU16F, EYEB, BV, s1, sorted_o, counts_o, idxo, loco, gso, xn1T_d, xnnat_d,
     cnt_d) = (
        io[n] for n in ("x1", "qkvw", "qkvb", "outw", "outb", "wg", "CCb", "SSb", "ZM4",
                        "ONESB", "ONESF", "LSTF", "IOTA16", "LSU16F", "EYEB", "BV",
                        "s1", "sorted_o", "counts_o", "idxo", "loco", "gso", "xn1T_d",
                        "xnnat_d", "cnt_d"))

    with stk:
        e = stk.enter_context
        onesb = e(nc.sbuf_tensor("onesb", [128, 512], BF16))
        onesf = e(nc.sbuf_tensor("onesf", [128, 128], F32))
        lstf = e(nc.sbuf_tensor("lstf", [128, 128], F32))
        iota16 = e(nc.sbuf_tensor("iota16", [128, NE], F32))
        lsu16 = e(nc.sbuf_tensor("lsu16", [NE, NE], F32))
        eyeb = e(nc.sbuf_tensor("eyeb", [128, 128], BF16))
        zm_sb = e(nc.sbuf_tensor("zm_sb", [128, 4 * 512], BF16))
        qkvb_sb = e(nc.sbuf_tensor("qkvb_sb", [128, 24], F32))
        outb_sb = e(nc.sbuf_tensor("outb_sb", [128, 8], F32))
        wg_sb = e(nc.sbuf_tensor("wg_sb", [128, 8 * NE], BF16))
        bvrow = e(nc.sbuf_tensor("bvrow", [1, D], BF16))
        xn_sb = e(nc.sbuf_tensor("xn_sb", [128, 8 * T], BF16))
        k_sb = e(nc.sbuf_tensor("k_sb", [128, 8 * T], BF16))
        q_sb = e(nc.sbuf_tensor("q_sb", [128, 8 * T], BF16))
        v_sb = e(nc.sbuf_tensor("v_sb", [128, 16 * 1024], BF16))
        wbuf = e(nc.sbuf_tensor("wbuf", [128, 8 * 1024], BF16))
        scrB = e(nc.sbuf_tensor("scrB", [128, SCRB_N], BF16))
        scrF = e(nc.sbuf_tensor("scrF", [128, 1536], F32))
        eq_sb = e(nc.sbuf_tensor("eq_sb", [128, 32 * NE], F32))
        idx_sb = e(nc.sbuf_tensor("idx_sb", [128, 32], F32))
        loc_sb = e(nc.sbuf_tensor("loc_sb", [128, 32], F32))
        gs_sb = e(nc.sbuf_tensor("gs_sb", [128, 32], F32))
        l_sb = e(nc.sbuf_tensor("l_sb", [128, NE], F32))
        et_sb = e(nc.sbuf_tensor("et_sb", [128, NE], F32))
        m_sb = e(nc.sbuf_tensor("m_sb", [128, 4], F32))
        carry = e(nc.sbuf_tensor("carry", [1, NE], F32))
        lst_bc = e(nc.sbuf_tensor("lst_bc", [128, NE], F32))
        nat_sb = e(nc.sbuf_tensor("nat_sb", [128, D], BF16))
        slot_sb = e(nc.sbuf_tensor("slot_sb", [128, 2], F32))
        slot_u = e(nc.sbuf_tensor("slot_u", [128, 1], U32))

        sb = {f"psS{i}": e(nc.psum_tensor(f"psS{i}", [128, 512], F32)) for i in range(4)}
        if part == "post":
            sb["ps_ctx"] = [e(nc.psum_tensor(f"ps_ctx{i}", [128, 512], F32)) for i in range(2)]
            sb["ps_den"] = [e(nc.psum_tensor(f"ps_den{i}", [128, 512], F32)) for i in range(2)]
            ps_tr = None
        else:
            sb["ps_ctx"] = e(nc.psum_tensor("ps_ctx", [128, 512], F32))
            sb["ps_den"] = e(nc.psum_tensor("ps_den", [128, 512], F32))
            ps_tr = e(nc.psum_tensor("ps_tr", [128, 1024], BF16))
        sb.update(onesb=onesb, q_sb=q_sb, k_sb=k_sb, v_sb=v_sb, wo_sb=wbuf,
                  scrB=scrB, scrF=scrF, cb=None, expb_d_t=None, zm=zm_sb)
        rope_e = scrB[:, S_TT:S_TT + 2048]
        rope_o = scrB[:, S_TT + 2048:S_TT + 4096]
        rt1 = scrB[:, S_TT + 4096:S_TT + 6144]
        rt2 = scrB[:, S_PRAW:S_PRAW + 2048]
        cc_sb = scrB[:, S_SEV:S_SEV + 2048]
        ss_sb = scrB[:, S_CTQ:S_CTQ + 2048]

        with nc.Block() as b0:
            @b0.sync
            def _(sy):
                for d, s_ in ((onesb[:], ONESB), (onesf[:], ONESF), (lstf[:], LSTF),
                              (iota16[:], IOTA16), (lsu16[:], LSU16F), (eyeb[:], EYEB),
                              (zm_sb[:], ZM4),
                              (qkvb_sb[:], qkvb), (outb_sb[:], outb)):
                    LD.inc(sy.dma_start(out=d, in_=s_), 16)

            @b0.gpsimd
            def _(g):
                GD.inc(g.dma_start(out=wg_sb[:].rearrange("p (c f) -> p c f", c=8),
                                   in_=wg.rearrange("(c p) f -> p c f", p=128)), 16)
                GD.inc(g.dma_start(out=bvrow[:], in_=BV), 16)

            @b0.vector
            def _(v):
                LD.wait(v)
                VC.inc(v.memset(carry[:], 0.0))

        for b in (range(B_LOC) if part in ("all", "pre") else []):
            emit_ln(nc, sems, onesb, scrB, scrF, x1, xn_sb, xn1T_d, b * T,
                    sb["psS0"], sb["psS1"])

            for tt in range(16):
                g_ = b * 16 + tt
                tn0r, vc0r, sc0r = TN.n, VC.n, SC.n
                with nc.Block() as br:
                    @br.tensor
                    def _(tn):
                        if g_ == 0:
                            GD.wait(tn)
                        tn.wait_ge(VC.h, vc0r)
                        for dc in range(8):
                            i = tn.matmul(sb["psS0"][:, 0:NE],
                                          xn_sb[:, dc * T + tt * 128: dc * T + tt * 128 + 128],
                                          wg_sb[:, dc * NE:(dc + 1) * NE],
                                          start=(dc == 0), stop=(dc == 7))
                        TN.inc(i)
                        for dc in range(8):
                            i = tn.matmul(ps_tr[:, dc * 128:(dc + 1) * 128],
                                          xn_sb[:, dc * T + tt * 128: dc * T + tt * 128 + 128],
                                          eyeb[:], start=True, stop=True, is_transpose=True)
                            TN.inc(i)

                    @br.vector
                    def _(v):
                        def step(instr):
                            VC.inc(instr)
                            v.wait_ge(VC.h, VC.n)
                        v.wait_ge(TN.h, tn0r + 1)
                        step(v.tensor_copy(l_sb[:], sb["psS0"][:, 0:NE]))
                        step(v.tensor_reduce(out=m_sb[:, 0:1], in_=l_sb[:], axis=AX.X, op=ALU.max))
                        i = v.tensor_scalar(out=m_sb[:, 1:2], in0=m_sb[:, 0:1], scalar1=-1.0,
                                            scalar2=None, op0=ALU.mult)
                        VC.inc(i)

                    @br.scalar
                    def _(s):
                        VC.wait(s)
                        SC.inc(s.activation(out=et_sb[:], in_=l_sb[:], func=AF.Exp,
                                            bias=m_sb[:, 1:2], accum_out=m_sb[:, 2:3]))

                    @br.vector
                    def _(v):
                        def step(instr):
                            VC.inc(instr)
                            v.wait_ge(VC.h, VC.n)
                        SC.wait(v)
                        step(v.reciprocal(gs_sb[:, g_:g_ + 1], m_sb[:, 2:3]))
                        step(v.tensor_scalar(out=eq_sb[:, g_ * NE:(g_ + 1) * NE], in0=l_sb[:],
                                             scalar1=m_sb[:, 0:1], scalar2=None, op0=ALU.is_ge))
                        step(v.tensor_tensor(out=et_sb[:], in0=eq_sb[:, g_ * NE:(g_ + 1) * NE],
                                             in1=iota16[:], op=ALU.mult))
                        i = v.tensor_reduce(out=idx_sb[:, g_:g_ + 1], in_=et_sb[:], axis=AX.X,
                                            op=ALU.add)
                        VC.inc(i)

                    @br.tensor
                    def _(tn):
                        VC.wait(tn)
                        i = tn.matmul(sb["psS1"][0:1, 0:NE], onesf[:, 0:1],
                                      eq_sb[:, g_ * NE:(g_ + 1) * NE], start=True, stop=True)
                        TN.inc(i)
                        tn.matmul(sb["psS2"][:, 0:NE], lstf[:],
                                  eq_sb[:, g_ * NE:(g_ + 1) * NE], start=True, stop=False)
                        i = tn.matmul(sb["psS2"][:, 0:NE], onesf[0:1, :], carry[:],
                                      start=False, stop=True)
                        TN.inc(i)

                    @br.vector
                    def _(v):
                        def step(instr):
                            VC.inc(instr)
                            v.wait_ge(VC.h, VC.n)
                        TN.wait(v)
                        step(v.tensor_tensor(out=et_sb[:], in0=eq_sb[:, g_ * NE:(g_ + 1) * NE],
                                             in1=sb["psS2"][:, 0:NE], op=ALU.mult))
                        step(v.tensor_reduce(out=loc_sb[:, g_:g_ + 1], in_=et_sb[:], axis=AX.X,
                                             op=ALU.add))
                        i = v.tensor_tensor(out=carry[:], in0=carry[:], in1=sb["psS1"][0:1, 0:NE],
                                            op=ALU.add)
                        VC.inc(i)

                    @br.vector
                    def _(v):
                        v.wait_ge(TN.h, TN.n)
                        for dc in range(8):
                            i = v.tensor_copy(nat_sb[:, dc * 128:(dc + 1) * 128],
                                              ps_tr[:, dc * 128:(dc + 1) * 128])
                        VC.inc(i)

                    @br.sync
                    def _(sy):
                        VC.wait(sy)
                        LD.inc(sy.dma_start(out=bass.AP(xnnat_d.tensor, g_ * 128 * D,
                                                        [[D, 128], [1, D]]),
                                            in_=nat_sb[:]), 16)

        if part in ("all", "pre"):
            with nc.Block() as bc1:
                @bc1.sync
                def _(sy):
                    VC.wait(sy)
                    LD.inc(sy.dma_start(out=counts_o, in_=carry[:]), 16)
                    LD.inc(sy.dma_start(out=cnt_d[:, 0:NE], in_=carry[:]), 16)
                    LD.inc(sy.dma_start(out=idxo, in_=idx_sb[:]), 16)
                    LD.inc(sy.dma_start(out=loco, in_=loc_sb[:]), 16)
                    LD.inc(sy.dma_start(out=gso, in_=gs_sb[:]), 16)
                    sy.wait_ge(LD.h, LD.n)
                    LD.inc(sy.dma_start(out=l_sb[0:NE, 0:1], in_=bass.AP(cnt_d.tensor, 0, [[1, NE], [1, 1]])), 16)

            tnl = TN.n
            with nc.Block() as bc2:
                @bc2.tensor
                def _(tn):
                    LD.wait(tn)
                    i = tn.matmul(sb["psS3"][0:1, 0:NE], l_sb[0:NE, 0:1], lsu16[:],
                                  start=True, stop=True)
                    TN.inc(i)

                @bc2.vector
                def _(v):
                    v.wait_ge(TN.h, tnl + 1)
                    VC.inc(v.tensor_copy(et_sb[0:1, 0:NE], sb["psS3"][0:1, 0:NE]))

                @bc2.sync
                def _(sy):
                    VC.wait(sy)
                    LD.inc(sy.dma_start(out=cnt_d[:, NE:2 * NE], in_=et_sb[0:1, 0:NE]), 16)
                    sy.wait_ge(LD.h, LD.n)
                    LD.inc(sy.dma_start(out=lst_bc[:], in_=bass.AP(cnt_d.tensor, NE, [[0, 128], [1, NE]])), 16)

            for g_ in range(32):
                ld0s, vc0s, gp0s = LD.n, VC.n, GP.n
                with nc.Block() as bs:
                    @bs.sync
                    def _(sy):
                        sy.wait_ge(GP.h, gp0s)
                        LD.inc(sy.dma_start(out=nat_sb[:], in_=bass.AP(xnnat_d.tensor, g_ * 128 * D,
                                                                       [[D, 128], [1, D]])), 16)

                    @bs.vector
                    def _(v):
                        def step(instr):
                            VC.inc(instr)
                            v.wait_ge(VC.h, VC.n)
                        v.wait_ge(LD.h, ld0s + 16)
                        step(v.tensor_tensor(out=et_sb[:], in0=eq_sb[:, g_ * NE:(g_ + 1) * NE],
                                             in1=lst_bc[:], op=ALU.mult))
                        step(v.tensor_reduce(out=slot_sb[:, 0:1], in_=et_sb[:], axis=AX.X, op=ALU.add))
                        step(v.tensor_tensor(out=slot_sb[:, 1:2], in0=slot_sb[:, 0:1],
                                             in1=loc_sb[:, g_:g_ + 1], op=ALU.add))
                        i = v.tensor_copy(slot_u[:], slot_sb[:, 1:2])
                        VC.inc(i)

                    @bs.gpsimd
                    def _(g):
                        v_ = VC.n
                        g.wait_ge(VC.h, v_)
                        i = g.indirect_dma_start(
                            out=sorted_o, out_offset=IndirectOffsetOnAxis(slot_u[:, 0:1], axis=0),
                            in_=nat_sb[:], in_offset=None)
                        GP.inc(i, 16)


        for b in (range(B_LOC) if part in ("all", "post") else []):
            tnr, vcr, scr_ = TN.n, VC.n, SC.n
            with nc.Block() as brl:
                @brl.sync
                def _(sy):
                    sy.wait_ge(TN.h, tnr)
                    sy.wait_ge(VC.h, vcr)
                    sy.wait_ge(SC.h, scr_)
                    LD.inc(sy.dma_start(
                        out=xn_sb[:].rearrange("p (c t) -> p c t", c=8),
                        in_=xn1T_d.rearrange("(c p) t -> p c t", p=128)[:, :, b * T:(b + 1) * T]), 16)
            ld_xn = LD.n
            with nc.Block() as brw:
                @brw.tensor
                def _(tn):
                    tn.wait_ge(LD.h, ld_xn)

            def load_third(i3):
                tn_w = TN.n
                with nc.Block() as blw:
                    @blw.gpsimd
                    def _(g):
                        g.wait_ge(TN.h, tn_w)
                        GD.inc(g.dma_start(out=wbuf[:].rearrange("p (c f) -> p c f", c=8),
                                           in_=qkvw.rearrange("(c p) f -> p c f", p=128)[:, :, i3 * D:(i3 + 1) * D]), 16)
                return GD.n

            def proj_qk(gd_mark, dst, i3):
                for gg in range(4):
                    tn0_, sc0_, vc0_ = TN.n, SC.n, VC.n
                    with nc.Block() as bp:
                        @bp.tensor
                        def _(tn):
                            tn.wait_ge(GD.h, gd_mark)
                            tn.wait_ge(VC.h, vc0_)
                            for jj in range(8):
                                fc, nb_ = 2 * gg + jj // 4, jj % 4
                                if jj >= 2:
                                    tn.wait_ge(SC.h, sc0_ + jj - 1)
                                for dc in range(8):
                                    i = tn.matmul(sb[f"psS{jj % 2}"][:],
                                                  wbuf[:, dc * 1024 + fc * 128: dc * 1024 + fc * 128 + 128],
                                                  xn_sb[:, dc * T + nb_ * 512: dc * T + nb_ * 512 + 512],
                                                  start=(dc == 0), stop=(dc == 7))
                                TN.inc(i)

                        @bp.scalar
                        def _(s):
                            for jj in range(8):
                                fc, nb_ = 2 * gg + jj // 4, jj % 4
                                s.wait_ge(TN.h, tn0_ + jj + 1)
                                dstt = rope_e if (jj // 4) == 0 else rope_o
                                i = s.activation(out=dstt[:, nb_ * 512:(nb_ + 1) * 512],
                                                 in_=sb[f"psS{jj % 2}"][:], func=AF.Identity,
                                                 bias=qkvb_sb[:, i3 * 8 + fc: i3 * 8 + fc + 1])
                                SC.inc(i)

                        @bp.vector
                        def _(v):
                            SC.wait(v)
                            v.wait_ge(LD.h, ld_rope)
                            cE, cO = 2 * gg, 2 * gg + 1
                            v.tensor_tensor(out=rt1, in0=rope_e, in1=cc_sb, op=ALU.mult)
                            v.tensor_tensor(out=rt2, in0=rope_o, in1=ss_sb, op=ALU.mult)
                            v.tensor_tensor(out=dst[:, cE * T:(cE + 1) * T], in0=rt1, in1=rt2,
                                            op=ALU.subtract)
                            v.tensor_tensor(out=rt1, in0=rope_e, in1=ss_sb, op=ALU.mult)
                            v.tensor_tensor(out=rt2, in0=rope_o, in1=cc_sb, op=ALU.mult)
                            i = v.tensor_tensor(out=dst[:, cO * T:(cO + 1) * T], in0=rt1, in1=rt2,
                                                op=ALU.add)
                            VC.inc(i)

            def proj_v(gd_mark):
                tn0_, vc0_ = TN.n, VC.n
                with nc.Block() as bp:
                    @bp.tensor
                    def _(tn):
                        tn.wait_ge(GD.h, gd_mark)
                        tn.wait_ge(VC.h, vc0_)
                        tn.wait_ge(GD.h, GD.n)
                        for jj in range(32):
                            tt, fb = jj // 2, jj % 2
                            if jj >= 2:
                                tn.wait_ge(VC.h, vc0_ + jj - 1)
                            for dc in range(8):
                                tn.matmul(sb[f"psS{jj % 2}"][:],
                                          xn_sb[:, dc * T + tt * 128: dc * T + tt * 128 + 128],
                                          wbuf[:, dc * 1024 + fb * 512: dc * 1024 + fb * 512 + 512],
                                          start=(dc == 0), stop=False)
                            i = tn.matmul(sb[f"psS{jj % 2}"][:], onesb[0:1, 0:128],
                                          bvrow[0:1, fb * 512:(fb + 1) * 512],
                                          start=False, stop=True)
                            TN.inc(i)

                    @bp.vector
                    def _(v):
                        for jj in range(32):
                            tt, fb = jj // 2, jj % 2
                            v.wait_ge(TN.h, tn0_ + jj + 1)
                            i = v.tensor_copy(v_sb[:, tt * 1024 + fb * 512: tt * 1024 + (fb + 1) * 512],
                                              sb[f"psS{jj % 2}"][:])
                            VC.inc(i)

            tnw0, scw0, ldw0 = TN.n, SC.n, LD.n
            with nc.Block() as brt:
                @brt.sync
                def _(sy):
                    sy.wait_ge(TN.h, tnw0)
                    sy.wait_ge(SC.h, scw0)
                    sy.wait_ge(LD.h, ldw0)
                    LD.inc(sy.dma_start(out=cc_sb, in_=CCb), 16)
                    LD.inc(sy.dma_start(out=ss_sb, in_=SSb), 16)
            ld_rope = LD.n

            proj_qk(load_third(0), q_sb, 0)
            proj_qk(load_third(1), k_sb, 1)
            proj_v(load_third(2))
            gd_mo = None
            tnw_ = TN.n
            with nc.Block() as blw2:
                @blw2.gpsimd
                def _(g):
                    g.wait_ge(TN.h, tnw_)
                    GD.inc(g.dma_start(out=wbuf[:].rearrange("p (c f) -> p c f", c=8),
                                       in_=outw.rearrange("(c p) f -> p c f", p=128)), 16)
                gd_mo = GD.n
            with nc.Block() as bwait:
                @bwait.tensor
                def _(tn):
                    tn.wait_ge(GD.h, gd_mo)
            emit_attention_batch(nc, sems, sb, 1, b, s1, wo_bias_col=outb_sb)

        with nc.Block() as bf:
            @bf.sync
            def _(sy):
                sy.wait_ge(LD.h, LD.n)
                sy.wait_ge(GD.h, GD.n)
                sy.wait_ge(GP.h, GP.n)
    return nc


W2C_DECLS = [("w2c", [2, FF, D], F32, "ExternalInput"),
             ("w2bf_d", [2, FF, D], BF16, "Internal")]


def build_w2cast(ctx=None):
    nc, stk, sems, io = _phase_setup(ctx, "W_", W2C_DECLS)
    LD, GD = sems["LD"], sems["GD"]
    w2c, w2bf_d = io["w2c"], io["w2bf_d"]
    with stk:
        e = stk.enter_context
        w2b = [e(nc.sbuf_tensor(f"w2b{i}", [128, D], BF16)) for i in range(2)]
        gd0 = GD.n
        with nc.Block() as bw2:
            @bw2.gpsimd
            def _(g):
                for jj in range(64):
                    el, fc = jj // 32, jj % 32
                    if jj >= 2:
                        g.wait_ge(GD.h, gd0 + (2 * (jj - 2) + 2) * 16)
                    GD.inc(g.dma_start(out=w2b[jj % 2][:],
                                       in_=bass.AP(w2c.tensor, el * FF * D + fc * 128 * D,
                                                   [[D, 128], [1, D]])), 16)
                    g.wait_ge(GD.h, gd0 + (2 * jj + 1) * 16)
                    GD.inc(g.dma_start(out=bass.AP(w2bf_d.tensor, el * FF * D + fc * 128 * D,
                                                   [[D, 128], [1, D]]),
                                       in_=w2b[jj % 2][:]), 16)
    return nc


D_DECLS = [("sorted_all", [8 * TOK, D], BF16, "ExternalInput"),
           ("counts_all", [8, NE], F32, "ExternalInput"),
           ("ESEL", [NE, 2], F32, "ExternalInput"), ("SELM", [NE, 2], F32, "ExternalInput"),
           ("w1c", [2, D, FF], F32, "ExternalInput"), ("w2c", [2, FF, D], F32, "ExternalInput"),
           ("LI8", [8, 8], F32, "ExternalInput"), ("SLOTC", [128, NE], F32, "ExternalInput"),
           ("C4096", [128, 8], F32, "ExternalInput"), ("EYEB", [128, 128], BF16, "ExternalInput"),
           ("eo", [2 * CAP, D], BF16, "ExternalOutput"),
           ("w2bf_d", [2, FF, D], BF16, "Internal")]


def build_D(ctx=None, skip_w2cast=False, eo_cb=None):
    nc, stk, sems, io = _phase_setup(ctx, "D_", D_DECLS)
    LD, GD, TN, VC, SC, GP = (sems[n] for n in ("LD", "GD", "TN", "VC", "SC", "GP"))
    (sorted_all, counts_all, ESEL, SELM, w1c, w2c, LI8, SLOTC, C4096, EYEB, eo,
     w2bf_d) = (
        io[n] for n in ("sorted_all", "counts_all", "ESEL", "SELM", "w1c", "w2c",
                        "LI8", "SLOTC", "C4096", "EYEB", "eo", "w2bf_d"))
    tbl_d = nc.dram_tensor("tbl_d", [1, 64], F32, kind="Internal").ap()
    TB = 256

    with stk:
        e = stk.enter_context
        li8 = e(nc.sbuf_tensor("li8", [8, 8], F32))
        selm = e(nc.sbuf_tensor("selm", [NE, 2], F32))
        esel = e(nc.sbuf_tensor("esel", [NE, 2], F32))
        csel = e(nc.sbuf_tensor("csel", [8, 2], F32))
        slotc = e(nc.sbuf_tensor("slotc", [128, NE], F32))
        c4096 = e(nc.sbuf_tensor("c4096", [128, 8], F32))
        eyeb = e(nc.sbuf_tensor("eyeb", [128, 128], BF16))

        cntsT = e(nc.sbuf_tensor("cntsT", [NE, 8], F32))
        small = e(nc.sbuf_tensor("small", [8, 4], F32))
        cumt = e(nc.sbuf_tensor("cumt", [128, 8], F32))
        cumpt = e(nc.sbuf_tensor("cumpt", [128, 8], F32))
        lstt = e(nc.sbuf_tensor("lstt", [128, 8], F32))
        addt = e(nc.sbuf_tensor("addt", [128, 8], F32))
        mt1 = e(nc.sbuf_tensor("mt1", [128, NE], F32))
        mt2 = e(nc.sbuf_tensor("mt2", [128, NE], F32))
        acc = e(nc.sbuf_tensor("acc", [128, NE], F32))
        gidx_u = e(nc.sbuf_tensor("gidx_u", [128, NE], U32))
        w1_sb = e(nc.sbuf_tensor("w1_sb", [128, 8 * FF], BF16))
        w2b = [e(nc.sbuf_tensor(f"w2b{i}", [128, D], BF16)) for i in range(2)]
        gath = e(nc.sbuf_tensor("gath", [128, 2 * D], BF16))
        dispT = e(nc.sbuf_tensor("dispT", [128, 8 * TB], BF16))
        h_sb = e(nc.sbuf_tensor("h_sb", [128, 32 * TB], BF16))
        out_sb = e(nc.sbuf_tensor("out_sb", [128, 2 * D], BF16))
        psO = [e(nc.psum_tensor(f"psO{i}", [128, 512], F32)) for i in range(6)]
        ps_trb = e(nc.psum_tensor("ps_trb", [128, 1024], BF16))

        with nc.Block() as b0:
            @b0.sync
            def _(sy):
                for d, s_ in ((li8[:], LI8), (selm[:], SELM), (slotc[:], SLOTC),
                              (c4096[:], C4096), (eyeb[:], EYEB), (esel[:], ESEL)):
                    LD.inc(sy.dma_start(out=d, in_=s_), 16)
                LD.inc(sy.dma_start(out=cntsT[:], in_=bass.AP(counts_all.tensor, 0,
                                                              [[1, NE], [NE, 8]]),
                                    allow_slow_non_contiguous=True), 16)

            tn_d0 = TN.n
            @b0.tensor
            def _(tn):
                LD.wait(tn)
                i = tn.matmul(psO[2][0:8, 0:2], cntsT[:], esel[:], start=True, stop=True)
                TN.inc(i)

            @b0.vector
            def _(v):
                v.wait_ge(TN.h, tn_d0 + 1)
                VC.inc(v.tensor_copy(csel[:], psO[2][0:8, 0:2]))

            @b0.tensor
            def _(tn):
                VC.wait(tn)
                i = tn.matmul(psO[0][0:8, 0:2], li8[:], csel[:], start=True, stop=True)
                TN.inc(i)
                i = tn.matmul(psO[1][0:8, 0:2], cntsT[:], selm[:], start=True, stop=True)
                TN.inc(i)

            @b0.vector
            def _(v):
                TN.wait(v)
                v.tensor_copy(small[:, 0:2], psO[0][0:8, 0:2])
                i = v.tensor_copy(small[:, 2:4], psO[1][0:8, 0:2])
                VC.inc(i)

            @b0.sync
            def _(sy):
                VC.wait(sy)
                LD.inc(sy.dma_start(out=bass.AP(tbl_d.tensor, 0, [[2, 8], [1, 2]]),
                                    in_=small[:, 0:2]), 16)
                LD.inc(sy.dma_start(out=bass.AP(tbl_d.tensor, 16, [[2, 8], [1, 2]]),
                                    in_=small[:, 2:4]), 16)
                sy.wait_ge(LD.h, LD.n)

        if not skip_w2cast:
            gd0 = GD.n
            with nc.Block() as bw2:
                @bw2.gpsimd
                def _(g):
                    for jj in range(64):
                        el, fc = jj // 32, jj % 32
                        if jj >= 2:
                            g.wait_ge(GD.h, gd0 + (2 * (jj - 2) + 2) * 16)
                        GD.inc(g.dma_start(out=w2b[jj % 2][:],
                                           in_=bass.AP(w2c.tensor, el * FF * D + fc * 128 * D,
                                                       [[D, 128], [1, D]])), 16)
                        g.wait_ge(GD.h, gd0 + (2 * jj + 1) * 16)
                        GD.inc(g.dma_start(out=bass.AP(w2bf_d.tensor, el * FF * D + fc * 128 * D,
                                                       [[D, 128], [1, D]]),
                                           in_=w2b[jj % 2][:]), 16)

        for el in range(2):
            tnw = TN.n
            with nc.Block() as bw1:
                @bw1.gpsimd
                def _(g):
                    g.wait_ge(TN.h, tnw)
                    g.wait_ge(LD.h, LD.n)
                    for dc in range(8):
                        GD.inc(g.dma_start(out=w1_sb[:, dc * FF:(dc + 1) * FF],
                                           in_=bass.AP(w1c.tensor, el * D * FF + dc * 128 * FF,
                                                       [[FF, 128], [1, FF]])), 16)
            gd_w1 = GD.n

            ld1, vc1, gp1 = LD.n, VC.n, GP.n
            with nc.Block() as bgi:
                @bgi.sync
                def _(sy):
                    sy.wait_ge(VC.h, vc1)
                    sy.wait_ge(GP.h, gp1)
                    LD.inc(sy.dma_start(out=cumt[:], in_=bass.AP(tbl_d.tensor, el, [[0, 128], [2, 8]]), allow_slow_non_contiguous=True), 16)
                    LD.inc(sy.dma_start(out=cumpt[:, 1:8], in_=bass.AP(tbl_d.tensor, el, [[0, 128], [2, 7]]), allow_slow_non_contiguous=True), 16)
                    LD.inc(sy.dma_start(out=lstt[:], in_=bass.AP(tbl_d.tensor, 16 + el, [[0, 128], [2, 8]]), allow_slow_non_contiguous=True), 16)

                @bgi.vector
                def _(v):
                    def step(instr):
                        VC.inc(instr)
                        v.wait_ge(VC.h, VC.n)
                    v.wait_ge(LD.h, ld1 + 48)
                    step(v.memset(cumpt[:, 0:1], 0.0))
                    step(v.tensor_tensor(out=addt[:], in0=lstt[:], in1=cumpt[:], op=ALU.subtract))
                    step(v.tensor_tensor(out=addt[:], in0=addt[:], in1=c4096[:], op=ALU.add))
                    step(v.memset(acc[:], 0.0))
                    for cc_ in range(8):
                        step(v.tensor_scalar(out=mt1[:], in0=slotc[:], scalar1=cumpt[:, cc_:cc_ + 1],
                                             scalar2=None, op0=ALU.is_ge))
                        step(v.tensor_scalar(out=mt2[:], in0=slotc[:], scalar1=cumt[:, cc_:cc_ + 1],
                                             scalar2=None, op0=ALU.is_lt))
                        step(v.tensor_tensor(out=mt1[:], in0=mt1[:], in1=mt2[:], op=ALU.mult))
                        step(v.tensor_scalar(out=mt2[:], in0=slotc[:], scalar1=addt[:, cc_:cc_ + 1],
                                             scalar2=None, op0=ALU.add))
                        step(v.tensor_tensor(out=mt1[:], in0=mt1[:], in1=mt2[:], op=ALU.mult))
                        step(v.tensor_tensor(out=acc[:], in0=acc[:], in1=mt1[:], op=ALU.add))
                    step(v.tensor_copy(gidx_u[:], acc[:]))
            vc_gi = VC.n

            for tb in range(8):
                gp2, tn2, vc2, sc2, ld2 = GP.n, TN.n, VC.n, SC.n, LD.n
                with nc.Block() as bg:
                    @bg.gpsimd
                    def _(g):
                        g.wait_ge(VC.h, vc_gi)
                        g.wait_ge(TN.h, tn2)
                        for i4 in range(2):
                            f_ = tb * 2 + i4
                            GP.inc(g.indirect_dma_start(
                                out=gath[:, i4 * D:(i4 + 1) * D], out_offset=None,
                                in_=sorted_all,
                                in_offset=IndirectOffsetOnAxis(gidx_u[:, f_:f_ + 1], axis=0)), 16)

                with nc.Block() as bt:
                    @bt.tensor
                    def _(tn):
                        tn.wait_ge(GP.h, gp2 + 32)
                        tn.wait_ge(VC.h, vc2)
                        for i4 in range(2):
                            if i4 == 1:
                                tn.wait_ge(VC.h, vc2 + 1)
                            for dc in range(8):
                                i = tn.matmul(ps_trb[:, dc * 128:(dc + 1) * 128],
                                              gath[:, i4 * D + dc * 128: i4 * D + dc * 128 + 128],
                                              eyeb[:], start=True, stop=True, is_transpose=True)
                            TN.inc(i)

                    @bt.vector
                    def _(v):
                        for i4 in range(2):
                            v.wait_ge(TN.h, tn2 + i4 + 1)
                            for dc in range(8):
                                i = v.tensor_copy(dispT[:, dc * TB + i4 * 128: dc * TB + (i4 + 1) * 128],
                                                  ps_trb[:, dc * 128:(dc + 1) * 128])
                            VC.inc(i)

                tn3, sc3 = TN.n, SC.n
                with nc.Block() as bh:
                    @bh.tensor
                    def _(tn):
                        VC.wait(tn)
                        tn.wait_ge(GD.h, gd_w1)
                        for fc in range(32):
                            if fc >= 2:
                                tn.wait_ge(SC.h, sc3 + fc - 1)
                            for dc in range(8):
                                i = tn.matmul(psO[4 + (fc % 2)][:, 0:TB],
                                              w1_sb[:, dc * FF + fc * 128: dc * FF + fc * 128 + 128],
                                              dispT[:, dc * TB:(dc + 1) * TB],
                                              start=(dc == 0), stop=(dc == 7))
                            TN.inc(i)

                    @bh.scalar
                    def _(s):
                        for fc in range(32):
                            s.wait_ge(TN.h, tn3 + fc + 1)
                            i = s.activation(out=h_sb[:, fc * TB:(fc + 1) * TB],
                                             in_=psO[4 + (fc % 2)][:, 0:TB], func=AF.Relu)
                            SC.inc(i)

                tn4, sc4, vc4, ld4 = TN.n, SC.n, VC.n, LD.n
                with nc.Block() as bo:
                    @bo.sync
                    def _(sy):
                        sy.wait_ge(VC.h, vc4)
                        for fc in range(32):
                            if fc >= 2:
                                sy.wait_ge(TN.h, tn4 + fc - 1)
                            LD.inc(sy.dma_start(out=w2b[fc % 2][:],
                                                in_=bass.AP(w2bf_d.tensor, el * FF * D + fc * 128 * D,
                                                            [[D, 128], [1, D]])), 16)

                    @bo.tensor
                    def _(tn):
                        for fc in range(32):
                            tn.wait_ge(LD.h, ld4 + (fc + 1) * 16)
                            tn.wait_ge(SC.h, sc4 - 32 + fc + 1)
                            for tt in range(2):
                                for db in range(2):
                                    i = tn.matmul(psO[tt * 2 + db][:],
                                                  h_sb[:, fc * TB + tt * 128: fc * TB + (tt + 1) * 128]
                                                  if tt == 0 else
                                                  h_sb[:, fc * TB + 128: fc * TB + 256],
                                                  w2b[fc % 2][:, db * 512:(db + 1) * 512],
                                                  start=(fc == 0), stop=(fc == 31))
                            TN.inc(i)

                    @bo.vector
                    def _(v):
                        v.wait_ge(TN.h, TN.n)
                        for tt in range(2):
                            for db in range(2):
                                i = v.tensor_copy(out_sb[:, tt * D + db * 512: tt * D + (db + 1) * 512],
                                                  psO[tt * 2 + db][:])
                        VC.inc(i)

                    @bo.sync
                    def _(sy):
                        VC.wait(sy)
                        LD.inc(sy.dma_start(
                            out=bass.AP(eo.tensor, el * CAP * D + tb * TB * D,
                                        [[D, 128], [128 * D, 2], [1, D]]),
                            in_=out_sb[:].rearrange("p (t d) -> p t d", t=2)), 16)

        with nc.Block() as bf:
            @bf.sync
            def _(sy):
                sy.wait_ge(LD.h, LD.n)
                sy.wait_ge(GD.h, GD.n)
                sy.wait_ge(GP.h, GP.n)
    return nc


E_DECLS = [("x1", [D, TOK], F32, "ExternalInput"), ("s1", [D, TOK], BF16, "ExternalInput"),
           ("eo_all", [NE * CAP, D], BF16, "ExternalInput"),
           ("idxo", [128, 32], F32, "ExternalInput"), ("loco", [128, 32], F32, "ExternalInput"),
           ("gso", [128, 32], F32, "ExternalInput"), ("EYEB", [128, 128], BF16, "ExternalInput"),
           ("xout", [D, TOK], F32, "ExternalOutput")]


def build_E(ctx=None):
    nc, stk, sems, io = _phase_setup(ctx, "E_", E_DECLS)
    LD, GD, TN, VC, SC, GP = (sems[n] for n in ("LD", "GD", "TN", "VC", "SC", "GP"))
    (x1, s1, eo_all, idxo, loco, gso, EYEB, xout) = (
        io[n] for n in ("x1", "s1", "eo_all", "idxo", "loco", "gso", "EYEB", "xout"))

    with stk:
        e = stk.enter_context
        eyeb = e(nc.sbuf_tensor("eyeb", [128, 128], BF16))
        idx_sb = e(nc.sbuf_tensor("idx_sb", [128, 32], F32))
        loc_sb = e(nc.sbuf_tensor("loc_sb", [128, 32], F32))
        gs_sb = e(nc.sbuf_tensor("gs_sb", [128, 32], F32))
        tmp = e(nc.sbuf_tensor("tmp", [128, 8], F32))
        gi_u = e(nc.sbuf_tensor("gi_u", [128, 1], U32))
        moe = e(nc.sbuf_tensor("moe", [128, D], BF16))
        gnat = e(nc.sbuf_tensor("gnat", [128, D], BF16))
        gT = e(nc.sbuf_tensor("gT", [128, D], BF16))
        s1t = e(nc.sbuf_tensor("s1t", [128, D], BF16))
        x1t = e(nc.sbuf_tensor("x1t", [128, D], F32))
        t2 = e(nc.sbuf_tensor("t2", [128, D], BF16))
        outt = e(nc.sbuf_tensor("outt", [128, D], F32))
        ps_trb = e(nc.psum_tensor("ps_trb", [128, 1024], BF16))

        x1_r = x1.rearrange("(c p) t -> p c t", p=128)
        s1_r = s1.rearrange("(c p) t -> p c t", p=128)
        xo_r = xout.rearrange("(c p) t -> p c t", p=128)

        with nc.Block() as b0:
            @b0.sync
            def _(sy):
                for d, s_ in ((eyeb[:], EYEB), (idx_sb[:], idxo), (loc_sb[:], loco),
                              (gs_sb[:], gso)):
                    LD.inc(sy.dma_start(out=d, in_=s_), 16)

        for g_ in range(32):
            ld0, vc0, gp0, tn0, sc0 = LD.n, VC.n, GP.n, TN.n, SC.n
            with nc.Block() as bt:
                @bt.vector
                def _(v):
                    def step(instr):
                        VC.inc(instr)
                        v.wait_ge(VC.h, VC.n)
                    if g_ == 0:
                        v.wait_ge(LD.h, ld0)
                    else:
                        v.wait_ge(LD.h, ld0)
                        v.wait_ge(GP.h, gp0)
                    step(v.tensor_scalar(out=tmp[:, 0:1], in0=loc_sb[:, g_:g_ + 1],
                                         scalar1=float(CAP - 1), scalar2=None, op0=ALU.min))
                    step(v.scalar_tensor_tensor(out=tmp[:, 1:2], in0=idx_sb[:, g_:g_ + 1],
                                                scalar=float(CAP), in1=tmp[:, 0:1],
                                                op0=ALU.mult, op1=ALU.add))
                    step(v.tensor_copy(gi_u[:], tmp[:, 1:2]))
                    step(v.tensor_scalar(out=tmp[:, 2:3], in0=loc_sb[:, g_:g_ + 1],
                                         scalar1=float(CAP), scalar2=None, op0=ALU.is_lt))
                    step(v.tensor_tensor(out=tmp[:, 3:4], in0=gs_sb[:, g_:g_ + 1],
                                         in1=tmp[:, 2:3], op=ALU.mult))

                @bt.gpsimd
                def _(g):
                    VC.wait(g)
                    GP.inc(g.indirect_dma_start(
                        out=moe[:], out_offset=None, in_=eo_all,
                        in_offset=IndirectOffsetOnAxis(gi_u[:, 0:1], axis=0)), 16)

                @bt.vector
                def _(v):
                    GP.wait(v)
                    if g_ > 0:
                        v.wait_ge(SC.h, sc0)
                    i = v.tensor_scalar(out=gnat[:], in0=moe[:], scalar1=tmp[:, 3:4],
                                        scalar2=None, op0=ALU.mult)
                    VC.inc(i)

                @bt.scalar
                def _(s):
                    VC.wait(s)
                    SC.inc(s.activation(out=gnat[:], in_=gnat[:], func=AF.Silu))

                @bt.tensor
                def _(tn):
                    SC.wait(tn)
                    if g_ > 0:
                        tn.wait_ge(VC.h, vc0)
                    for dc in range(8):
                        i = tn.matmul(ps_trb[:, dc * 128:(dc + 1) * 128],
                                      gnat[:, dc * 128:(dc + 1) * 128], eyeb[:],
                                      start=True, stop=True, is_transpose=True)
                    TN.inc(i)

                @bt.sync
                def _(sy):
                    sy.wait_ge(VC.h, vc0)
                    LD.inc(sy.dma_start(out=s1t[:].rearrange("p (c t) -> p c t", c=8),
                                        in_=s1_r[:, :, g_ * 128:(g_ + 1) * 128]), 16)
                    LD.inc(sy.dma_start(out=x1t[:].rearrange("p (c t) -> p c t", c=8),
                                        in_=x1_r[:, :, g_ * 128:(g_ + 1) * 128]), 16)

                @bt.vector
                def _(v):
                    v.wait_ge(TN.h, TN.n)
                    for dc in range(8):
                        i = v.tensor_copy(gT[:, dc * 128:(dc + 1) * 128],
                                          ps_trb[:, dc * 128:(dc + 1) * 128])
                    VC.inc(i)
                    v.wait_ge(LD.h, ld0 + (32 if g_ == 0 else 32))
                    v.wait_ge(LD.h, LD.n)
                    v.tensor_tensor(out=t2[:], in0=s1t[:], in1=gT[:], op=ALU.mult)
                    i = v.tensor_tensor(out=outt[:], in0=t2[:], in1=x1t[:], op=ALU.add)
                    VC.inc(i)

                @bt.sync
                def _(sy):
                    VC.wait(sy)
                    LD.inc(sy.dma_start(out=xo_r[:, :, g_ * 128:(g_ + 1) * 128],
                                        in_=outt[:].rearrange("p (c t) -> p c t", c=8)), 16)

        with nc.Block() as bf:
            @bf.sync
            def _(sy):
                sy.wait_ge(LD.h, LD.n)
                sy.wait_ge(GP.h, GP.n)
    return nc


_PROGS = {}


def _prog(name, builder):
    if name not in _PROGS:
        _PROGS[name] = builder()
    return _PROGS[name]


def _qkv_perm():
    perm = np.zeros(3072, np.int64)
    for h in range(H):
        gg, s = h // 4, h % 4
        for d_ in range(DH):
            fold = h * DH + d_
            if d_ % 2 == 0:
                new = (2 * gg) * 128 + s * 32 + d_ // 2
            else:
                new = (2 * gg + 1) * 128 + s * 32 + (d_ - 1) // 2
            perm[new] = fold
            perm[1024 + new] = 1024 + fold
    perm[2048:] = np.arange(2048, 3072)
    return perm


def kernel_unfused(**inputs):
    f = np.float32
    cst = host_consts()
    x = np.ascontiguousarray(np.asarray(inputs["L0_attn_x"], f))
    onesf = np.ones((128, 128), f)
    lstf = (np.arange(128)[:, None] < np.arange(128)[None, :]).astype(f)
    zm4 = np.ascontiguousarray(np.concatenate(list(cst["ZM"]), axis=1))
    eyeb = np.eye(128).astype(BF)
    perm = _qkv_perm()

    xT = [np.ascontiguousarray(x[2 * c:2 * c + 2].reshape(TOK, D).T) for c in range(NCORES)]

    wq = np.ascontiguousarray(np.asarray(inputs["L0_attn_Wq"], f))
    wk = np.ascontiguousarray(np.asarray(inputs["L0_attn_Wk"], f))
    wv = np.ascontiguousarray(np.asarray(inputs["L0_attn_Wv"], f))
    wo = np.ascontiguousarray(np.asarray(inputs["L0_attn_Wo"], f))
    rel = np.ascontiguousarray(np.asarray(inputs["L0_attn_rel_emb"], f))
    resA = run_bass_kernel_spmd(
        _prog("A", build_A),
        [dict(xT=xT[c], wq=wq, wk=wk, wv=wv, wo=wo, relemb=rel,
              OH=cst["OH"], ONESB=cst["ONESB"]) for c in range(NCORES)],
        core_ids=list(range(NCORES)))

    wfcT = np.ascontiguousarray(np.asarray(inputs["L0_ffn_W_fc"], f).T)
    wprojT = np.ascontiguousarray(np.asarray(inputs["L0_ffn_W_proj"], f).T)
    lnw = np.ascontiguousarray(np.asarray(inputs["L0_ffn_ln_w"], f).reshape(32, 128).T)
    lnb = np.ascontiguousarray(np.asarray(inputs["L0_ffn_ln_b"], f).reshape(32, 128).T)
    resB = run_bass_kernel_spmd(
        _prog("B", build_B),
        [dict(xT=xT[c], xn0=np.asarray(resA.results[c]["xn0"]),
              s0=np.asarray(resA.results[c]["s0"]),
              wfcT=wfcT, wprojT=wprojT, lnw=lnw, lnb=lnb,
              ONESB=cst["ONESB"], ONESF=onesf) for c in range(NCORES)],
        core_ids=list(range(NCORES)))
    x1 = [np.asarray(resB.results[c]["x1"]) for c in range(NCORES)]

    qkvw_p = np.ascontiguousarray(np.asarray(inputs["L1_attn_qkv_w"], f)[:, perm])
    qkvb_p = np.ascontiguousarray(np.asarray(inputs["L1_attn_qkv_b"], f)[perm].reshape(24, 128).T)
    bv = np.ascontiguousarray(np.asarray(inputs["L1_attn_qkv_b"], f)[2048:3072][None, :])
    outw = np.ascontiguousarray(np.asarray(inputs["L1_attn_out_w"], f))
    outb = np.ascontiguousarray(np.asarray(inputs["L1_attn_out_b"], f).reshape(8, 128).T)
    wg = np.ascontiguousarray(np.asarray(inputs["L1_ffn_Wg"], f))
    resC = run_bass_kernel_spmd(
        _prog("C", build_C),
        [dict(x1=x1[c], qkvw=qkvw_p, qkvb=qkvb_p, outw=outw, outb=outb, wg=wg,
              CCb=cst["CC"].astype(BF), SSb=cst["SS"].astype(BF), ZM4=zm4.astype(BF), BV=bv,
              ONESB=cst["ONESB"], ONESF=onesf, LSTF=lstf,
              IOTA16=cst["IOTA16"], LSU16F=cst["LSU16"], EYEB=eyeb) for c in range(NCORES)],
        core_ids=list(range(NCORES)))

    sorted_all = np.ascontiguousarray(np.concatenate(
        [np.asarray(resC.results[c]["sorted_o"]) for c in range(NCORES)], 0))
    counts_all = np.ascontiguousarray(np.concatenate(
        [np.asarray(resC.results[c]["counts_o"]) for c in range(NCORES)], 0))
    w1 = np.asarray(inputs["L1_ffn_w1"], f)
    w2 = np.asarray(inputs["L1_ffn_w2"], f)
    dmaps = []
    for c in range(NCORES):
        selm = (np.arange(NE)[:, None] < (2 * c + np.arange(2))[None, :]).astype(f)
        esel = (np.arange(NE)[:, None] == (2 * c + np.arange(2))[None, :]).astype(f)
        dmaps.append(dict(
            sorted_all=sorted_all, counts_all=counts_all,
            ESEL=np.ascontiguousarray(esel),
            SELM=np.ascontiguousarray(selm),
            w1c=np.ascontiguousarray(w1[2 * c:2 * c + 2]),
            w2c=np.ascontiguousarray(w2[2 * c:2 * c + 2]),
            LI8=cst["LI8"], SLOTC=cst["SLOT"],
            C4096=(4096.0 * np.arange(8, dtype=f))[None, :].repeat(128, 0),
            EYEB=eyeb))
    resD = run_bass_kernel_spmd(_prog("D", build_D), dmaps, core_ids=list(range(NCORES)))

    eo_all = np.ascontiguousarray(np.concatenate(
        [np.asarray(resD.results[c]["eo"]) for c in range(NCORES)], 0))
    resE = run_bass_kernel_spmd(
        _prog("E", build_E),
        [dict(x1=x1[c], s1=np.asarray(resC.results[c]["s1"]), eo_all=eo_all,
              idxo=np.asarray(resC.results[c]["idxo"]),
              loco=np.asarray(resC.results[c]["loco"]),
              gso=np.asarray(resC.results[c]["gso"]), EYEB=eyeb) for c in range(NCORES)],
        core_ids=list(range(NCORES)))

    out = np.concatenate(
        [np.asarray(resE.results[c]["xout"], f).T.reshape(B_LOC, T, D) for c in range(NCORES)], 0)
    return out



def emit_allgather_issue(nc, sems, cc, in_ap, out_ap, waits=()):
    with nc.Block() as b:
        @b.gpsimd
        def _(g):
            for sem, val in waits:
                g.wait_ge(sem.h, val)
            i = g.collective_compute("AllGather", ALU.bypass,
                                     replica_groups=[list(range(NCORES))],
                                     ins=[in_ap.opt()], outs=[out_ap.opt()])
            cc.inc(i, 1)


def emit_cc_wait(nc, ccs):
    with nc.Block() as b2:
        for en in ("tensor", "vector", "scalar", "gpsimd", "sync"):
            def _mk(e, _ccs=tuple((c, c.n) for c in ccs)):
                for c, m in _ccs:
                    e.wait_ge(c.h, m)
            getattr(b2, en)(_mk)


def emit_allgather(nc, sems, cc, in_ap, out_ap):
    emit_allgather_issue(nc, sems, cc, in_ap, out_ap)
    emit_cc_wait(nc, [cc])


def build_fused():
    nc, stk, sems = new_prog()
    reg = {}
    ctx = (nc, sems, reg)
    with stk:
        cc = [Sem(stk, nc, f"CC{i}") for i in range(4)]
        build_A(ctx=ctx)
        emit_fence(nc, sems)
        build_w2cast(ctx=ctx)
        build_B(ctx=ctx)
        emit_fence(nc, sems)
        build_C(ctx=ctx, part="pre")
        emit_fence(nc, sems)
        reg["sorted_all"] = nc.dram_tensor("sorted_all", [8 * TOK, D], BF16, kind="Internal").ap()
        reg["counts_all"] = nc.dram_tensor("counts_all", [8, NE], F32, kind="Internal").ap()
        emit_allgather_issue(nc, sems, cc[0], reg["sorted_o"], reg["sorted_all"])
        emit_allgather_issue(nc, sems, cc[1], reg["counts_o"], reg["counts_all"])
        build_C(ctx=ctx, part="post")
        emit_fence(nc, sems)
        emit_cc_wait(nc, [cc[0], cc[1]])
        reg["eo_all"] = nc.dram_tensor("eo_all", [NE * CAP, D], BF16, kind="Internal").ap()

        build_D(ctx=ctx, skip_w2cast=True)
        emit_fence(nc, sems)
        emit_allgather(nc, sems, cc[2], reg["eo"], reg["eo_all"])
        build_E(ctx=ctx)
        emit_fence(nc, sems)
    return nc


def _host_inmaps(inputs):
    f = np.float32
    cst = host_consts()
    x = np.ascontiguousarray(np.asarray(inputs["L0_attn_x"], f))
    onesf = np.ones((128, 128), f)
    lstf = (np.arange(128)[:, None] < np.arange(128)[None, :]).astype(f)
    zm4 = np.ascontiguousarray(np.concatenate(list(cst["ZM"]), axis=1))
    eyeb = np.eye(128).astype(BF)
    perm = _qkv_perm()
    w1 = np.asarray(inputs["L1_ffn_w1"], f)
    w2 = np.asarray(inputs["L1_ffn_w2"], f)
    shared = dict(
        wq=np.ascontiguousarray(np.asarray(inputs["L0_attn_Wq"], f)),
        wk=np.ascontiguousarray(np.asarray(inputs["L0_attn_Wk"], f)),
        wv=np.ascontiguousarray(np.asarray(inputs["L0_attn_Wv"], f)),
        wo=np.ascontiguousarray(np.asarray(inputs["L0_attn_Wo"], f)),
        relemb=np.ascontiguousarray(np.asarray(inputs["L0_attn_rel_emb"], f)),
        OH=cst["OH"], ONESB=cst["ONESB"], ONESF=onesf,
        wfcT=np.ascontiguousarray(np.asarray(inputs["L0_ffn_W_fc"], f).T),
        wprojT=np.ascontiguousarray(np.asarray(inputs["L0_ffn_W_proj"], f).T),
        lnw=np.ascontiguousarray(np.asarray(inputs["L0_ffn_ln_w"], f).reshape(32, 128).T),
        lnb=np.ascontiguousarray(np.asarray(inputs["L0_ffn_ln_b"], f).reshape(32, 128).T),
        qkvw=np.ascontiguousarray(np.asarray(inputs["L1_attn_qkv_w"], f)[:, perm]),
        qkvb=np.ascontiguousarray(np.asarray(inputs["L1_attn_qkv_b"], f)[perm].reshape(24, 128).T),
        BV=np.ascontiguousarray(np.asarray(inputs["L1_attn_qkv_b"], f)[2048:3072][None, :]),
        outw=np.ascontiguousarray(np.asarray(inputs["L1_attn_out_w"], f)),
        outb=np.ascontiguousarray(np.asarray(inputs["L1_attn_out_b"], f).reshape(8, 128).T),
        wg=np.ascontiguousarray(np.asarray(inputs["L1_ffn_Wg"], f)),
        CCb=cst["CC"].astype(BF), SSb=cst["SS"].astype(BF), ZM4=zm4.astype(BF),
        LSTF=lstf, IOTA16=cst["IOTA16"], LSU16F=cst["LSU16"], EYEB=eyeb,
        LI8=cst["LI8"], SLOTC=cst["SLOT"],
        C4096=(4096.0 * np.arange(8, dtype=f))[None, :].repeat(128, 0),
    )
    maps = []
    for c in range(NCORES):
        m = dict(shared)
        m["xT"] = np.ascontiguousarray(x[2 * c:2 * c + 2].reshape(TOK, D).T)
        m["SELM"] = np.ascontiguousarray(
            (np.arange(NE)[:, None] < (2 * c + np.arange(2))[None, :]).astype(f))
        m["ESEL"] = np.ascontiguousarray(
            (np.arange(NE)[:, None] == (2 * c + np.arange(2))[None, :]).astype(f))
        m["w1c"] = np.ascontiguousarray(w1[2 * c:2 * c + 2])
        m["w2c"] = np.ascontiguousarray(w2[2 * c:2 * c + 2])
        maps.append(m)
    return maps


def kernel(**inputs):
    f = np.float32
    maps = _host_inmaps(inputs)
    res = run_bass_kernel_spmd(_prog("F", build_fused), maps, core_ids=list(range(NCORES)))
    out = np.concatenate(
        [np.asarray(res.results[c]["xout"], f).T.reshape(B_LOC, T, D) for c in range(NCORES)], 0)
    return out


# version 06
# speedup vs baseline: 1.0846x; 1.0005x over previous
import math
from contextlib import ExitStack

import numpy as np
import ml_dtypes

import concourse.bass as bass
import concourse.mybir as mybir
from concourse.bass import IndirectOffsetOnAxis
from concourse.bass_utils import run_bass_kernel_spmd

DT = mybir.dt
F32 = DT.float32
BF16 = DT.bfloat16
U32 = DT.uint32
AF = mybir.ActivationFunctionType
ALU = mybir.AluOpType
AX = mybir.AxisListType
BF = ml_dtypes.bfloat16

NCORES = 8
B_LOC = 2
T = 2048
TOK = B_LOC * T
D = 1024
H = 16
DH = 64
NB = 32
FF = 4096
NE = 16
CAP = 2048
BLEN = 2 * T - 1
BPAD = 4608


class Sem:
    def __init__(self, stack, nc, name):
        self.h = stack.enter_context(nc.semaphore(name))
        self.n = 0

    def inc(self, instr, k=1):
        instr.then_inc(self.h, k)
        self.n += k
        return self.n

    def wait(self, eng, n=None):
        eng.wait_ge(self.h, self.n if n is None else n)


def new_prog():
    nc = bass.Bass("TRN2", target_bir_lowering=False)
    stk = ExitStack()
    sems = {n: Sem(stk, nc, n) for n in ("LD", "GD", "TN", "VC", "SC", "GP")}
    return nc, stk, sems


class _NCProxy:

    def __init__(self, nc, prefix):
        self._nc = nc
        self._p = prefix

    def sbuf_tensor(self, name, *a, **k):
        return self._nc.sbuf_tensor(self._p + name, *a, **k)

    def psum_tensor(self, name, *a, **k):
        return self._nc.psum_tensor(self._p + name, *a, **k)

    def __getattr__(self, a):
        return getattr(self._nc, a)


def _phase_setup(ctx, prefix, decls):
    if ctx is None:
        nc, stk, sems = new_prog()
        reg = {}
    else:
        nc0, sems, reg = ctx
        nc = _NCProxy(nc0, prefix)
        stk = ExitStack()
    io = {}
    for name, shape, dt_, kind in decls:
        if name in reg:
            io[name] = reg[name]
        else:
            k = kind
            if ctx is not None and kind == "ExternalOutput" and name != "xout":
                k = "Internal"
            ap = nc.dram_tensor(name, shape, dt_, kind=k).ap()
            reg[name] = ap
            io[name] = ap
    return nc, stk, sems, io


def emit_fence(nc, sems):
    marks = {k: s.n for k, s in sems.items()}
    with nc.Block() as bf:
        for en in ("tensor", "vector", "scalar", "gpsimd", "sync"):
            def _mk(e, _m=marks):
                for k, s in sems.items():
                    e.wait_ge(s.h, _m[k])
            getattr(bf, en)(_mk)


def _bucket_of(d):
    nb = NB // 2
    n = -d
    ret = (n < 0).astype(np.int32) * nb
    n = np.abs(n)
    max_exact = nb // 2
    is_small = n < max_exact
    n_safe = np.maximum(n, 1).astype(np.float32)
    val = max_exact + (np.log(n_safe / max_exact) / math.log(128 / max_exact)
                       * (nb - max_exact)).astype(np.int32)
    val = np.minimum(val, nb - 1)
    return ret + np.where(is_small, n, val)


def host_consts():
    c = {}
    j = np.arange(BLEN)
    bk = _bucket_of((j - (T - 1)).astype(np.int64))
    oh = np.zeros((NB, BPAD), np.float32)
    oh[bk, j] = 1.0
    c["OH"] = oh.astype(BF)
    c["ONESB"] = np.ones((128, 512), BF)
    zm = np.zeros((4, 128, 512), np.float32)
    ki = np.arange(128)[:, None]
    qi = np.arange(512)[None, :]
    for jj in range(4):
        zm[jj] = (qi >= ki + jj * 128)
    c["ZM"] = zm.astype(BF)
    hd = DH
    inv_freq = (1.0 / 10000.0 ** (np.arange(0, hd, 2, dtype=np.float32) / hd))
    t = np.arange(T, dtype=np.float32)
    freqs = t[:, None] * inv_freq[None, :]
    rope = np.concatenate([np.sin(freqs), np.cos(freqs)], -1).astype(np.float32)
    sinv = rope[:, 0::2]
    cosv = rope[:, 1::2]
    rows = np.tile(np.arange(32), 4)
    c["CC"] = np.ascontiguousarray(cosv[:, rows].T)
    c["SS"] = np.ascontiguousarray(sinv[:, rows].T)
    c["LSTRICT"] = (np.arange(128)[:, None] < np.arange(128)[None, :]).astype(BF)
    c["IOTA16"] = np.tile(np.arange(NE, dtype=np.float32), (128, 1))
    c["LSU16"] = (np.arange(NE)[:, None] < np.arange(NE)[None, :]).astype(np.float32)
    c["LI8"] = (np.arange(8)[:, None] <= np.arange(8)[None, :]).astype(np.float32)
    c["SLOT"] = (np.arange(128)[:, None] + 128.0 * np.arange(16)[None, :]).astype(np.float32)
    return c


S_TT = 0
S_PRAW = 6144
S_PT = 8192
S_RCP = 10240
S_SEV = 10752
S_CTQ = 14848
S_XBF = 0
S_XSQ = 4096
S_MU = 8192
S_RS = 8704
S_TD = 9216
S_OHC = 0
S_BIA = 4608
S_EXB = 9216
SCRB_N = 18944


def emit_ln(nc, sems, onesb, scrB, scrF, src_dram, xn_sb, xn_out_dram, bofs, ps_sum, ps_sq):
    LD, GD, TN, VC, SC = (sems[n] for n in ("LD", "GD", "TN", "VC", "SC"))
    xbf = [scrB[:, 0:4096], scrB[:, 4096:8192]]
    xsq = scrB[:, 8192:12288]
    mu = scrB[:, 12288:12800]
    rs = scrB[:, 12800:13312]
    td = scrB[:, 13312:13824]
    ex2 = scrF[:, 0:512]
    mu2 = scrF[:, 512:1024]
    var = scrF[:, 1024:1536]
    src_r = src_dram.rearrange("(c p) t -> p c t", p=128)
    marks = {}
    tn_w, vc_w, sc_w = TN.n, VC.n, SC.n

    for blk in range(4):
        t0 = bofs + blk * 512
        pr = blk % 2
        with nc.Block() as b:
            @b.gpsimd
            def _(g):
                if blk < 2:
                    g.wait_ge(TN.h, tn_w)
                    g.wait_ge(VC.h, vc_w)
                    g.wait_ge(SC.h, sc_w)
                else:
                    g.wait_ge(TN.h, marks[blk - 2]["tn"])
                    g.wait_ge(VC.h, marks[blk - 2]["vc"])
                GD.inc(g.dma_start(out=xbf[pr].rearrange("p (c t) -> p c t", c=8),
                                   in_=src_r[:, :, t0:t0 + 512]), 16)
        gd_b = GD.n

        with nc.Block() as b:
            @b.scalar
            def _(s):
                s.wait_ge(GD.h, gd_b)
                if blk >= 1:
                    s.wait_ge(TN.h, marks[blk - 1]["tn"])
                SC.inc(s.activation(out=xsq, in_=xbf[pr], func=AF.Square))

            @b.tensor
            def _(tn):
                tn.wait_ge(GD.h, gd_b)
                if blk >= 1:
                    tn.wait_ge(VC.h, marks[blk - 1]["vc_stats"])
                for c in range(8):
                    tn.matmul(ps_sum[:], onesb[:, 0:128], xbf[pr][:, c * 512:(c + 1) * 512],
                              start=(c == 0), stop=(c == 7))
                SC.wait(tn)
                for c in range(8):
                    i = tn.matmul(ps_sq[:], onesb[:, 0:128], xsq[:, c * 512:(c + 1) * 512],
                                  start=(c == 0), stop=(c == 7))
                TN.inc(i)

            @b.vector
            def _(v):
                TN.wait(v)
                v.tensor_scalar(out=mu, in0=ps_sum[:], scalar1=1.0 / D, scalar2=None, op0=ALU.mult)
                v.tensor_scalar(out=ex2, in0=ps_sq[:], scalar1=1.0 / D, scalar2=1e-5,
                                op0=ALU.mult, op1=ALU.add)
                v.tensor_tensor(out=mu2, in0=mu, in1=mu, op=ALU.mult)
                VC.inc(v.tensor_tensor(out=var, in0=ex2, in1=mu2, op=ALU.subtract))
            marks.setdefault(blk, {})["vc_stats"] = VC.n

            @b.scalar
            def _(s):
                VC.wait(s)
                s.activation(out=var, in_=var, func=AF.Ln)
                SC.inc(s.activation(out=rs, in_=var, func=AF.Exp, scale=-0.5))

            @b.vector
            def _(v):
                SC.wait(v)
                for c in range(8):
                    v.tensor_tensor(out=td, in0=xbf[pr][:, c * 512:(c + 1) * 512], in1=mu,
                                    op=ALU.subtract)
                    i = v.tensor_tensor(out=xn_sb[:, c * T + blk * 512: c * T + blk * 512 + 512],
                                        in0=td, in1=rs, op=ALU.mult)
                VC.inc(i)

            @b.sync
            def _(sy):
                VC.wait(sy)
                if xn_out_dram is not None:
                    xr = xn_out_dram.rearrange("(c p) t -> p c t", p=128)
                    LD.inc(sy.dma_start(
                        out=xr[:, :, t0:t0 + 512],
                        in_=xn_sb[:].rearrange("p (c t) -> p c t", c=8)[:, :, blk * 512:(blk + 1) * 512]), 16)
        marks[blk].update(tn=TN.n, vc=VC.n, sc=SC.n)


def emit_attention_batch(nc, sems, sb, layer, b, s_out_dram, wo_bias_col=None):
    LD, GD, TN, VC, SC = (sems[n] for n in ("LD", "GD", "TN", "VC", "SC"))
    is_l1 = (layer == 1)
    scale = (DH ** -0.5) if is_l1 else 1.0
    q_sb, k_sb, v_sb, wo_sb = sb["q_sb"], sb["k_sb"], sb["v_sb"], sb["wo_sb"]
    onesb = sb["onesb"]
    scrB = sb["scrB"]
    psS = [sb[f"psS{i}"] for i in range(4)]
    ps_ctxs = sb["ps_ctx"] if isinstance(sb["ps_ctx"], list) else [sb["ps_ctx"]]
    ps_dens = sb["ps_den"] if isinstance(sb["ps_den"], list) else [sb["ps_den"]]
    nrot = len(ps_ctxs)
    evac_mark = {i: 0 for i in range(nrot)}
    ttile = scrB[:, S_TT:S_TT + 6144]
    praw = scrB[:, S_PRAW:S_PRAW + 2048]
    ptile = scrB[:, S_PT:S_PT + 2048]
    rcp = sb["scrF"][:, 0:512]
    sev = scrB[:, S_SEV:S_SEV + 4096]
    ctxq = scrB[:, S_CTQ:S_CTQ + 4096]

    def tile_kind(o):
        if is_l1:
            if o > 384:
                return "skip"
            return "mask" if o >= 0 else "plain"
        if -128 <= o <= 512:
            return "band"
        return "cpos" if o > 512 else "cneg"

    for qb in range(4):
        for hp in range(8):
            hA, hB = 2 * hp, 2 * hp + 1
            par = (qb * 8 + hp) % nrot
            ps_ctx, ps_den = ps_ctxs[par], ps_dens[par]
            if not is_l1:
                vc_w = VC.n
                with nc.Block() as blb:
                    @blb.sync
                    def _(sy):
                        sy.wait_ge(VC.h, vc_w)
                        for hl, h in ((0, hA), (1, hB)):
                            src = bass.AP(sb["expb_d_t"], h * BPAD + 1408,
                                          [[1, 128], [128, 6], [1, 512]])
                            LD.inc(sy.dma_start(
                                out=ttile[:, hl * 3072:(hl + 1) * 3072]
                                .rearrange("p (j t) -> p j t", j=6),
                                in_=src), 16)
            ld_t = LD.n

            alive = [kt for kt in range(16) if tile_kind(kt * 128 - qb * 512) != "skip"]
            n = len(alive)
            tn0, sc0, vc0 = TN.n, SC.n, VC.n
            order = []
            for j in range(n):
                order.append(("qk", j))
                if j >= 2:
                    order.append(("pv", j - 2))
            order += [("pv", n - 2), ("pv", n - 1)] if n >= 2 else [("pv", 0)]
            tn_after_qk, tn_after_pv, sc_after_exp, vc_after_mul = {}, {}, {}, {}
            tn_c, sc_c, vc_c = tn0, sc0, vc0
            for op, j in order:
                tn_c += 1
                (tn_after_qk if op == "qk" else tn_after_pv)[j] = tn_c
            for j in range(n):
                kind = tile_kind(alive[j] * 128 - qb * 512)
                sc_c += 1
                sc_after_exp[j] = sc_c
                if kind in ("band", "mask"):
                    vc_c += 1
                    vc_after_mul[j] = vc_c

            with nc.Block() as bl:
                @bl.tensor
                def _(tn):
                    if hp == 0:
                        tn.wait_ge(SC.h, sc0)
                        tn.wait_ge(VC.h, vc0)
                    for op, j in order:
                        kt = alive[j]
                        o = kt * 128 - qb * 512
                        t = j % 2
                        if op == "qk":
                            if j >= 2:
                                tn.wait_ge(SC.h, sc_after_exp[j - 2])
                                if (j - 2) in vc_after_mul:
                                    tn.wait_ge(VC.h, vc_after_mul[j - 2])
                            for (ps, h) in ((psS[t], hA), (psS[2 + t], hB)):
                                if not is_l1:
                                    c = h // 2
                                    r0 = (h % 2) * 64
                                    i = tn.matmul(
                                        ps[:],
                                        k_sb[r0:r0 + 64, c * T + kt * 128: c * T + kt * 128 + 128],
                                        q_sb[r0:r0 + 64, c * T + qb * 512: c * T + qb * 512 + 512],
                                        start=True, stop=True)
                                else:
                                    gg, s_ = h // 4, h % 4
                                    r0 = 32 * s_
                                    tp = (96, 0) if s_ == 3 else None
                                    for ii, cch in enumerate((2 * gg, 2 * gg + 1)):
                                        i = tn.matmul(
                                            ps[:],
                                            k_sb[r0:r0 + 32, cch * T + kt * 128: cch * T + kt * 128 + 128],
                                            q_sb[r0:r0 + 32, cch * T + qb * 512: cch * T + qb * 512 + 512],
                                            start=(ii == 0), stop=(ii == 1), tile_position=tp)
                            i.then_inc(TN.h, 1)
                        else:
                            kind = tile_kind(o)
                            if kind in ("band", "mask"):
                                tn.wait_ge(VC.h, vc_after_mul[j])
                            else:
                                tn.wait_ge(SC.h, sc_after_exp[j])
                            if j == 0:
                                tn.wait_ge(VC.h, evac_mark[par])
                            first = (j == 0)
                            last = (j == n - 1)
                            for hl, h in ((0, hA), (1, hB)):
                                p_ap = ptile[:, (hl * 2 + t) * 512:(hl * 2 + t + 1) * 512]
                                vt = v_sb[:, kt * 1024 + h * 64: kt * 1024 + h * 64 + 64]
                                tn.matmul(ps_ctx[hl * 64:(hl + 1) * 64, :], vt, p_ap,
                                          start=first, stop=last, skip_group_check=True)
                                i = tn.matmul(ps_den[hl * 64:(hl + 1) * 64, :], onesb[:, 0:64], p_ap,
                                              start=first, stop=last, skip_group_check=True)
                            i.then_inc(TN.h, 1)
                    TN.n = tn_c

                @bl.scalar
                def _(s):
                    for j in range(n):
                        kt = alive[j]
                        o = kt * 128 - qb * 512
                        kind = tile_kind(o)
                        t = j % 2
                        if j == 0:
                            s.wait_ge(VC.h, vc0)
                        s.wait_ge(TN.h, tn_after_pv.get(j - 2, tn_after_qk[j]))
                        if (j - 2) in vc_after_mul:
                            s.wait_ge(VC.h, vc_after_mul[j - 2])
                        for (ps, h, hl) in ((psS[t], hA, 0), (psS[2 + t], hB, 1)):
                            if kind in ("band", "mask"):
                                dst = praw[:, (hl * 2 + t) * 512:(hl * 2 + t + 1) * 512]
                            else:
                                dst = ptile[:, (hl * 2 + t) * 512:(hl * 2 + t + 1) * 512]
                            bias = 0.0
                            if kind == "cpos":
                                bias = sb["cb"][:, 2 * h:2 * h + 1]
                            elif kind == "cneg":
                                bias = sb["cb"][:, 2 * h + 1:2 * h + 2]
                            i = s.activation(out=dst, in_=ps[:], func=AF.Exp, scale=scale, bias=bias)
                        i.then_inc(SC.h, 1)
                    SC.n = sc_c

                @bl.vector
                def _(v):
                    first_band = True
                    for j in range(n):
                        kt = alive[j]
                        o = kt * 128 - qb * 512
                        kind = tile_kind(o)
                        if kind not in ("band", "mask"):
                            continue
                        t = j % 2
                        if first_band:
                            if not is_l1:
                                v.wait_ge(LD.h, ld_t)
                            v.wait_ge(TN.h, tn0)
                            first_band = False
                        v.wait_ge(SC.h, sc_after_exp[j])
                        if j >= 2:
                            v.wait_ge(TN.h, tn_after_pv[j - 2])
                        for hl in range(2):
                            if kind == "band":
                                jdx = (o + 128) // 128
                                m = ttile[:, hl * 3072 + jdx * 512: hl * 3072 + (jdx + 1) * 512][:, ::-1]
                            else:
                                m = sb["zm"][:, (o // 128) * 512:(o // 128 + 1) * 512]
                            i = v.tensor_tensor(
                                out=ptile[:, (hl * 2 + t) * 512:(hl * 2 + t + 1) * 512],
                                in0=praw[:, (hl * 2 + t) * 512:(hl * 2 + t + 1) * 512], in1=m,
                                op=ALU.mult)
                        i.then_inc(VC.h, 1)
                    VC.n = vc_c

                @bl.vector
                def _(v):
                    v.wait_ge(TN.h, tn_after_pv[n - 1])
                    v.reciprocal(rcp, ps_den[:])
                    i = v.tensor_tensor(out=ctxq[:, hp * 512:(hp + 1) * 512],
                                        in0=ps_ctx[:], in1=rcp, op=ALU.mult)
                    VC.inc(i)
            evac_mark[par] = VC.n

        vc_w, sc_w, ld_w = VC.n, SC.n, LD.n
        with nc.Block() as bw:
            @bw.tensor
            def _(tn):
                tn.wait_ge(VC.h, vc_w)
                tn.wait_ge(SC.h, sc_w)
                for dc in range(8):
                    if dc >= 2:
                        tn.wait_ge(SC.h, sc_w + dc - 1)
                    for c in range(8):
                        i = tn.matmul(psS[dc % 2][:],
                                      wo_sb[:, c * 1024 + dc * 128: c * 1024 + dc * 128 + 128],
                                      ctxq[:, c * 512:(c + 1) * 512],
                                      start=(c == 0), stop=(c == 7))
                    TN.inc(i)

            @bw.scalar
            def _(s):
                base = TN.n - 8
                s.wait_ge(LD.h, ld_w)
                for dc in range(8):
                    s.wait_ge(TN.h, base + dc + 1)
                    bias = 0.0 if wo_bias_col is None else wo_bias_col[:, dc:dc + 1]
                    i = s.activation(out=sev[:, dc * 512:(dc + 1) * 512], in_=psS[dc % 2][:],
                                     func=AF.Copy if wo_bias_col is None else AF.Identity,
                                     bias=bias)
                    SC.inc(i)

            @bw.sync
            def _(sy):
                SC.wait(sy)
                sr = s_out_dram.rearrange("(c p) t -> p c t", p=128)
                LD.inc(sy.dma_start(
                    out=sr[:, :, b * T + qb * 512: b * T + qb * 512 + 512],
                    in_=sev.rearrange("p (c t) -> p c t", c=8)), 16)


A_DECLS = [("xT", [D, TOK], F32, "ExternalInput"), ("wq", [D, D], F32, "ExternalInput"),
           ("wk", [D, D], F32, "ExternalInput"), ("wv", [D, D], F32, "ExternalInput"),
           ("wo", [D, D], F32, "ExternalInput"), ("relemb", [NB, H], F32, "ExternalInput"),
           ("OH", [NB, BPAD], BF16, "ExternalInput"), ("ONESB", [128, 512], BF16, "ExternalInput"),
           ("s0", [D, TOK], BF16, "ExternalOutput"), ("xn0", [D, TOK], BF16, "ExternalOutput")]


def build_A(debug=False, ctx=None):
    nc, stk, sems, io = _phase_setup(ctx, "A_", A_DECLS)
    LD, GD, TN, VC, SC = (sems[n] for n in ("LD", "GD", "TN", "VC", "SC"))
    xT, wq, wk, wv, wo, relemb, OH, ONESB, s0, xn0 = (
        io[n] for n in ("xT", "wq", "wk", "wv", "wo", "relemb", "OH", "ONESB", "s0", "xn0"))

    if debug:
        dbg_expb = nc.dram_tensor("dbg_expb", [H, BPAD], BF16, kind="ExternalOutput").ap()
        dbg_cb = nc.dram_tensor("dbg_cb", [128, 32], BF16, kind="ExternalOutput").ap()
        dbg_k = nc.dram_tensor("dbg_k", [128, 8 * T], BF16, kind="ExternalOutput").ap()
        dbg_q = nc.dram_tensor("dbg_q", [128, 8 * T], BF16, kind="ExternalOutput").ap()
        dbg_v = nc.dram_tensor("dbg_v", [128, 16 * 1024], BF16, kind="ExternalOutput").ap()
        dbg_ctx = nc.dram_tensor("dbg_ctx", [128, 8 * 512], BF16, kind="ExternalOutput").ap()
    expb_d = nc.dram_tensor("expb_d", [H, BPAD], BF16, kind="Internal").ap()
    cvec_d = nc.dram_tensor("cvec_d", [H, 2], BF16, kind="Internal").ap()

    with stk:
        e = stk.enter_context
        onesb = e(nc.sbuf_tensor("onesb", [128, 512], BF16))
        xn_sb = e(nc.sbuf_tensor("xn_sb", [128, 8 * T], BF16))
        k_sb = e(nc.sbuf_tensor("k_sb", [128, 8 * T], BF16))
        q_sb = e(nc.sbuf_tensor("q_sb", [128, 8 * T], BF16))
        v_sb = e(nc.sbuf_tensor("v_sb", [128, 16 * 1024], BF16))
        wbuf = e(nc.sbuf_tensor("wbuf", [128, 8 * 1024], BF16))
        cb = e(nc.sbuf_tensor("cb", [128, 32], BF16))
        scrB = e(nc.sbuf_tensor("scrB", [128, SCRB_N], BF16))
        scrF = e(nc.sbuf_tensor("scrF", [128, 1536], F32))
        sb = {f"psS{i}": e(nc.psum_tensor(f"psS{i}", [128, 512], F32)) for i in range(4)}
        sb["ps_ctx"] = [e(nc.psum_tensor(f"ps_ctx{i}", [128, 512], F32)) for i in range(2)]
        sb["ps_den"] = [e(nc.psum_tensor(f"ps_den{i}", [128, 512], F32)) for i in range(2)]
        sb.update(onesb=onesb, q_sb=q_sb, k_sb=k_sb, v_sb=v_sb, wo_sb=wbuf,
                  scrB=scrB, scrF=scrF, cb=cb, expb_d_t=expb_d.tensor, zm=None)

        oh_sb = scrB[0:NB, S_OHC:S_OHC + BPAD]
        bias_sb = scrB[0:H, S_BIA:S_BIA + BPAD]
        expb_sb = scrB[0:H, S_EXB:S_EXB + BPAD]
        relb_sb = scrB[0:NB, 13824:13824 + H]
        with ExitStack() as st0:
            rel_sb = st0.enter_context(nc.sbuf_tensor("rel_sb", [NB, H], F32))
            NJ = BPAD // 512
            tn0, vc0 = TN.n, VC.n
            with nc.Block() as b0:
                @b0.sync
                def _(sy):
                    LD.inc(sy.dma_start(out=onesb[:], in_=ONESB), 16)
                    LD.inc(sy.dma_start(out=rel_sb[:], in_=relemb), 16)
                    LD.inc(sy.dma_start(out=oh_sb, in_=OH), 16)

                @b0.vector
                def _(v):
                    LD.wait(v)
                    VC.inc(v.tensor_copy(relb_sb, rel_sb[:]))

                @b0.tensor
                def _(tn):
                    tn.wait_ge(VC.h, vc0 + 1)
                    for jj in range(NJ):
                        if jj >= 2:
                            tn.wait_ge(VC.h, vc0 + jj)
                        i = tn.matmul(sb[f"psS{jj % 2}"][0:H, :], relb_sb,
                                      oh_sb[:, jj * 512:(jj + 1) * 512], start=True, stop=True)
                        TN.inc(i)

                @b0.vector
                def _(v):
                    for jj in range(NJ):
                        v.wait_ge(TN.h, tn0 + jj + 1)
                        i = v.tensor_copy(bias_sb[:, jj * 512:(jj + 1) * 512],
                                          sb[f"psS{jj % 2}"][0:H, :])
                        VC.inc(i)

                @b0.scalar
                def _(s):
                    VC.wait(s)
                    SC.inc(s.activation(out=expb_sb, in_=bias_sb, func=AF.Exp))

                @b0.sync
                def _(sy):
                    SC.wait(sy)
                    LD.inc(sy.dma_start(out=expb_d, in_=expb_sb), 16)
                    LD.inc(sy.dma_start(out=cvec_d[:, 0:1], in_=bias_sb[:, 4094:4095],
                                        allow_slow_non_contiguous=True), 16)
                    LD.inc(sy.dma_start(out=cvec_d[:, 1:2], in_=bias_sb[:, 0:1],
                                        allow_slow_non_contiguous=True), 16)
                    sy.wait_ge(LD.h, LD.n)
                    for h in range(H):
                        src = bass.AP(cvec_d.tensor, h * 2, [[0, 128], [1, 2]])
                        LD.inc(sy.dma_start(out=cb[:, 2 * h:2 * h + 2], in_=src), 16)

        for b in range(B_LOC):
            emit_ln(nc, sems, onesb, scrB, scrF, xT, xn_sb, xn0, b * T,
                    sb["psS0"], sb["psS1"])

            def load_w(src):
                tn_w = TN.n
                with nc.Block() as blw:
                    @blw.gpsimd
                    def _(g):
                        g.wait_ge(TN.h, tn_w)
                        GD.inc(g.dma_start(out=wbuf[:].rearrange("p (c f) -> p c f", c=8),
                                           in_=src.rearrange("(c p) f -> p c f", p=128)), 16)
                return GD.n

            def proj_kq(gd_mark, dst):
                tn0_, sc0_, vc0_ = TN.n, SC.n, VC.n
                with nc.Block() as bp:
                    @bp.tensor
                    def _(tn):
                        tn.wait_ge(GD.h, gd_mark)
                        tn.wait_ge(VC.h, vc0_)
                        tn.wait_ge(SC.h, sc0_)
                        for jj in range(32):
                            fc, nb_ = jj // 4, jj % 4
                            if jj >= 2:
                                tn.wait_ge(SC.h, sc0_ + jj - 1)
                            for dc in range(8):
                                i = tn.matmul(sb[f"psS{jj % 2}"][:],
                                              wbuf[:, dc * 1024 + fc * 128: dc * 1024 + fc * 128 + 128],
                                              xn_sb[:, dc * T + nb_ * 512: dc * T + nb_ * 512 + 512],
                                              start=(dc == 0), stop=(dc == 7))
                            TN.inc(i)

                    @bp.scalar
                    def _(s):
                        for jj in range(32):
                            fc, nb_ = jj // 4, jj % 4
                            s.wait_ge(TN.h, tn0_ + jj + 1)
                            i = s.activation(out=dst[:, fc * T + nb_ * 512: fc * T + nb_ * 512 + 512],
                                             in_=sb[f"psS{jj % 2}"][:], func=AF.Copy)
                            SC.inc(i)

            def proj_v(gd_mark):
                tn0_, sc0_, vc0_ = TN.n, SC.n, VC.n
                with nc.Block() as bp:
                    @bp.tensor
                    def _(tn):
                        tn.wait_ge(GD.h, gd_mark)
                        tn.wait_ge(VC.h, vc0_)
                        tn.wait_ge(SC.h, sc0_)
                        for jj in range(32):
                            tt, fb = jj // 2, jj % 2
                            if jj >= 2:
                                tn.wait_ge(VC.h, vc0_ + jj - 1)
                            for dc in range(8):
                                i = tn.matmul(sb[f"psS{jj % 2}"][:],
                                              xn_sb[:, dc * T + tt * 128: dc * T + tt * 128 + 128],
                                              wbuf[:, dc * 1024 + fb * 512: dc * 1024 + fb * 512 + 512],
                                              start=(dc == 0), stop=(dc == 7))
                            TN.inc(i)

                    @bp.vector
                    def _(v):
                        for jj in range(32):
                            tt, fb = jj // 2, jj % 2
                            v.wait_ge(TN.h, tn0_ + jj + 1)
                            i = v.tensor_copy(v_sb[:, tt * 1024 + fb * 512: tt * 1024 + (fb + 1) * 512],
                                              sb[f"psS{jj % 2}"][:])
                            VC.inc(i)

            proj_kq(load_w(wk), k_sb)
            proj_v(load_w(wv))
            proj_kq(load_w(wq), q_sb)
            gd_mo = load_w(wo)
            with nc.Block() as bwait:
                @bwait.tensor
                def _(tn):
                    tn.wait_ge(GD.h, gd_mo)
            emit_attention_batch(nc, sems, sb, 0, b, s0, wo_bias_col=None)

        if debug:
            with nc.Block() as bd:
                @bd.sync
                def _(sy):
                    sy.wait_ge(TN.h, TN.n)
                    sy.wait_ge(VC.h, VC.n)
                    sy.wait_ge(SC.h, SC.n)
                    LD.inc(sy.dma_start(out=scrB[0:H, 0:BPAD], in_=expb_d), 16)
                    sy.wait_ge(LD.h, LD.n)
                    LD.inc(sy.dma_start(out=dbg_expb, in_=scrB[0:H, 0:BPAD]), 16)
                    LD.inc(sy.dma_start(out=dbg_cb, in_=cb[:]), 16)
                    LD.inc(sy.dma_start(out=dbg_k, in_=k_sb[:]), 16)
                    LD.inc(sy.dma_start(out=dbg_q, in_=q_sb[:]), 16)
                    LD.inc(sy.dma_start(out=dbg_v, in_=v_sb[:]), 16)
                    LD.inc(sy.dma_start(out=dbg_ctx, in_=scrB[:, S_CTQ:S_CTQ + 4096]), 16)
        with nc.Block() as bf:
            @bf.sync
            def _(sy):
                sy.wait_ge(LD.h, LD.n)
                sy.wait_ge(GD.h, GD.n)
    return nc




B_DECLS = [("xT", [D, TOK], F32, "ExternalInput"), ("xn0", [D, TOK], BF16, "ExternalInput"),
           ("s0", [D, TOK], BF16, "ExternalInput"), ("wfcT", [D, FF], F32, "ExternalInput"),
           ("wprojT", [FF, D], F32, "ExternalInput"), ("lnw", [128, 32], F32, "ExternalInput"),
           ("lnb", [128, 32], F32, "ExternalInput"), ("ONESB", [128, 512], BF16, "ExternalInput"),
           ("ONESF", [128, 128], F32, "ExternalInput"), ("x1", [D, TOK], F32, "ExternalOutput")]


def build_B(debug=False, ctx=None):
    nc, stk, sems, io = _phase_setup(ctx, "B_", B_DECLS)
    LD, GD, TN, VC, SC, GP = (sems[n] for n in ("LD", "GD", "TN", "VC", "SC", "GP"))
    xT, xn0, s0, wfcT, wprojT, lnw, lnb, ONESB, ONESF, x1 = (
        io[n] for n in ("xT", "xn0", "s0", "wfcT", "wprojT", "lnw", "lnb", "ONESB", "ONESF", "x1"))
    if debug:
        wfcq_d = nc.dram_tensor("wfcq_d", [D, FF], BF16, kind="ExternalOutput").ap()
        wprojq_d = nc.dram_tensor("wprojq_d", [FF, D], BF16, kind="ExternalOutput").ap()
        dbg_h = nc.dram_tensor("dbg_h", [128, 32 * 256], BF16, kind="ExternalOutput").ap()
        dbg_mu = nc.dram_tensor("dbg_mu", [128, 256], BF16, kind="ExternalOutput").ap()
        dbg_rs = nc.dram_tensor("dbg_rs", [128, 256], BF16, kind="ExternalOutput").ap()
        dbg_cv = nc.dram_tensor("dbg_cv", [1, 1024], BF16, kind="ExternalOutput").ap()
        dbg_thr = nc.dram_tensor("dbg_thr", [128, 4], F32, kind="ExternalOutput").ap()
        dbg_parts = nc.dram_tensor("dbg_parts", [128, 66], F32, kind="ExternalOutput").ap()
    else:
        wfcq_d = nc.dram_tensor("wfcq_d", [D, FF], BF16, kind="Internal").ap()
        wprojq_d = nc.dram_tensor("wprojq_d", [FF, D], BF16, kind="Internal").ap()

    TB = 256
    NTB = TOK // TB

    with stk:
        e = stk.enter_context
        onesb = e(nc.sbuf_tensor("onesb", [128, 512], BF16))
        onesf = e(nc.sbuf_tensor("onesf", [128, 128], F32))
        lnw_sb = e(nc.sbuf_tensor("lnw_sb", [128, 32], F32))
        lnb_sb = e(nc.sbuf_tensor("lnb_sb", [128, 32], F32))
        lnb_bf = e(nc.sbuf_tensor("lnb_bf", [128, 32], BF16))
        wfcq = e(nc.sbuf_tensor("wfcq", [128, 8 * FF], BF16))
        wpjq = e(nc.sbuf_tensor("wpjq", [128, 32 * 1024], BF16))
        cvec = e(nc.sbuf_tensor("cvec", [128, 1024], BF16))
        h_sb = e(nc.sbuf_tensor("h_sb", [128, 32 * TB], BF16))
        hsq = e(nc.sbuf_tensor("hsq", [128, TB], BF16))
        xn_blk = [e(nc.sbuf_tensor(f"xn_blk{i}", [128, 8 * TB], BF16)) for i in range(2)]
        s_blk = [e(nc.sbuf_tensor(f"s_blk{i}", [128, 8 * TB], BF16)) for i in range(2)]
        x_blk = [e(nc.sbuf_tensor(f"x_blk{i}", [128, 8 * TB], F32)) for i in range(2)]
        out_sb = e(nc.sbuf_tensor("out_sb", [128, 8 * TB], F32))
        g_sb = e(nc.sbuf_tensor("g_sb", [128, TB], BF16))
        t2_sb = e(nc.sbuf_tensor("t2_sb", [128, TB], BF16))
        mu_sb = e(nc.sbuf_tensor("mu_sb", [128, TB], BF16))
        rs_sb = e(nc.sbuf_tensor("rs_sb", [128, TB], BF16))
        td_sb = e(nc.sbuf_tensor("td_sb", [128, TB], BF16))
        scf = e(nc.sbuf_tensor("scf", [128, 3 * TB], F32))
        parts = e(nc.sbuf_tensor("parts", [128, 66], F32))
        thr2 = e(nc.sbuf_tensor("thr2", [128, 4], F32))
        ps = [e(nc.psum_tensor(f"psB{i}", [128, 512], F32)) for i in range(4)]
        ps_g = e(nc.psum_tensor("ps_g", [128, 4], F32))
        wscr = [x_blk[0][:, 0:2048], x_blk[1][:, 0:2048]]
        a_scr = h_sb[:, 0:2048]
        b_scr = h_sb[:, 2048:4096]
        q_scr = h_sb[:, 4096:6144]

        with nc.Block() as b0:
            @b0.sync
            def _(sy):
                LD.inc(sy.dma_start(out=onesb[:], in_=ONESB), 16)
                LD.inc(sy.dma_start(out=onesf[:], in_=ONESF), 16)
                LD.inc(sy.dma_start(out=lnw_sb[:], in_=lnw), 16)
                LD.inc(sy.dma_start(out=lnb_sb[:], in_=lnb), 16)

            @b0.vector
            def _(v):
                LD.wait(v)
                VC.inc(v.tensor_copy(lnb_bf[:], lnb_sb[:]))

        for mi, (wsrc, wdst) in enumerate(((wfcT, wfcq_d), (wprojT, wprojq_d))):
            npc = (D * FF) // (128 * 2048)
            ld0, vc0 = LD.n, VC.n
            with nc.Block() as bq:
                @bq.sync
                def _(sy):
                    for pc in range(npc):
                        if pc >= 2:
                            sy.wait_ge(VC.h, vc0 + pc - 1)
                        LD.inc(sy.dma_start(out=wscr[pc % 2],
                                            in_=bass.AP(wsrc.tensor, pc * 128 * 2048,
                                                        [[2048, 128], [1, 2048]])), 16)

                @bq.vector
                def _(v):
                    for pc in range(npc):
                        v.wait_ge(LD.h, ld0 + (pc + 1) * 16)
                        i = v.tensor_reduce(out=parts[:, pc:pc + 1], in_=wscr[pc % 2], axis=AX.X,
                                            op=ALU.add, apply_absolute_value=True)
                        VC.inc(i)

                tnq = TN.n
                with ExitStack() as _s:
                    pass
                @bq.tensor
                def _(tn):
                    VC.wait(tn)
                    i = tn.matmul(ps_g[0:npc, 2:3], parts[:, 0:npc], onesf[:, 0:1],
                                  start=True, stop=True)
                    TN.inc(i)

                @bq.scalar
                def _(s):
                    s.wait_ge(TN.h, tnq + 1)
                    SC.inc(s.copy(parts[0:npc, npc + 2:npc + 3], ps_g[0:npc, 2:3]))

                @bq.tensor
                def _(tn):
                    SC.wait(tn)
                    i = tn.matmul(ps_g[0:1, mi:mi + 1], parts[0:npc, npc + 2:npc + 3], onesf[0:npc, 0:1],
                                  start=True, stop=True)
                    TN.inc(i)

                @bq.vector
                def _(v):
                    TN.wait(v)
                    i = v.tensor_scalar(out=parts[:, npc + 1:npc + 2][0:1, :], in0=ps_g[0:1, mi:mi + 1],
                                        scalar1=0.5 / (D * FF), scalar2=0.5e-6, op0=ALU.mult, op1=ALU.add)
                    VC.inc(i)

                @bq.tensor
                def _(tn):
                    TN.wait(tn)
                    VC.wait(tn)
                    i = tn.matmul(ps[2][0:128, 0:1], onesf[0:1, :], parts[0:1, npc + 1:npc + 2],
                                  start=True, stop=True)
                    TN.inc(i)

                @bq.vector
                def _(v):
                    TN.wait(v)
                    v.tensor_copy(thr2[:, 2 * mi:2 * mi + 1], ps[2][:, 0:1])
                    i = v.tensor_scalar(out=thr2[:, 2 * mi + 1:2 * mi + 2], in0=ps[2][:, 0:1],
                                        scalar1=-1.0, scalar2=None, op0=ALU.mult)
                    VC.inc(i)

            ld1, vc1, gp1 = LD.n, VC.n, GP.n
            with nc.Block() as bq2:
                @bq2.sync
                def _(sy):
                    for pc in range(npc):
                        if pc >= 2:
                            sy.wait_ge(VC.h, vc1 + 2 * (pc - 1))
                        LD.inc(sy.dma_start(out=wscr[pc % 2],
                                            in_=bass.AP(wsrc.tensor, pc * 128 * 2048,
                                                        [[2048, 128], [1, 2048]])), 16)
                        sy.wait_ge(VC.h, vc1 + 2 * pc + 2)
                        dst_r = bass.AP(wdst.tensor, pc * 128 * 2048,
                                        [[2048, 128], [1, 2048]])
                        LD.inc(sy.dma_start(out=dst_r, in_=q_scr), 16)

                @bq2.vector
                def _(v):
                    for pc in range(npc):
                        v.wait_ge(LD.h, ld1 + pc * 32 + 16)
                        if pc == 0:
                            v.wait_ge(VC.h, vc1)
                        v.tensor_scalar(out=b_scr, in0=wscr[pc % 2], scalar1=thr2[:, 2 * mi + 1:2 * mi + 2],
                                        scalar2=None, op0=ALU.is_lt)
                        if pc >= 1:
                            v.wait_ge(LD.h, ld1 + pc * 32)
                        i = v.scalar_tensor_tensor(out=q_scr, in0=wscr[pc % 2],
                                                   scalar=thr2[:, 2 * mi:2 * mi + 1], in1=b_scr,
                                                   op0=ALU.is_gt, op1=ALU.subtract)
                        VC.inc(i, 2)

        with nc.Block() as bl:
            @bl.sync
            def _(sy):
                sy.wait_ge(LD.h, LD.n)
                for dc in range(8):
                    LD.inc(sy.dma_start(out=wfcq[:, dc * FF:(dc + 1) * FF],
                                        in_=bass.AP(wfcq_d.tensor, dc * 128 * FF, [[FF, 128], [1, FF]])), 16)
                for fc in range(32):
                    LD.inc(sy.dma_start(out=wpjq[:, fc * 1024:(fc + 1) * 1024],
                                        in_=bass.AP(wprojq_d.tensor, fc * 128 * D, [[D, 128], [1, D]])), 16)

        tnc = TN.n
        with nc.Block() as bc:
            @bc.tensor
            def _(tn):
                LD.wait(tn)
                for db in range(2):
                    for fc in range(32):
                        i = tn.matmul(ps[db][0:1, :], lnb_bf[:, fc:fc + 1],
                                      wpjq[:, fc * 1024 + db * 512: fc * 1024 + db * 512 + 512],
                                      start=(fc == 0), stop=(fc == 31))
                    TN.inc(i)

            @bc.vector
            def _(v):
                for db in range(2):
                    v.wait_ge(TN.h, tnc + db + 1)
                    i = v.tensor_copy(cvec[0:1, db * 512:(db + 1) * 512], ps[db][0:1, :])
                    VC.inc(i)

        xr_r = xT.rearrange("(c p) t -> p c t", p=128)
        xn_r = xn0.rearrange("(c p) t -> p c t", p=128)
        s_r = s0.rearrange("(c p) t -> p c t", p=128)
        x1_r = x1.rearrange("(c p) t -> p c t", p=128)
        blk_done_marks = {}
        vc_q = VC.n
        for tb in range(NTB):
            T0 = tb * TB
            pr = tb % 2
            with nc.Block() as bld:
                @bld.sync
                def _(sy):
                    if tb < 2:
                        sy.wait_ge(VC.h, vc_q)
                    if tb >= 2:
                        mk = blk_done_marks[tb - 2]
                        sy.wait_ge(TN.h, mk["tn"])
                        sy.wait_ge(VC.h, mk["vc"])
                        sy.wait_ge(SC.h, mk["sc"])
                    LD.inc(sy.dma_start(out=xn_blk[pr][:].rearrange("p (c t) -> p c t", c=8),
                                        in_=xn_r[:, :, T0:T0 + TB]), 16)
                    LD.inc(sy.dma_start(out=s_blk[pr][:].rearrange("p (c t) -> p c t", c=8),
                                        in_=s_r[:, :, T0:T0 + TB]), 16)
                    LD.inc(sy.dma_start(out=x_blk[pr][:].rearrange("p (c t) -> p c t", c=8),
                                        in_=xr_r[:, :, T0:T0 + TB]), 16)
            ld_w = LD.n - 48

            tn0_, sc0_ = TN.n, SC.n
            tn_after_h, tn_after_st = {}, {}
            tcnt = tn0_
            torder = []
            for fc in range(32):
                torder.append(("h", fc))
                if fc >= 1:
                    torder.append(("st", fc - 1))
            torder.append(("st", 31))
            for op, fc in torder:
                tcnt += 1
                (tn_after_h if op == "h" else tn_after_st)[fc] = tcnt

            with nc.Block() as bh:
                @bh.tensor
                def _(tn):
                    tn.wait_ge(LD.h, ld_w + 48)
                    if tb >= 1:
                        tn.wait_ge(SC.h, blk_done_marks[tb - 1]["sc"])
                    for op, fc in torder:
                        if op == "h":
                            if fc >= 2:
                                tn.wait_ge(SC.h, sc0_ + 2 * (fc - 2) + 1)
                            for dc in range(8):
                                i = tn.matmul(ps[fc % 2][:, 0:TB],
                                              wfcq[:, dc * FF + fc * 128: dc * FF + fc * 128 + 128],
                                              xn_blk[pr][:, dc * TB:(dc + 1) * TB],
                                              start=(dc == 0), stop=(dc == 7))
                            i.then_inc(TN.h, 1)
                        else:
                            tn.wait_ge(SC.h, sc0_ + 2 * fc + 2)
                            tn.matmul(ps[2][:, 0:TB], onesb[:, 0:128], h_sb[:, fc * TB:(fc + 1) * TB],
                                      start=(fc == 0), stop=(fc == 31), skip_group_check=True)
                            i = tn.matmul(ps[3][:, 0:TB], onesb[:, 0:128], hsq[:],
                                          start=(fc == 0), stop=(fc == 31), skip_group_check=True)
                            i.then_inc(TN.h, 1)
                    TN.n = tcnt

                @bh.scalar
                def _(s):
                    for fc in range(32):
                        s.wait_ge(TN.h, tn_after_h[fc])
                        if fc >= 1:
                            s.wait_ge(TN.h, tn_after_st[fc - 1])
                        i = s.activation(out=h_sb[:, fc * TB:(fc + 1) * TB], in_=ps[fc % 2][:, 0:TB],
                                         func=AF.Gelu)
                        i.then_inc(SC.h, 1)
                        SC.n += 1
                        i = s.activation(out=hsq[:], in_=h_sb[:, fc * TB:(fc + 1) * TB], func=AF.Square)
                        i.then_inc(SC.h, 1)
                        SC.n += 1

            vc0_ = VC.n
            with nc.Block() as bs:
                @bs.vector
                def _(v):
                    v.wait_ge(TN.h, tn_after_st[31])
                    v.tensor_scalar(out=mu_sb[:], in0=ps[2][:, 0:TB], scalar1=1.0 / FF, scalar2=None,
                                    op0=ALU.mult)
                    v.tensor_scalar(out=scf[:, 0:TB], in0=ps[3][:, 0:TB], scalar1=1.0 / FF,
                                    scalar2=1e-5, op0=ALU.mult, op1=ALU.add)
                    v.tensor_tensor(out=scf[:, TB:2 * TB], in0=mu_sb[:], in1=mu_sb[:], op=ALU.mult)
                    VC.inc(v.tensor_tensor(out=scf[:, 2 * TB:3 * TB], in0=scf[:, 0:TB],
                                           in1=scf[:, TB:2 * TB], op=ALU.subtract))

                @bs.scalar
                def _(s):
                    VC.wait(s)
                    s.activation(out=scf[:, 2 * TB:3 * TB], in_=scf[:, 2 * TB:3 * TB], func=AF.Ln)
                    SC.inc(s.activation(out=rs_sb[:], in_=scf[:, 2 * TB:3 * TB], func=AF.Exp, scale=-0.5))

                @bs.vector
                def _(v):
                    SC.wait(v)
                    for fc in range(32):
                        v.tensor_tensor(out=td_sb[:], in0=h_sb[:, fc * TB:(fc + 1) * TB], in1=mu_sb[:],
                                        op=ALU.subtract)
                        i = v.scalar_tensor_tensor(out=h_sb[:, fc * TB:(fc + 1) * TB], in0=td_sb[:],
                                                   scalar=lnw_sb[:, fc:fc + 1], in1=rs_sb[:],
                                                   op0=ALU.mult, op1=ALU.mult)
                    VC.inc(i)

            tnp, scp, vcp = TN.n, SC.n, VC.n
            with nc.Block() as bp:
                @bp.tensor
                def _(tn):
                    tn.wait_ge(VC.h, vcp)
                    for dc in range(8):
                        if dc >= 2:
                            tn.wait_ge(SC.h, scp + dc - 1)
                        for fc in range(32):
                            tn.matmul(ps[dc % 2][:, 0:TB],
                                      wpjq[:, fc * 1024 + dc * 128: fc * 1024 + dc * 128 + 128],
                                      h_sb[:, fc * TB:(fc + 1) * TB],
                                      start=(fc == 0), stop=False)
                        i = tn.matmul(ps[dc % 2][:, 0:TB], cvec[0:1, dc * 128:(dc + 1) * 128],
                                      onesb[0:1, 0:TB], start=False, stop=True)
                        TN.inc(i)

                @bp.scalar
                def _(s):
                    for dc in range(8):
                        s.wait_ge(TN.h, tnp + dc + 1)
                        i = s.activation(out=g_sb[:], in_=ps[dc % 2][:, 0:TB], func=AF.Silu)
                        SC.inc(i)
                        s.wait_ge(VC.h, vcp + 2 * dc + 1)

                @bp.vector
                def _(v):
                    for dc in range(8):
                        v.wait_ge(SC.h, scp + dc + 1)
                        v.tensor_tensor(out=t2_sb[:], in0=g_sb[:], in1=s_blk[pr][:, dc * TB:(dc + 1) * TB],
                                        op=ALU.mult)
                        VC.n += 1
                        i = v.tensor_tensor(out=out_sb[:, dc * TB:(dc + 1) * TB], in0=t2_sb[:],
                                            in1=x_blk[pr][:, dc * TB:(dc + 1) * TB], op=ALU.add)
                        i.then_inc(VC.h, 2)
                        VC.n += 1

                @bp.sync
                def _(sy):
                    VC.wait(sy)
                    LD.inc(sy.dma_start(out=x1_r[:, :, T0:T0 + TB],
                                        in_=out_sb[:].rearrange("p (c t) -> p c t", c=8)), 16)
            blk_done_marks[tb] = dict(tn=TN.n, vc=VC.n, sc=SC.n)

        if debug:
            with nc.Block() as bd:
                @bd.sync
                def _(sy):
                    sy.wait_ge(TN.h, TN.n)
                    sy.wait_ge(VC.h, VC.n)
                    sy.wait_ge(SC.h, SC.n)
                    LD.inc(sy.dma_start(out=dbg_h, in_=h_sb[:]), 16)
                    LD.inc(sy.dma_start(out=dbg_mu, in_=mu_sb[:]), 16)
                    LD.inc(sy.dma_start(out=dbg_rs, in_=rs_sb[:]), 16)
                    LD.inc(sy.dma_start(out=dbg_cv, in_=cvec[0:1, :]), 16)
                    LD.inc(sy.dma_start(out=dbg_thr, in_=thr2[:]), 16)
                    LD.inc(sy.dma_start(out=dbg_parts, in_=parts[:]), 16)
        with nc.Block() as bf:
            @bf.sync
            def _(sy):
                sy.wait_ge(LD.h, LD.n)
                sy.wait_ge(GD.h, GD.n)

    return nc


C_DECLS = [("x1", [D, TOK], F32, "ExternalInput"), ("qkvw", [D, 3 * D], F32, "ExternalInput"),
           ("qkvb", [128, 24], F32, "ExternalInput"), ("outw", [D, D], F32, "ExternalInput"),
           ("outb", [128, 8], F32, "ExternalInput"), ("wg", [D, NE], F32, "ExternalInput"),
           ("CCb", [128, T], BF16, "ExternalInput"), ("SSb", [128, T], BF16, "ExternalInput"),
           ("ZM4", [128, 4 * 512], BF16, "ExternalInput"), ("ONESB", [128, 512], BF16, "ExternalInput"),
           ("ONESF", [128, 128], F32, "ExternalInput"), ("LSTF", [128, 128], F32, "ExternalInput"),
           ("IOTA16", [128, NE], F32, "ExternalInput"), ("LSU16F", [NE, NE], F32, "ExternalInput"),
           ("EYEB", [128, 128], BF16, "ExternalInput"), ("BV", [1, D], F32, "ExternalInput"),
           ("s1", [D, TOK], BF16, "ExternalOutput"), ("sorted_o", [TOK, D], BF16, "ExternalOutput"),
           ("counts_o", [1, NE], F32, "ExternalOutput"), ("idxo", [128, 32], F32, "ExternalOutput"),
           ("loco", [128, 32], F32, "ExternalOutput"), ("gso", [128, 32], F32, "ExternalOutput"),
           ("xn1T_d", [D, TOK], BF16, "Internal"), ("xnnat_d", [TOK, D], BF16, "Internal"),
           ("cnt_d", [1, 64], F32, "Internal")]


def build_C(core_id=0, debug=False, ctx=None, part="all"):
    nc, stk, sems, io = _phase_setup(ctx, "C_" + ("" if part == "all" else part + "_"), C_DECLS)
    LD, GD, TN, VC, SC, GP = (sems[n] for n in ("LD", "GD", "TN", "VC", "SC", "GP"))
    (x1, qkvw, qkvb, outw, outb, wg, CCb, SSb, ZM4, ONESB, ONESF, LSTF, IOTA16,
     LSU16F, EYEB, BV, s1, sorted_o, counts_o, idxo, loco, gso, xn1T_d, xnnat_d,
     cnt_d) = (
        io[n] for n in ("x1", "qkvw", "qkvb", "outw", "outb", "wg", "CCb", "SSb", "ZM4",
                        "ONESB", "ONESF", "LSTF", "IOTA16", "LSU16F", "EYEB", "BV",
                        "s1", "sorted_o", "counts_o", "idxo", "loco", "gso", "xn1T_d",
                        "xnnat_d", "cnt_d"))

    with stk:
        e = stk.enter_context
        onesb = e(nc.sbuf_tensor("onesb", [128, 512], BF16))
        onesf = e(nc.sbuf_tensor("onesf", [128, 128], F32))
        lstf = e(nc.sbuf_tensor("lstf", [128, 128], F32))
        iota16 = e(nc.sbuf_tensor("iota16", [128, NE], F32))
        lsu16 = e(nc.sbuf_tensor("lsu16", [NE, NE], F32))
        eyeb = e(nc.sbuf_tensor("eyeb", [128, 128], BF16))
        zm_sb = e(nc.sbuf_tensor("zm_sb", [128, 4 * 512], BF16))
        qkvb_sb = e(nc.sbuf_tensor("qkvb_sb", [128, 24], F32))
        outb_sb = e(nc.sbuf_tensor("outb_sb", [128, 8], F32))
        wg_sb = e(nc.sbuf_tensor("wg_sb", [128, 8 * NE], BF16))
        bvrow = e(nc.sbuf_tensor("bvrow", [1, D], BF16))
        xn_sb = e(nc.sbuf_tensor("xn_sb", [128, 8 * T], BF16))
        k_sb = e(nc.sbuf_tensor("k_sb", [128, 8 * T], BF16))
        q_sb = e(nc.sbuf_tensor("q_sb", [128, 8 * T], BF16))
        v_sb = e(nc.sbuf_tensor("v_sb", [128, 16 * 1024], BF16))
        wbuf = e(nc.sbuf_tensor("wbuf", [128, 8 * 1024], BF16))
        scrB = e(nc.sbuf_tensor("scrB", [128, SCRB_N], BF16))
        scrF = e(nc.sbuf_tensor("scrF", [128, 1536], F32))
        eq_sb = e(nc.sbuf_tensor("eq_sb", [128, 32 * NE], F32))
        idx_sb = e(nc.sbuf_tensor("idx_sb", [128, 32], F32))
        loc_sb = e(nc.sbuf_tensor("loc_sb", [128, 32], F32))
        gs_sb = e(nc.sbuf_tensor("gs_sb", [128, 32], F32))
        l_sb = e(nc.sbuf_tensor("l_sb", [128, NE], F32))
        et_sb = e(nc.sbuf_tensor("et_sb", [128, NE], F32))
        m_sb = e(nc.sbuf_tensor("m_sb", [128, 4], F32))
        carry = e(nc.sbuf_tensor("carry", [1, NE], F32))
        lst_bc = e(nc.sbuf_tensor("lst_bc", [128, NE], F32))
        nat_sb = e(nc.sbuf_tensor("nat_sb", [128, D], BF16))
        slot_sb = e(nc.sbuf_tensor("slot_sb", [128, 2], F32))
        slot_u = e(nc.sbuf_tensor("slot_u", [128, 1], U32))

        sb = {f"psS{i}": e(nc.psum_tensor(f"psS{i}", [128, 512], F32)) for i in range(4)}
        if part == "post":
            sb["ps_ctx"] = [e(nc.psum_tensor(f"ps_ctx{i}", [128, 512], F32)) for i in range(2)]
            sb["ps_den"] = [e(nc.psum_tensor(f"ps_den{i}", [128, 512], F32)) for i in range(2)]
            ps_tr = None
        else:
            sb["ps_ctx"] = e(nc.psum_tensor("ps_ctx", [128, 512], F32))
            sb["ps_den"] = e(nc.psum_tensor("ps_den", [128, 512], F32))
            ps_tr = e(nc.psum_tensor("ps_tr", [128, 1024], BF16))
        sb.update(onesb=onesb, q_sb=q_sb, k_sb=k_sb, v_sb=v_sb, wo_sb=wbuf,
                  scrB=scrB, scrF=scrF, cb=None, expb_d_t=None, zm=zm_sb)
        rope_e = scrB[:, S_TT:S_TT + 2048]
        rope_o = scrB[:, S_TT + 2048:S_TT + 4096]
        rt1 = scrB[:, S_TT + 4096:S_TT + 6144]
        rt2 = scrB[:, S_PRAW:S_PRAW + 2048]
        cc_sb = scrB[:, S_SEV:S_SEV + 2048]
        ss_sb = scrB[:, S_CTQ:S_CTQ + 2048]

        with nc.Block() as b0:
            @b0.sync
            def _(sy):
                for d, s_ in ((onesb[:], ONESB), (onesf[:], ONESF), (lstf[:], LSTF),
                              (iota16[:], IOTA16), (lsu16[:], LSU16F), (eyeb[:], EYEB),
                              (zm_sb[:], ZM4),
                              (qkvb_sb[:], qkvb), (outb_sb[:], outb)):
                    LD.inc(sy.dma_start(out=d, in_=s_), 16)

            @b0.gpsimd
            def _(g):
                GD.inc(g.dma_start(out=wg_sb[:].rearrange("p (c f) -> p c f", c=8),
                                   in_=wg.rearrange("(c p) f -> p c f", p=128)), 16)
                GD.inc(g.dma_start(out=bvrow[:], in_=BV), 16)

            @b0.vector
            def _(v):
                LD.wait(v)
                VC.inc(v.memset(carry[:], 0.0))

        for b in (range(B_LOC) if part in ("all", "pre") else []):
            emit_ln(nc, sems, onesb, scrB, scrF, x1, xn_sb, xn1T_d, b * T,
                    sb["psS0"], sb["psS1"])

            for tt in range(16):
                g_ = b * 16 + tt
                tn0r, vc0r, sc0r = TN.n, VC.n, SC.n
                with nc.Block() as br:
                    @br.tensor
                    def _(tn):
                        if g_ == 0:
                            GD.wait(tn)
                        tn.wait_ge(VC.h, vc0r)
                        for dc in range(8):
                            i = tn.matmul(sb["psS0"][:, 0:NE],
                                          xn_sb[:, dc * T + tt * 128: dc * T + tt * 128 + 128],
                                          wg_sb[:, dc * NE:(dc + 1) * NE],
                                          start=(dc == 0), stop=(dc == 7))
                        TN.inc(i)
                        for dc in range(8):
                            i = tn.matmul(ps_tr[:, dc * 128:(dc + 1) * 128],
                                          xn_sb[:, dc * T + tt * 128: dc * T + tt * 128 + 128],
                                          eyeb[:], start=True, stop=True, is_transpose=True)
                            TN.inc(i)

                    @br.vector
                    def _(v):
                        def step(instr):
                            VC.inc(instr)
                            v.wait_ge(VC.h, VC.n)
                        v.wait_ge(TN.h, tn0r + 1)
                        step(v.tensor_copy(l_sb[:], sb["psS0"][:, 0:NE]))
                        step(v.tensor_reduce(out=m_sb[:, 0:1], in_=l_sb[:], axis=AX.X, op=ALU.max))
                        i = v.tensor_scalar(out=m_sb[:, 1:2], in0=m_sb[:, 0:1], scalar1=-1.0,
                                            scalar2=None, op0=ALU.mult)
                        VC.inc(i)

                    @br.scalar
                    def _(s):
                        VC.wait(s)
                        SC.inc(s.activation(out=et_sb[:], in_=l_sb[:], func=AF.Exp,
                                            bias=m_sb[:, 1:2], accum_out=m_sb[:, 2:3]))

                    @br.vector
                    def _(v):
                        def step(instr):
                            VC.inc(instr)
                            v.wait_ge(VC.h, VC.n)
                        SC.wait(v)
                        step(v.reciprocal(gs_sb[:, g_:g_ + 1], m_sb[:, 2:3]))
                        step(v.tensor_scalar(out=eq_sb[:, g_ * NE:(g_ + 1) * NE], in0=l_sb[:],
                                             scalar1=m_sb[:, 0:1], scalar2=None, op0=ALU.is_ge))
                        step(v.tensor_tensor(out=et_sb[:], in0=eq_sb[:, g_ * NE:(g_ + 1) * NE],
                                             in1=iota16[:], op=ALU.mult))
                        i = v.tensor_reduce(out=idx_sb[:, g_:g_ + 1], in_=et_sb[:], axis=AX.X,
                                            op=ALU.add)
                        VC.inc(i)

                    @br.tensor
                    def _(tn):
                        VC.wait(tn)
                        i = tn.matmul(sb["psS1"][0:1, 0:NE], onesf[:, 0:1],
                                      eq_sb[:, g_ * NE:(g_ + 1) * NE], start=True, stop=True)
                        TN.inc(i)
                        tn.matmul(sb["psS2"][:, 0:NE], lstf[:],
                                  eq_sb[:, g_ * NE:(g_ + 1) * NE], start=True, stop=False)
                        i = tn.matmul(sb["psS2"][:, 0:NE], onesf[0:1, :], carry[:],
                                      start=False, stop=True)
                        TN.inc(i)

                    @br.vector
                    def _(v):
                        def step(instr):
                            VC.inc(instr)
                            v.wait_ge(VC.h, VC.n)
                        TN.wait(v)
                        step(v.tensor_tensor(out=et_sb[:], in0=eq_sb[:, g_ * NE:(g_ + 1) * NE],
                                             in1=sb["psS2"][:, 0:NE], op=ALU.mult))
                        step(v.tensor_reduce(out=loc_sb[:, g_:g_ + 1], in_=et_sb[:], axis=AX.X,
                                             op=ALU.add))
                        i = v.tensor_tensor(out=carry[:], in0=carry[:], in1=sb["psS1"][0:1, 0:NE],
                                            op=ALU.add)
                        VC.inc(i)

                    @br.vector
                    def _(v):
                        v.wait_ge(TN.h, TN.n)
                        for dc in range(8):
                            i = v.tensor_copy(nat_sb[:, dc * 128:(dc + 1) * 128],
                                              ps_tr[:, dc * 128:(dc + 1) * 128])
                        VC.inc(i)

                    @br.sync
                    def _(sy):
                        VC.wait(sy)
                        LD.inc(sy.dma_start(out=bass.AP(xnnat_d.tensor, g_ * 128 * D,
                                                        [[D, 128], [1, D]]),
                                            in_=nat_sb[:]), 16)

        if part in ("all", "pre"):
            with nc.Block() as bc1:
                @bc1.sync
                def _(sy):
                    VC.wait(sy)
                    LD.inc(sy.dma_start(out=counts_o, in_=carry[:]), 16)
                    LD.inc(sy.dma_start(out=cnt_d[:, 0:NE], in_=carry[:]), 16)
                    LD.inc(sy.dma_start(out=idxo, in_=idx_sb[:]), 16)
                    LD.inc(sy.dma_start(out=loco, in_=loc_sb[:]), 16)
                    LD.inc(sy.dma_start(out=gso, in_=gs_sb[:]), 16)
                    sy.wait_ge(LD.h, LD.n)
                    LD.inc(sy.dma_start(out=l_sb[0:NE, 0:1], in_=bass.AP(cnt_d.tensor, 0, [[1, NE], [1, 1]])), 16)

            tnl = TN.n
            with nc.Block() as bc2:
                @bc2.tensor
                def _(tn):
                    LD.wait(tn)
                    i = tn.matmul(sb["psS3"][0:1, 0:NE], l_sb[0:NE, 0:1], lsu16[:],
                                  start=True, stop=True)
                    TN.inc(i)

                @bc2.vector
                def _(v):
                    v.wait_ge(TN.h, tnl + 1)
                    VC.inc(v.tensor_copy(et_sb[0:1, 0:NE], sb["psS3"][0:1, 0:NE]))

                @bc2.sync
                def _(sy):
                    VC.wait(sy)
                    LD.inc(sy.dma_start(out=cnt_d[:, NE:2 * NE], in_=et_sb[0:1, 0:NE]), 16)
                    sy.wait_ge(LD.h, LD.n)
                    LD.inc(sy.dma_start(out=lst_bc[:], in_=bass.AP(cnt_d.tensor, NE, [[0, 128], [1, NE]])), 16)

            for g_ in range(32):
                ld0s, vc0s, gp0s = LD.n, VC.n, GP.n
                with nc.Block() as bs:
                    @bs.sync
                    def _(sy):
                        sy.wait_ge(GP.h, gp0s)
                        LD.inc(sy.dma_start(out=nat_sb[:], in_=bass.AP(xnnat_d.tensor, g_ * 128 * D,
                                                                       [[D, 128], [1, D]])), 16)

                    @bs.vector
                    def _(v):
                        def step(instr):
                            VC.inc(instr)
                            v.wait_ge(VC.h, VC.n)
                        v.wait_ge(LD.h, ld0s + 16)
                        step(v.tensor_tensor(out=et_sb[:], in0=eq_sb[:, g_ * NE:(g_ + 1) * NE],
                                             in1=lst_bc[:], op=ALU.mult))
                        step(v.tensor_reduce(out=slot_sb[:, 0:1], in_=et_sb[:], axis=AX.X, op=ALU.add))
                        step(v.tensor_tensor(out=slot_sb[:, 1:2], in0=slot_sb[:, 0:1],
                                             in1=loc_sb[:, g_:g_ + 1], op=ALU.add))
                        i = v.tensor_copy(slot_u[:], slot_sb[:, 1:2])
                        VC.inc(i)

                    @bs.gpsimd
                    def _(g):
                        v_ = VC.n
                        g.wait_ge(VC.h, v_)
                        i = g.indirect_dma_start(
                            out=sorted_o, out_offset=IndirectOffsetOnAxis(slot_u[:, 0:1], axis=0),
                            in_=nat_sb[:], in_offset=None)
                        GP.inc(i, 16)


        for b in (range(B_LOC) if part in ("all", "post") else []):
            tnr, vcr, scr_ = TN.n, VC.n, SC.n
            with nc.Block() as brl:
                @brl.sync
                def _(sy):
                    sy.wait_ge(TN.h, tnr)
                    sy.wait_ge(VC.h, vcr)
                    sy.wait_ge(SC.h, scr_)
                    LD.inc(sy.dma_start(
                        out=xn_sb[:].rearrange("p (c t) -> p c t", c=8),
                        in_=xn1T_d.rearrange("(c p) t -> p c t", p=128)[:, :, b * T:(b + 1) * T]), 16)
            ld_xn = LD.n
            with nc.Block() as brw:
                @brw.tensor
                def _(tn):
                    tn.wait_ge(LD.h, ld_xn)

            def load_third(i3):
                tn_w = TN.n
                with nc.Block() as blw:
                    @blw.gpsimd
                    def _(g):
                        g.wait_ge(TN.h, tn_w)
                        GD.inc(g.dma_start(out=wbuf[:].rearrange("p (c f) -> p c f", c=8),
                                           in_=qkvw.rearrange("(c p) f -> p c f", p=128)[:, :, i3 * D:(i3 + 1) * D]), 16)
                return GD.n

            def proj_qk(gd_mark, dst, i3):
                for gg in range(4):
                    tn0_, sc0_, vc0_ = TN.n, SC.n, VC.n
                    with nc.Block() as bp:
                        @bp.tensor
                        def _(tn):
                            tn.wait_ge(GD.h, gd_mark)
                            tn.wait_ge(VC.h, vc0_)
                            for jj in range(8):
                                fc, nb_ = 2 * gg + jj // 4, jj % 4
                                if jj >= 2:
                                    tn.wait_ge(SC.h, sc0_ + jj - 1)
                                for dc in range(8):
                                    i = tn.matmul(sb[f"psS{jj % 2}"][:],
                                                  wbuf[:, dc * 1024 + fc * 128: dc * 1024 + fc * 128 + 128],
                                                  xn_sb[:, dc * T + nb_ * 512: dc * T + nb_ * 512 + 512],
                                                  start=(dc == 0), stop=(dc == 7))
                                TN.inc(i)

                        @bp.scalar
                        def _(s):
                            for jj in range(8):
                                fc, nb_ = 2 * gg + jj // 4, jj % 4
                                s.wait_ge(TN.h, tn0_ + jj + 1)
                                dstt = rope_e if (jj // 4) == 0 else rope_o
                                i = s.activation(out=dstt[:, nb_ * 512:(nb_ + 1) * 512],
                                                 in_=sb[f"psS{jj % 2}"][:], func=AF.Identity,
                                                 bias=qkvb_sb[:, i3 * 8 + fc: i3 * 8 + fc + 1])
                                SC.inc(i)

                        @bp.vector
                        def _(v):
                            SC.wait(v)
                            v.wait_ge(LD.h, ld_rope)
                            cE, cO = 2 * gg, 2 * gg + 1
                            v.tensor_tensor(out=rt1, in0=rope_e, in1=cc_sb, op=ALU.mult)
                            v.tensor_tensor(out=rt2, in0=rope_o, in1=ss_sb, op=ALU.mult)
                            v.tensor_tensor(out=dst[:, cE * T:(cE + 1) * T], in0=rt1, in1=rt2,
                                            op=ALU.subtract)
                            v.tensor_tensor(out=rt1, in0=rope_e, in1=ss_sb, op=ALU.mult)
                            v.tensor_tensor(out=rt2, in0=rope_o, in1=cc_sb, op=ALU.mult)
                            i = v.tensor_tensor(out=dst[:, cO * T:(cO + 1) * T], in0=rt1, in1=rt2,
                                                op=ALU.add)
                            VC.inc(i)

            def proj_v(gd_mark):
                tn0_, vc0_ = TN.n, VC.n
                with nc.Block() as bp:
                    @bp.tensor
                    def _(tn):
                        tn.wait_ge(GD.h, gd_mark)
                        tn.wait_ge(VC.h, vc0_)
                        tn.wait_ge(GD.h, GD.n)
                        for jj in range(32):
                            tt, fb = jj // 2, jj % 2
                            if jj >= 2:
                                tn.wait_ge(VC.h, vc0_ + jj - 1)
                            for dc in range(8):
                                tn.matmul(sb[f"psS{jj % 2}"][:],
                                          xn_sb[:, dc * T + tt * 128: dc * T + tt * 128 + 128],
                                          wbuf[:, dc * 1024 + fb * 512: dc * 1024 + fb * 512 + 512],
                                          start=(dc == 0), stop=False)
                            i = tn.matmul(sb[f"psS{jj % 2}"][:], onesb[0:1, 0:128],
                                          bvrow[0:1, fb * 512:(fb + 1) * 512],
                                          start=False, stop=True)
                            TN.inc(i)

                    @bp.vector
                    def _(v):
                        for jj in range(32):
                            tt, fb = jj // 2, jj % 2
                            v.wait_ge(TN.h, tn0_ + jj + 1)
                            i = v.tensor_copy(v_sb[:, tt * 1024 + fb * 512: tt * 1024 + (fb + 1) * 512],
                                              sb[f"psS{jj % 2}"][:])
                            VC.inc(i)

            tnw0, scw0, ldw0 = TN.n, SC.n, LD.n
            with nc.Block() as brt:
                @brt.sync
                def _(sy):
                    sy.wait_ge(TN.h, tnw0)
                    sy.wait_ge(SC.h, scw0)
                    sy.wait_ge(LD.h, ldw0)
                    LD.inc(sy.dma_start(out=cc_sb, in_=CCb), 16)
                    LD.inc(sy.dma_start(out=ss_sb, in_=SSb), 16)
            ld_rope = LD.n

            proj_qk(load_third(0), q_sb, 0)
            proj_qk(load_third(1), k_sb, 1)
            proj_v(load_third(2))
            gd_mo = None
            tnw_ = TN.n
            with nc.Block() as blw2:
                @blw2.gpsimd
                def _(g):
                    g.wait_ge(TN.h, tnw_)
                    GD.inc(g.dma_start(out=wbuf[:].rearrange("p (c f) -> p c f", c=8),
                                       in_=outw.rearrange("(c p) f -> p c f", p=128)), 16)
                gd_mo = GD.n
            with nc.Block() as bwait:
                @bwait.tensor
                def _(tn):
                    tn.wait_ge(GD.h, gd_mo)
            emit_attention_batch(nc, sems, sb, 1, b, s1, wo_bias_col=outb_sb)

        with nc.Block() as bf:
            @bf.sync
            def _(sy):
                sy.wait_ge(LD.h, LD.n)
                sy.wait_ge(GD.h, GD.n)
                sy.wait_ge(GP.h, GP.n)
    return nc


W2C_DECLS = [("w2c", [2, FF, D], F32, "ExternalInput"),
             ("w2bf_d", [2, FF, D], BF16, "Internal")]


def build_w2cast(ctx=None):
    nc, stk, sems, io = _phase_setup(ctx, "W_", W2C_DECLS)
    LD, GD = sems["LD"], sems["GD"]
    w2c, w2bf_d = io["w2c"], io["w2bf_d"]
    with stk:
        e = stk.enter_context
        w2b = [e(nc.sbuf_tensor(f"w2b{i}", [128, D], BF16)) for i in range(2)]
        gd0 = GD.n
        with nc.Block() as bw2:
            @bw2.gpsimd
            def _(g):
                for jj in range(64):
                    el, fc = jj // 32, jj % 32
                    if jj >= 2:
                        g.wait_ge(GD.h, gd0 + (2 * (jj - 2) + 2) * 16)
                    GD.inc(g.dma_start(out=w2b[jj % 2][:],
                                       in_=bass.AP(w2c.tensor, el * FF * D + fc * 128 * D,
                                                   [[D, 128], [1, D]])), 16)
                    g.wait_ge(GD.h, gd0 + (2 * jj + 1) * 16)
                    GD.inc(g.dma_start(out=bass.AP(w2bf_d.tensor, el * FF * D + fc * 128 * D,
                                                   [[D, 128], [1, D]]),
                                       in_=w2b[jj % 2][:]), 16)
    return nc


D_DECLS = [("sorted_all", [8 * TOK, D], BF16, "ExternalInput"),
           ("counts_all", [8, NE], F32, "ExternalInput"),
           ("ESEL", [NE, 2], F32, "ExternalInput"), ("SELM", [NE, 2], F32, "ExternalInput"),
           ("w1c", [2, D, FF], F32, "ExternalInput"), ("w2c", [2, FF, D], F32, "ExternalInput"),
           ("LI8", [8, 8], F32, "ExternalInput"), ("SLOTC", [128, NE], F32, "ExternalInput"),
           ("C4096", [128, 8], F32, "ExternalInput"), ("EYEB", [128, 128], BF16, "ExternalInput"),
           ("eo", [2 * CAP, D], BF16, "ExternalOutput"),
           ("w2bf_d", [2, FF, D], BF16, "Internal")]


def build_D(ctx=None, skip_w2cast=False, eo_cb=None):
    nc, stk, sems, io = _phase_setup(ctx, "D_", D_DECLS)
    LD, GD, TN, VC, SC, GP = (sems[n] for n in ("LD", "GD", "TN", "VC", "SC", "GP"))
    (sorted_all, counts_all, ESEL, SELM, w1c, w2c, LI8, SLOTC, C4096, EYEB, eo,
     w2bf_d) = (
        io[n] for n in ("sorted_all", "counts_all", "ESEL", "SELM", "w1c", "w2c",
                        "LI8", "SLOTC", "C4096", "EYEB", "eo", "w2bf_d"))
    tbl_d = nc.dram_tensor("tbl_d", [1, 64], F32, kind="Internal").ap()
    TB = 256

    with stk:
        e = stk.enter_context
        li8 = e(nc.sbuf_tensor("li8", [8, 8], F32))
        selm = e(nc.sbuf_tensor("selm", [NE, 2], F32))
        esel = e(nc.sbuf_tensor("esel", [NE, 2], F32))
        csel = e(nc.sbuf_tensor("csel", [8, 2], F32))
        slotc = e(nc.sbuf_tensor("slotc", [128, NE], F32))
        c4096 = e(nc.sbuf_tensor("c4096", [128, 8], F32))
        eyeb = e(nc.sbuf_tensor("eyeb", [128, 128], BF16))

        cntsT = e(nc.sbuf_tensor("cntsT", [NE, 8], F32))
        small = e(nc.sbuf_tensor("small", [8, 4], F32))
        cumt = e(nc.sbuf_tensor("cumt", [128, 8], F32))
        cumpt = e(nc.sbuf_tensor("cumpt", [128, 8], F32))
        lstt = e(nc.sbuf_tensor("lstt", [128, 8], F32))
        addt = e(nc.sbuf_tensor("addt", [128, 8], F32))
        mt1 = e(nc.sbuf_tensor("mt1", [128, NE], F32))
        mt2 = e(nc.sbuf_tensor("mt2", [128, NE], F32))
        acc = e(nc.sbuf_tensor("acc", [128, NE], F32))
        gidx_u = e(nc.sbuf_tensor("gidx_u", [128, NE], U32))
        w1_sb = e(nc.sbuf_tensor("w1_sb", [128, 8 * FF], BF16))
        w2b = [e(nc.sbuf_tensor(f"w2b{i}", [128, D], BF16)) for i in range(2)]
        gath = e(nc.sbuf_tensor("gath", [128, 2 * D], BF16))
        dispT = e(nc.sbuf_tensor("dispT", [128, 8 * TB], BF16))
        h_sb = e(nc.sbuf_tensor("h_sb", [128, 32 * TB], BF16))
        out_sb = e(nc.sbuf_tensor("out_sb", [128, 2 * D], BF16))
        psO = [e(nc.psum_tensor(f"psO{i}", [128, 512], F32)) for i in range(6)]
        ps_trb = e(nc.psum_tensor("ps_trb", [128, 1024], BF16))

        with nc.Block() as b0:
            @b0.sync
            def _(sy):
                for d, s_ in ((li8[:], LI8), (selm[:], SELM), (slotc[:], SLOTC),
                              (c4096[:], C4096), (eyeb[:], EYEB), (esel[:], ESEL)):
                    LD.inc(sy.dma_start(out=d, in_=s_), 16)
                LD.inc(sy.dma_start(out=cntsT[:], in_=bass.AP(counts_all.tensor, 0,
                                                              [[1, NE], [NE, 8]]),
                                    allow_slow_non_contiguous=True), 16)

            tn_d0 = TN.n
            @b0.tensor
            def _(tn):
                LD.wait(tn)
                i = tn.matmul(psO[2][0:8, 0:2], cntsT[:], esel[:], start=True, stop=True)
                TN.inc(i)

            @b0.vector
            def _(v):
                v.wait_ge(TN.h, tn_d0 + 1)
                VC.inc(v.tensor_copy(csel[:], psO[2][0:8, 0:2]))

            @b0.tensor
            def _(tn):
                VC.wait(tn)
                i = tn.matmul(psO[0][0:8, 0:2], li8[:], csel[:], start=True, stop=True)
                TN.inc(i)
                i = tn.matmul(psO[1][0:8, 0:2], cntsT[:], selm[:], start=True, stop=True)
                TN.inc(i)

            @b0.vector
            def _(v):
                TN.wait(v)
                v.tensor_copy(small[:, 0:2], psO[0][0:8, 0:2])
                i = v.tensor_copy(small[:, 2:4], psO[1][0:8, 0:2])
                VC.inc(i)

            @b0.sync
            def _(sy):
                VC.wait(sy)
                LD.inc(sy.dma_start(out=bass.AP(tbl_d.tensor, 0, [[2, 8], [1, 2]]),
                                    in_=small[:, 0:2]), 16)
                LD.inc(sy.dma_start(out=bass.AP(tbl_d.tensor, 16, [[2, 8], [1, 2]]),
                                    in_=small[:, 2:4]), 16)
                sy.wait_ge(LD.h, LD.n)

        if not skip_w2cast:
            gd0 = GD.n
            with nc.Block() as bw2:
                @bw2.gpsimd
                def _(g):
                    for jj in range(64):
                        el, fc = jj // 32, jj % 32
                        if jj >= 2:
                            g.wait_ge(GD.h, gd0 + (2 * (jj - 2) + 2) * 16)
                        GD.inc(g.dma_start(out=w2b[jj % 2][:],
                                           in_=bass.AP(w2c.tensor, el * FF * D + fc * 128 * D,
                                                       [[D, 128], [1, D]])), 16)
                        g.wait_ge(GD.h, gd0 + (2 * jj + 1) * 16)
                        GD.inc(g.dma_start(out=bass.AP(w2bf_d.tensor, el * FF * D + fc * 128 * D,
                                                       [[D, 128], [1, D]]),
                                           in_=w2b[jj % 2][:]), 16)

        for el in range(2):
            tnw = TN.n
            with nc.Block() as bw1:
                @bw1.gpsimd
                def _(g):
                    g.wait_ge(TN.h, tnw)
                    g.wait_ge(LD.h, LD.n)
                    for dc in range(8):
                        GD.inc(g.dma_start(out=w1_sb[:, dc * FF:(dc + 1) * FF],
                                           in_=bass.AP(w1c.tensor, el * D * FF + dc * 128 * FF,
                                                       [[FF, 128], [1, FF]])), 16)
            gd_w1 = GD.n

            ld1, vc1, gp1 = LD.n, VC.n, GP.n
            with nc.Block() as bgi:
                @bgi.sync
                def _(sy):
                    sy.wait_ge(VC.h, vc1)
                    sy.wait_ge(GP.h, gp1)
                    LD.inc(sy.dma_start(out=cumt[:], in_=bass.AP(tbl_d.tensor, el, [[0, 128], [2, 8]]), allow_slow_non_contiguous=True), 16)
                    LD.inc(sy.dma_start(out=cumpt[:, 1:8], in_=bass.AP(tbl_d.tensor, el, [[0, 128], [2, 7]]), allow_slow_non_contiguous=True), 16)
                    LD.inc(sy.dma_start(out=lstt[:], in_=bass.AP(tbl_d.tensor, 16 + el, [[0, 128], [2, 8]]), allow_slow_non_contiguous=True), 16)

                @bgi.vector
                def _(v):
                    def step(instr):
                        VC.inc(instr)
                        v.wait_ge(VC.h, VC.n)
                    v.wait_ge(LD.h, ld1 + 48)
                    step(v.memset(cumpt[:, 0:1], 0.0))
                    step(v.tensor_tensor(out=addt[:], in0=lstt[:], in1=cumpt[:], op=ALU.subtract))
                    step(v.tensor_tensor(out=addt[:], in0=addt[:], in1=c4096[:], op=ALU.add))
                    step(v.memset(acc[:], 0.0))
                    for cc_ in range(8):
                        step(v.tensor_scalar(out=mt1[:], in0=slotc[:], scalar1=cumpt[:, cc_:cc_ + 1],
                                             scalar2=None, op0=ALU.is_ge))
                        step(v.tensor_scalar(out=mt2[:], in0=slotc[:], scalar1=cumt[:, cc_:cc_ + 1],
                                             scalar2=None, op0=ALU.is_lt))
                        step(v.tensor_tensor(out=mt1[:], in0=mt1[:], in1=mt2[:], op=ALU.mult))
                        step(v.tensor_scalar(out=mt2[:], in0=slotc[:], scalar1=addt[:, cc_:cc_ + 1],
                                             scalar2=None, op0=ALU.add))
                        step(v.tensor_tensor(out=mt1[:], in0=mt1[:], in1=mt2[:], op=ALU.mult))
                        step(v.tensor_tensor(out=acc[:], in0=acc[:], in1=mt1[:], op=ALU.add))
                    step(v.tensor_copy(gidx_u[:], acc[:]))
            vc_gi = VC.n

            for tb in range(8):
                gp2, tn2, vc2, sc2, ld2 = GP.n, TN.n, VC.n, SC.n, LD.n
                with nc.Block() as bg:
                    @bg.gpsimd
                    def _(g):
                        g.wait_ge(VC.h, vc_gi)
                        g.wait_ge(TN.h, tn2)
                        for i4 in range(2):
                            f_ = tb * 2 + i4
                            GP.inc(g.indirect_dma_start(
                                out=gath[:, i4 * D:(i4 + 1) * D], out_offset=None,
                                in_=sorted_all,
                                in_offset=IndirectOffsetOnAxis(gidx_u[:, f_:f_ + 1], axis=0)), 16)

                with nc.Block() as bt:
                    @bt.tensor
                    def _(tn):
                        tn.wait_ge(GP.h, gp2 + 32)
                        tn.wait_ge(VC.h, vc2)
                        for i4 in range(2):
                            if i4 == 1:
                                tn.wait_ge(VC.h, vc2 + 1)
                            for dc in range(8):
                                i = tn.matmul(ps_trb[:, dc * 128:(dc + 1) * 128],
                                              gath[:, i4 * D + dc * 128: i4 * D + dc * 128 + 128],
                                              eyeb[:], start=True, stop=True, is_transpose=True)
                            TN.inc(i)

                    @bt.vector
                    def _(v):
                        for i4 in range(2):
                            v.wait_ge(TN.h, tn2 + i4 + 1)
                            for dc in range(8):
                                i = v.tensor_copy(dispT[:, dc * TB + i4 * 128: dc * TB + (i4 + 1) * 128],
                                                  ps_trb[:, dc * 128:(dc + 1) * 128])
                            VC.inc(i)

                tn3, sc3 = TN.n, SC.n
                with nc.Block() as bh:
                    @bh.tensor
                    def _(tn):
                        VC.wait(tn)
                        tn.wait_ge(GD.h, gd_w1)
                        for fc in range(32):
                            if fc >= 2:
                                tn.wait_ge(SC.h, sc3 + fc - 1)
                            for dc in range(8):
                                i = tn.matmul(psO[4 + (fc % 2)][:, 0:TB],
                                              w1_sb[:, dc * FF + fc * 128: dc * FF + fc * 128 + 128],
                                              dispT[:, dc * TB:(dc + 1) * TB],
                                              start=(dc == 0), stop=(dc == 7))
                            TN.inc(i)

                    @bh.scalar
                    def _(s):
                        for fc in range(32):
                            s.wait_ge(TN.h, tn3 + fc + 1)
                            i = s.activation(out=h_sb[:, fc * TB:(fc + 1) * TB],
                                             in_=psO[4 + (fc % 2)][:, 0:TB], func=AF.Relu)
                            SC.inc(i)

                tn4, sc4, vc4, ld4 = TN.n, SC.n, VC.n, LD.n
                with nc.Block() as bo:
                    @bo.sync
                    def _(sy):
                        sy.wait_ge(VC.h, vc4)
                        for fc in range(32):
                            if fc >= 2:
                                sy.wait_ge(TN.h, tn4 + fc - 1)
                            LD.inc(sy.dma_start(out=w2b[fc % 2][:],
                                                in_=bass.AP(w2bf_d.tensor, el * FF * D + fc * 128 * D,
                                                            [[D, 128], [1, D]])), 16)

                    @bo.tensor
                    def _(tn):
                        for fc in range(32):
                            tn.wait_ge(LD.h, ld4 + (fc + 1) * 16)
                            tn.wait_ge(SC.h, sc4 - 32 + fc + 1)
                            for tt in range(2):
                                for db in range(2):
                                    i = tn.matmul(psO[tt * 2 + db][:],
                                                  h_sb[:, fc * TB + tt * 128: fc * TB + (tt + 1) * 128]
                                                  if tt == 0 else
                                                  h_sb[:, fc * TB + 128: fc * TB + 256],
                                                  w2b[fc % 2][:, db * 512:(db + 1) * 512],
                                                  start=(fc == 0), stop=(fc == 31))
                            TN.inc(i)

                    @bo.vector
                    def _(v):
                        v.wait_ge(TN.h, TN.n)
                        for tt in range(2):
                            for db in range(2):
                                i = v.tensor_copy(out_sb[:, tt * D + db * 512: tt * D + (db + 1) * 512],
                                                  psO[tt * 2 + db][:])
                        VC.inc(i)

                    @bo.sync
                    def _(sy):
                        VC.wait(sy)
                        LD.inc(sy.dma_start(
                            out=bass.AP(eo.tensor, el * CAP * D + tb * TB * D,
                                        [[D, 128], [128 * D, 2], [1, D]]),
                            in_=out_sb[:].rearrange("p (t d) -> p t d", t=2)), 16)

        with nc.Block() as bf:
            @bf.sync
            def _(sy):
                sy.wait_ge(LD.h, LD.n)
                sy.wait_ge(GD.h, GD.n)
                sy.wait_ge(GP.h, GP.n)
    return nc


E_DECLS = [("x1", [D, TOK], F32, "ExternalInput"), ("s1", [D, TOK], BF16, "ExternalInput"),
           ("eo_all", [NE * CAP, D], BF16, "ExternalInput"),
           ("idxo", [128, 32], F32, "ExternalInput"), ("loco", [128, 32], F32, "ExternalInput"),
           ("gso", [128, 32], F32, "ExternalInput"), ("EYEB", [128, 128], BF16, "ExternalInput"),
           ("xout", [D, TOK], F32, "ExternalOutput")]


def build_E(ctx=None):
    nc, stk, sems, io = _phase_setup(ctx, "E_", E_DECLS)
    LD, GD, TN, VC, SC, GP = (sems[n] for n in ("LD", "GD", "TN", "VC", "SC", "GP"))
    (x1, s1, eo_all, idxo, loco, gso, EYEB, xout) = (
        io[n] for n in ("x1", "s1", "eo_all", "idxo", "loco", "gso", "EYEB", "xout"))

    with stk:
        e = stk.enter_context
        eyeb = e(nc.sbuf_tensor("eyeb", [128, 128], BF16))
        idx_sb = e(nc.sbuf_tensor("idx_sb", [128, 32], F32))
        loc_sb = e(nc.sbuf_tensor("loc_sb", [128, 32], F32))
        gs_sb = e(nc.sbuf_tensor("gs_sb", [128, 32], F32))
        tmp = [e(nc.sbuf_tensor(f"tmp{i}", [128, 8], F32)) for i in range(2)]
        gi_u = [e(nc.sbuf_tensor(f"gi_u{i}", [128, 1], U32)) for i in range(2)]
        moe = [e(nc.sbuf_tensor(f"moe{i}", [128, D], BF16)) for i in range(2)]
        gnat = [e(nc.sbuf_tensor(f"gnat{i}", [128, D], BF16)) for i in range(2)]
        gT = e(nc.sbuf_tensor("gT", [128, D], BF16))
        s1t = [e(nc.sbuf_tensor(f"s1t{i}", [128, D], BF16)) for i in range(2)]
        x1t = [e(nc.sbuf_tensor(f"x1t{i}", [128, D], F32)) for i in range(2)]
        t2 = e(nc.sbuf_tensor("t2", [128, D], BF16))
        outt = [e(nc.sbuf_tensor(f"outt{i}", [128, D], F32)) for i in range(2)]
        ps_trb = e(nc.psum_tensor("ps_trb", [128, 1024], BF16))

        x1_r = x1.rearrange("(c p) t -> p c t", p=128)
        s1_r = s1.rearrange("(c p) t -> p c t", p=128)
        xo_r = xout.rearrange("(c p) t -> p c t", p=128)

        with nc.Block() as b0:
            @b0.sync
            def _(sy):
                for d, s_ in ((eyeb[:], EYEB), (idx_sb[:], idxo), (loc_sb[:], loco),
                              (gs_sb[:], gso)):
                    LD.inc(sy.dma_start(out=d, in_=s_), 16)
        ld_c0 = LD.n

        marks = {}
        for g_ in range(32):
            pr = g_ % 2
            with nc.Block() as bt:
                @bt.vector
                def _(v):
                    def step(instr):
                        VC.inc(instr)
                        v.wait_ge(VC.h, VC.n)
                    if g_ < 2:
                        v.wait_ge(LD.h, ld_c0)
                    else:
                        v.wait_ge(GP.h, marks[g_ - 2]["gp"])
                    step(v.tensor_scalar(out=tmp[pr][:, 0:1], in0=loc_sb[:, g_:g_ + 1],
                                         scalar1=float(CAP - 1), scalar2=None, op0=ALU.min))
                    step(v.scalar_tensor_tensor(out=tmp[pr][:, 1:2], in0=idx_sb[:, g_:g_ + 1],
                                                scalar=float(CAP), in1=tmp[pr][:, 0:1],
                                                op0=ALU.mult, op1=ALU.add))
                    step(v.tensor_copy(gi_u[pr][:], tmp[pr][:, 1:2]))
                    step(v.tensor_scalar(out=tmp[pr][:, 2:3], in0=loc_sb[:, g_:g_ + 1],
                                         scalar1=float(CAP), scalar2=None, op0=ALU.is_lt))
                    step(v.tensor_tensor(out=tmp[pr][:, 3:4], in0=gs_sb[:, g_:g_ + 1],
                                         in1=tmp[pr][:, 2:3], op=ALU.mult))

                @bt.gpsimd
                def _(g):
                    VC.wait(g)
                    if g_ >= 2:
                        g.wait_ge(VC.h, marks[g_ - 2]["vc_scale"])
                    GP.inc(g.indirect_dma_start(
                        out=moe[pr][:], out_offset=None, in_=eo_all,
                        in_offset=IndirectOffsetOnAxis(gi_u[pr][:, 0:1], axis=0)), 16)
                marks.setdefault(g_, {})["gp"] = GP.n

                @bt.sync
                def _(sy):
                    if g_ >= 2:
                        sy.wait_ge(VC.h, marks[g_ - 2]["vc_comb"])
                    LD.inc(sy.dma_start(out=s1t[pr][:].rearrange("p (c t) -> p c t", c=8),
                                        in_=s1_r[:, :, g_ * 128:(g_ + 1) * 128]), 16)
                    LD.inc(sy.dma_start(out=x1t[pr][:].rearrange("p (c t) -> p c t", c=8),
                                        in_=x1_r[:, :, g_ * 128:(g_ + 1) * 128]), 16)
                marks[g_]["ld"] = LD.n

                @bt.vector
                def _(v):
                    GP.wait(v)
                    if g_ >= 2:
                        v.wait_ge(TN.h, marks[g_ - 2]["tn"])
                    i = v.tensor_scalar(out=gnat[pr][:], in0=moe[pr][:], scalar1=tmp[pr][:, 3:4],
                                        scalar2=None, op0=ALU.mult)
                    VC.inc(i)
                marks[g_]["vc_scale"] = VC.n

                @bt.scalar
                def _(s):
                    VC.wait(s)
                    if g_ >= 1:
                        s.wait_ge(TN.h, marks[g_ - 1]["tn"])
                    SC.inc(s.activation(out=gnat[pr][:], in_=gnat[pr][:], func=AF.Silu))
                marks[g_]["sc"] = SC.n

                @bt.tensor
                def _(tn):
                    SC.wait(tn)
                    if g_ >= 1:
                        tn.wait_ge(VC.h, marks[g_ - 1]["vc_evac"])
                    for dc in range(8):
                        i = tn.matmul(ps_trb[:, dc * 128:(dc + 1) * 128],
                                      gnat[pr][:, dc * 128:(dc + 1) * 128], eyeb[:],
                                      start=True, stop=True, is_transpose=True)
                    TN.inc(i)
                marks[g_]["tn"] = TN.n

                @bt.vector
                def _(v):
                    v.wait_ge(TN.h, marks[g_]["tn"])
                    for dc in range(8):
                        i = v.tensor_copy(gT[:, dc * 128:(dc + 1) * 128],
                                          ps_trb[:, dc * 128:(dc + 1) * 128])
                    VC.inc(i)
                marks[g_]["vc_evac"] = VC.n

                @bt.vector
                def _(v):
                    v.wait_ge(LD.h, marks[g_]["ld"])
                    if g_ >= 2:
                        v.wait_ge(LD.h, marks[g_ - 2]["st"])
                    v.tensor_tensor(out=t2[:], in0=s1t[pr][:], in1=gT[:], op=ALU.mult)
                    i = v.tensor_tensor(out=outt[pr][:], in0=t2[:], in1=x1t[pr][:], op=ALU.add)
                    VC.inc(i)
                marks[g_]["vc_comb"] = VC.n

                @bt.sync
                def _(sy):
                    VC.wait(sy)
                    LD.inc(sy.dma_start(out=xo_r[:, :, g_ * 128:(g_ + 1) * 128],
                                        in_=outt[pr][:].rearrange("p (c t) -> p c t", c=8)), 16)
                marks[g_]["st"] = LD.n

        with nc.Block() as bf:
            @bf.sync
            def _(sy):
                sy.wait_ge(LD.h, LD.n)
                sy.wait_ge(GP.h, GP.n)
    return nc


_PROGS = {}


def _prog(name, builder):
    if name not in _PROGS:
        _PROGS[name] = builder()
    return _PROGS[name]


def _qkv_perm():
    perm = np.zeros(3072, np.int64)
    for h in range(H):
        gg, s = h // 4, h % 4
        for d_ in range(DH):
            fold = h * DH + d_
            if d_ % 2 == 0:
                new = (2 * gg) * 128 + s * 32 + d_ // 2
            else:
                new = (2 * gg + 1) * 128 + s * 32 + (d_ - 1) // 2
            perm[new] = fold
            perm[1024 + new] = 1024 + fold
    perm[2048:] = np.arange(2048, 3072)
    return perm


def kernel_unfused(**inputs):
    f = np.float32
    cst = host_consts()
    x = np.ascontiguousarray(np.asarray(inputs["L0_attn_x"], f))
    onesf = np.ones((128, 128), f)
    lstf = (np.arange(128)[:, None] < np.arange(128)[None, :]).astype(f)
    zm4 = np.ascontiguousarray(np.concatenate(list(cst["ZM"]), axis=1))
    eyeb = np.eye(128).astype(BF)
    perm = _qkv_perm()

    xT = [np.ascontiguousarray(x[2 * c:2 * c + 2].reshape(TOK, D).T) for c in range(NCORES)]

    wq = np.ascontiguousarray(np.asarray(inputs["L0_attn_Wq"], f))
    wk = np.ascontiguousarray(np.asarray(inputs["L0_attn_Wk"], f))
    wv = np.ascontiguousarray(np.asarray(inputs["L0_attn_Wv"], f))
    wo = np.ascontiguousarray(np.asarray(inputs["L0_attn_Wo"], f))
    rel = np.ascontiguousarray(np.asarray(inputs["L0_attn_rel_emb"], f))
    resA = run_bass_kernel_spmd(
        _prog("A", build_A),
        [dict(xT=xT[c], wq=wq, wk=wk, wv=wv, wo=wo, relemb=rel,
              OH=cst["OH"], ONESB=cst["ONESB"]) for c in range(NCORES)],
        core_ids=list(range(NCORES)))

    wfcT = np.ascontiguousarray(np.asarray(inputs["L0_ffn_W_fc"], f).T)
    wprojT = np.ascontiguousarray(np.asarray(inputs["L0_ffn_W_proj"], f).T)
    lnw = np.ascontiguousarray(np.asarray(inputs["L0_ffn_ln_w"], f).reshape(32, 128).T)
    lnb = np.ascontiguousarray(np.asarray(inputs["L0_ffn_ln_b"], f).reshape(32, 128).T)
    resB = run_bass_kernel_spmd(
        _prog("B", build_B),
        [dict(xT=xT[c], xn0=np.asarray(resA.results[c]["xn0"]),
              s0=np.asarray(resA.results[c]["s0"]),
              wfcT=wfcT, wprojT=wprojT, lnw=lnw, lnb=lnb,
              ONESB=cst["ONESB"], ONESF=onesf) for c in range(NCORES)],
        core_ids=list(range(NCORES)))
    x1 = [np.asarray(resB.results[c]["x1"]) for c in range(NCORES)]

    qkvw_p = np.ascontiguousarray(np.asarray(inputs["L1_attn_qkv_w"], f)[:, perm])
    qkvb_p = np.ascontiguousarray(np.asarray(inputs["L1_attn_qkv_b"], f)[perm].reshape(24, 128).T)
    bv = np.ascontiguousarray(np.asarray(inputs["L1_attn_qkv_b"], f)[2048:3072][None, :])
    outw = np.ascontiguousarray(np.asarray(inputs["L1_attn_out_w"], f))
    outb = np.ascontiguousarray(np.asarray(inputs["L1_attn_out_b"], f).reshape(8, 128).T)
    wg = np.ascontiguousarray(np.asarray(inputs["L1_ffn_Wg"], f))
    resC = run_bass_kernel_spmd(
        _prog("C", build_C),
        [dict(x1=x1[c], qkvw=qkvw_p, qkvb=qkvb_p, outw=outw, outb=outb, wg=wg,
              CCb=cst["CC"].astype(BF), SSb=cst["SS"].astype(BF), ZM4=zm4.astype(BF), BV=bv,
              ONESB=cst["ONESB"], ONESF=onesf, LSTF=lstf,
              IOTA16=cst["IOTA16"], LSU16F=cst["LSU16"], EYEB=eyeb) for c in range(NCORES)],
        core_ids=list(range(NCORES)))

    sorted_all = np.ascontiguousarray(np.concatenate(
        [np.asarray(resC.results[c]["sorted_o"]) for c in range(NCORES)], 0))
    counts_all = np.ascontiguousarray(np.concatenate(
        [np.asarray(resC.results[c]["counts_o"]) for c in range(NCORES)], 0))
    w1 = np.asarray(inputs["L1_ffn_w1"], f)
    w2 = np.asarray(inputs["L1_ffn_w2"], f)
    dmaps = []
    for c in range(NCORES):
        selm = (np.arange(NE)[:, None] < (2 * c + np.arange(2))[None, :]).astype(f)
        esel = (np.arange(NE)[:, None] == (2 * c + np.arange(2))[None, :]).astype(f)
        dmaps.append(dict(
            sorted_all=sorted_all, counts_all=counts_all,
            ESEL=np.ascontiguousarray(esel),
            SELM=np.ascontiguousarray(selm),
            w1c=np.ascontiguousarray(w1[2 * c:2 * c + 2]),
            w2c=np.ascontiguousarray(w2[2 * c:2 * c + 2]),
            LI8=cst["LI8"], SLOTC=cst["SLOT"],
            C4096=(4096.0 * np.arange(8, dtype=f))[None, :].repeat(128, 0),
            EYEB=eyeb))
    resD = run_bass_kernel_spmd(_prog("D", build_D), dmaps, core_ids=list(range(NCORES)))

    eo_all = np.ascontiguousarray(np.concatenate(
        [np.asarray(resD.results[c]["eo"]) for c in range(NCORES)], 0))
    resE = run_bass_kernel_spmd(
        _prog("E", build_E),
        [dict(x1=x1[c], s1=np.asarray(resC.results[c]["s1"]), eo_all=eo_all,
              idxo=np.asarray(resC.results[c]["idxo"]),
              loco=np.asarray(resC.results[c]["loco"]),
              gso=np.asarray(resC.results[c]["gso"]), EYEB=eyeb) for c in range(NCORES)],
        core_ids=list(range(NCORES)))

    out = np.concatenate(
        [np.asarray(resE.results[c]["xout"], f).T.reshape(B_LOC, T, D) for c in range(NCORES)], 0)
    return out



def emit_allgather_issue(nc, sems, cc, in_ap, out_ap, waits=()):
    with nc.Block() as b:
        @b.gpsimd
        def _(g):
            for sem, val in waits:
                g.wait_ge(sem.h, val)
            i = g.collective_compute("AllGather", ALU.bypass,
                                     replica_groups=[list(range(NCORES))],
                                     ins=[in_ap.opt()], outs=[out_ap.opt()])
            cc.inc(i, 1)


def emit_cc_wait(nc, ccs):
    with nc.Block() as b2:
        for en in ("tensor", "vector", "scalar", "gpsimd", "sync"):
            def _mk(e, _ccs=tuple((c, c.n) for c in ccs)):
                for c, m in _ccs:
                    e.wait_ge(c.h, m)
            getattr(b2, en)(_mk)


def emit_allgather(nc, sems, cc, in_ap, out_ap):
    emit_allgather_issue(nc, sems, cc, in_ap, out_ap)
    emit_cc_wait(nc, [cc])


def build_fused():
    nc, stk, sems = new_prog()
    reg = {}
    ctx = (nc, sems, reg)
    with stk:
        cc = [Sem(stk, nc, f"CC{i}") for i in range(4)]
        build_A(ctx=ctx)
        emit_fence(nc, sems)
        build_w2cast(ctx=ctx)
        build_B(ctx=ctx)
        emit_fence(nc, sems)
        build_C(ctx=ctx, part="pre")
        emit_fence(nc, sems)
        reg["sorted_all"] = nc.dram_tensor("sorted_all", [8 * TOK, D], BF16, kind="Internal").ap()
        reg["counts_all"] = nc.dram_tensor("counts_all", [8, NE], F32, kind="Internal").ap()
        emit_allgather_issue(nc, sems, cc[0], reg["sorted_o"], reg["sorted_all"])
        emit_allgather_issue(nc, sems, cc[1], reg["counts_o"], reg["counts_all"])
        build_C(ctx=ctx, part="post")
        emit_fence(nc, sems)
        emit_cc_wait(nc, [cc[0], cc[1]])
        reg["eo_all"] = nc.dram_tensor("eo_all", [NE * CAP, D], BF16, kind="Internal").ap()

        build_D(ctx=ctx, skip_w2cast=True)
        emit_fence(nc, sems)
        emit_allgather(nc, sems, cc[2], reg["eo"], reg["eo_all"])
        build_E(ctx=ctx)
        emit_fence(nc, sems)
    return nc


def _host_inmaps(inputs):
    f = np.float32
    cst = host_consts()
    x = np.ascontiguousarray(np.asarray(inputs["L0_attn_x"], f))
    onesf = np.ones((128, 128), f)
    lstf = (np.arange(128)[:, None] < np.arange(128)[None, :]).astype(f)
    zm4 = np.ascontiguousarray(np.concatenate(list(cst["ZM"]), axis=1))
    eyeb = np.eye(128).astype(BF)
    perm = _qkv_perm()
    w1 = np.asarray(inputs["L1_ffn_w1"], f)
    w2 = np.asarray(inputs["L1_ffn_w2"], f)
    shared = dict(
        wq=np.ascontiguousarray(np.asarray(inputs["L0_attn_Wq"], f)),
        wk=np.ascontiguousarray(np.asarray(inputs["L0_attn_Wk"], f)),
        wv=np.ascontiguousarray(np.asarray(inputs["L0_attn_Wv"], f)),
        wo=np.ascontiguousarray(np.asarray(inputs["L0_attn_Wo"], f)),
        relemb=np.ascontiguousarray(np.asarray(inputs["L0_attn_rel_emb"], f)),
        OH=cst["OH"], ONESB=cst["ONESB"], ONESF=onesf,
        wfcT=np.ascontiguousarray(np.asarray(inputs["L0_ffn_W_fc"], f).T),
        wprojT=np.ascontiguousarray(np.asarray(inputs["L0_ffn_W_proj"], f).T),
        lnw=np.ascontiguousarray(np.asarray(inputs["L0_ffn_ln_w"], f).reshape(32, 128).T),
        lnb=np.ascontiguousarray(np.asarray(inputs["L0_ffn_ln_b"], f).reshape(32, 128).T),
        qkvw=np.ascontiguousarray(np.asarray(inputs["L1_attn_qkv_w"], f)[:, perm]),
        qkvb=np.ascontiguousarray(np.asarray(inputs["L1_attn_qkv_b"], f)[perm].reshape(24, 128).T),
        BV=np.ascontiguousarray(np.asarray(inputs["L1_attn_qkv_b"], f)[2048:3072][None, :]),
        outw=np.ascontiguousarray(np.asarray(inputs["L1_attn_out_w"], f)),
        outb=np.ascontiguousarray(np.asarray(inputs["L1_attn_out_b"], f).reshape(8, 128).T),
        wg=np.ascontiguousarray(np.asarray(inputs["L1_ffn_Wg"], f)),
        CCb=cst["CC"].astype(BF), SSb=cst["SS"].astype(BF), ZM4=zm4.astype(BF),
        LSTF=lstf, IOTA16=cst["IOTA16"], LSU16F=cst["LSU16"], EYEB=eyeb,
        LI8=cst["LI8"], SLOTC=cst["SLOT"],
        C4096=(4096.0 * np.arange(8, dtype=f))[None, :].repeat(128, 0),
    )
    maps = []
    for c in range(NCORES):
        m = dict(shared)
        m["xT"] = np.ascontiguousarray(x[2 * c:2 * c + 2].reshape(TOK, D).T)
        m["SELM"] = np.ascontiguousarray(
            (np.arange(NE)[:, None] < (2 * c + np.arange(2))[None, :]).astype(f))
        m["ESEL"] = np.ascontiguousarray(
            (np.arange(NE)[:, None] == (2 * c + np.arange(2))[None, :]).astype(f))
        m["w1c"] = np.ascontiguousarray(w1[2 * c:2 * c + 2])
        m["w2c"] = np.ascontiguousarray(w2[2 * c:2 * c + 2])
        maps.append(m)
    return maps


def kernel(**inputs):
    f = np.float32
    maps = _host_inmaps(inputs)
    res = run_bass_kernel_spmd(_prog("F", build_fused), maps, core_ids=list(range(NCORES)))
    out = np.concatenate(
        [np.asarray(res.results[c]["xout"], f).T.reshape(B_LOC, T, D) for c in range(NCORES)], 0)
    return out
